# version 03
# baseline (speedup 1.0000x reference)
import numpy as np
from contextlib import ExitStack

import concourse.bass as bass
import concourse.mybir as mybir
from concourse.bass_utils import run_bass_kernel_spmd
from concourse.alu_op_type import AluOpType as ALU

F32 = mybir.dt.float32
BF16 = mybir.dt.bfloat16
I32 = mybir.dt.int32
AF = mybir.ActivationFunctionType
AX = mybir.AxisListType

C = 2048
H = 16
HID = 5632
NTOK = 4096
NT = 32
OT = 16
FQKV = 3 * C
NEG = np.float32(-9e15)
EPS = float(np.float32(1e-5))
SCALE = float(np.float32(128.0 ** -0.5))
M32 = 12582912.0
M64 = 6755399441055744.0
ZSH = 384.0

ENGINES = ("sync", "scalar", "vector", "tensor", "gpsimd")
DEBUG_DUMPS = []


class Plan:
    def __init__(self, sem):
        self.ops = []
        self.sem = sem

    def op(self, engine, fn, waits=(), inc=None, inc_n=1):
        w = [(s, int(v)) for (s, v) in waits if v > 0]
        iv = None
        if inc is not None:
            self.sem[inc] = self.sem.get(inc, 0) + inc_n
            iv = self.sem[inc]
        self.ops.append([engine, fn, w, inc, inc_n, iv])

    def dma(self, engine, fn, waits=(), inc=None):
        self.op(engine, fn, waits, inc, 16)

    def cur(self, name):
        return self.sem.get(name, 0)

    def finalize(self):
        last = {}
        for o in self.ops:
            en, fn, waits, inc, inc_n, iv = o
            if en in ("vector", "scalar", "gpsimd") and inc_n != 16:
                if en in last and last[en] not in waits:
                    waits.append(last[en])
                if inc is None:
                    eo = "eo_" + en
                    self.sem[eo] = self.sem.get(eo, 0) + 1
                    o[3], o[5] = eo, self.sem[eo]
                last[en] = (o[3], o[5])


PHASE_LIMIT = 9
ATT_MODE = 4
_PH = [0]


def emit(nc, plan, sems):
    i = _PH[0]
    _PH[0] += 1
    if i >= PHASE_LIMIT:
        return
    plan.finalize()
    with nc.Block() as block:
        def mk(engname):
            def body(eng):
                for (en, fn, waits, inc, inc_n, iv) in plan.ops:
                    if en != engname:
                        continue
                    for (s, v) in waits:
                        eng.wait_ge(sems[s], v)
                    instr = fn(eng)
                    if inc is not None:
                        instr.then_inc(sems[inc], inc_n)
            body.__name__ = "_"
            return body
        for en in ENGINES:
            getattr(block, en)(mk(en))


SEM_NAMES = ["ld", "wld", "vst", "sq", "y1", "xq", "xtp", "xev", "mm", "ev",
             "tp", "tev", "qld", "se", "sv", "mx", "xp",
             "pt", "pe", "pv", "oe", "ot", "ov", "gp", "out", "prx", "gu",
             "dn", "hst", "fx", "dnv",
             "da", "db", "dc", "dd", "de", "df", "dg", "dh", "di", "dj", "dk",
             "eo_vector", "eo_scalar", "eo_gpsimd"]


def build_program():
    _PH[0] = 0
    nc = bass.Bass("TRN2", target_bir_lowering=False)

    xp_d = nc.dram_tensor("xp", [NTOK, C], F32, kind="ExternalInput").ap()
    wqkv = nc.dram_tensor("wqkv", [FQKV, C], F32, kind="ExternalInput").ap()
    wproj = nc.dram_tensor("wproj", [C, C], F32, kind="ExternalInput").ap()
    wg_r = nc.dram_tensor("wg", [HID, C], I32, kind="ExternalInput").ap()
    wu_r = nc.dram_tensor("wu", [HID, C], I32, kind="ExternalInput").ap()
    wd_r = nc.dram_tensor("wd", [C, HID], I32, kind="ExternalInput").ap()
    sg_d = nc.dram_tensor("sg", [HID], F32, kind="ExternalInput").ap()
    su_d = nc.dram_tensor("su", [HID], F32, kind="ExternalInput").ap()
    sd_d = nc.dram_tensor("sd", [C], F32, kind="ExternalInput").ap()
    id_d = nc.dram_tensor("ident", [128, 128], F32, kind="ExternalInput").ap()
    tri_d = nc.dram_tensor("tri", [128, 128], F32, kind="ExternalInput").ap()
    oma_d = nc.dram_tensor("oma", [128, 128], F32, kind="ExternalInput").ap()
    y_d = nc.dram_tensor("y", [2048, C], F32, kind="ExternalOutput").ap()

    WQKVT = nc.dram_tensor("WQKVT", [C, FQKV], BF16, kind="Internal").ap()
    WPROJT = nc.dram_tensor("WPROJT", [C, C], BF16, kind="Internal").ap()
    WGT = nc.dram_tensor("WGT", [C, HID], BF16, kind="Internal").ap()
    WUT = nc.dram_tensor("WUT", [C, HID], BF16, kind="Internal").ap()
    WDT = nc.dram_tensor("WDT", [HID, C], BF16, kind="Internal").ap()
    QT = nc.dram_tensor("QT", [C, 2048], F32, kind="Internal").ap()
    KT = nc.dram_tensor("KT", [C, NTOK], F32, kind="Internal").ap()
    VS = nc.dram_tensor("VS", [NTOK, C], F32, kind="Internal").ap()
    OS = nc.dram_tensor("OS", [2048, C], F32, kind="Internal").ap()
    XMID = nc.dram_tensor("XMID", [2048, C], F32, kind="Internal").ap()
    XHs = nc.dram_tensor("XHs", [C, 2048], BF16, kind="Internal").ap()
    XLs = nc.dram_tensor("XLs", [C, 2048], BF16, kind="Internal").ap()
    FFO = nc.dram_tensor("FFO", [2048, C], F32, kind="Internal").ap()
    scr = {"QT": QT, "KT": KT, "VS": VS, "OS": OS, "XMID": XMID, "FFO": FFO,
           "WQKVT": WQKVT, "WPROJT": WPROJT, "WGT": WGT, "WUT": WUT, "WDT": WDT,
           "XHs": XHs, "XLs": XLs}
    dbg = {}
    for nm in DEBUG_DUMPS:
        t = scr[nm]
        dbg[nm] = nc.dram_tensor("dbg_" + nm, list(t.shape), t.dtype,
                                 kind="ExternalOutput").ap()

    semc = {}
    with ExitStack() as ctx:
        e = ctx.enter_context
        sems = {n: e(nc.semaphore(n)) for n in SEM_NAMES}

        identf = e(nc.sbuf_tensor("identf", [128, 128], F32))
        identb = e(nc.sbuf_tensor("identb", [128, 128], BF16))
        tri = e(nc.sbuf_tensor("tri_s", [128, 128], F32))
        oma = e(nc.sbuf_tensor("oma_s", [128, 128], F32))
        ones = e(nc.sbuf_tensor("ones", [128, C], F32))
        invs = e(nc.sbuf_tensor("invs", [128, NT], F32))
        qsc = e(nc.sbuf_tensor("qsc", [128, NT], F32))
        ksc = e(nc.sbuf_tensor("ksc", [128, NT], F32))
        bcw = e(nc.sbuf_tensor("bcw", [128, 4], F32))
        invl = e(nc.sbuf_tensor("invl", [128, OT * H], F32))
        sgv = e(nc.sbuf_tensor("sgv", [128, 44], F32))
        suv = e(nc.sbuf_tensor("suv", [128, 44], F32))
        sdv = e(nc.sbuf_tensor("sdv", [128, 16], F32))
        st = e(nc.sbuf_tensor("st", [128, 64], F32))
        sws = e(nc.sbuf_tensor("sws", [128, 8], F32))
        cst = e(nc.sbuf_tensor("cst", [128, 4], F32))
        nc.const_aps.register(F32, 0.5, cst[:, 0:1])
        nc.const_aps.register(F32, 0.0, cst[:, 1:2])
        nc.const_aps.register(F32, 1.0, cst[:, 2:3])
        nc.const_aps.register(F32, ZSH, cst[:, 3:4])

        wpart = e(nc.sbuf_tensor("wpart", [128, 64], F32))
        with ExitStack() as _ph:
            _en = _ph.enter_context
            wb0 = _en(nc.sbuf_tensor("wb0", [128, C], F32))
            wb1 = _en(nc.sbuf_tensor("wb1", [128, C], F32))
            ps_sw = _en(nc.psum_tensor("ps_sw", [128, 8], F32))
            sgn = _en(nc.sbuf_tensor("sgn", [128, 128], F32))
            sun = _en(nc.sbuf_tensor("sun", [128, 128], F32))
            sdn = _en(nc.sbuf_tensor("sdn", [128, 128], F32))
            psv = _en(nc.psum_tensor("psv", [128, 512], F32))
            wbuf = [wb0, wb1]
            p = Plan(semc)
            p.dma("sync", lambda s: s.dma_start(out=identf[:], in_=id_d), inc="ld")
            p.dma("sync", lambda s: s.dma_start(out=tri[:], in_=tri_d), inc="ld")
            p.dma("sync", lambda s: s.dma_start(out=oma[:], in_=oma_d), inc="ld")
            p.dma("sync", lambda s: s.dma_start(
                out=sgn[0:44, :], in_=sg_d.rearrange("(a p) -> a p", p=128)), inc="ld")
            p.dma("sync", lambda s: s.dma_start(
                out=sun[0:44, :], in_=su_d.rearrange("(a p) -> a p", p=128)), inc="ld")
            p.dma("sync", lambda s: s.dma_start(
                out=sdn[0:16, :], in_=sd_d.rearrange("(a p) -> a p", p=128)), inc="ld")
            p.op("tensor", lambda t: t.matmul(psv[:, 0:44], sgn[0:44, :],
                                              identf[0:44, 0:44], is_transpose=True,
                                              start=True, stop=True),
                 waits=[("ld", 16 * 6)], inc="tp")
            p.op("tensor", lambda t: t.matmul(psv[:, 64:108], sun[0:44, :],
                                              identf[0:44, 0:44], is_transpose=True,
                                              start=True, stop=True), inc="tp")
            p.op("tensor", lambda t: t.matmul(psv[:, 128:144], sdn[0:16, :],
                                              identf[0:16, 0:16], is_transpose=True,
                                              start=True, stop=True), inc="tp")
            p.op("vector", lambda v: v.tensor_copy(sgv[:], psv[:, 0:44]),
                 waits=[("tp", p.cur("tp"))], inc="vst")
            p.op("vector", lambda v: v.tensor_copy(suv[:], psv[:, 64:108]), inc="vst")
            p.op("vector", lambda v: v.tensor_copy(sdv[:], psv[:, 128:144]), inc="vst")
            p.op("gpsimd", lambda g: g.memset(ones[:], 1.0), inc="gp")
            p.op("gpsimd", lambda g: g.memset(cst[:, 0:1], 0.5), inc="gp")
            p.op("gpsimd", lambda g: g.memset(cst[:, 1:2], 0.0), inc="gp")
            p.op("gpsimd", lambda g: g.memset(cst[:, 2:3], 1.0), inc="gp")
            p.op("gpsimd", lambda g: g.memset(cst[:, 3:4], ZSH), inc="gp")
            p.op("vector", lambda v: v.tensor_copy(identb[:], identf[:]),
                 waits=[("ld", 96)], inc="vst")
            for t in range(64):
                src = (wqkv[t * 128:(t + 1) * 128, :] if t < 48 else
                       wproj[(t - 48) * 128:(t - 47) * 128, :])
                dsem = "da" if t % 2 == 0 else "db"
                waits = [("vst", p.cur("vst") - 1)] if t >= 2 else []
                p.dma("sync", (lambda s, _s=src, _b=wbuf[t % 2]:
                               s.dma_start(out=_b[:], in_=_s)), waits=waits, inc=dsem)
                p.op("vector", (lambda v, _b=wbuf[t % 2], _t=t:
                                v.tensor_reduce(wpart[:, _t:_t + 1], _b[:], AX.X,
                                                ALU.add, apply_absolute_value=True)),
                     waits=[(dsem, p.cur(dsem))], inc="vst")
            p.op("vector", lambda v: v.tensor_reduce(
                sws[:, 0:1], wpart[:, 0:48], AX.X, ALU.add), inc="vst")
            p.op("vector", lambda v: v.tensor_reduce(
                sws[:, 1:2], wpart[:, 48:64], AX.X, ALU.add), inc="vst")
            p.op("tensor", lambda t: t.matmul(ps_sw[0:1, 0:1], sws[:, 0:1],
                                              ones[:, 0:1], start=True, stop=True),
                 waits=[("vst", p.cur("vst")), ("gp", 1)], inc="mm")
            p.op("tensor", lambda t: t.matmul(ps_sw[0:1, 1:2], sws[:, 1:2],
                                              ones[:, 0:1], start=True, stop=True),
                 inc="mm")
            p.op("vector", lambda v: v.tensor_copy(sws[0:1, 2:4], ps_sw[0:1, 0:2]),
                 waits=[("mm", p.cur("mm"))], inc="vst")
            p.op("vector", lambda v: v.tensor_scalar(
                sws[0:1, 4:5], sws[0:1, 2:3], 1.0 / (FQKV * C), 1e-5,
                op0=ALU.mult, op1=ALU.max), inc="vst")
            p.op("vector", lambda v: v.tensor_scalar(
                sws[0:1, 5:6], sws[0:1, 3:4], 1.0 / (C * C), 1e-5,
                op0=ALU.mult, op1=ALU.max), inc="vst")
            p.op("vector", lambda v: v.reciprocal(sws[0:1, 6:8], sws[0:1, 4:6]),
                 inc="vst")
            vv = p.cur("vst")
            p.op("tensor", lambda t: t.matmul(ps_sw[:, 4:8], ones[0:1, 0:128],
                                              sws[0:1, 4:8], start=True, stop=True),
                 waits=[("vst", vv)], inc="mm")
            p.op("vector", lambda v: v.tensor_copy(bcw[:, 0:2], ps_sw[:, 6:8]),
                 waits=[("mm", p.cur("mm"))], inc="vst")
            p.op("vector", lambda v: v.tensor_copy(bcw[:, 2:4], ps_sw[:, 4:6]),
                 inc="vst")
            emit(nc, p, sems)

            with ExitStack() as _ph:
                _en = _ph.enter_context
                ybuf = _en(nc.sbuf_tensor("ybuf", [128, C], F32))
                mbuf = _en(nc.sbuf_tensor("mbuf", [128, C], F32))
                qu32 = _en(nc.sbuf_tensor("qu32", [128, C], F32))
                qu64 = _en(nc.sbuf_tensor("qu64", [128, C], F32))
                tbuf = _en(nc.sbuf_tensor("tbuf", [128, C], BF16))
                te0 = _en(nc.sbuf_tensor("te0", [128, C], BF16))
                te1 = _en(nc.sbuf_tensor("te1", [128, C], BF16))
                pt0 = _en(nc.psum_tensor("pt0", [128, C], BF16))
                pt1 = _en(nc.psum_tensor("pt1", [128, C], BF16))
                tev_ = [te0, te1]
                ptp_ = [pt0, pt1]
                p = Plan(semc)
                spq = {0: None, 1: None}
                for t in range(64):
                    qkv = t < 48
                    src = (wqkv[t * 128:(t + 1) * 128, :] if qkv else
                           wproj[(t - 48) * 128:(t - 47) * 128, :])
                    wb = wbuf[t % 2]
                    dsem = "da" if t % 2 == 0 else "db"
                    waits = [("y1", p.cur("y1") - 1)] if t >= 2 else []
                    p.dma("sync", (lambda s, _s=src, _b=wb: s.dma_start(out=_b[:], in_=_s)),
                          waits=waits, inc=dsem)
                    sc = bcw[:, 0:1] if qkv else bcw[:, 1:2]
                    p.op("scalar", (lambda s, _b=wb, _sc=sc: s.activation(
                        ybuf[:], _b[:], AF.Identity, bias=0.0, scale=_sc)),
                        waits=[(dsem, p.cur(dsem)), ("xq", p.cur("xq"))], inc="y1")
                    p.op("vector", (lambda v: v.tensor_scalar(
                        mbuf[:], ybuf[:], -2.0, 2.0, op0=ALU.max,
                        op1=ALU.min)), waits=[("y1", p.cur("y1"))], inc="vst")
                    p.op("vector", (lambda v: v.tensor_scalar(
                        qu32[:], mbuf[:], M32, -M32, op0=ALU.add, op1=ALU.add)),
                        inc="vst")
                    p.op("vector", (lambda v: v.tensor_scalar(
                        qu64[:], mbuf[:], ZSH, M64, op0=ALU.add, op1=ALU.add)),
                        inc="vst")
                    p.op("vector", (lambda v: v.tensor_scalar(
                        qu64[:], qu64[:], -M64, -ZSH, op0=ALU.add, op1=ALU.add)),
                        inc="vst")
                    p.op("vector", (lambda v: v.scalar_tensor_tensor(
                        ybuf[:], mbuf[:], -1.0, qu64[:], ALU.mult, ALU.add)),
                        inc="vst")
                    p.op("vector", (lambda v: v.tensor_scalar(
                        mbuf[:], ybuf[:], -0.5, 0.5, op0=ALU.max, op1=ALU.min)),
                        inc="vst")
                    p.op("vector", (lambda v: v.tensor_scalar(
                        qu64[:], ybuf[:], 1.0, -0.5, op0=ALU.add, op1=ALU.max)),
                        inc="vst")
                    p.op("vector", (lambda v: v.scalar_tensor_tensor(
                        ybuf[:], qu64[:], 0.0, mbuf[:], ALU.min, ALU.subtract)),
                        inc="vst")
                    p.op("vector", (lambda v: v.scalar_tensor_tensor(
                        mbuf[:], qu32[:], 0.0, ybuf[:], ALU.add, ALU.subtract)),
                        inc="vst")
                    p.op("vector", (lambda v: v.tensor_scalar(
                        tbuf[:], mbuf[:], -1.0, 1.0, op0=ALU.max, op1=ALU.min)),
                        inc="xq")
                    ptp = ptp_[t % 2]
                    for k in range(16):
                        wt = []
                        if k == 0:
                            wt.append(("xq", p.cur("xq")))
                            if t >= 2:
                                wt.append(("tev", p.cur("tev") - 1))
                        p.op("tensor", (lambda te, _k=k, _p=ptp: te.matmul(
                            _p[:, _k * 128:(_k + 1) * 128],
                            tbuf[:, _k * 128:(_k + 1) * 128], identb[:],
                            is_transpose=True, start=(_k % 8 == 0), stop=(_k % 8 == 7))),
                            waits=wt, inc=("tp" if k == 15 else None))
                    evw = [("tp", p.cur("tp"))]
                    if spq[t % 2] is not None:
                        evw.append(spq[t % 2])
                    p.op("vector", (lambda v, _p=ptp, _e=tev_[t % 2]:
                                    v.tensor_copy(_e[:], _p[:])),
                         waits=evw, inc="tev")
                    if qkv:
                        dst = WQKVT.rearrange("(k p) f -> p k f", p=128)[:, :, t * 128:(t + 1) * 128]
                    else:
                        dst = WPROJT.rearrange("(k p) f -> p k f", p=128)[:, :, (t - 48) * 128:(t - 47) * 128]
                    ssem = "dc" if t % 2 == 0 else "dd"
                    p.dma("gpsimd", (lambda g, _d=dst, _e=tev_[t % 2]: g.dma_start(
                        out=_d, in_=_e[:].rearrange("p (k j) -> p k j", j=128))),
                        waits=[("tev", p.cur("tev"))], inc=ssem)
                    spq[t % 2] = (ssem, p.cur(ssem))
                p.op("gpsimd", lambda g: g.engine_nop(),
                     waits=[("dc", p.cur("dc")), ("dd", p.cur("dd"))], inc="gp")
                emit(nc, p, sems)

        _xq_scope = ExitStack()
        xqT = _xq_scope.enter_context(nc.sbuf_tensor("xqT", [128, 16 * NTOK], BF16))
        xqT3 = xqT[:].rearrange("p (k n) -> p k n", n=NTOK)
        with ExitStack() as _ph:
            _en = _ph.enter_context
            xb0 = _en(nc.sbuf_tensor("xb0", [128, C], F32))
            xb1 = _en(nc.sbuf_tensor("xb1", [128, C], F32))
            sqb = _en(nc.sbuf_tensor("sqb", [128, C], F32))
            yb = _en(nc.sbuf_tensor("yb", [128, C], F32))
            mb = _en(nc.sbuf_tensor("mb", [128, C], F32))
            ub = _en(nc.sbuf_tensor("ub", [128, C], F32))
            qbv = _en(nc.sbuf_tensor("qbv", [128, C], BF16))
            px0 = _en(nc.psum_tensor("px0", [128, C], BF16))
            px1 = _en(nc.psum_tensor("px1", [128, C], BF16))
            xb = [xb0, xb1]
            pxt = [px0, px1]
            p = Plan(semc)
            for t in range(NT):
                b = xb[t % 2]
                dsem = "da" if t % 2 == 0 else "db"
                waits = [("y1", p.cur("y1") - 1)] if t >= 2 else []
                p.dma("sync", (lambda s, _b=b, _t=t: s.dma_start(
                    out=_b[:], in_=xp_d[_t * 128:(_t + 1) * 128, :])), waits=waits, inc=dsem)
                ldv = p.cur(dsem)
                o = (t % 2) * 32

                def tiny(fn, waits=()):
                    p.op("vector", fn, waits=waits, inc="vst")
                p.op("vector", (lambda v, _b=b, _o=o: v.tensor_reduce(
                    st[:, _o:_o + 1], _b[:], AX.X, ALU.add, negate=True)),
                    waits=[(dsem, ldv)], inc="vst")
                p.op("vector", (lambda v, _b=b, _o=o: v.tensor_reduce(
                    st[:, _o + 1:_o + 2], _b[:], AX.X, ALU.max)), inc="vst")
                p.op("vector", (lambda v, _b=b, _o=o: v.tensor_reduce(
                    st[:, _o + 2:_o + 3], _b[:], AX.X, ALU.min)), inc="vst")
                p.op("scalar", (lambda s, _b=b, _o=o: s.activation(
                    sqb[:], _b[:], AF.Square, accum_out=st[:, _o + 3:_o + 4])),
                    waits=[(dsem, ldv), ("xq", p.cur("xq"))], inc="sq")
                tiny(lambda v, _o=o: v.tensor_scalar(
                    st[:, _o + 4:_o + 5], st[:, _o:_o + 1], 1.0 / C, None, op0=ALU.mult))
                p.op("vector", (lambda v, _o=o: v.tensor_scalar(
                    st[:, _o + 5:_o + 6], st[:, _o + 3:_o + 4], 1.0 / C, EPS,
                    op0=ALU.mult, op1=ALU.add)),
                    waits=[("sq", p.cur("sq"))], inc="vst")
                tiny(lambda v, _o=o: v.scalar_tensor_tensor(
                    st[:, _o + 6:_o + 7], st[:, _o + 4:_o + 5], -1.0,
                    st[:, _o + 4:_o + 5], ALU.mult, ALU.mult))
                tiny(lambda v, _o=o: v.tensor_tensor(
                    st[:, _o + 7:_o + 8], st[:, _o + 5:_o + 6],
                    st[:, _o + 6:_o + 7], ALU.add))
                p.op("scalar", (lambda s, _o=o: s.activation(
                    st[:, _o + 8:_o + 9], st[:, _o + 7:_o + 8], AF.Sqrt)),
                    waits=[("vst", p.cur("vst"))], inc="sq")
                p.op("vector", (lambda v, _o=o: v.reciprocal(
                    st[:, _o + 9:_o + 10], st[:, _o + 8:_o + 9])),
                    waits=[("sq", p.cur("sq"))], inc="vst")
                tiny(lambda v, _o=o: v.tensor_tensor(
                    st[:, _o + 10:_o + 11], st[:, _o + 1:_o + 2],
                    st[:, _o + 4:_o + 5], ALU.add))
                tiny(lambda v, _o=o: v.scalar_tensor_tensor(
                    st[:, _o + 11:_o + 12], st[:, _o + 2:_o + 3], -1.0,
                    st[:, _o + 4:_o + 5], ALU.mult, ALU.subtract))
                tiny(lambda v, _o=o: v.tensor_tensor(
                    st[:, _o + 12:_o + 13], st[:, _o + 10:_o + 11],
                    st[:, _o + 11:_o + 12], ALU.max))
                tiny(lambda v, _o=o: v.tensor_tensor(
                    st[:, _o + 13:_o + 14], st[:, _o + 12:_o + 13],
                    st[:, _o + 9:_o + 10], ALU.mult))
                tiny(lambda v, _o=o: v.tensor_scalar(
                    st[:, _o + 14:_o + 15], st[:, _o + 13:_o + 14], 1e-5, 1.0 / 127.0,
                    op0=ALU.max, op1=ALU.mult))
                tiny(lambda v, _o=o, _t=t: v.tensor_copy(
                    invs[:, _t:_t + 1], st[:, _o + 14:_o + 15]))
                tiny(lambda v, _o=o: v.reciprocal(
                    st[:, _o + 15:_o + 16], st[:, _o + 14:_o + 15]))
                tiny(lambda v, _o=o: v.tensor_tensor(
                    st[:, _o + 16:_o + 17], st[:, _o + 9:_o + 10],
                    st[:, _o + 15:_o + 16], ALU.mult))
                tiny(lambda v, _o=o: v.tensor_scalar(
                    st[:, _o + 17:_o + 18], st[:, _o + 4:_o + 5],
                    st[:, _o + 16:_o + 17], None, op0=ALU.mult))
                p.op("scalar", (lambda s, _b=b, _o=o: s.activation(
                    yb[:], _b[:], AF.Identity, bias=st[:, _o + 17:_o + 18],
                    scale=st[:, _o + 16:_o + 17])),
                    waits=[("vst", p.cur("vst"))], inc="y1")
                p.op("vector", (lambda v: v.tensor_scalar(
                    ub[:], yb[:], M32, -M32, op0=ALU.add, op1=ALU.add)),
                    waits=[("y1", p.cur("y1"))], inc="vst")
                p.op("vector", (lambda v: v.tensor_scalar(
                    mb[:], yb[:], ZSH, M64, op0=ALU.add, op1=ALU.add)), inc="vst")
                p.op("vector", (lambda v: v.tensor_scalar(
                    mb[:], mb[:], -M64, -ZSH, op0=ALU.add, op1=ALU.add)), inc="vst")
                p.op("vector", (lambda v: v.scalar_tensor_tensor(
                    yb[:], yb[:], -1.0, mb[:], ALU.mult, ALU.add)), inc="vst")
                p.op("vector", (lambda v: v.tensor_scalar(
                    mb[:], yb[:], -0.5, 0.5, op0=ALU.max, op1=ALU.min)), inc="vst")
                p.op("vector", (lambda v: v.tensor_scalar(
                    yb[:], yb[:], 1.0, -0.5, op0=ALU.add, op1=ALU.max)), inc="vst")
                p.op("vector", (lambda v: v.scalar_tensor_tensor(
                    yb[:], yb[:], 0.0, mb[:], ALU.min, ALU.subtract)), inc="vst")
                p.op("vector", (lambda v: v.scalar_tensor_tensor(
                    qbv[:], ub[:], 0.0, yb[:], ALU.add, ALU.subtract)), inc="xq")
                ptp = pxt[t % 2]
                for k in range(16):
                    wt = []
                    if k == 0:
                        wt.append(("xq", p.cur("xq")))
                        if t >= 2:
                            wt.append(("xev", p.cur("xev") - 1))
                    p.op("tensor", (lambda te, _k=k, _p=ptp: te.matmul(
                        _p[:, _k * 128:(_k + 1) * 128], qbv[:, _k * 128:(_k + 1) * 128],
                        identb[:], is_transpose=True,
                        start=(_k % 8 == 0), stop=(_k % 8 == 7))),
                        waits=wt, inc=("xtp" if k == 15 else None))
                p.op("vector", (lambda v, _p=ptp, _t=t: v.tensor_copy(
                    xqT3[:, :, _t * 128:(_t + 1) * 128],
                    _p[:].rearrange("p (k j) -> p k j", j=128))),
                    waits=[("xtp", p.cur("xtp"))], inc="xev")
            emit(nc, p, sems)

        with ExitStack() as _ph:
            _en = _ph.enter_context
            wc0 = _en(nc.sbuf_tensor("wc0", [128, 16 * 512], BF16))
            wc1 = _en(nc.sbuf_tensor("wc1", [128, 16 * 512], BF16))
            eb0 = _en(nc.sbuf_tensor("eb0", [128, 512], F32))
            eb1 = _en(nc.sbuf_tensor("eb1", [128, 512], F32))
            tb0 = _en(nc.sbuf_tensor("tb0", [128, 512], F32))
            tb1 = _en(nc.sbuf_tensor("tb1", [128, 512], F32))
            pq0 = _en(nc.psum_tensor("pq0", [128, 512], F32))
            pq1 = _en(nc.psum_tensor("pq1", [128, 512], F32))
            pq2 = _en(nc.psum_tensor("pq2", [128, 512], F32))
            pq3 = _en(nc.psum_tensor("pq3", [128, 512], F32))
            pr0 = _en(nc.psum_tensor("pr0", [128, 512], F32))
            pr1 = _en(nc.psum_tensor("pr1", [128, 512], F32))
            wck = [wc0, wc1]
            ebv = [eb0, eb1]
            tbv = [tb0, tb1]
            pq = [pq0, pq1, pq2, pq3]
            ptr = [pr0, pr1]
            p = Plan(semc)
            p.op("vector", lambda v: v.tensor_scalar(
                qsc[:], invs[:], bcw[:, 2:3], SCALE, op0=ALU.mult, op1=ALU.mult),
                waits=[("xev", 32)], inc="vst")
            p.op("vector", lambda v: v.tensor_scalar(
                ksc[:], invs[:], bcw[:, 2:3], None, op0=ALU.mult), inc="vst")
            vsc = p.cur("vst")
            ntile = 0
            nch = 0
            mm_after_chunk = {}
            spq = {0: None, 1: None}
            for half in range(2):
                for j in (range(12) if half == 0 else range(4, 12)):
                    wcb = wck[nch % 2]
                    wsem = "de" if nch % 2 == 0 else "df"
                    waits = []
                    if nch >= 2:
                        waits.append(("mm", mm_after_chunk[nch - 2]))
                    src = WQKVT.rearrange("(k p) f -> p k f", p=128)[:, :, j * 512:(j + 1) * 512]
                    p.dma("sync", (lambda s, _w=wcb, _s=src: s.dma_start(
                        out=_w[:].rearrange("p (k f) -> p k f", f=512), in_=_s)),
                        waits=waits, inc=wsem)
                    wldv = p.cur(wsem)
                    for i in range(16):
                        t = half * 16 + i
                        if j < 4 and t >= 16:
                            continue
                        bank = pq[ntile % 4]
                        eb = ebv[ntile % 2]
                        tbx = tbv[ntile % 2]
                        for k in range(16):
                            wt = []
                            if k == 0:
                                wt.append((wsem, wldv))
                                if ntile >= 4:
                                    wt.append(("ev", p.cur("ev") - 3))
                            p.op("tensor", (lambda te, _k=k, _b=bank, _t=t, _w=wcb:
                                            te.matmul(_b[:],
                                                      xqT3[:, _k, _t * 128:(_t + 1) * 128],
                                                      _w[:, _k * 512:(_k + 1) * 512],
                                                      start=(_k == 0), stop=(_k == 15))),
                                 waits=wt, inc=("mm" if k == 15 else None))
                        sc = qsc[:, t:t + 1] if j < 4 else ksc[:, t:t + 1]
                        wt = [("mm", p.cur("mm")), ("vst", vsc)]
                        if spq[ntile % 2] is not None:
                            wt.append(spq[ntile % 2])
                        p.op("scalar", (lambda s, _b=bank, _e=eb, _sc=sc:
                                        s.activation(_e[:], _b[:], AF.Identity,
                                                     scale=_sc)),
                             waits=wt, inc="ev")
                        evv = p.cur("ev")
                        ssem = "dc" if ntile % 2 == 0 else "dd"
                        if j >= 8:
                            dst = VS[t * 128:(t + 1) * 128, (j - 8) * 512:(j - 7) * 512]
                            p.dma("gpsimd", (lambda g, _d=dst, _e=eb: g.dma_start(
                                out=_d, in_=_e[:])), waits=[("ev", evv)], inc=ssem)
                            spq[ntile % 2] = (ssem, p.cur(ssem))
                        else:
                            ptb = ptr[ntile % 2]
                            for k in range(4):
                                wt = []
                                if k == 0:
                                    wt.append(("ev", evv))
                                    if ntile >= 2:
                                        wt.append(("tev", p.cur("tev") - 1))
                                p.op("tensor", (lambda te, _k=k, _p=ptb, _e=eb:
                                                te.matmul(_p[:, _k * 128:(_k + 1) * 128],
                                                          _e[:, _k * 128:(_k + 1) * 128],
                                                          identf[:], is_transpose=True,
                                                          start=(_k == 0), stop=(_k == 3))),
                                    waits=wt, inc=("tp" if k == 3 else None))
                            p.op("vector", (lambda v, _p=ptb, _t=tbx: v.tensor_copy(
                                _t[:], _p[:])), waits=[("tp", p.cur("tp"))], inc="tev")
                            if j < 4:
                                dd = QT.rearrange("(c k p) r -> c p k r", k=4, p=128)[j][:, :, t * 128:(t + 1) * 128]
                            else:
                                dd = KT.rearrange("(c k p) r -> c p k r", k=4, p=128)[j - 4][:, :, t * 128:(t + 1) * 128]
                            p.dma("gpsimd", (lambda g, _d=dd, _t=tbx: g.dma_start(
                                out=_d, in_=_t[:].rearrange("p (k j) -> p k j", j=128))),
                                waits=[("tev", p.cur("tev"))], inc=ssem)
                            spq[ntile % 2] = (ssem, p.cur(ssem))
                        ntile += 1
                    mm_after_chunk[nch] = p.cur("mm")
                    nch += 1
            p.op("gpsimd", lambda g: g.engine_nop(),
                 waits=[("dc", p.cur("dc")), ("dd", p.cur("dd"))], inc="gp")
            emit(nc, p, sems)

        _xq_scope.close()

        with ExitStack() as _ph:
            _en = _ph.enter_context
            ib0 = _en(nc.sbuf_tensor("ib0", [128, HID], I32))
            ib1 = _en(nc.sbuf_tensor("ib1", [128, HID], I32))
            cb = _en(nc.sbuf_tensor("cb", [128, HID], BF16))
            fe0 = _en(nc.sbuf_tensor("fe0", [128, C], BF16))
            fe1 = _en(nc.sbuf_tensor("fe1", [128, C], BF16))
            pf0 = _en(nc.psum_tensor("pf0", [128, C], BF16))
            pf1 = _en(nc.psum_tensor("pf1", [128, C], BF16))
            ib = [ib0, ib1]
            fev = [fe0, fe1]
            pft = [pf0, pf1]
            p = Plan(semc)
            items = ([("g", t) for t in range(44)] + [("u", t) for t in range(44)]
                     + [("d", t) for t in range(16)])
            nr = 0
            spq = {0: None, 1: None}
            for it, (mat, t) in enumerate(items):
                if mat == "g":
                    src, nblk, dstT = wg_r[t * 128:(t + 1) * 128, :], 16, WGT
                elif mat == "u":
                    src, nblk, dstT = wu_r[t * 128:(t + 1) * 128, :], 16, WUT
                else:
                    src, nblk, dstT = wd_r[t * 128:(t + 1) * 128, :], 44, WDT
                b = ib[it % 2]
                dsem = "da" if it % 2 == 0 else "db"
                ncols = nblk * 128
                waits = [("y1", p.cur("y1") - 1)] if it >= 2 else []
                p.dma("sync", (lambda s, _b=b, _s=src, _n=ncols: s.dma_start(
                    out=_b[:, 0:_n], in_=_s)), waits=waits, inc=dsem)
                p.op("scalar", (lambda s, _b=b, _n=ncols: s.activation(
                    cb[:, 0:_n], _b[:, 0:_n], AF.Copy, bias=-128.0)),
                    waits=[(dsem, p.cur(dsem)), ("tev", p.cur("tev"))], inc="y1")
                for (k0, cnt) in [(r * 16, min(16, nblk - r * 16))
                                  for r in range((nblk + 15) // 16)]:
                    ptb = pft[nr % 2]
                    for kk in range(cnt):
                        wt = []
                        if kk == 0:
                            wt.append(("y1", p.cur("y1")))
                            if nr >= 2:
                                wt.append(("tev", p.cur("tev") - 1))
                        p.op("tensor", (lambda te, _k=k0 + kk, _kk=kk, _p=ptb, _c=cnt:
                                        te.matmul(_p[:, _kk * 128:(_kk + 1) * 128],
                                                  cb[:, _k * 128:(_k + 1) * 128], identb[:],
                                                  is_transpose=True,
                                                  start=(_kk % 8 == 0),
                                                  stop=(_kk % 8 == 7 or _kk == _c - 1))),
                            waits=wt, inc=("tp" if kk == cnt - 1 else None))
                    fv = fev[nr % 2]
                    evw = [("tp", p.cur("tp"))]
                    if spq[nr % 2] is not None:
                        evw.append(spq[nr % 2])
                    p.op("vector", (lambda v, _p=ptb, _f=fv, _c=cnt: v.tensor_copy(
                        _f[:, 0:_c * 128], _p[:, 0:_c * 128])),
                        waits=evw, inc="tev")
                    ssem = "dc" if nr % 2 == 0 else "dd"
                    dd = dstT.rearrange("(k p) f -> p k f", p=128)[:, k0:k0 + cnt, t * 128:(t + 1) * 128]
                    p.dma("gpsimd", (lambda g, _d=dd, _f=fv, _c=cnt: g.dma_start(
                        out=_d, in_=_f[:, 0:_c * 128].rearrange("p (k j) -> p k j", j=128))),
                        waits=[("tev", p.cur("tev"))], inc=ssem)
                    spq[nr % 2] = (ssem, p.cur(ssem))
                    nr += 1
            p.op("gpsimd", lambda g: g.engine_nop(),
                 waits=[("dc", p.cur("dc")), ("dd", p.cur("dd"))], inc="gp")
            emit(nc, p, sems)

        with ExitStack() as _ph:
            _en = _ph.enter_context
            sA = _en(nc.sbuf_tensor("sA", [128, 4096], F32))
            sB = _en(nc.sbuf_tensor("sB", [128, 4096], F32))
            sC = _en(nc.sbuf_tensor("sC", [128, 4096], F32))
            sD = _en(nc.sbuf_tensor("sD", [128, 4096], F32))
            eA = _en(nc.sbuf_tensor("eA", [128, 4096], F32))
            eB = _en(nc.sbuf_tensor("eB", [128, 4096], F32))
            eC = _en(nc.sbuf_tensor("eC", [128, 4096], F32))
            eD = _en(nc.sbuf_tensor("eD", [128, 4096], F32))
            qg = _en(nc.sbuf_tensor("qg", [128, 512], F32))
            kb0 = _en(nc.sbuf_tensor("kb0", [128, 512], F32))
            kb1 = _en(nc.sbuf_tensor("kb1", [128, 512], F32))
            vb0 = _en(nc.sbuf_tensor("vb0", [128, 128], F32))
            vb1 = _en(nc.sbuf_tensor("vb1", [128, 128], F32))
            pb0 = _en(nc.sbuf_tensor("pb0", [128, 512], F32))
            pb1 = _en(nc.sbuf_tensor("pb1", [128, 512], F32))
            otm = _en(nc.sbuf_tensor("otm", [128, 512], F32))
            obv = _en(nc.sbuf_tensor("obv", [128, 512], F32))
            pS0 = _en(nc.psum_tensor("pS0", [128, 512], F32))
            pS1 = _en(nc.psum_tensor("pS1", [128, 512], F32))
            pP0 = _en(nc.psum_tensor("pP0", [128, 512], F32))
            pP1 = _en(nc.psum_tensor("pP1", [128, 512], F32))
            pO = _en(nc.psum_tensor("pO", [128, 512], F32))
            pOt = _en(nc.psum_tensor("pOt", [128, 512], F32))
            strips = [sA, sB, sC, sD]
            estrips = [eA, eB, eC, eD]
            kbv = [kb0, kb1]
            vbv = [vb0, vb1]
            ptv = [pb0, pb1]
            pS = [pS0, pS1]
            pP = [pP0, pP1]
            p = Plan(semc)
            nkc = 0
            nvc = 0
            nsb = 0
            npt = 0
            se_bank = [[], []]
            for g in range(4):
                NCH = g + 1
                for h in range(H):
                    ghi = g * H + h
                    wq_ = [("mm", p.cur("mm"))] if ghi else []
                    p.dma("sync", (lambda s, _h=h, _g=g: s.dma_start(
                        out=qg[:], in_=QT[_h * 128:(_h + 1) * 128, _g * 512:(_g + 1) * 512])),
                        waits=wq_, inc="qld")
                    qldv = p.cur("qld")
                    for part in range(2):
                        for ch in range(NCH):
                            kbx = kbv[nkc % 2]
                            ksem = "da" if nkc % 2 == 0 else "db"
                            wt = []
                            if nkc >= 2:
                                wt.append(("mm", p.sem.get("_kmm%d" % (nkc - 2), 0)))
                            ksrc = KT[h * 128:(h + 1) * 128,
                                      part * 2048 + ch * 512: part * 2048 + (ch + 1) * 512]
                            p.dma("sync", (lambda s, _k=kbx, _s=ksrc: s.dma_start(
                                out=_k[:], in_=_s)), waits=[w for w in wt if w[1] > 0], inc=ksem)
                            kldv = p.cur(ksem)
                            for i in range(4):
                                ti = 4 * g + i
                                L = (ti + 1) * 128
                                w = min(512, L - ch * 512)
                                if w <= 0:
                                    continue
                                bank = pS[nsb % 2]
                                wt = [(ksem, kldv), ("qld", qldv)]
                                if nsb >= 2:
                                    wt.extend(se_bank[nsb % 2])
                                p.op("tensor", (lambda te, _b=bank, _k=kbx, _i=i, _w=w:
                                                te.matmul(_b[:, 0:_w],
                                                          qg[:, _i * 128:(_i + 1) * 128],
                                                          _k[:, 0:_w],
                                                          start=True, stop=True)),
                                     waits=wt, inc="mm")
                                mmv = p.cur("mm")
                                stripbase = 0 if part == 0 else L
                                dst0 = stripbase + ch * 512
                                diag = (w < 512) or (ch == NCH - 1 and i == 3)
                                strip = strips[i]
                                if diag:
                                    wfull = w - 128
                                    if wfull > 0:
                                        p.op("vector", (lambda v, _st=strip, _d=dst0, _w=wfull, _b=bank:
                                                        v.tensor_copy(_st[:, _d:_d + _w], _b[:, 0:_w])),
                                             waits=[("mm", mmv)], inc="sv")
                                    mk = tri if part == 0 else oma
                                    p.op("vector", (lambda v, _st=strip, _d=dst0 + wfull,
                                                    _b=bank, _w=wfull, _m=mk:
                                                    v.tensor_tensor(_st[:, _d:_d + 128],
                                                                    _b[:, _w:_w + 128],
                                                                    _m[:], ALU.add)),
                                         waits=[("mm", mmv)], inc="sv")
                                else:
                                    p.op("scalar", (lambda s, _st=strip, _d=dst0, _w=w, _b=bank:
                                                    s.copy(_st[:, _d:_d + _w], _b[:, 0:_w])),
                                         waits=[("mm", mmv)], inc="se")
                                se_bank[nsb % 2] = [("se", p.cur("se")), ("sv", p.cur("sv"))]
                                nsb += 1
                            p.sem["_kmm%d" % nkc] = p.cur("mm")
                            nkc += 1
                    if ATT_MODE >= 2:
                        sev, svv = p.cur("se"), p.cur("sv")
                        for i in range(4):
                            ti = 4 * g + i
                            L2 = 2 * (ti + 1) * 128
                            strip = strips[i]
                            p.op("vector", (lambda v, _st=strip, _L=L2, _i=i: v.tensor_reduce(
                                st[:, 48 + _i:49 + _i], _st[:, 0:_L], AX.X, ALU.max,
                                negate=True)),
                                waits=[("se", sev), ("sv", svv)], inc="mx")
                            p.op("scalar", (lambda s, _st=strip, _e=estrips[i], _L=L2, _i=i:
                                            s.activation(
                                _e[:, 0:_L], _st[:, 0:_L], AF.Exp,
                                bias=st[:, 48 + _i:49 + _i],
                                accum_out=st[:, 52 + _i:53 + _i])),
                                waits=[("mx", p.cur("mx"))], inc="xp")
                            col = (4 * g + i) * H + h
                            p.op("vector", (lambda v, _i=i, _c=col:
                                            v.reciprocal(invl[:, _c:_c + 1],
                                                         st[:, 52 + _i:53 + _i])),
                                 waits=[("xp", p.cur("xp"))], inc="vst")
                        xpv = p.cur("xp")
                    if ATT_MODE >= 3:
                        first_pv = True
                        for part in range(2):
                            for kc in range(4 * g + 4):
                                vs = max(0, kc - 4 * g)
                                ptb = pP[npt % 2]
                                for i in range(vs, 4):
                                    ti = 4 * g + i
                                    L = (ti + 1) * 128
                                    srccol = (0 if part == 0 else L) + kc * 128
                                    wt = []
                                    if i == vs:
                                        wt.append(("xp", xpv))
                                        if npt >= 2:
                                            wt.append(("pe", p.cur("pe") - 1))
                                    p.op("tensor", (lambda te, _i=i, _p=ptb, _st=estrips[i],
                                                    _sc=srccol, _vs=vs:
                                                    te.matmul(_p[:, _i * 128:(_i + 1) * 128],
                                                              _st[:, _sc:_sc + 128], identf[:],
                                                              is_transpose=True,
                                                              start=(_i == _vs), stop=(_i == 3))),
                                         waits=wt, inc=("pt" if i == 3 else None))
                                if vs > 0:
                                    p.op("vector", (lambda v, _b=ptv[npt % 2], _vs=vs:
                                                    v.memset(_b[:, 0:_vs * 128], 0.0)),
                                         waits=[("pv", p.cur("pv") - 1 if npt >= 2 else 0)])
                                p.op("vector", (lambda v, _p=ptb, _b=ptv[npt % 2], _vs=vs:
                                                v.tensor_copy(_b[:, _vs * 128:512],
                                                              _p[:, _vs * 128:512])),
                                     waits=[("pt", p.cur("pt")), ("pv", p.cur("pv") - 1 if npt >= 2 else 0)],
                                     inc="pe")
                                vbx = vbv[nvc % 2]
                                vsem = "de" if nvc % 2 == 0 else "df"
                                vsrc = VS[part * 2048 + kc * 128: part * 2048 + (kc + 1) * 128,
                                          h * 128:(h + 1) * 128]
                                wt = [("pv", p.sem.get("_vpv%d" % (nvc - 2), 0))]
                                p.dma("sync", (lambda s, _v=vbx, _s=vsrc: s.dma_start(
                                    out=_v[:], in_=_s)), waits=[w for w in wt if w[1] > 0], inc=vsem)
                                wt = [("pe", p.cur("pe")), (vsem, p.cur(vsem))]
                                if first_pv and ghi:
                                    wt.append(("oe", p.cur("oe")))
                                p.op("tensor", (lambda te, _v=vbx, _b=ptv[npt % 2],
                                                _st=first_pv, _sp=(part == 1 and kc == 4 * g + 3):
                                                te.matmul(pO[:], _v[:], _b[:],
                                                          start=_st, stop=_sp)),
                                     waits=wt, inc="pv")
                                p.sem["_vpv%d" % nvc] = p.cur("pv")
                                first_pv = False
                                npt += 1
                                nvc += 1
                    if ATT_MODE >= 4:
                        p.op("scalar", lambda s: s.copy(otm[:], pO[:]),
                             waits=[("pv", p.cur("pv")), ("ov", p.cur("ov"))], inc="oe")
                        for k in range(4):
                            wt = [("oe", p.cur("oe"))] if k == 0 else []
                            if ghi >= 1 and k == 0:
                                wt.append(("ov", p.cur("ov")))
                            p.op("tensor", (lambda te, _k=k: te.matmul(
                                pOt[:, _k * 128:(_k + 1) * 128], otm[:, _k * 128:(_k + 1) * 128],
                                identf[:], is_transpose=True, start=(_k == 0), stop=(_k == 3))),
                                waits=wt, inc=("ot" if k == 3 else None))
                        p.op("vector", (lambda v: v.tensor_copy(obv[:], pOt[:])),
                             waits=[("ot", p.cur("ot")), ("dg", p.cur("dg"))],
                             inc="ov")
                        dd = OS.rearrange("(t p) f -> p t f", p=128)[:, 4 * g:4 * g + 4,
                                                                    h * 128:(h + 1) * 128]
                        p.dma("gpsimd", (lambda g_, _d=dd: g_.dma_start(
                            out=_d, in_=obv[:].rearrange("p (i f) -> p i f", f=128))),
                            waits=[("ov", p.cur("ov"))], inc="dg")
            p.op("gpsimd", lambda g_: g_.engine_nop(),
                 waits=[("dg", p.cur("dg"))], inc="gp")
            emit(nc, p, sems)

        with ExitStack() as _ph:
            _en = _ph.enter_context
            wpt = _en(nc.sbuf_tensor("wpt", [128, 16 * C], BF16))
            ob0 = _en(nc.sbuf_tensor("ob0", [128, C], F32))
            ob1 = _en(nc.sbuf_tensor("ob1", [128, C], F32))
            nb = _en(nc.sbuf_tensor("nb", [128, C], F32))
            yb2 = _en(nc.sbuf_tensor("yb2", [128, C], F32))
            mb2 = _en(nc.sbuf_tensor("mb2", [128, C], F32))
            ub2 = _en(nc.sbuf_tensor("ub2", [128, C], F32))
            oqb = _en(nc.sbuf_tensor("oqb", [128, C], BF16))
            oqT = _en(nc.sbuf_tensor("oqT", [128, C], BF16))
            prb = _en(nc.sbuf_tensor("prb", [128, C], F32))
            lnb = _en(nc.sbuf_tensor("lnb", [128, C], F32))
            xo0 = _en(nc.sbuf_tensor("xo0", [128, C], F32))
            xo1 = _en(nc.sbuf_tensor("xo1", [128, C], F32))
            xmb = _en(nc.sbuf_tensor("xmb", [128, C], F32))
            x2b = _en(nc.sbuf_tensor("x2b", [128, C], F32))
            hib = _en(nc.sbuf_tensor("hib", [128, C], BF16))
            lob = _en(nc.sbuf_tensor("lob", [128, C], BF16))
            hev = _en(nc.sbuf_tensor("hev", [128, C], BF16))
            lev = _en(nc.sbuf_tensor("lev", [128, C], BF16))
            pqt = _en(nc.psum_tensor("pqt", [128, C], BF16))
            ppr0 = _en(nc.psum_tensor("ppr0", [128, 512], F32))
            ppr1 = _en(nc.psum_tensor("ppr1", [128, 512], F32))
            phl0 = _en(nc.psum_tensor("phl0", [128, C], BF16))
            phl1 = _en(nc.psum_tensor("phl1", [128, C], BF16))
            obx = [ob0, ob1]
            xox = [xo0, xo1]
            ppr = [ppr0, ppr1]
            p = Plan(semc)
            p.dma("sync", lambda s: s.dma_start(
                out=wpt[:].rearrange("p (k f) -> p k f", f=C),
                in_=WPROJT.rearrange("(k p) f -> p k f", p=128)), inc="wld")
            wptv = p.cur("wld")
            for t in range(OT):
                ob = obx[t % 2]
                xo = xox[t % 2]
                osem = "da" if t % 2 == 0 else "db"
                xsem = "de" if t % 2 == 0 else "df"
                waits = [("prx", p.cur("prx") - 1)] if t >= 2 else []
                p.dma("sync", (lambda s, _b=ob, _t=t: s.dma_start(
                    out=_b[:], in_=OS[_t * 128:(_t + 1) * 128, :])), waits=waits, inc=osem)
                osv = p.cur(osem)
                p.dma("sync", (lambda s, _b=xo, _t=t: s.dma_start(
                    out=_b[:], in_=xp_d[_t * 128:(_t + 1) * 128, :])),
                    waits=list(waits), inc=xsem)
                xsv = p.cur(xsem)
                for hh in range(H):
                    wt = [(osem, osv)] if hh == 0 else []
                    if hh == 0 and t >= 2:
                        wt.append(("xq", p.cur("xq")))
                    p.op("scalar", (lambda s, _h=hh, _t=t, _b=ob: s.activation(
                        nb[:, _h * 128:(_h + 1) * 128],
                        _b[:, _h * 128:(_h + 1) * 128],
                        AF.Identity, scale=invl[:, _t * H + _h:_t * H + _h + 1])),
                        waits=wt, inc=("y1" if hh == H - 1 else None))
                y1v = p.cur("y1")
                o = (t % 2) * 32
                p.op("vector", (lambda v, _o=o: v.tensor_reduce(
                    st[:, _o:_o + 1], nb[:], AX.X, ALU.max, apply_absolute_value=True)),
                    waits=[("y1", y1v)], inc="vst")
                p.op("vector", (lambda v, _o=o: v.tensor_scalar(
                    st[:, _o + 1:_o + 2], st[:, _o:_o + 1], 1e-5, 1.0 / 127.0,
                    op0=ALU.max, op1=ALU.mult)), inc="vst")
                p.op("vector", (lambda v, _o=o: v.reciprocal(
                    st[:, _o + 2:_o + 3], st[:, _o + 1:_o + 2])), inc="vst")
                p.op("vector", (lambda v, _o=o: v.tensor_scalar(
                    st[:, _o + 3:_o + 4], st[:, _o + 1:_o + 2], bcw[:, 3:4], None,
                    op0=ALU.mult)), inc="vst")
                p.op("scalar", (lambda s, _o=o: s.activation(
                    yb2[:], nb[:], AF.Identity, bias=0.0, scale=st[:, _o + 2:_o + 3])),
                    waits=[("vst", p.cur("vst"))], inc="y1")
                p.op("vector", (lambda v: v.tensor_scalar(
                    ub2[:], yb2[:], M32, -M32, op0=ALU.add, op1=ALU.add)),
                    waits=[("y1", p.cur("y1"))], inc="vst")
                p.op("vector", (lambda v: v.tensor_scalar(
                    mb2[:], yb2[:], ZSH, M64, op0=ALU.add, op1=ALU.add)), inc="vst")
                p.op("vector", (lambda v: v.tensor_scalar(
                    mb2[:], mb2[:], -M64, -ZSH, op0=ALU.add, op1=ALU.add)), inc="vst")
                p.op("vector", (lambda v: v.scalar_tensor_tensor(
                    yb2[:], yb2[:], -1.0, mb2[:], ALU.mult, ALU.add)), inc="vst")
                p.op("vector", (lambda v: v.tensor_scalar(
                    mb2[:], yb2[:], -0.5, 0.5, op0=ALU.max, op1=ALU.min)), inc="vst")
                p.op("vector", (lambda v: v.tensor_scalar(
                    yb2[:], yb2[:], 1.0, -0.5, op0=ALU.add, op1=ALU.max)), inc="vst")
                p.op("vector", (lambda v: v.scalar_tensor_tensor(
                    yb2[:], yb2[:], 0.0, mb2[:], ALU.min, ALU.subtract)), inc="vst")
                p.op("vector", (lambda v: v.scalar_tensor_tensor(
                    oqb[:], ub2[:], 0.0, yb2[:], ALU.add, ALU.subtract)), inc="xq")
                for k in range(16):
                    wt = []
                    if k == 0:
                        wt.append(("xq", p.cur("xq")))
                        if t >= 1:
                            wt.append(("tev", p.cur("tev")))
                    p.op("tensor", (lambda te, _k=k: te.matmul(
                        pqt[:, _k * 128:(_k + 1) * 128], oqb[:, _k * 128:(_k + 1) * 128],
                        identb[:], is_transpose=True,
                        start=(_k % 8 == 0), stop=(_k % 8 == 7))),
                        waits=wt, inc=("tp" if k == 15 else None))
                p.op("vector", (lambda v: v.tensor_copy(oqT[:], pqt[:])),
                     waits=[("tp", p.cur("tp"))], inc="tev")
                tevv = p.cur("tev")
                for j in range(4):
                    bank = ppr[j % 2]
                    for k in range(16):
                        wt = []
                        if k == 0:
                            wt.append(("tev", tevv))
                            wt.append(("wld", wptv))
                            if t * 4 + j >= 2:
                                wt.append(("ev", p.cur("ev") - 1))
                        p.op("tensor", (lambda te, _k=k, _j=j, _b=bank: te.matmul(
                            _b[:], oqT[:, _k * 128:(_k + 1) * 128],
                            wpt[:, _k * C + _j * 512:_k * C + (_j + 1) * 512],
                            start=(_k == 0), stop=(_k == 15))),
                            waits=wt, inc=("mm" if k == 15 else None))
                    p.op("scalar", (lambda s, _j=j, _b=bank, _o=o: s.activation(
                        prb[:, _j * 512:(_j + 1) * 512], _b[:], AF.Identity,
                        scale=st[:, _o + 3:_o + 4])),
                        waits=[("mm", p.cur("mm"))], inc="ev")
                evv = p.cur("ev")
                p.op("vector", (lambda v, _o=o: v.tensor_reduce(
                    st[:, _o + 4:_o + 5], prb[:], AX.X, ALU.add, negate=True)),
                    waits=[("ev", evv)], inc="vst")
                p.op("scalar", (lambda s, _o=o: s.activation(
                    mb2[:], prb[:], AF.Square, accum_out=st[:, _o + 5:_o + 6])),
                    waits=[("ev", evv)], inc="sq")
                p.op("vector", (lambda v, _o=o: v.tensor_scalar(
                    st[:, _o + 6:_o + 7], st[:, _o + 4:_o + 5], 1.0 / C, None,
                    op0=ALU.mult)), inc="vst")
                p.op("vector", (lambda v, _o=o: v.tensor_scalar(
                    st[:, _o + 7:_o + 8], st[:, _o + 5:_o + 6], 1.0 / C, EPS,
                    op0=ALU.mult, op1=ALU.add)),
                    waits=[("sq", p.cur("sq"))], inc="vst")
                p.op("vector", (lambda v, _o=o: v.scalar_tensor_tensor(
                    st[:, _o + 8:_o + 9], st[:, _o + 6:_o + 7], -1.0,
                    st[:, _o + 6:_o + 7], ALU.mult, ALU.mult)), inc="vst")
                p.op("vector", (lambda v, _o=o: v.tensor_tensor(
                    st[:, _o + 9:_o + 10], st[:, _o + 7:_o + 8],
                    st[:, _o + 8:_o + 9], ALU.add)), inc="vst")
                p.op("scalar", (lambda s, _o=o: s.activation(
                    st[:, _o + 10:_o + 11], st[:, _o + 9:_o + 10], AF.Sqrt)),
                    waits=[("vst", p.cur("vst"))], inc="sq")
                p.op("vector", (lambda v, _o=o: v.reciprocal(
                    st[:, _o + 11:_o + 12], st[:, _o + 10:_o + 11])),
                    waits=[("sq", p.cur("sq"))], inc="vst")
                p.op("vector", (lambda v, _o=o: v.tensor_tensor(
                    st[:, _o + 12:_o + 13], st[:, _o + 6:_o + 7],
                    st[:, _o + 11:_o + 12], ALU.mult)), inc="vst")
                p.op("scalar", (lambda s, _o=o: s.activation(
                    lnb[:], prb[:], AF.Identity, bias=st[:, _o + 12:_o + 13],
                    scale=st[:, _o + 11:_o + 12])),
                    waits=[("vst", p.cur("vst"))], inc="y1")
                xw = [("y1", p.cur("y1")), (xsem, xsv)]
                if t >= 1:
                    xw.append(("dg", p.sem.get("_xmidsp%d" % (t - 1), 0)))
                p.op("vector", (lambda v, _x=xo: v.tensor_tensor(
                    xmb[:], _x[:], lnb[:], ALU.add)),
                    waits=xw, inc="fx")
                fxv = p.cur("fx")
                p.dma("gpsimd", (lambda g, _t=t: g.dma_start(
                    out=XMID[_t * 128:(_t + 1) * 128, :], in_=xmb[:])),
                    waits=[("fx", fxv)], inc="dg")
                p.sem["_xmidsp%d" % t] = p.cur("dg")
                p.op("vector", (lambda v, _o=o: v.tensor_reduce(
                    st[:, _o + 13:_o + 14], xmb[:], AX.X, ALU.add, negate=True)),
                    waits=[("fx", fxv)], inc="vst")
                p.op("scalar", (lambda s, _o=o: s.activation(
                    mb2[:], xmb[:], AF.Square, accum_out=st[:, _o + 14:_o + 15])),
                    waits=[("fx", fxv)], inc="sq")
                p.op("vector", (lambda v, _o=o: v.tensor_scalar(
                    st[:, _o + 15:_o + 16], st[:, _o + 13:_o + 14], 1.0 / C, None,
                    op0=ALU.mult)), inc="vst")
                p.op("vector", (lambda v, _o=o: v.tensor_scalar(
                    st[:, _o + 16:_o + 17], st[:, _o + 14:_o + 15], 1.0 / C, EPS,
                    op0=ALU.mult, op1=ALU.add)),
                    waits=[("sq", p.cur("sq"))], inc="vst")
                p.op("vector", (lambda v, _o=o: v.scalar_tensor_tensor(
                    st[:, _o + 17:_o + 18], st[:, _o + 15:_o + 16], -1.0,
                    st[:, _o + 15:_o + 16], ALU.mult, ALU.mult)), inc="vst")
                p.op("vector", (lambda v, _o=o: v.tensor_tensor(
                    st[:, _o + 18:_o + 19], st[:, _o + 16:_o + 17],
                    st[:, _o + 17:_o + 18], ALU.add)), inc="vst")
                p.op("scalar", (lambda s, _o=o: s.activation(
                    st[:, _o + 19:_o + 20], st[:, _o + 18:_o + 19], AF.Sqrt)),
                    waits=[("vst", p.cur("vst"))], inc="sq")
                p.op("vector", (lambda v, _o=o: v.reciprocal(
                    st[:, _o + 20:_o + 21], st[:, _o + 19:_o + 20])),
                    waits=[("sq", p.cur("sq"))], inc="vst")
                p.op("vector", (lambda v, _o=o: v.tensor_tensor(
                    st[:, _o + 21:_o + 22], st[:, _o + 15:_o + 16],
                    st[:, _o + 20:_o + 21], ALU.mult)), inc="vst")
                p.op("scalar", (lambda s, _o=o: s.activation(
                    x2b[:], xmb[:], AF.Identity, bias=st[:, _o + 21:_o + 22],
                    scale=st[:, _o + 20:_o + 21])),
                    waits=[("vst", p.cur("vst"))], inc="y1")
                y1v2 = p.cur("y1")
                hw = [("y1", y1v2)]
                if t >= 1:
                    hw.append(("tp", p.sem.get("_hltp%d" % (t - 1), 0)))
                p.op("vector", (lambda v: v.tensor_copy(hib[:], x2b[:])),
                     waits=hw, inc="vst")
                p.op("vector", (lambda v: v.scalar_tensor_tensor(
                    lob[:], hib[:], -1.0, x2b[:], ALU.mult, ALU.add)), inc="hst")
                hstv = p.cur("hst")
                for (src_, psx, evb, dstT, ssem) in ((hib, phl0, hev, XHs, "dc"),
                                                     (lob, phl1, lev, XLs, "dd")):
                    for k in range(16):
                        wt = []
                        if k == 0:
                            wt.append(("hst", hstv))
                            if t >= 1:
                                wt.append(("xev", p.cur("xev")))
                        p.op("tensor", (lambda te, _k=k, _s=src_, _p=psx: te.matmul(
                            _p[:, _k * 128:(_k + 1) * 128], _s[:, _k * 128:(_k + 1) * 128],
                            identb[:], is_transpose=True,
                            start=(_k % 8 == 0), stop=(_k % 8 == 7))),
                            waits=wt, inc=("tp" if k == 15 else None))
                    p.op("vector", (lambda v, _p=psx, _e=evb: v.tensor_copy(_e[:], _p[:])),
                         waits=[("tp", p.cur("tp")), (ssem, p.cur(ssem))],
                         inc="xev")
                    dd = dstT.rearrange("(k p) r -> p k r", p=128)[:, :, t * 128:(t + 1) * 128]
                    p.dma("gpsimd", (lambda g, _d=dd, _e=evb: g.dma_start(
                        out=_d, in_=_e[:].rearrange("p (k j) -> p k j", j=128))),
                        waits=[("xev", p.cur("xev"))], inc=ssem)
                p.sem["_hltp%d" % t] = p.cur("tp")
                p.op("vector", lambda v: v.engine_nop(), waits=[], inc="prx")
            p.op("gpsimd", lambda g: g.engine_nop(),
                 waits=[("dg", p.cur("dg")), ("dc", p.cur("dc")), ("dd", p.cur("dd"))],
                 inc="gp")
            emit(nc, p, sems)

        with ExitStack() as _ph:
            _en = _ph.enter_context
            xh = _en(nc.sbuf_tensor("xh", [128, 16 * 512], BF16))
            xl = _en(nc.sbuf_tensor("xl", [128, 16 * 512], BF16))
            sh = _en(nc.sbuf_tensor("sh", [128, 44 * 512], BF16))
            sl = _en(nc.sbuf_tensor("sl", [128, 44 * 512], BF16))
            wgb0 = _en(nc.sbuf_tensor("wgb0", [128, 16 * 128], BF16))
            wgb1 = _en(nc.sbuf_tensor("wgb1", [128, 16 * 128], BF16))
            wub0 = _en(nc.sbuf_tensor("wub0", [128, 16 * 128], BF16))
            wub1 = _en(nc.sbuf_tensor("wub1", [128, 16 * 128], BF16))
            wdb0 = _en(nc.sbuf_tensor("wdb0", [128, 44 * 128], BF16))
            wdb1 = _en(nc.sbuf_tensor("wdb1", [128, 44 * 128], BF16))
            sgb = _en(nc.sbuf_tensor("sgb", [128, 512], F32))
            sub = _en(nc.sbuf_tensor("sub", [128, 512], F32))
            hid = _en(nc.sbuf_tensor("hid", [128, 512], F32))
            dob = _en(nc.sbuf_tensor("dob", [128, 512], F32))
            tdb = _en(nc.sbuf_tensor("tdb", [128, 512], F32))
            pgg = _en(nc.psum_tensor("pgg", [128, 512], F32))
            puu = _en(nc.psum_tensor("puu", [128, 512], F32))
            pdd0 = _en(nc.psum_tensor("pdd0", [128, 512], F32))
            pdd1 = _en(nc.psum_tensor("pdd1", [128, 512], F32))
            pdt = _en(nc.psum_tensor("pdt", [128, 512], F32))
            wgb = [wgb0, wgb1]
            wub = [wub0, wub1]
            wdb = [wdb0, wdb1]
            pdd = [pdd0, pdd1]
            p = Plan(semc)
            for q in range(4):
                p.dma("sync", (lambda s, _q=q: s.dma_start(
                    out=xh[:].rearrange("p (k r) -> p k r", r=512),
                    in_=XHs.rearrange("(k p) r -> p k r", p=128)[:, :, _q * 512:(_q + 1) * 512])),
                    waits=[("dn", p.cur("dn"))], inc="da")
                p.dma("sync", (lambda s, _q=q: s.dma_start(
                    out=xl[:].rearrange("p (k r) -> p k r", r=512),
                    in_=XLs.rearrange("(k p) r -> p k r", p=128)[:, :, _q * 512:(_q + 1) * 512])),
                    waits=[("dn", p.cur("dn"))], inc="db")
                xldv = [("da", p.cur("da")), ("db", p.cur("db"))]
                for ht in range(44):
                    wgx = wgb[ht % 2]
                    wux = wub[ht % 2]
                    gsem = "de" if ht % 2 == 0 else "df"
                    usem = "dh" if ht % 2 == 0 else "di"
                    wt = [("gu", p.cur("gu") - 1)] if (q * 44 + ht) >= 2 else []
                    p.dma("sync", (lambda s, _w=wgx, _h=ht: s.dma_start(
                        out=_w[:].rearrange("p (k j) -> p k j", j=128),
                        in_=WGT.rearrange("(k p) f -> p k f", p=128)[:, :, _h * 128:(_h + 1) * 128])),
                        waits=wt, inc=gsem)
                    p.dma("sync", (lambda s, _w=wux, _h=ht: s.dma_start(
                        out=_w[:].rearrange("p (k j) -> p k j", j=128),
                        in_=WUT.rearrange("(k p) f -> p k f", p=128)[:, :, _h * 128:(_h + 1) * 128])),
                        waits=list(wt), inc=usem)
                    wglv, wulv = p.cur(gsem), p.cur(usem)
                    for (wsem, wv, wbx, bank, name) in ((gsem, wglv, wgx, pgg, "g"),
                                                        (usem, wulv, wux, puu, "u")):
                        for z in range(32):
                            k, hl = z % 16, z // 16
                            rhs = xh if hl == 0 else xl
                            wt = []
                            if z == 0:
                                wt.append((wsem, wv))
                                wt.extend(xldv)
                                if q * 44 + ht >= 1:
                                    wt.append(("gu", p.cur("gu")))
                            p.op("tensor", (lambda te, _k=k, _w=wbx, _r=rhs, _b=bank:
                                            te.matmul(_b[:], _w[:, _k * 128:(_k + 1) * 128],
                                                      _r[:, _k * 512:(_k + 1) * 512],
                                                      start=(_k == 0 and _r is xh),
                                                      stop=(_k == 15 and _r is xl))),
                                 waits=wt, inc=("mm" if z == 31 else None))
                        if name == "g":
                            p.op("scalar", (lambda s, _h=ht: s.activation(
                                sgb[:], pgg[:], AF.Silu, scale=sgv[:, _h:_h + 1])),
                                waits=[("mm", p.cur("mm"))], inc="ev")
                        else:
                            p.op("scalar", (lambda s, _h=ht: s.activation(
                                sub[:], puu[:], AF.Identity, scale=suv[:, _h:_h + 1])),
                                waits=[("mm", p.cur("mm"))], inc="ev")
                    p.op("vector", (lambda v: v.tensor_tensor(
                        hid[:], sgb[:], sub[:], ALU.mult)),
                        waits=[("ev", p.cur("ev"))], inc="vst")
                    p.op("vector", (lambda v, _h=ht: v.tensor_copy(
                        sh[:, _h * 512:(_h + 1) * 512], hid[:])), inc="vst")
                    p.op("vector", (lambda v, _h=ht: v.scalar_tensor_tensor(
                        sl[:, _h * 512:(_h + 1) * 512], sh[:, _h * 512:(_h + 1) * 512],
                        -1.0, hid[:], ALU.mult, ALU.add)), inc="gu")
                guv = p.cur("gu")
                for dt in range(16):
                    wdx = wdb[dt % 2]
                    dsem = "dj" if dt % 2 == 0 else "dk"
                    wt = [("dnv", p.cur("dnv") - 1)] if (q * 16 + dt) >= 2 else []
                    p.dma("sync", (lambda s, _w=wdx, _d=dt: s.dma_start(
                        out=_w[:].rearrange("p (k j) -> p k j", j=128),
                        in_=WDT.rearrange("(k p) f -> p k f", p=128)[:, :, _d * 128:(_d + 1) * 128])),
                        waits=wt, inc=dsem)
                    wdlv = p.cur(dsem)
                    bank = pdd[dt % 2]
                    for z in range(88):
                        k, hl = z % 44, z // 44
                        rhs = sh if hl == 0 else sl
                        wt = []
                        if z == 0:
                            wt.append((dsem, wdlv))
                            wt.append(("gu", guv))
                            if q * 16 + dt >= 2:
                                wt.append(("dnv", p.cur("dnv") - 1))
                        p.op("tensor", (lambda te, _k=k, _w=wdx, _r=rhs, _b=bank:
                                        te.matmul(_b[:], _w[:, _k * 128:(_k + 1) * 128],
                                                  _r[:, _k * 512:(_k + 1) * 512],
                                                  start=(_k == 0 and _r is sh),
                                                  stop=(_k == 43 and _r is sl))),
                             waits=wt, inc=("mm" if z == 87 else None))
                    p.op("scalar", (lambda s, _d=dt, _b=bank: s.activation(
                        dob[:], _b[:], AF.Identity, scale=sdv[:, _d:_d + 1])),
                        waits=[("mm", p.cur("mm")), ("ot", p.cur("ot"))], inc="dnv")
                    for k in range(4):
                        wt = []
                        if k == 0:
                            wt.append(("dnv", p.cur("dnv")))
                            wt.append(("ov", p.cur("ov")))
                        p.op("tensor", (lambda te, _k=k: te.matmul(
                            pdt[:, _k * 128:(_k + 1) * 128], dob[:, _k * 128:(_k + 1) * 128],
                            identf[:], is_transpose=True, start=(_k == 0), stop=(_k == 3))),
                            waits=wt, inc=("ot" if k == 3 else None))
                    p.op("vector", (lambda v: v.tensor_copy(tdb[:], pdt[:])),
                         waits=[("ot", p.cur("ot")), ("dg", p.cur("dg"))], inc="ov")
                    dd = FFO.rearrange("(t p) f -> p t f", p=128)[:, q * 4:q * 4 + 4,
                                                                 dt * 128:(dt + 1) * 128]
                    p.dma("gpsimd", (lambda g, _d=dd: g.dma_start(
                        out=_d, in_=tdb[:].rearrange("p (i f) -> p i f", f=128))),
                        waits=[("ov", p.cur("ov"))], inc="dg")
                p.op("vector", lambda v: v.engine_nop(),
                     waits=[("ov", p.cur("ov"))], inc="dn")
            p.op("gpsimd", lambda g: g.engine_nop(),
                 waits=[("dg", p.cur("dg"))], inc="gp")
            emit(nc, p, sems)

        with ExitStack() as _ph:
            _en = _ph.enter_context
            fb0 = _en(nc.sbuf_tensor("fb0", [128, C], F32))
            fb1 = _en(nc.sbuf_tensor("fb1", [128, C], F32))
            xm0 = _en(nc.sbuf_tensor("xm0", [128, C], F32))
            xm1 = _en(nc.sbuf_tensor("xm1", [128, C], F32))
            sq2 = _en(nc.sbuf_tensor("sq2", [128, C], F32))
            ln5 = _en(nc.sbuf_tensor("ln5", [128, C], F32))
            yo0 = _en(nc.sbuf_tensor("yo0", [128, C], F32))
            yo1 = _en(nc.sbuf_tensor("yo1", [128, C], F32))
            fbx = [fb0, fb1]
            xmx = [xm0, xm1]
            yox = [yo0, yo1]
            p = Plan(semc)
            for t in range(OT):
                fb = fbx[t % 2]
                xm = xmx[t % 2]
                yo = yox[t % 2]
                fsem = "da" if t % 2 == 0 else "db"
                msem = "de" if t % 2 == 0 else "df"
                waits = [("fx", p.cur("fx") - 1)] if t >= 2 else []
                p.dma("sync", (lambda s, _b=fb, _t=t: s.dma_start(
                    out=_b[:], in_=FFO[_t * 128:(_t + 1) * 128, :])), waits=waits, inc=fsem)
                fsv = p.cur(fsem)
                p.dma("sync", (lambda s, _b=xm, _t=t: s.dma_start(
                    out=_b[:], in_=XMID[_t * 128:(_t + 1) * 128, :])),
                    waits=list(waits), inc=msem)
                msv = p.cur(msem)
                o = (t % 2) * 32
                p.op("vector", (lambda v, _b=fb, _o=o: v.tensor_reduce(
                    st[:, _o:_o + 1], _b[:], AX.X, ALU.add, negate=True)),
                    waits=[(fsem, fsv)], inc="vst")
                p.op("scalar", (lambda s, _b=fb, _o=o: s.activation(
                    sq2[:], _b[:], AF.Square, accum_out=st[:, _o + 1:_o + 2])),
                    waits=[(fsem, fsv)], inc="sq")
                p.op("vector", (lambda v, _o=o: v.tensor_scalar(
                    st[:, _o + 2:_o + 3], st[:, _o:_o + 1], 1.0 / C, None,
                    op0=ALU.mult)), inc="vst")
                p.op("vector", (lambda v, _o=o: v.tensor_scalar(
                    st[:, _o + 3:_o + 4], st[:, _o + 1:_o + 2], 1.0 / C, EPS,
                    op0=ALU.mult, op1=ALU.add)),
                    waits=[("sq", p.cur("sq"))], inc="vst")
                p.op("vector", (lambda v, _o=o: v.scalar_tensor_tensor(
                    st[:, _o + 4:_o + 5], st[:, _o + 2:_o + 3], -1.0,
                    st[:, _o + 2:_o + 3], ALU.mult, ALU.mult)), inc="vst")
                p.op("vector", (lambda v, _o=o: v.tensor_tensor(
                    st[:, _o + 5:_o + 6], st[:, _o + 3:_o + 4],
                    st[:, _o + 4:_o + 5], ALU.add)), inc="vst")
                p.op("scalar", (lambda s, _o=o: s.activation(
                    st[:, _o + 6:_o + 7], st[:, _o + 5:_o + 6], AF.Sqrt)),
                    waits=[("vst", p.cur("vst"))], inc="sq")
                p.op("vector", (lambda v, _o=o: v.reciprocal(
                    st[:, _o + 7:_o + 8], st[:, _o + 6:_o + 7])),
                    waits=[("sq", p.cur("sq"))], inc="vst")
                p.op("vector", (lambda v, _o=o: v.tensor_tensor(
                    st[:, _o + 8:_o + 9], st[:, _o + 2:_o + 3],
                    st[:, _o + 7:_o + 8], ALU.mult)), inc="vst")
                p.op("scalar", (lambda s, _b=fb, _o=o: s.activation(
                    ln5[:], _b[:], AF.Identity, bias=st[:, _o + 8:_o + 9],
                    scale=st[:, _o + 7:_o + 8])),
                    waits=[("vst", p.cur("vst"))], inc="y1")
                ysem = "dc" if t % 2 == 0 else "dd"
                yw = [("y1", p.cur("y1")), (msem, msv)]
                if t >= 2:
                    yw.append((ysem, p.cur(ysem)))
                p.op("vector", (lambda v, _x=xm, _y=yo: v.tensor_tensor(
                    _y[:], _x[:], ln5[:], ALU.add)),
                    waits=yw, inc="fx")
                p.dma("gpsimd", (lambda g, _y=yo, _t=t: g.dma_start(
                    out=y_d[_t * 128:(_t + 1) * 128, :], in_=_y[:])),
                    waits=[("fx", p.cur("fx"))], inc=ysem)
            for nm in DEBUG_DUMPS:
                p.dma("gpsimd", (lambda g, _nm=nm: g.dma_start(
                    out=dbg[_nm], in_=scr[_nm])), inc="out")
            p.op("gpsimd", lambda g: g.engine_nop(),
                 waits=[("dc", p.cur("dc")), ("dd", p.cur("dd")),
                        ("out", p.cur("out"))], inc="gp")
            emit(nc, p, sems)

    if PHASE_LIMIT < 9:
        with nc.sbuf_tensor("zout", [128, C], F32) as zo, \
             nc.semaphore("zs") as zs, nc.Block() as blk:
            @blk.gpsimd
            def _(g):
                g.memset(zo[:], 0.0).then_inc(zs, 1)
                g.wait_ge(zs, 1)
                for t in range(OT):
                    g.dma_start(out=y_d[t * 128:(t + 1) * 128, :],
                                in_=zo[:]).then_inc(zs, 16)
                g.wait_ge(zs, 1 + 16 * OT)

    nc.finalize()
    return nc


def kernel(L0_attn_x, L0_attn_w_qkv, L0_attn_w_proj, L0_ffn_wq_gate,
           L0_ffn_s_gate, L0_ffn_wq_up, L0_ffn_s_up, L0_ffn_wq_down,
           L0_ffn_s_down):
    x = np.ascontiguousarray(np.asarray(L0_attn_x, dtype=np.float32))
    wq = np.ascontiguousarray(np.asarray(L0_attn_w_qkv, dtype=np.float32))
    wp = np.ascontiguousarray(np.asarray(L0_attn_w_proj, dtype=np.float32))
    wg = np.ascontiguousarray(np.asarray(L0_ffn_wq_gate, dtype=np.int32))
    wu = np.ascontiguousarray(np.asarray(L0_ffn_wq_up, dtype=np.int32))
    wd = np.ascontiguousarray(np.asarray(L0_ffn_wq_down, dtype=np.int32))
    sg = np.ascontiguousarray(np.asarray(L0_ffn_s_gate, dtype=np.float32))
    su = np.ascontiguousarray(np.asarray(L0_ffn_s_up, dtype=np.float32))
    sd = np.ascontiguousarray(np.asarray(L0_ffn_s_down, dtype=np.float32))
    ident = np.eye(128, dtype=np.float32)
    trim = (np.triu(np.ones((128, 128), dtype=np.float32), 1) * NEG).astype(np.float32)

    nc = build_program()
    in_maps = []
    orders = []
    for c in range(8):
        b, par = c // 2, c % 2
        order = list(range(par, NT, 2)) + list(range(1 - par, NT, 2))
        orders.append(order)
        xp = np.concatenate([x[b, t * 128:(t + 1) * 128, :] for t in order], axis=0)
        oma = (np.zeros((128, 128), np.float32) if par == 1 else
               np.full((128, 128), NEG, np.float32))
        in_maps.append(dict(xp=np.ascontiguousarray(xp), wqkv=wq, wproj=wp,
                            wg=wg, wu=wu, wd=wd, sg=sg, su=su, sd=sd,
                            ident=ident, tri=trim, oma=oma))
    res = run_bass_kernel_spmd(nc, in_maps, core_ids=list(range(8)))
    y = np.empty((4, NTOK, C), np.float32)
    for c in range(8):
        b, par = c // 2, c % 2
        yc = np.asarray(res.results[c]["y"])
        for i, t in enumerate(range(par, NT, 2)):
            y[b, t * 128:(t + 1) * 128, :] = yc[i * 128:(i + 1) * 128, :]
    return y


# version 04
# speedup vs baseline: 1.0420x; 1.0420x over previous
import numpy as np
from contextlib import ExitStack

import concourse.bass as bass
import concourse.mybir as mybir
from concourse.bass_utils import run_bass_kernel_spmd
from concourse.alu_op_type import AluOpType as ALU

F32 = mybir.dt.float32
BF16 = mybir.dt.bfloat16
I32 = mybir.dt.int32
AF = mybir.ActivationFunctionType
AX = mybir.AxisListType

C = 2048
H = 16
HID = 5632
NTOK = 4096
NT = 32
OT = 16
FQKV = 3 * C
NEG = np.float32(-9e15)
EPS = float(np.float32(1e-5))
SCALE = float(np.float32(128.0 ** -0.5))
M32 = 12582912.0
M64 = 6755399441055744.0
ZSH = 384.0

ENGINES = ("sync", "scalar", "vector", "tensor", "gpsimd")
DEBUG_DUMPS = []


class Plan:
    def __init__(self, sem):
        self.ops = []
        self.sem = sem

    def op(self, engine, fn, waits=(), inc=None, inc_n=1):
        w = [(s, int(v)) for (s, v) in waits if v > 0]
        iv = None
        if inc is not None:
            self.sem[inc] = self.sem.get(inc, 0) + inc_n
            iv = self.sem[inc]
        self.ops.append([engine, fn, w, inc, inc_n, iv])

    def dma(self, engine, fn, waits=(), inc=None):
        self.op(engine, fn, waits, inc, 16)

    def cur(self, name):
        return self.sem.get(name, 0)

    def finalize(self):
        last = {}
        for o in self.ops:
            en, fn, waits, inc, inc_n, iv = o
            if en in ("vector", "scalar", "gpsimd") and inc_n != 16:
                if en in last and last[en] not in waits:
                    waits.append(last[en])
                if inc is None:
                    eo = "eo_" + en
                    self.sem[eo] = self.sem.get(eo, 0) + 1
                    o[3], o[5] = eo, self.sem[eo]
                last[en] = (o[3], o[5])


PHASE_LIMIT = 9
ATT_MODE = 4
_PH = [0]


def emit(nc, plan, sems):
    i = _PH[0]
    _PH[0] += 1
    if i >= PHASE_LIMIT:
        return
    plan.finalize()
    with nc.Block() as block:
        def mk(engname):
            def body(eng):
                for (en, fn, waits, inc, inc_n, iv) in plan.ops:
                    if en != engname:
                        continue
                    for (s, v) in waits:
                        eng.wait_ge(sems[s], v)
                    instr = fn(eng)
                    if inc is not None:
                        instr.then_inc(sems[inc], inc_n)
            body.__name__ = "_"
            return body
        for en in ENGINES:
            getattr(block, en)(mk(en))


SEM_NAMES = ["ld", "wld", "vst", "sq", "y1", "xq", "xtp", "xev", "mm", "ev",
             "tp", "tev", "qld", "se", "sv", "mx", "xp",
             "pt", "pe", "pv", "oe", "ot", "ov", "gp", "out", "prx", "gu",
             "dn", "hst", "fx", "dnv",
             "da", "db", "dc", "dd", "de", "df", "dg", "dh", "di", "dj", "dk",
             "eo_vector", "eo_scalar", "eo_gpsimd"]


def build_program():
    _PH[0] = 0
    nc = bass.Bass("TRN2", target_bir_lowering=False)

    xp_d = nc.dram_tensor("xp", [NTOK, C], F32, kind="ExternalInput").ap()
    wqkv = nc.dram_tensor("wqkv", [FQKV, C], F32, kind="ExternalInput").ap()
    wproj = nc.dram_tensor("wproj", [C, C], F32, kind="ExternalInput").ap()
    wg_r = nc.dram_tensor("wg", [HID, C], I32, kind="ExternalInput").ap()
    wu_r = nc.dram_tensor("wu", [HID, C], I32, kind="ExternalInput").ap()
    wd_r = nc.dram_tensor("wd", [C, HID], I32, kind="ExternalInput").ap()
    sg_d = nc.dram_tensor("sg", [HID], F32, kind="ExternalInput").ap()
    su_d = nc.dram_tensor("su", [HID], F32, kind="ExternalInput").ap()
    sd_d = nc.dram_tensor("sd", [C], F32, kind="ExternalInput").ap()
    id_d = nc.dram_tensor("ident", [128, 128], F32, kind="ExternalInput").ap()
    tri_d = nc.dram_tensor("tri", [128, 128], F32, kind="ExternalInput").ap()
    oma_d = nc.dram_tensor("oma", [128, 128], F32, kind="ExternalInput").ap()
    y_d = nc.dram_tensor("y", [2048, C], F32, kind="ExternalOutput").ap()

    WQKVT = nc.dram_tensor("WQKVT", [C, FQKV], BF16, kind="Internal").ap()
    WPROJT = nc.dram_tensor("WPROJT", [C, C], BF16, kind="Internal").ap()
    WGT = nc.dram_tensor("WGT", [C, HID], BF16, kind="Internal").ap()
    WUT = nc.dram_tensor("WUT", [C, HID], BF16, kind="Internal").ap()
    WDT = nc.dram_tensor("WDT", [HID, C], BF16, kind="Internal").ap()
    QT = nc.dram_tensor("QT", [C, 2048], F32, kind="Internal").ap()
    KT = nc.dram_tensor("KT", [C, NTOK], F32, kind="Internal").ap()
    VS = nc.dram_tensor("VS", [NTOK, C], F32, kind="Internal").ap()
    OS = nc.dram_tensor("OS", [2048, C], F32, kind="Internal").ap()
    XMID = nc.dram_tensor("XMID", [2048, C], F32, kind="Internal").ap()
    XHs = nc.dram_tensor("XHs", [C, 2048], BF16, kind="Internal").ap()
    XLs = nc.dram_tensor("XLs", [C, 2048], BF16, kind="Internal").ap()
    FFO = nc.dram_tensor("FFO", [2048, C], F32, kind="Internal").ap()
    scr = {"QT": QT, "KT": KT, "VS": VS, "OS": OS, "XMID": XMID, "FFO": FFO,
           "WQKVT": WQKVT, "WPROJT": WPROJT, "WGT": WGT, "WUT": WUT, "WDT": WDT,
           "XHs": XHs, "XLs": XLs}
    dbg = {}
    for nm in DEBUG_DUMPS:
        t = scr[nm]
        dbg[nm] = nc.dram_tensor("dbg_" + nm, list(t.shape), t.dtype,
                                 kind="ExternalOutput").ap()

    semc = {}
    with ExitStack() as ctx:
        e = ctx.enter_context
        sems = {n: e(nc.semaphore(n)) for n in SEM_NAMES}

        identf = e(nc.sbuf_tensor("identf", [128, 128], F32))
        identb = e(nc.sbuf_tensor("identb", [128, 128], BF16))
        tri = e(nc.sbuf_tensor("tri_s", [128, 128], F32))
        oma = e(nc.sbuf_tensor("oma_s", [128, 128], F32))
        ones = e(nc.sbuf_tensor("ones", [128, C], F32))
        invs = e(nc.sbuf_tensor("invs", [128, NT], F32))
        qsc = e(nc.sbuf_tensor("qsc", [128, NT], F32))
        ksc = e(nc.sbuf_tensor("ksc", [128, NT], F32))
        bcw = e(nc.sbuf_tensor("bcw", [128, 4], F32))
        invl = e(nc.sbuf_tensor("invl", [128, OT * H], F32))
        sgv = e(nc.sbuf_tensor("sgv", [128, 44], F32))
        suv = e(nc.sbuf_tensor("suv", [128, 44], F32))
        sdv = e(nc.sbuf_tensor("sdv", [128, 16], F32))
        st = e(nc.sbuf_tensor("st", [128, 64], F32))
        sws = e(nc.sbuf_tensor("sws", [128, 8], F32))
        cst = e(nc.sbuf_tensor("cst", [128, 4], F32))
        nc.const_aps.register(F32, 0.5, cst[:, 0:1])
        nc.const_aps.register(F32, 0.0, cst[:, 1:2])
        nc.const_aps.register(F32, 1.0, cst[:, 2:3])
        nc.const_aps.register(F32, ZSH, cst[:, 3:4])

        wpart = e(nc.sbuf_tensor("wpart", [128, 64], F32))
        with ExitStack() as _ph:
            _en = _ph.enter_context
            wb0 = _en(nc.sbuf_tensor("wb0", [128, C], F32))
            wb1 = _en(nc.sbuf_tensor("wb1", [128, C], F32))
            ps_sw = _en(nc.psum_tensor("ps_sw", [128, 8], F32))
            sgn = _en(nc.sbuf_tensor("sgn", [128, 128], F32))
            sun = _en(nc.sbuf_tensor("sun", [128, 128], F32))
            sdn = _en(nc.sbuf_tensor("sdn", [128, 128], F32))
            psv = _en(nc.psum_tensor("psv", [128, 512], F32))
            wbuf = [wb0, wb1]
            p = Plan(semc)
            p.dma("sync", lambda s: s.dma_start(out=identf[:], in_=id_d), inc="ld")
            p.dma("sync", lambda s: s.dma_start(out=tri[:], in_=tri_d), inc="ld")
            p.dma("sync", lambda s: s.dma_start(out=oma[:], in_=oma_d), inc="ld")
            p.dma("sync", lambda s: s.dma_start(
                out=sgn[0:44, :], in_=sg_d.rearrange("(a p) -> a p", p=128)), inc="ld")
            p.dma("sync", lambda s: s.dma_start(
                out=sun[0:44, :], in_=su_d.rearrange("(a p) -> a p", p=128)), inc="ld")
            p.dma("sync", lambda s: s.dma_start(
                out=sdn[0:16, :], in_=sd_d.rearrange("(a p) -> a p", p=128)), inc="ld")
            p.op("tensor", lambda t: t.matmul(psv[:, 0:44], sgn[0:44, :],
                                              identf[0:44, 0:44], is_transpose=True,
                                              start=True, stop=True),
                 waits=[("ld", 16 * 6)], inc="tp")
            p.op("tensor", lambda t: t.matmul(psv[:, 64:108], sun[0:44, :],
                                              identf[0:44, 0:44], is_transpose=True,
                                              start=True, stop=True), inc="tp")
            p.op("tensor", lambda t: t.matmul(psv[:, 128:144], sdn[0:16, :],
                                              identf[0:16, 0:16], is_transpose=True,
                                              start=True, stop=True), inc="tp")
            p.op("vector", lambda v: v.tensor_copy(sgv[:], psv[:, 0:44]),
                 waits=[("tp", p.cur("tp"))], inc="vst")
            p.op("vector", lambda v: v.tensor_copy(suv[:], psv[:, 64:108]), inc="vst")
            p.op("vector", lambda v: v.tensor_copy(sdv[:], psv[:, 128:144]), inc="vst")
            p.op("gpsimd", lambda g: g.memset(ones[:], 1.0), inc="gp")
            p.op("gpsimd", lambda g: g.memset(cst[:, 0:1], 0.5), inc="gp")
            p.op("gpsimd", lambda g: g.memset(cst[:, 1:2], 0.0), inc="gp")
            p.op("gpsimd", lambda g: g.memset(cst[:, 2:3], 1.0), inc="gp")
            p.op("gpsimd", lambda g: g.memset(cst[:, 3:4], ZSH), inc="gp")
            p.op("vector", lambda v: v.tensor_copy(identb[:], identf[:]),
                 waits=[("ld", 96)], inc="vst")
            for t in range(64):
                src = (wqkv[t * 128:(t + 1) * 128, :] if t < 48 else
                       wproj[(t - 48) * 128:(t - 47) * 128, :])
                dsem = "da" if t % 2 == 0 else "db"
                waits = [("vst", p.cur("vst") - 1)] if t >= 2 else []
                p.dma("sync", (lambda s, _s=src, _b=wbuf[t % 2]:
                               s.dma_start(out=_b[:], in_=_s)), waits=waits, inc=dsem)
                p.op("vector", (lambda v, _b=wbuf[t % 2], _t=t:
                                v.tensor_reduce(wpart[:, _t:_t + 1], _b[:], AX.X,
                                                ALU.add, apply_absolute_value=True)),
                     waits=[(dsem, p.cur(dsem))], inc="vst")
            p.op("vector", lambda v: v.tensor_reduce(
                sws[:, 0:1], wpart[:, 0:48], AX.X, ALU.add), inc="vst")
            p.op("vector", lambda v: v.tensor_reduce(
                sws[:, 1:2], wpart[:, 48:64], AX.X, ALU.add), inc="vst")
            p.op("tensor", lambda t: t.matmul(ps_sw[0:1, 0:1], sws[:, 0:1],
                                              ones[:, 0:1], start=True, stop=True),
                 waits=[("vst", p.cur("vst")), ("gp", 1)], inc="mm")
            p.op("tensor", lambda t: t.matmul(ps_sw[0:1, 1:2], sws[:, 1:2],
                                              ones[:, 0:1], start=True, stop=True),
                 inc="mm")
            p.op("vector", lambda v: v.tensor_copy(sws[0:1, 2:4], ps_sw[0:1, 0:2]),
                 waits=[("mm", p.cur("mm"))], inc="vst")
            p.op("vector", lambda v: v.tensor_scalar(
                sws[0:1, 4:5], sws[0:1, 2:3], 1.0 / (FQKV * C), 1e-5,
                op0=ALU.mult, op1=ALU.max), inc="vst")
            p.op("vector", lambda v: v.tensor_scalar(
                sws[0:1, 5:6], sws[0:1, 3:4], 1.0 / (C * C), 1e-5,
                op0=ALU.mult, op1=ALU.max), inc="vst")
            p.op("vector", lambda v: v.reciprocal(sws[0:1, 6:8], sws[0:1, 4:6]),
                 inc="vst")
            vv = p.cur("vst")
            p.op("tensor", lambda t: t.matmul(ps_sw[:, 4:8], ones[0:1, 0:128],
                                              sws[0:1, 4:8], start=True, stop=True),
                 waits=[("vst", vv)], inc="mm")
            p.op("vector", lambda v: v.tensor_copy(bcw[:, 0:2], ps_sw[:, 6:8]),
                 waits=[("mm", p.cur("mm"))], inc="vst")
            p.op("vector", lambda v: v.tensor_copy(bcw[:, 2:4], ps_sw[:, 4:6]),
                 inc="vst")
            emit(nc, p, sems)

            with ExitStack() as _ph:
                _en = _ph.enter_context
                ybuf = _en(nc.sbuf_tensor("ybuf", [128, C], F32))
                mbuf = _en(nc.sbuf_tensor("mbuf", [128, C], F32))
                qu32 = _en(nc.sbuf_tensor("qu32", [128, C], F32))
                qu64 = _en(nc.sbuf_tensor("qu64", [128, C], F32))
                tbuf = _en(nc.sbuf_tensor("tbuf", [128, C], BF16))
                te0 = _en(nc.sbuf_tensor("te0", [128, C], BF16))
                te1 = _en(nc.sbuf_tensor("te1", [128, C], BF16))
                pt0 = _en(nc.psum_tensor("pt0", [128, C], BF16))
                pt1 = _en(nc.psum_tensor("pt1", [128, C], BF16))
                tev_ = [te0, te1]
                ptp_ = [pt0, pt1]
                p = Plan(semc)
                spq = {0: None, 1: None}
                for t in range(64):
                    qkv = t < 48
                    src = (wqkv[t * 128:(t + 1) * 128, :] if qkv else
                           wproj[(t - 48) * 128:(t - 47) * 128, :])
                    wb = wbuf[t % 2]
                    dsem = "da" if t % 2 == 0 else "db"
                    waits = [("y1", p.cur("y1") - 1)] if t >= 2 else []
                    p.dma("sync", (lambda s, _s=src, _b=wb: s.dma_start(out=_b[:], in_=_s)),
                          waits=waits, inc=dsem)
                    sc = bcw[:, 0:1] if qkv else bcw[:, 1:2]
                    p.op("scalar", (lambda s, _b=wb, _sc=sc: s.activation(
                        ybuf[:], _b[:], AF.Identity, bias=0.0, scale=_sc)),
                        waits=[(dsem, p.cur(dsem)), ("xq", p.cur("xq"))], inc="y1")
                    p.op("vector", (lambda v: v.tensor_scalar(
                        mbuf[:], ybuf[:], -2.0, 2.0, op0=ALU.max,
                        op1=ALU.min)), waits=[("y1", p.cur("y1"))], inc="vst")
                    p.op("vector", (lambda v: v.tensor_scalar(
                        qu32[:], mbuf[:], M32, -M32, op0=ALU.add, op1=ALU.add)),
                        inc="vst")
                    p.op("vector", (lambda v: v.tensor_scalar(
                        qu64[:], mbuf[:], ZSH, M64, op0=ALU.add, op1=ALU.add)),
                        inc="vst")
                    p.op("vector", (lambda v: v.tensor_scalar(
                        qu64[:], qu64[:], -M64, -ZSH, op0=ALU.add, op1=ALU.add)),
                        inc="vst")
                    p.op("vector", (lambda v: v.scalar_tensor_tensor(
                        ybuf[:], mbuf[:], -1.0, qu64[:], ALU.mult, ALU.add)),
                        inc="vst")
                    p.op("vector", (lambda v: v.tensor_scalar(
                        mbuf[:], ybuf[:], -0.5, 0.5, op0=ALU.max, op1=ALU.min)),
                        inc="vst")
                    p.op("vector", (lambda v: v.tensor_scalar(
                        qu64[:], ybuf[:], 1.0, -0.5, op0=ALU.add, op1=ALU.max)),
                        inc="vst")
                    p.op("vector", (lambda v: v.scalar_tensor_tensor(
                        ybuf[:], qu64[:], 0.0, mbuf[:], ALU.min, ALU.subtract)),
                        inc="vst")
                    p.op("vector", (lambda v: v.scalar_tensor_tensor(
                        mbuf[:], qu32[:], 0.0, ybuf[:], ALU.add, ALU.subtract)),
                        inc="vst")
                    p.op("vector", (lambda v: v.tensor_scalar(
                        tbuf[:], mbuf[:], -1.0, 1.0, op0=ALU.max, op1=ALU.min)),
                        inc="xq")
                    ptp = ptp_[t % 2]
                    for k in range(16):
                        wt = []
                        if k == 0:
                            wt.append(("xq", p.cur("xq")))
                            if t >= 2:
                                wt.append(("tev", p.cur("tev") - 1))
                        p.op("tensor", (lambda te, _k=k, _p=ptp: te.matmul(
                            _p[:, _k * 128:(_k + 1) * 128],
                            tbuf[:, _k * 128:(_k + 1) * 128], identb[:],
                            is_transpose=True, start=(_k % 8 == 0), stop=(_k % 8 == 7))),
                            waits=wt, inc=("tp" if k == 15 else None))
                    evw = [("tp", p.cur("tp"))]
                    if spq[t % 2] is not None:
                        evw.append(spq[t % 2])
                    p.op("vector", (lambda v, _p=ptp, _e=tev_[t % 2]:
                                    v.tensor_copy(_e[:], _p[:])),
                         waits=evw, inc="tev")
                    if qkv:
                        dst = WQKVT.rearrange("(k p) f -> p k f", p=128)[:, :, t * 128:(t + 1) * 128]
                    else:
                        dst = WPROJT.rearrange("(k p) f -> p k f", p=128)[:, :, (t - 48) * 128:(t - 47) * 128]
                    ssem = "dc" if t % 2 == 0 else "dd"
                    p.dma("gpsimd", (lambda g, _d=dst, _e=tev_[t % 2]: g.dma_start(
                        out=_d, in_=_e[:].rearrange("p (k j) -> p k j", j=128))),
                        waits=[("tev", p.cur("tev"))], inc=ssem)
                    spq[t % 2] = (ssem, p.cur(ssem))
                p.op("gpsimd", lambda g: g.engine_nop(),
                     waits=[("dc", p.cur("dc")), ("dd", p.cur("dd"))], inc="gp")
                emit(nc, p, sems)

        _xq_scope = ExitStack()
        xqT = _xq_scope.enter_context(nc.sbuf_tensor("xqT", [128, 16 * NTOK], BF16))
        xqT3 = xqT[:].rearrange("p (k n) -> p k n", n=NTOK)
        with ExitStack() as _ph:
            _en = _ph.enter_context
            xb0 = _en(nc.sbuf_tensor("xb0", [128, C], F32))
            xb1 = _en(nc.sbuf_tensor("xb1", [128, C], F32))
            sqb = _en(nc.sbuf_tensor("sqb", [128, C], F32))
            yb = _en(nc.sbuf_tensor("yb", [128, C], F32))
            mb = _en(nc.sbuf_tensor("mb", [128, C], F32))
            ub = _en(nc.sbuf_tensor("ub", [128, C], F32))
            qbv = _en(nc.sbuf_tensor("qbv", [128, C], BF16))
            px0 = _en(nc.psum_tensor("px0", [128, C], BF16))
            px1 = _en(nc.psum_tensor("px1", [128, C], BF16))
            xb = [xb0, xb1]
            pxt = [px0, px1]
            p = Plan(semc)
            for t in range(NT):
                b = xb[t % 2]
                dsem = "da" if t % 2 == 0 else "db"
                waits = [("y1", p.cur("y1") - 1)] if t >= 2 else []
                p.dma("sync", (lambda s, _b=b, _t=t: s.dma_start(
                    out=_b[:], in_=xp_d[_t * 128:(_t + 1) * 128, :])), waits=waits, inc=dsem)
                ldv = p.cur(dsem)
                o = (t % 2) * 32

                def tiny(fn, waits=()):
                    p.op("vector", fn, waits=waits, inc="vst")
                p.op("vector", (lambda v, _b=b, _o=o: v.tensor_reduce(
                    st[:, _o:_o + 1], _b[:], AX.X, ALU.add, negate=True)),
                    waits=[(dsem, ldv)], inc="vst")
                p.op("vector", (lambda v, _b=b, _o=o: v.tensor_reduce(
                    st[:, _o + 1:_o + 2], _b[:], AX.X, ALU.max)), inc="vst")
                p.op("vector", (lambda v, _b=b, _o=o: v.tensor_reduce(
                    st[:, _o + 2:_o + 3], _b[:], AX.X, ALU.min)), inc="vst")
                p.op("scalar", (lambda s, _b=b, _o=o: s.activation(
                    sqb[:], _b[:], AF.Square, accum_out=st[:, _o + 3:_o + 4])),
                    waits=[(dsem, ldv), ("xq", p.cur("xq"))], inc="sq")
                tiny(lambda v, _o=o: v.tensor_scalar(
                    st[:, _o + 4:_o + 5], st[:, _o:_o + 1], 1.0 / C, None, op0=ALU.mult))
                p.op("vector", (lambda v, _o=o: v.tensor_scalar(
                    st[:, _o + 5:_o + 6], st[:, _o + 3:_o + 4], 1.0 / C, EPS,
                    op0=ALU.mult, op1=ALU.add)),
                    waits=[("sq", p.cur("sq"))], inc="vst")
                tiny(lambda v, _o=o: v.scalar_tensor_tensor(
                    st[:, _o + 6:_o + 7], st[:, _o + 4:_o + 5], -1.0,
                    st[:, _o + 4:_o + 5], ALU.mult, ALU.mult))
                tiny(lambda v, _o=o: v.tensor_tensor(
                    st[:, _o + 7:_o + 8], st[:, _o + 5:_o + 6],
                    st[:, _o + 6:_o + 7], ALU.add))
                p.op("scalar", (lambda s, _o=o: s.activation(
                    st[:, _o + 8:_o + 9], st[:, _o + 7:_o + 8], AF.Sqrt)),
                    waits=[("vst", p.cur("vst"))], inc="sq")
                p.op("vector", (lambda v, _o=o: v.reciprocal(
                    st[:, _o + 9:_o + 10], st[:, _o + 8:_o + 9])),
                    waits=[("sq", p.cur("sq"))], inc="vst")
                tiny(lambda v, _o=o: v.tensor_tensor(
                    st[:, _o + 10:_o + 11], st[:, _o + 1:_o + 2],
                    st[:, _o + 4:_o + 5], ALU.add))
                tiny(lambda v, _o=o: v.scalar_tensor_tensor(
                    st[:, _o + 11:_o + 12], st[:, _o + 2:_o + 3], -1.0,
                    st[:, _o + 4:_o + 5], ALU.mult, ALU.subtract))
                tiny(lambda v, _o=o: v.tensor_tensor(
                    st[:, _o + 12:_o + 13], st[:, _o + 10:_o + 11],
                    st[:, _o + 11:_o + 12], ALU.max))
                tiny(lambda v, _o=o: v.tensor_tensor(
                    st[:, _o + 13:_o + 14], st[:, _o + 12:_o + 13],
                    st[:, _o + 9:_o + 10], ALU.mult))
                tiny(lambda v, _o=o: v.tensor_scalar(
                    st[:, _o + 14:_o + 15], st[:, _o + 13:_o + 14], 1e-5, 1.0 / 127.0,
                    op0=ALU.max, op1=ALU.mult))
                tiny(lambda v, _o=o, _t=t: v.tensor_copy(
                    invs[:, _t:_t + 1], st[:, _o + 14:_o + 15]))
                tiny(lambda v, _o=o: v.reciprocal(
                    st[:, _o + 15:_o + 16], st[:, _o + 14:_o + 15]))
                tiny(lambda v, _o=o: v.tensor_tensor(
                    st[:, _o + 16:_o + 17], st[:, _o + 9:_o + 10],
                    st[:, _o + 15:_o + 16], ALU.mult))
                tiny(lambda v, _o=o: v.tensor_scalar(
                    st[:, _o + 17:_o + 18], st[:, _o + 4:_o + 5],
                    st[:, _o + 16:_o + 17], None, op0=ALU.mult))
                p.op("scalar", (lambda s, _b=b, _o=o: s.activation(
                    yb[:], _b[:], AF.Identity, bias=st[:, _o + 17:_o + 18],
                    scale=st[:, _o + 16:_o + 17])),
                    waits=[("vst", p.cur("vst"))], inc="y1")
                p.op("vector", (lambda v: v.tensor_scalar(
                    ub[:], yb[:], M32, -M32, op0=ALU.add, op1=ALU.add)),
                    waits=[("y1", p.cur("y1"))], inc="vst")
                p.op("vector", (lambda v: v.tensor_scalar(
                    mb[:], yb[:], ZSH, M64, op0=ALU.add, op1=ALU.add)), inc="vst")
                p.op("vector", (lambda v: v.tensor_scalar(
                    mb[:], mb[:], -M64, -ZSH, op0=ALU.add, op1=ALU.add)), inc="vst")
                p.op("vector", (lambda v: v.scalar_tensor_tensor(
                    yb[:], yb[:], -1.0, mb[:], ALU.mult, ALU.add)), inc="vst")
                p.op("vector", (lambda v: v.tensor_scalar(
                    mb[:], yb[:], -0.5, 0.5, op0=ALU.max, op1=ALU.min)), inc="vst")
                p.op("vector", (lambda v: v.tensor_scalar(
                    yb[:], yb[:], 1.0, -0.5, op0=ALU.add, op1=ALU.max)), inc="vst")
                p.op("vector", (lambda v: v.scalar_tensor_tensor(
                    yb[:], yb[:], 0.0, mb[:], ALU.min, ALU.subtract)), inc="vst")
                p.op("vector", (lambda v: v.scalar_tensor_tensor(
                    qbv[:], ub[:], 0.0, yb[:], ALU.add, ALU.subtract)), inc="xq")
                ptp = pxt[t % 2]
                for k in range(16):
                    wt = []
                    if k == 0:
                        wt.append(("xq", p.cur("xq")))
                        if t >= 2:
                            wt.append(("xev", p.cur("xev") - 1))
                    p.op("tensor", (lambda te, _k=k, _p=ptp: te.matmul(
                        _p[:, _k * 128:(_k + 1) * 128], qbv[:, _k * 128:(_k + 1) * 128],
                        identb[:], is_transpose=True,
                        start=(_k % 8 == 0), stop=(_k % 8 == 7))),
                        waits=wt, inc=("xtp" if k == 15 else None))
                p.op("vector", (lambda v, _p=ptp, _t=t: v.tensor_copy(
                    xqT3[:, :, _t * 128:(_t + 1) * 128],
                    _p[:].rearrange("p (k j) -> p k j", j=128))),
                    waits=[("xtp", p.cur("xtp"))], inc="xev")
            emit(nc, p, sems)

        with ExitStack() as _ph:
            _en = _ph.enter_context
            wc0 = _en(nc.sbuf_tensor("wc0", [128, 16 * 512], BF16))
            wc1 = _en(nc.sbuf_tensor("wc1", [128, 16 * 512], BF16))
            eb0 = _en(nc.sbuf_tensor("eb0", [128, 512], F32))
            eb1 = _en(nc.sbuf_tensor("eb1", [128, 512], F32))
            tb0 = _en(nc.sbuf_tensor("tb0", [128, 512], F32))
            tb1 = _en(nc.sbuf_tensor("tb1", [128, 512], F32))
            pq0 = _en(nc.psum_tensor("pq0", [128, 512], F32))
            pq1 = _en(nc.psum_tensor("pq1", [128, 512], F32))
            pq2 = _en(nc.psum_tensor("pq2", [128, 512], F32))
            pq3 = _en(nc.psum_tensor("pq3", [128, 512], F32))
            pr0 = _en(nc.psum_tensor("pr0", [128, 512], F32))
            pr1 = _en(nc.psum_tensor("pr1", [128, 512], F32))
            wck = [wc0, wc1]
            ebv = [eb0, eb1]
            tbv = [tb0, tb1]
            pq = [pq0, pq1, pq2, pq3]
            ptr = [pr0, pr1]
            p = Plan(semc)
            p.op("vector", lambda v: v.tensor_scalar(
                qsc[:], invs[:], bcw[:, 2:3], SCALE, op0=ALU.mult, op1=ALU.mult),
                waits=[("xev", 32)], inc="vst")
            p.op("vector", lambda v: v.tensor_scalar(
                ksc[:], invs[:], bcw[:, 2:3], None, op0=ALU.mult), inc="vst")
            vsc = p.cur("vst")
            ntile = 0
            nch = 0
            mm_after_chunk = {}
            spq = {0: None, 1: None}
            for half in range(2):
                for j in (range(12) if half == 0 else range(4, 12)):
                    wcb = wck[nch % 2]
                    wsem = "de" if nch % 2 == 0 else "df"
                    waits = []
                    if nch >= 2:
                        waits.append(("mm", mm_after_chunk[nch - 2]))
                    src = WQKVT.rearrange("(k p) f -> p k f", p=128)[:, :, j * 512:(j + 1) * 512]
                    p.dma("sync", (lambda s, _w=wcb, _s=src: s.dma_start(
                        out=_w[:].rearrange("p (k f) -> p k f", f=512), in_=_s)),
                        waits=waits, inc=wsem)
                    wldv = p.cur(wsem)
                    for i in range(16):
                        t = half * 16 + i
                        if j < 4 and t >= 16:
                            continue
                        bank = pq[ntile % 4]
                        eb = ebv[ntile % 2]
                        tbx = tbv[ntile % 2]
                        for k in range(16):
                            wt = []
                            if k == 0:
                                wt.append((wsem, wldv))
                                if ntile >= 4:
                                    wt.append(("ev", p.cur("ev") - 3))
                            p.op("tensor", (lambda te, _k=k, _b=bank, _t=t, _w=wcb:
                                            te.matmul(_b[:],
                                                      xqT3[:, _k, _t * 128:(_t + 1) * 128],
                                                      _w[:, _k * 512:(_k + 1) * 512],
                                                      start=(_k == 0), stop=(_k == 15))),
                                 waits=wt, inc=("mm" if k == 15 else None))
                        sc = qsc[:, t:t + 1] if j < 4 else ksc[:, t:t + 1]
                        wt = [("mm", p.cur("mm")), ("vst", vsc)]
                        if spq[ntile % 2] is not None:
                            wt.append(spq[ntile % 2])
                        p.op("scalar", (lambda s, _b=bank, _e=eb, _sc=sc:
                                        s.activation(_e[:], _b[:], AF.Identity,
                                                     scale=_sc)),
                             waits=wt, inc="ev")
                        evv = p.cur("ev")
                        ssem = "dc" if ntile % 2 == 0 else "dd"
                        if j >= 8:
                            dst = VS[t * 128:(t + 1) * 128, (j - 8) * 512:(j - 7) * 512]
                            p.dma("gpsimd", (lambda g, _d=dst, _e=eb: g.dma_start(
                                out=_d, in_=_e[:])), waits=[("ev", evv)], inc=ssem)
                            spq[ntile % 2] = (ssem, p.cur(ssem))
                        else:
                            ptb = ptr[ntile % 2]
                            for k in range(4):
                                wt = []
                                if k == 0:
                                    wt.append(("ev", evv))
                                    if ntile >= 2:
                                        wt.append(("tev", p.cur("tev") - 1))
                                p.op("tensor", (lambda te, _k=k, _p=ptb, _e=eb:
                                                te.matmul(_p[:, _k * 128:(_k + 1) * 128],
                                                          _e[:, _k * 128:(_k + 1) * 128],
                                                          identf[:], is_transpose=True,
                                                          start=(_k == 0), stop=(_k == 3))),
                                    waits=wt, inc=("tp" if k == 3 else None))
                            p.op("vector", (lambda v, _p=ptb, _t=tbx: v.tensor_copy(
                                _t[:], _p[:])), waits=[("tp", p.cur("tp"))], inc="tev")
                            if j < 4:
                                dd = QT.rearrange("(c k p) r -> c p k r", k=4, p=128)[j][:, :, t * 128:(t + 1) * 128]
                            else:
                                dd = KT.rearrange("(c k p) r -> c p k r", k=4, p=128)[j - 4][:, :, t * 128:(t + 1) * 128]
                            p.dma("gpsimd", (lambda g, _d=dd, _t=tbx: g.dma_start(
                                out=_d, in_=_t[:].rearrange("p (k j) -> p k j", j=128))),
                                waits=[("tev", p.cur("tev"))], inc=ssem)
                            spq[ntile % 2] = (ssem, p.cur(ssem))
                        ntile += 1
                    mm_after_chunk[nch] = p.cur("mm")
                    nch += 1
            p.op("gpsimd", lambda g: g.engine_nop(),
                 waits=[("dc", p.cur("dc")), ("dd", p.cur("dd"))], inc="gp")
            emit(nc, p, sems)

        _xq_scope.close()

        with ExitStack() as _ph:
            _en = _ph.enter_context
            ib0 = _en(nc.sbuf_tensor("ib0", [128, HID], I32))
            ib1 = _en(nc.sbuf_tensor("ib1", [128, HID], I32))
            cb = _en(nc.sbuf_tensor("cb", [128, HID], BF16))
            fe0 = _en(nc.sbuf_tensor("fe0", [128, C], BF16))
            fe1 = _en(nc.sbuf_tensor("fe1", [128, C], BF16))
            pf0 = _en(nc.psum_tensor("pf0", [128, C], BF16))
            pf1 = _en(nc.psum_tensor("pf1", [128, C], BF16))
            ib = [ib0, ib1]
            fev = [fe0, fe1]
            pft = [pf0, pf1]
            p = Plan(semc)
            items = ([("g", t) for t in range(44)] + [("u", t) for t in range(44)]
                     + [("d", t) for t in range(16)])
            nr = 0
            spq = {0: None, 1: None}
            for it, (mat, t) in enumerate(items):
                if mat == "g":
                    src, nblk, dstT = wg_r[t * 128:(t + 1) * 128, :], 16, WGT
                elif mat == "u":
                    src, nblk, dstT = wu_r[t * 128:(t + 1) * 128, :], 16, WUT
                else:
                    src, nblk, dstT = wd_r[t * 128:(t + 1) * 128, :], 44, WDT
                b = ib[it % 2]
                dsem = "da" if it % 2 == 0 else "db"
                ncols = nblk * 128
                waits = [("y1", p.cur("y1") - 1)] if it >= 2 else []
                p.dma("sync", (lambda s, _b=b, _s=src, _n=ncols: s.dma_start(
                    out=_b[:, 0:_n], in_=_s)), waits=waits, inc=dsem)
                p.op("scalar", (lambda s, _b=b, _n=ncols: s.activation(
                    cb[:, 0:_n], _b[:, 0:_n], AF.Copy, bias=-128.0)),
                    waits=[(dsem, p.cur(dsem)), ("tev", p.cur("tev"))], inc="y1")
                for (k0, cnt) in [(r * 16, min(16, nblk - r * 16))
                                  for r in range((nblk + 15) // 16)]:
                    ptb = pft[nr % 2]
                    for kk in range(cnt):
                        wt = []
                        if kk == 0:
                            wt.append(("y1", p.cur("y1")))
                            if nr >= 2:
                                wt.append(("tev", p.cur("tev") - 1))
                        p.op("tensor", (lambda te, _k=k0 + kk, _kk=kk, _p=ptb, _c=cnt:
                                        te.matmul(_p[:, _kk * 128:(_kk + 1) * 128],
                                                  cb[:, _k * 128:(_k + 1) * 128], identb[:],
                                                  is_transpose=True,
                                                  start=(_kk % 8 == 0),
                                                  stop=(_kk % 8 == 7 or _kk == _c - 1))),
                            waits=wt, inc=("tp" if kk == cnt - 1 else None))
                    fv = fev[nr % 2]
                    evw = [("tp", p.cur("tp"))]
                    if spq[nr % 2] is not None:
                        evw.append(spq[nr % 2])
                    p.op("vector", (lambda v, _p=ptb, _f=fv, _c=cnt: v.tensor_copy(
                        _f[:, 0:_c * 128], _p[:, 0:_c * 128])),
                        waits=evw, inc="tev")
                    ssem = "dc" if nr % 2 == 0 else "dd"
                    dd = dstT.rearrange("(k p) f -> p k f", p=128)[:, k0:k0 + cnt, t * 128:(t + 1) * 128]
                    p.dma("gpsimd", (lambda g, _d=dd, _f=fv, _c=cnt: g.dma_start(
                        out=_d, in_=_f[:, 0:_c * 128].rearrange("p (k j) -> p k j", j=128))),
                        waits=[("tev", p.cur("tev"))], inc=ssem)
                    spq[nr % 2] = (ssem, p.cur(ssem))
                    nr += 1
            p.op("gpsimd", lambda g: g.engine_nop(),
                 waits=[("dc", p.cur("dc")), ("dd", p.cur("dd"))], inc="gp")
            emit(nc, p, sems)

        with ExitStack() as _ph:
            _en = _ph.enter_context
            sA = _en(nc.sbuf_tensor("sA", [128, 4096], F32))
            sB = _en(nc.sbuf_tensor("sB", [128, 4096], F32))
            sC = _en(nc.sbuf_tensor("sC", [128, 4096], F32))
            sD = _en(nc.sbuf_tensor("sD", [128, 4096], F32))
            eA = _en(nc.sbuf_tensor("eA", [128, 4096], F32))
            eB = _en(nc.sbuf_tensor("eB", [128, 4096], F32))
            eC = _en(nc.sbuf_tensor("eC", [128, 4096], F32))
            eD = _en(nc.sbuf_tensor("eD", [128, 4096], F32))
            qg = _en(nc.sbuf_tensor("qg", [128, 512], F32))
            kb0 = _en(nc.sbuf_tensor("kb0", [128, 512], F32))
            kb1 = _en(nc.sbuf_tensor("kb1", [128, 512], F32))
            vb0 = _en(nc.sbuf_tensor("vb0", [128, 128], F32))
            vb1 = _en(nc.sbuf_tensor("vb1", [128, 128], F32))
            pb0 = _en(nc.sbuf_tensor("pb0", [128, 512], F32))
            pb1 = _en(nc.sbuf_tensor("pb1", [128, 512], F32))
            otm = _en(nc.sbuf_tensor("otm", [128, 512], F32))
            obv = _en(nc.sbuf_tensor("obv", [128, 512], F32))
            pS0 = _en(nc.psum_tensor("pS0", [128, 512], F32))
            pS1 = _en(nc.psum_tensor("pS1", [128, 512], F32))
            pP0 = _en(nc.psum_tensor("pP0", [128, 512], F32))
            pP1 = _en(nc.psum_tensor("pP1", [128, 512], F32))
            pO = _en(nc.psum_tensor("pO", [128, 512], F32))
            pOt = _en(nc.psum_tensor("pOt", [128, 512], F32))
            strips = [sA, sB, sC, sD]
            estrips = [eA, eB, eC, eD]
            kbv = [kb0, kb1]
            vbv = [vb0, vb1]
            ptv = [pb0, pb1]
            pS = [pS0, pS1]
            pP = [pP0, pP1]
            p = Plan(semc)
            nkc = 0
            nvc = 0
            nsb = 0
            npt = 0
            se_bank = [[], []]
            for g in range(4):
                NCH = g + 1
                for h in range(H):
                    ghi = g * H + h
                    wq_ = [("mm", p.cur("mm"))] if ghi else []
                    p.dma("sync", (lambda s, _h=h, _g=g: s.dma_start(
                        out=qg[:], in_=QT[_h * 128:(_h + 1) * 128, _g * 512:(_g + 1) * 512])),
                        waits=wq_, inc="qld")
                    qldv = p.cur("qld")
                    for part in range(2):
                        for ch in range(NCH):
                            kbx = kbv[nkc % 2]
                            ksem = "da" if nkc % 2 == 0 else "db"
                            wt = []
                            if nkc >= 2:
                                wt.append(("mm", p.sem.get("_kmm%d" % (nkc - 2), 0)))
                            ksrc = KT[h * 128:(h + 1) * 128,
                                      part * 2048 + ch * 512: part * 2048 + (ch + 1) * 512]
                            p.dma("sync", (lambda s, _k=kbx, _s=ksrc: s.dma_start(
                                out=_k[:], in_=_s)), waits=[w for w in wt if w[1] > 0], inc=ksem)
                            kldv = p.cur(ksem)
                            for i in range(4):
                                ti = 4 * g + i
                                L = (ti + 1) * 128
                                w = min(512, L - ch * 512)
                                if w <= 0:
                                    continue
                                bank = pS[nsb % 2]
                                wt = [(ksem, kldv), ("qld", qldv)]
                                if nsb >= 2:
                                    wt.extend(se_bank[nsb % 2])
                                p.op("tensor", (lambda te, _b=bank, _k=kbx, _i=i, _w=w:
                                                te.matmul(_b[:, 0:_w],
                                                          qg[:, _i * 128:(_i + 1) * 128],
                                                          _k[:, 0:_w],
                                                          start=True, stop=True)),
                                     waits=wt, inc="mm")
                                mmv = p.cur("mm")
                                stripbase = 0 if part == 0 else L
                                dst0 = stripbase + ch * 512
                                diag = (w < 512) or (ch == NCH - 1 and i == 3)
                                strip = strips[i]
                                if diag:
                                    wfull = w - 128
                                    if wfull > 0:
                                        p.op("vector", (lambda v, _st=strip, _d=dst0, _w=wfull, _b=bank:
                                                        v.tensor_copy(_st[:, _d:_d + _w], _b[:, 0:_w])),
                                             waits=[("mm", mmv)], inc="sv")
                                    mk = tri if part == 0 else oma
                                    p.op("vector", (lambda v, _st=strip, _d=dst0 + wfull,
                                                    _b=bank, _w=wfull, _m=mk:
                                                    v.tensor_tensor(_st[:, _d:_d + 128],
                                                                    _b[:, _w:_w + 128],
                                                                    _m[:], ALU.add)),
                                         waits=[("mm", mmv)], inc="sv")
                                else:
                                    p.op("scalar", (lambda s, _st=strip, _d=dst0, _w=w, _b=bank:
                                                    s.copy(_st[:, _d:_d + _w], _b[:, 0:_w])),
                                         waits=[("mm", mmv)], inc="se")
                                se_bank[nsb % 2] = [("se", p.cur("se")), ("sv", p.cur("sv"))]
                                nsb += 1
                            p.sem["_kmm%d" % nkc] = p.cur("mm")
                            nkc += 1
                    if ATT_MODE >= 2:
                        sev, svv = p.cur("se"), p.cur("sv")
                        for i in range(4):
                            ti = 4 * g + i
                            L2 = 2 * (ti + 1) * 128
                            strip = strips[i]
                            p.op("vector", (lambda v, _st=strip, _L=L2, _i=i: v.tensor_reduce(
                                st[:, 48 + _i:49 + _i], _st[:, 0:_L], AX.X, ALU.max,
                                negate=True)),
                                waits=[("se", sev), ("sv", svv)], inc="mx")
                            p.op("scalar", (lambda s, _st=strip, _e=estrips[i], _L=L2, _i=i:
                                            s.activation(
                                _e[:, 0:_L], _st[:, 0:_L], AF.Exp,
                                bias=st[:, 48 + _i:49 + _i],
                                accum_out=st[:, 52 + _i:53 + _i])),
                                waits=[("mx", p.cur("mx"))], inc="xp")
                            col = (4 * g + i) * H + h
                            p.op("vector", (lambda v, _i=i, _c=col:
                                            v.reciprocal(invl[:, _c:_c + 1],
                                                         st[:, 52 + _i:53 + _i])),
                                 waits=[("xp", p.cur("xp"))], inc="vst")
                        xpv = p.cur("xp")
                    if ATT_MODE >= 3:
                        first_pv = True
                        for part in range(2):
                            for kc in range(4 * g + 4):
                                vs = max(0, kc - 4 * g)
                                ptb = pP[npt % 2]
                                for i in range(vs, 4):
                                    ti = 4 * g + i
                                    L = (ti + 1) * 128
                                    srccol = (0 if part == 0 else L) + kc * 128
                                    wt = []
                                    if i == vs:
                                        wt.append(("xp", xpv))
                                        if npt >= 2:
                                            wt.append(("pe", p.cur("pe") - 1))
                                    p.op("tensor", (lambda te, _i=i, _p=ptb, _st=estrips[i],
                                                    _sc=srccol, _vs=vs:
                                                    te.matmul(_p[:, _i * 128:(_i + 1) * 128],
                                                              _st[:, _sc:_sc + 128], identf[:],
                                                              is_transpose=True,
                                                              start=(_i == _vs), stop=(_i == 3))),
                                         waits=wt, inc=("pt" if i == 3 else None))
                                if vs > 0:
                                    p.op("vector", (lambda v, _b=ptv[npt % 2], _vs=vs:
                                                    v.memset(_b[:, 0:_vs * 128], 0.0)),
                                         waits=[("pv", p.cur("pv") - 1 if npt >= 2 else 0)])
                                p.op("vector", (lambda v, _p=ptb, _b=ptv[npt % 2], _vs=vs:
                                                v.tensor_copy(_b[:, _vs * 128:512],
                                                              _p[:, _vs * 128:512])),
                                     waits=[("pt", p.cur("pt")), ("pv", p.cur("pv") - 1 if npt >= 2 else 0)],
                                     inc="pe")
                                vbx = vbv[nvc % 2]
                                vsem = "de" if nvc % 2 == 0 else "df"
                                vsrc = VS[part * 2048 + kc * 128: part * 2048 + (kc + 1) * 128,
                                          h * 128:(h + 1) * 128]
                                wt = [("pv", p.sem.get("_vpv%d" % (nvc - 2), 0))]
                                p.dma("sync", (lambda s, _v=vbx, _s=vsrc: s.dma_start(
                                    out=_v[:], in_=_s)), waits=[w for w in wt if w[1] > 0], inc=vsem)
                                wt = [("pe", p.cur("pe")), (vsem, p.cur(vsem))]
                                if first_pv and ghi:
                                    wt.append(("oe", p.cur("oe")))
                                p.op("tensor", (lambda te, _v=vbx, _b=ptv[npt % 2],
                                                _st=first_pv, _sp=(part == 1 and kc == 4 * g + 3):
                                                te.matmul(pO[:], _v[:], _b[:],
                                                          start=_st, stop=_sp)),
                                     waits=wt, inc="pv")
                                p.sem["_vpv%d" % nvc] = p.cur("pv")
                                first_pv = False
                                npt += 1
                                nvc += 1
                    if ATT_MODE >= 4:
                        p.op("scalar", lambda s: s.copy(otm[:], pO[:]),
                             waits=[("pv", p.cur("pv")), ("ov", p.cur("ov"))], inc="oe")
                        for k in range(4):
                            wt = [("oe", p.cur("oe"))] if k == 0 else []
                            if ghi >= 1 and k == 0:
                                wt.append(("ov", p.cur("ov")))
                            p.op("tensor", (lambda te, _k=k: te.matmul(
                                pOt[:, _k * 128:(_k + 1) * 128], otm[:, _k * 128:(_k + 1) * 128],
                                identf[:], is_transpose=True, start=(_k == 0), stop=(_k == 3))),
                                waits=wt, inc=("ot" if k == 3 else None))
                        p.op("vector", (lambda v: v.tensor_copy(obv[:], pOt[:])),
                             waits=[("ot", p.cur("ot")), ("dg", p.cur("dg"))],
                             inc="ov")
                        dd = OS.rearrange("(t p) f -> p t f", p=128)[:, 4 * g:4 * g + 4,
                                                                    h * 128:(h + 1) * 128]
                        p.dma("gpsimd", (lambda g_, _d=dd: g_.dma_start(
                            out=_d, in_=obv[:].rearrange("p (i f) -> p i f", f=128))),
                            waits=[("ov", p.cur("ov"))], inc="dg")
            p.op("gpsimd", lambda g_: g_.engine_nop(),
                 waits=[("dg", p.cur("dg"))], inc="gp")
            emit(nc, p, sems)

        with ExitStack() as _ph:
            _en = _ph.enter_context
            wpt = _en(nc.sbuf_tensor("wpt", [128, 16 * C], BF16))
            ob0 = _en(nc.sbuf_tensor("ob0", [128, C], F32))
            ob1 = _en(nc.sbuf_tensor("ob1", [128, C], F32))
            nb = _en(nc.sbuf_tensor("nb", [128, C], F32))
            yb2 = _en(nc.sbuf_tensor("yb2", [128, C], F32))
            mb2 = _en(nc.sbuf_tensor("mb2", [128, C], F32))
            ub2 = _en(nc.sbuf_tensor("ub2", [128, C], F32))
            oqb = _en(nc.sbuf_tensor("oqb", [128, C], BF16))
            oqT = _en(nc.sbuf_tensor("oqT", [128, C], BF16))
            prb = _en(nc.sbuf_tensor("prb", [128, C], F32))
            lnb = _en(nc.sbuf_tensor("lnb", [128, C], F32))
            xo0 = _en(nc.sbuf_tensor("xo0", [128, C], F32))
            xo1 = _en(nc.sbuf_tensor("xo1", [128, C], F32))
            xmb = _en(nc.sbuf_tensor("xmb", [128, C], F32))
            x2b = _en(nc.sbuf_tensor("x2b", [128, C], F32))
            hib = _en(nc.sbuf_tensor("hib", [128, C], BF16))
            lob = _en(nc.sbuf_tensor("lob", [128, C], BF16))
            hev = _en(nc.sbuf_tensor("hev", [128, C], BF16))
            lev = _en(nc.sbuf_tensor("lev", [128, C], BF16))
            pqt = _en(nc.psum_tensor("pqt", [128, C], BF16))
            ppr0 = _en(nc.psum_tensor("ppr0", [128, 512], F32))
            ppr1 = _en(nc.psum_tensor("ppr1", [128, 512], F32))
            phl0 = _en(nc.psum_tensor("phl0", [128, C], BF16))
            phl1 = _en(nc.psum_tensor("phl1", [128, C], BF16))
            obx = [ob0, ob1]
            xox = [xo0, xo1]
            ppr = [ppr0, ppr1]
            p = Plan(semc)
            p.dma("sync", lambda s: s.dma_start(
                out=wpt[:].rearrange("p (k f) -> p k f", f=C),
                in_=WPROJT.rearrange("(k p) f -> p k f", p=128)), inc="wld")
            wptv = p.cur("wld")
            for t in range(OT):
                ob = obx[t % 2]
                xo = xox[t % 2]
                osem = "da" if t % 2 == 0 else "db"
                xsem = "de" if t % 2 == 0 else "df"
                waits = [("prx", p.cur("prx") - 1)] if t >= 2 else []
                p.dma("sync", (lambda s, _b=ob, _t=t: s.dma_start(
                    out=_b[:], in_=OS[_t * 128:(_t + 1) * 128, :])), waits=waits, inc=osem)
                osv = p.cur(osem)
                p.dma("sync", (lambda s, _b=xo, _t=t: s.dma_start(
                    out=_b[:], in_=xp_d[_t * 128:(_t + 1) * 128, :])),
                    waits=list(waits), inc=xsem)
                xsv = p.cur(xsem)
                for hh in range(H):
                    wt = [(osem, osv)] if hh == 0 else []
                    if hh == 0 and t >= 2:
                        wt.append(("xq", p.cur("xq")))
                    p.op("scalar", (lambda s, _h=hh, _t=t, _b=ob: s.activation(
                        nb[:, _h * 128:(_h + 1) * 128],
                        _b[:, _h * 128:(_h + 1) * 128],
                        AF.Identity, scale=invl[:, _t * H + _h:_t * H + _h + 1])),
                        waits=wt, inc=("y1" if hh == H - 1 else None))
                y1v = p.cur("y1")
                o = (t % 2) * 32
                p.op("vector", (lambda v, _o=o: v.tensor_reduce(
                    st[:, _o:_o + 1], nb[:], AX.X, ALU.max, apply_absolute_value=True)),
                    waits=[("y1", y1v)], inc="vst")
                p.op("vector", (lambda v, _o=o: v.tensor_scalar(
                    st[:, _o + 1:_o + 2], st[:, _o:_o + 1], 1e-5, 1.0 / 127.0,
                    op0=ALU.max, op1=ALU.mult)), inc="vst")
                p.op("vector", (lambda v, _o=o: v.reciprocal(
                    st[:, _o + 2:_o + 3], st[:, _o + 1:_o + 2])), inc="vst")
                p.op("vector", (lambda v, _o=o: v.tensor_scalar(
                    st[:, _o + 3:_o + 4], st[:, _o + 1:_o + 2], bcw[:, 3:4], None,
                    op0=ALU.mult)), inc="vst")
                p.op("scalar", (lambda s, _o=o: s.activation(
                    yb2[:], nb[:], AF.Identity, bias=0.0, scale=st[:, _o + 2:_o + 3])),
                    waits=[("vst", p.cur("vst"))], inc="y1")
                p.op("vector", (lambda v: v.tensor_scalar(
                    ub2[:], yb2[:], M32, -M32, op0=ALU.add, op1=ALU.add)),
                    waits=[("y1", p.cur("y1"))], inc="vst")
                p.op("vector", (lambda v: v.tensor_scalar(
                    mb2[:], yb2[:], ZSH, M64, op0=ALU.add, op1=ALU.add)), inc="vst")
                p.op("vector", (lambda v: v.tensor_scalar(
                    mb2[:], mb2[:], -M64, -ZSH, op0=ALU.add, op1=ALU.add)), inc="vst")
                p.op("vector", (lambda v: v.scalar_tensor_tensor(
                    yb2[:], yb2[:], -1.0, mb2[:], ALU.mult, ALU.add)), inc="vst")
                p.op("vector", (lambda v: v.tensor_scalar(
                    mb2[:], yb2[:], -0.5, 0.5, op0=ALU.max, op1=ALU.min)), inc="vst")
                p.op("vector", (lambda v: v.tensor_scalar(
                    yb2[:], yb2[:], 1.0, -0.5, op0=ALU.add, op1=ALU.max)), inc="vst")
                p.op("vector", (lambda v: v.scalar_tensor_tensor(
                    yb2[:], yb2[:], 0.0, mb2[:], ALU.min, ALU.subtract)), inc="vst")
                p.op("vector", (lambda v: v.scalar_tensor_tensor(
                    oqb[:], ub2[:], 0.0, yb2[:], ALU.add, ALU.subtract)), inc="xq")
                for k in range(16):
                    wt = []
                    if k == 0:
                        wt.append(("xq", p.cur("xq")))
                        if t >= 1:
                            wt.append(("tev", p.cur("tev")))
                    p.op("tensor", (lambda te, _k=k: te.matmul(
                        pqt[:, _k * 128:(_k + 1) * 128], oqb[:, _k * 128:(_k + 1) * 128],
                        identb[:], is_transpose=True,
                        start=(_k % 8 == 0), stop=(_k % 8 == 7))),
                        waits=wt, inc=("tp" if k == 15 else None))
                p.op("vector", (lambda v: v.tensor_copy(oqT[:], pqt[:])),
                     waits=[("tp", p.cur("tp"))], inc="tev")
                tevv = p.cur("tev")
                for j in range(4):
                    bank = ppr[j % 2]
                    for k in range(16):
                        wt = []
                        if k == 0:
                            wt.append(("tev", tevv))
                            wt.append(("wld", wptv))
                            if t * 4 + j >= 2:
                                wt.append(("ev", p.cur("ev") - 1))
                        p.op("tensor", (lambda te, _k=k, _j=j, _b=bank: te.matmul(
                            _b[:], oqT[:, _k * 128:(_k + 1) * 128],
                            wpt[:, _k * C + _j * 512:_k * C + (_j + 1) * 512],
                            start=(_k == 0), stop=(_k == 15))),
                            waits=wt, inc=("mm" if k == 15 else None))
                    p.op("scalar", (lambda s, _j=j, _b=bank, _o=o: s.activation(
                        prb[:, _j * 512:(_j + 1) * 512], _b[:], AF.Identity,
                        scale=st[:, _o + 3:_o + 4])),
                        waits=[("mm", p.cur("mm"))], inc="ev")
                evv = p.cur("ev")
                p.op("vector", (lambda v, _o=o: v.tensor_reduce(
                    st[:, _o + 4:_o + 5], prb[:], AX.X, ALU.add, negate=True)),
                    waits=[("ev", evv)], inc="vst")
                p.op("scalar", (lambda s, _o=o: s.activation(
                    mb2[:], prb[:], AF.Square, accum_out=st[:, _o + 5:_o + 6])),
                    waits=[("ev", evv)], inc="sq")
                p.op("vector", (lambda v, _o=o: v.tensor_scalar(
                    st[:, _o + 6:_o + 7], st[:, _o + 4:_o + 5], 1.0 / C, None,
                    op0=ALU.mult)), inc="vst")
                p.op("vector", (lambda v, _o=o: v.tensor_scalar(
                    st[:, _o + 7:_o + 8], st[:, _o + 5:_o + 6], 1.0 / C, EPS,
                    op0=ALU.mult, op1=ALU.add)),
                    waits=[("sq", p.cur("sq"))], inc="vst")
                p.op("vector", (lambda v, _o=o: v.scalar_tensor_tensor(
                    st[:, _o + 8:_o + 9], st[:, _o + 6:_o + 7], -1.0,
                    st[:, _o + 6:_o + 7], ALU.mult, ALU.mult)), inc="vst")
                p.op("vector", (lambda v, _o=o: v.tensor_tensor(
                    st[:, _o + 9:_o + 10], st[:, _o + 7:_o + 8],
                    st[:, _o + 8:_o + 9], ALU.add)), inc="vst")
                p.op("scalar", (lambda s, _o=o: s.activation(
                    st[:, _o + 10:_o + 11], st[:, _o + 9:_o + 10], AF.Sqrt)),
                    waits=[("vst", p.cur("vst"))], inc="sq")
                p.op("vector", (lambda v, _o=o: v.reciprocal(
                    st[:, _o + 11:_o + 12], st[:, _o + 10:_o + 11])),
                    waits=[("sq", p.cur("sq"))], inc="vst")
                p.op("vector", (lambda v, _o=o: v.tensor_tensor(
                    st[:, _o + 12:_o + 13], st[:, _o + 6:_o + 7],
                    st[:, _o + 11:_o + 12], ALU.mult)), inc="vst")
                p.op("scalar", (lambda s, _o=o: s.activation(
                    lnb[:], prb[:], AF.Identity, bias=st[:, _o + 12:_o + 13],
                    scale=st[:, _o + 11:_o + 12])),
                    waits=[("vst", p.cur("vst"))], inc="y1")
                xw = [("y1", p.cur("y1")), (xsem, xsv)]
                if t >= 1:
                    xw.append(("dg", p.sem.get("_xmidsp%d" % (t - 1), 0)))
                p.op("vector", (lambda v, _x=xo: v.tensor_tensor(
                    xmb[:], _x[:], lnb[:], ALU.add)),
                    waits=xw, inc="fx")
                fxv = p.cur("fx")
                p.dma("gpsimd", (lambda g, _t=t: g.dma_start(
                    out=XMID[_t * 128:(_t + 1) * 128, :], in_=xmb[:])),
                    waits=[("fx", fxv)], inc="dg")
                p.sem["_xmidsp%d" % t] = p.cur("dg")
                p.op("vector", (lambda v, _o=o: v.tensor_reduce(
                    st[:, _o + 13:_o + 14], xmb[:], AX.X, ALU.add, negate=True)),
                    waits=[("fx", fxv)], inc="vst")
                p.op("scalar", (lambda s, _o=o: s.activation(
                    mb2[:], xmb[:], AF.Square, accum_out=st[:, _o + 14:_o + 15])),
                    waits=[("fx", fxv)], inc="sq")
                p.op("vector", (lambda v, _o=o: v.tensor_scalar(
                    st[:, _o + 15:_o + 16], st[:, _o + 13:_o + 14], 1.0 / C, None,
                    op0=ALU.mult)), inc="vst")
                p.op("vector", (lambda v, _o=o: v.tensor_scalar(
                    st[:, _o + 16:_o + 17], st[:, _o + 14:_o + 15], 1.0 / C, EPS,
                    op0=ALU.mult, op1=ALU.add)),
                    waits=[("sq", p.cur("sq"))], inc="vst")
                p.op("vector", (lambda v, _o=o: v.scalar_tensor_tensor(
                    st[:, _o + 17:_o + 18], st[:, _o + 15:_o + 16], -1.0,
                    st[:, _o + 15:_o + 16], ALU.mult, ALU.mult)), inc="vst")
                p.op("vector", (lambda v, _o=o: v.tensor_tensor(
                    st[:, _o + 18:_o + 19], st[:, _o + 16:_o + 17],
                    st[:, _o + 17:_o + 18], ALU.add)), inc="vst")
                p.op("scalar", (lambda s, _o=o: s.activation(
                    st[:, _o + 19:_o + 20], st[:, _o + 18:_o + 19], AF.Sqrt)),
                    waits=[("vst", p.cur("vst"))], inc="sq")
                p.op("vector", (lambda v, _o=o: v.reciprocal(
                    st[:, _o + 20:_o + 21], st[:, _o + 19:_o + 20])),
                    waits=[("sq", p.cur("sq"))], inc="vst")
                p.op("vector", (lambda v, _o=o: v.tensor_tensor(
                    st[:, _o + 21:_o + 22], st[:, _o + 15:_o + 16],
                    st[:, _o + 20:_o + 21], ALU.mult)), inc="vst")
                p.op("scalar", (lambda s, _o=o: s.activation(
                    x2b[:], xmb[:], AF.Identity, bias=st[:, _o + 21:_o + 22],
                    scale=st[:, _o + 20:_o + 21])),
                    waits=[("vst", p.cur("vst"))], inc="y1")
                y1v2 = p.cur("y1")
                hw = [("y1", y1v2)]
                if t >= 1:
                    hw.append(("tp", p.sem.get("_hltp%d" % (t - 1), 0)))
                p.op("vector", (lambda v: v.tensor_copy(hib[:], x2b[:])),
                     waits=hw, inc="vst")
                p.op("vector", (lambda v: v.scalar_tensor_tensor(
                    lob[:], hib[:], -1.0, x2b[:], ALU.mult, ALU.add)), inc="hst")
                hstv = p.cur("hst")
                for (src_, psx, evb, dstT, ssem) in ((hib, phl0, hev, XHs, "dc"),
                                                     (lob, phl1, lev, XLs, "dd")):
                    for k in range(16):
                        wt = []
                        if k == 0:
                            wt.append(("hst", hstv))
                            if t >= 1:
                                wt.append(("xev", p.cur("xev")))
                        p.op("tensor", (lambda te, _k=k, _s=src_, _p=psx: te.matmul(
                            _p[:, _k * 128:(_k + 1) * 128], _s[:, _k * 128:(_k + 1) * 128],
                            identb[:], is_transpose=True,
                            start=(_k % 8 == 0), stop=(_k % 8 == 7))),
                            waits=wt, inc=("tp" if k == 15 else None))
                    p.op("vector", (lambda v, _p=psx, _e=evb: v.tensor_copy(_e[:], _p[:])),
                         waits=[("tp", p.cur("tp")), (ssem, p.cur(ssem))],
                         inc="xev")
                    dd = dstT.rearrange("(k p) r -> p k r", p=128)[:, :, t * 128:(t + 1) * 128]
                    p.dma("gpsimd", (lambda g, _d=dd, _e=evb: g.dma_start(
                        out=_d, in_=_e[:].rearrange("p (k j) -> p k j", j=128))),
                        waits=[("xev", p.cur("xev"))], inc=ssem)
                p.sem["_hltp%d" % t] = p.cur("tp")
                p.op("vector", lambda v: v.engine_nop(), waits=[], inc="prx")
            p.op("gpsimd", lambda g: g.engine_nop(),
                 waits=[("dg", p.cur("dg")), ("dc", p.cur("dc")), ("dd", p.cur("dd"))],
                 inc="gp")
            emit(nc, p, sems)

        with ExitStack() as _ph:
            _en = _ph.enter_context
            xh = _en(nc.sbuf_tensor("xh", [128, 16 * 512], BF16))
            xl = _en(nc.sbuf_tensor("xl", [128, 16 * 512], BF16))
            sh = _en(nc.sbuf_tensor("sh", [128, 44 * 512], BF16))
            sl = _en(nc.sbuf_tensor("sl", [128, 44 * 512], BF16))
            wgb0 = _en(nc.sbuf_tensor("wgb0", [128, 16 * 128], BF16))
            wgb1 = _en(nc.sbuf_tensor("wgb1", [128, 16 * 128], BF16))
            wub0 = _en(nc.sbuf_tensor("wub0", [128, 16 * 128], BF16))
            wub1 = _en(nc.sbuf_tensor("wub1", [128, 16 * 128], BF16))
            wdb0 = _en(nc.sbuf_tensor("wdb0", [128, 44 * 128], BF16))
            wdb1 = _en(nc.sbuf_tensor("wdb1", [128, 44 * 128], BF16))
            sgb0 = _en(nc.sbuf_tensor("sgb0", [128, 512], F32))
            sgb1 = _en(nc.sbuf_tensor("sgb1", [128, 512], F32))
            sub0 = _en(nc.sbuf_tensor("sub0", [128, 512], F32))
            sub1 = _en(nc.sbuf_tensor("sub1", [128, 512], F32))
            hid = _en(nc.sbuf_tensor("hid", [128, 512], F32))
            dob = _en(nc.sbuf_tensor("dob", [128, 512], F32))
            tdb = _en(nc.sbuf_tensor("tdb", [128, 512], F32))
            pgg0 = _en(nc.psum_tensor("pgg0", [128, 512], F32))
            pgg1 = _en(nc.psum_tensor("pgg1", [128, 512], F32))
            puu0 = _en(nc.psum_tensor("puu0", [128, 512], F32))
            puu1 = _en(nc.psum_tensor("puu1", [128, 512], F32))
            pdd0 = _en(nc.psum_tensor("pdd0", [128, 512], F32))
            pdd1 = _en(nc.psum_tensor("pdd1", [128, 512], F32))
            pdt = _en(nc.psum_tensor("pdt", [128, 512], F32))
            wgb = [wgb0, wgb1]
            wub = [wub0, wub1]
            wdb = [wdb0, wdb1]
            pdd = [pdd0, pdd1]
            pgg = [pgg0, pgg1]
            puu = [puu0, puu1]
            sgbv = [sgb0, sgb1]
            subv = [sub0, sub1]
            p = Plan(semc)
            gev = {0: 0, 1: 0}
            hmul = {0: 0, 1: 0}
            p.sem.setdefault("vst", 0)
            for q in range(4):
                p.dma("sync", (lambda s, _q=q: s.dma_start(
                    out=xh[:].rearrange("p (k r) -> p k r", r=512),
                    in_=XHs.rearrange("(k p) r -> p k r", p=128)[:, :, _q * 512:(_q + 1) * 512])),
                    waits=[("dn", p.cur("dn"))], inc="da")
                p.dma("sync", (lambda s, _q=q: s.dma_start(
                    out=xl[:].rearrange("p (k r) -> p k r", r=512),
                    in_=XLs.rearrange("(k p) r -> p k r", p=128)[:, :, _q * 512:(_q + 1) * 512])),
                    waits=[("dn", p.cur("dn"))], inc="db")
                xldv = [("da", p.cur("da")), ("db", p.cur("db"))]
                for ht in range(44):
                    wgx = wgb[ht % 2]
                    wux = wub[ht % 2]
                    gsem = "de" if ht % 2 == 0 else "df"
                    usem = "dh" if ht % 2 == 0 else "di"
                    wt = []
                    if gev[ht % 2]:
                        wt.append(("ev", gev[ht % 2]))
                    p.dma("sync", (lambda s, _w=wgx, _h=ht: s.dma_start(
                        out=_w[:].rearrange("p (k j) -> p k j", j=128),
                        in_=WGT.rearrange("(k p) f -> p k f", p=128)[:, :, _h * 128:(_h + 1) * 128])),
                        waits=wt, inc=gsem)
                    p.dma("sync", (lambda s, _w=wux, _h=ht: s.dma_start(
                        out=_w[:].rearrange("p (k j) -> p k j", j=128),
                        in_=WUT.rearrange("(k p) f -> p k f", p=128)[:, :, _h * 128:(_h + 1) * 128])),
                        waits=list(wt), inc=usem)
                    wglv, wulv = p.cur(gsem), p.cur(usem)
                    par = ht % 2
                    for (wsem, wv, wbx, bank, name) in ((gsem, wglv, wgx, pgg[par], "g"),
                                                        (usem, wulv, wux, puu[par], "u")):
                        for z in range(32):
                            k, hl = z % 16, z // 16
                            rhs = xh if hl == 0 else xl
                            wt = []
                            if z == 0:
                                wt.append((wsem, wv))
                                wt.extend(xldv)
                                if gev[par]:
                                    wt.append(("ev", gev[par]))
                            p.op("tensor", (lambda te, _k=k, _w=wbx, _r=rhs, _b=bank:
                                            te.matmul(_b[:], _w[:, _k * 128:(_k + 1) * 128],
                                                      _r[:, _k * 512:(_k + 1) * 512],
                                                      start=(_k == 0 and _r is xh),
                                                      stop=(_k == 15 and _r is xl))),
                                 waits=wt, inc=("mm" if z == 31 else None))
                        wt = [("mm", p.cur("mm"))]
                        if hmul[par]:
                            wt.append(("vst", hmul[par]))
                        if name == "g":
                            p.op("scalar", (lambda s, _h=ht, _b=bank, _o=sgbv[par]:
                                            s.activation(
                                _o[:], _b[:], AF.Silu, scale=sgv[:, _h:_h + 1])),
                                waits=wt, inc="ev")
                        else:
                            p.op("scalar", (lambda s, _h=ht, _b=bank, _o=subv[par]:
                                            s.activation(
                                _o[:], _b[:], AF.Identity, scale=suv[:, _h:_h + 1])),
                                waits=wt, inc="ev")
                    gev[par] = p.cur("ev")
                    p.op("vector", (lambda v, _g=sgbv[par], _u=subv[par]:
                                    v.tensor_tensor(hid[:], _g[:], _u[:], ALU.mult)),
                        waits=[("ev", p.cur("ev"))], inc="vst")
                    hmul[par] = p.cur("vst")
                    p.op("vector", (lambda v, _h=ht: v.tensor_copy(
                        sh[:, _h * 512:(_h + 1) * 512], hid[:])), inc="vst")
                    p.op("vector", (lambda v, _h=ht: v.scalar_tensor_tensor(
                        sl[:, _h * 512:(_h + 1) * 512], sh[:, _h * 512:(_h + 1) * 512],
                        -1.0, hid[:], ALU.mult, ALU.add)), inc="gu")
                guv = p.cur("gu")
                for dt in range(16):
                    wdx = wdb[dt % 2]
                    dsem = "dj" if dt % 2 == 0 else "dk"
                    wt = [("dnv", p.cur("dnv") - 1)] if (q * 16 + dt) >= 2 else []
                    p.dma("sync", (lambda s, _w=wdx, _d=dt: s.dma_start(
                        out=_w[:].rearrange("p (k j) -> p k j", j=128),
                        in_=WDT.rearrange("(k p) f -> p k f", p=128)[:, :, _d * 128:(_d + 1) * 128])),
                        waits=wt, inc=dsem)
                    wdlv = p.cur(dsem)
                    bank = pdd[dt % 2]
                    for z in range(88):
                        k, hl = z % 44, z // 44
                        rhs = sh if hl == 0 else sl
                        wt = []
                        if z == 0:
                            wt.append((dsem, wdlv))
                            wt.append(("gu", guv))
                            if q * 16 + dt >= 2:
                                wt.append(("dnv", p.cur("dnv") - 1))
                        p.op("tensor", (lambda te, _k=k, _w=wdx, _r=rhs, _b=bank:
                                        te.matmul(_b[:], _w[:, _k * 128:(_k + 1) * 128],
                                                  _r[:, _k * 512:(_k + 1) * 512],
                                                  start=(_k == 0 and _r is sh),
                                                  stop=(_k == 43 and _r is sl))),
                             waits=wt, inc=("mm" if z == 87 else None))
                    p.op("scalar", (lambda s, _d=dt, _b=bank: s.activation(
                        dob[:], _b[:], AF.Identity, scale=sdv[:, _d:_d + 1])),
                        waits=[("mm", p.cur("mm")), ("ot", p.cur("ot"))], inc="dnv")
                    for k in range(4):
                        wt = []
                        if k == 0:
                            wt.append(("dnv", p.cur("dnv")))
                            wt.append(("ov", p.cur("ov")))
                        p.op("tensor", (lambda te, _k=k: te.matmul(
                            pdt[:, _k * 128:(_k + 1) * 128], dob[:, _k * 128:(_k + 1) * 128],
                            identf[:], is_transpose=True, start=(_k == 0), stop=(_k == 3))),
                            waits=wt, inc=("ot" if k == 3 else None))
                    p.op("vector", (lambda v: v.tensor_copy(tdb[:], pdt[:])),
                         waits=[("ot", p.cur("ot")), ("dg", p.cur("dg"))], inc="ov")
                    dd = FFO.rearrange("(t p) f -> p t f", p=128)[:, q * 4:q * 4 + 4,
                                                                 dt * 128:(dt + 1) * 128]
                    p.dma("gpsimd", (lambda g, _d=dd: g.dma_start(
                        out=_d, in_=tdb[:].rearrange("p (i f) -> p i f", f=128))),
                        waits=[("ov", p.cur("ov"))], inc="dg")
                p.op("vector", lambda v: v.engine_nop(),
                     waits=[("ov", p.cur("ov"))], inc="dn")
            p.op("gpsimd", lambda g: g.engine_nop(),
                 waits=[("dg", p.cur("dg"))], inc="gp")
            emit(nc, p, sems)

        with ExitStack() as _ph:
            _en = _ph.enter_context
            fb0 = _en(nc.sbuf_tensor("fb0", [128, C], F32))
            fb1 = _en(nc.sbuf_tensor("fb1", [128, C], F32))
            xm0 = _en(nc.sbuf_tensor("xm0", [128, C], F32))
            xm1 = _en(nc.sbuf_tensor("xm1", [128, C], F32))
            sq2 = _en(nc.sbuf_tensor("sq2", [128, C], F32))
            ln5 = _en(nc.sbuf_tensor("ln5", [128, C], F32))
            yo0 = _en(nc.sbuf_tensor("yo0", [128, C], F32))
            yo1 = _en(nc.sbuf_tensor("yo1", [128, C], F32))
            fbx = [fb0, fb1]
            xmx = [xm0, xm1]
            yox = [yo0, yo1]
            p = Plan(semc)
            for t in range(OT):
                fb = fbx[t % 2]
                xm = xmx[t % 2]
                yo = yox[t % 2]
                fsem = "da" if t % 2 == 0 else "db"
                msem = "de" if t % 2 == 0 else "df"
                waits = [("fx", p.cur("fx") - 1)] if t >= 2 else []
                p.dma("sync", (lambda s, _b=fb, _t=t: s.dma_start(
                    out=_b[:], in_=FFO[_t * 128:(_t + 1) * 128, :])), waits=waits, inc=fsem)
                fsv = p.cur(fsem)
                p.dma("sync", (lambda s, _b=xm, _t=t: s.dma_start(
                    out=_b[:], in_=XMID[_t * 128:(_t + 1) * 128, :])),
                    waits=list(waits), inc=msem)
                msv = p.cur(msem)
                o = (t % 2) * 32
                p.op("vector", (lambda v, _b=fb, _o=o: v.tensor_reduce(
                    st[:, _o:_o + 1], _b[:], AX.X, ALU.add, negate=True)),
                    waits=[(fsem, fsv)], inc="vst")
                p.op("scalar", (lambda s, _b=fb, _o=o: s.activation(
                    sq2[:], _b[:], AF.Square, accum_out=st[:, _o + 1:_o + 2])),
                    waits=[(fsem, fsv)], inc="sq")
                p.op("vector", (lambda v, _o=o: v.tensor_scalar(
                    st[:, _o + 2:_o + 3], st[:, _o:_o + 1], 1.0 / C, None,
                    op0=ALU.mult)), inc="vst")
                p.op("vector", (lambda v, _o=o: v.tensor_scalar(
                    st[:, _o + 3:_o + 4], st[:, _o + 1:_o + 2], 1.0 / C, EPS,
                    op0=ALU.mult, op1=ALU.add)),
                    waits=[("sq", p.cur("sq"))], inc="vst")
                p.op("vector", (lambda v, _o=o: v.scalar_tensor_tensor(
                    st[:, _o + 4:_o + 5], st[:, _o + 2:_o + 3], -1.0,
                    st[:, _o + 2:_o + 3], ALU.mult, ALU.mult)), inc="vst")
                p.op("vector", (lambda v, _o=o: v.tensor_tensor(
                    st[:, _o + 5:_o + 6], st[:, _o + 3:_o + 4],
                    st[:, _o + 4:_o + 5], ALU.add)), inc="vst")
                p.op("scalar", (lambda s, _o=o: s.activation(
                    st[:, _o + 6:_o + 7], st[:, _o + 5:_o + 6], AF.Sqrt)),
                    waits=[("vst", p.cur("vst"))], inc="sq")
                p.op("vector", (lambda v, _o=o: v.reciprocal(
                    st[:, _o + 7:_o + 8], st[:, _o + 6:_o + 7])),
                    waits=[("sq", p.cur("sq"))], inc="vst")
                p.op("vector", (lambda v, _o=o: v.tensor_tensor(
                    st[:, _o + 8:_o + 9], st[:, _o + 2:_o + 3],
                    st[:, _o + 7:_o + 8], ALU.mult)), inc="vst")
                p.op("scalar", (lambda s, _b=fb, _o=o: s.activation(
                    ln5[:], _b[:], AF.Identity, bias=st[:, _o + 8:_o + 9],
                    scale=st[:, _o + 7:_o + 8])),
                    waits=[("vst", p.cur("vst"))], inc="y1")
                ysem = "dc" if t % 2 == 0 else "dd"
                yw = [("y1", p.cur("y1")), (msem, msv)]
                if t >= 2:
                    yw.append((ysem, p.cur(ysem)))
                p.op("vector", (lambda v, _x=xm, _y=yo: v.tensor_tensor(
                    _y[:], _x[:], ln5[:], ALU.add)),
                    waits=yw, inc="fx")
                p.dma("gpsimd", (lambda g, _y=yo, _t=t: g.dma_start(
                    out=y_d[_t * 128:(_t + 1) * 128, :], in_=_y[:])),
                    waits=[("fx", p.cur("fx"))], inc=ysem)
            for nm in DEBUG_DUMPS:
                p.dma("gpsimd", (lambda g, _nm=nm: g.dma_start(
                    out=dbg[_nm], in_=scr[_nm])), inc="out")
            p.op("gpsimd", lambda g: g.engine_nop(),
                 waits=[("dc", p.cur("dc")), ("dd", p.cur("dd")),
                        ("out", p.cur("out"))], inc="gp")
            emit(nc, p, sems)

    if PHASE_LIMIT < 9:
        with nc.sbuf_tensor("zout", [128, C], F32) as zo, \
             nc.semaphore("zs") as zs, nc.Block() as blk:
            @blk.gpsimd
            def _(g):
                g.memset(zo[:], 0.0).then_inc(zs, 1)
                g.wait_ge(zs, 1)
                for t in range(OT):
                    g.dma_start(out=y_d[t * 128:(t + 1) * 128, :],
                                in_=zo[:]).then_inc(zs, 16)
                g.wait_ge(zs, 1 + 16 * OT)

    nc.finalize()
    return nc


def kernel(L0_attn_x, L0_attn_w_qkv, L0_attn_w_proj, L0_ffn_wq_gate,
           L0_ffn_s_gate, L0_ffn_wq_up, L0_ffn_s_up, L0_ffn_wq_down,
           L0_ffn_s_down):
    x = np.ascontiguousarray(np.asarray(L0_attn_x, dtype=np.float32))
    wq = np.ascontiguousarray(np.asarray(L0_attn_w_qkv, dtype=np.float32))
    wp = np.ascontiguousarray(np.asarray(L0_attn_w_proj, dtype=np.float32))
    wg = np.ascontiguousarray(np.asarray(L0_ffn_wq_gate, dtype=np.int32))
    wu = np.ascontiguousarray(np.asarray(L0_ffn_wq_up, dtype=np.int32))
    wd = np.ascontiguousarray(np.asarray(L0_ffn_wq_down, dtype=np.int32))
    sg = np.ascontiguousarray(np.asarray(L0_ffn_s_gate, dtype=np.float32))
    su = np.ascontiguousarray(np.asarray(L0_ffn_s_up, dtype=np.float32))
    sd = np.ascontiguousarray(np.asarray(L0_ffn_s_down, dtype=np.float32))
    ident = np.eye(128, dtype=np.float32)
    trim = (np.triu(np.ones((128, 128), dtype=np.float32), 1) * NEG).astype(np.float32)

    nc = build_program()
    in_maps = []
    orders = []
    for c in range(8):
        b, par = c // 2, c % 2
        order = list(range(par, NT, 2)) + list(range(1 - par, NT, 2))
        orders.append(order)
        xp = np.concatenate([x[b, t * 128:(t + 1) * 128, :] for t in order], axis=0)
        oma = (np.zeros((128, 128), np.float32) if par == 1 else
               np.full((128, 128), NEG, np.float32))
        in_maps.append(dict(xp=np.ascontiguousarray(xp), wqkv=wq, wproj=wp,
                            wg=wg, wu=wu, wd=wd, sg=sg, su=su, sd=sd,
                            ident=ident, tri=trim, oma=oma))
    res = run_bass_kernel_spmd(nc, in_maps, core_ids=list(range(8)))
    y = np.empty((4, NTOK, C), np.float32)
    for c in range(8):
        b, par = c // 2, c % 2
        yc = np.asarray(res.results[c]["y"])
        for i, t in enumerate(range(par, NT, 2)):
            y[b, t * 128:(t + 1) * 128, :] = yc[i * 128:(i + 1) * 128, :]
    return y


# version 05
# speedup vs baseline: 1.1523x; 1.1060x over previous
import numpy as np
from contextlib import ExitStack

import concourse.bass as bass
import concourse.mybir as mybir
from concourse.bass_utils import run_bass_kernel_spmd
from concourse.alu_op_type import AluOpType as ALU

F32 = mybir.dt.float32
BF16 = mybir.dt.bfloat16
I32 = mybir.dt.int32
AF = mybir.ActivationFunctionType
AX = mybir.AxisListType

C = 2048
H = 16
HID = 5632
NTOK = 4096
NT = 32
OT = 16
FQKV = 3 * C
NEG = np.float32(-9e15)
EPS = float(np.float32(1e-5))
SCALE = float(np.float32(128.0 ** -0.5))
M32 = 12582912.0
M64 = 6755399441055744.0
ZSH = 384.0

ENGINES = ("sync", "scalar", "vector", "tensor", "gpsimd")
DEBUG_DUMPS = []


class Plan:
    def __init__(self, sem):
        self.ops = []
        self.sem = sem

    def op(self, engine, fn, waits=(), inc=None, inc_n=1):
        w = [(s, int(v)) for (s, v) in waits if v > 0]
        iv = None
        if inc is not None:
            self.sem[inc] = self.sem.get(inc, 0) + inc_n
            iv = self.sem[inc]
        self.ops.append([engine, fn, w, inc, inc_n, iv])

    def dma(self, engine, fn, waits=(), inc=None):
        self.op(engine, fn, waits, inc, 16)

    def cur(self, name):
        return self.sem.get(name, 0)

    def finalize(self):
        last = {}
        for o in self.ops:
            en, fn, waits, inc, inc_n, iv = o
            if en in ("vector", "scalar", "gpsimd") and inc_n != 16:
                if en in last and last[en] not in waits:
                    waits.append(last[en])
                if inc is None:
                    eo = "eo_" + en
                    self.sem[eo] = self.sem.get(eo, 0) + 1
                    o[3], o[5] = eo, self.sem[eo]
                last[en] = (o[3], o[5])


PHASE_LIMIT = 9
ATT_MODE = 4
_PH = [0]


def interleave_ops(a_ops, b_ops):
    out = []
    na, nb = len(a_ops), len(b_ops)
    j = 0
    for i, op in enumerate(a_ops):
        out.append(op)
        while j < nb and j * na <= i * nb:
            out.append(b_ops[j])
            j += 1
    out.extend(b_ops[j:])
    return out


def emit(nc, plan, sems):
    i = _PH[0]
    _PH[0] += 1
    if i >= PHASE_LIMIT:
        return
    plan.finalize()
    with nc.Block() as block:
        def mk(engname):
            def body(eng):
                for (en, fn, waits, inc, inc_n, iv) in plan.ops:
                    if en != engname:
                        continue
                    for (s, v) in waits:
                        eng.wait_ge(sems[s], v)
                    instr = fn(eng)
                    if inc is not None:
                        instr.then_inc(sems[inc], inc_n)
            body.__name__ = "_"
            return body
        for en in ENGINES:
            getattr(block, en)(mk(en))


SEM_NAMES = ["ld", "wld", "vst", "sq", "y1", "xq", "xtp", "xev", "mm", "ev",
             "tp", "tev", "qld", "se", "sv", "mx", "xp",
             "pt", "pe", "pv", "oe", "ot", "ov", "gp", "out", "prx", "gu",
             "dn", "hst", "fx", "dnv",
             "da", "db", "dc", "dd", "de", "df", "dg", "dh", "di", "dj", "dk",
             "eo_vector", "eo_scalar", "eo_gpsimd"]


def build_program():
    _PH[0] = 0
    nc = bass.Bass("TRN2", target_bir_lowering=False)

    xp_d = nc.dram_tensor("xp", [NTOK, C], F32, kind="ExternalInput").ap()
    wqkv = nc.dram_tensor("wqkv", [FQKV, C], F32, kind="ExternalInput").ap()
    wproj = nc.dram_tensor("wproj", [C, C], F32, kind="ExternalInput").ap()
    wg_r = nc.dram_tensor("wg", [HID, C], I32, kind="ExternalInput").ap()
    wu_r = nc.dram_tensor("wu", [HID, C], I32, kind="ExternalInput").ap()
    wd_r = nc.dram_tensor("wd", [C, HID], I32, kind="ExternalInput").ap()
    sg_d = nc.dram_tensor("sg", [HID], F32, kind="ExternalInput").ap()
    su_d = nc.dram_tensor("su", [HID], F32, kind="ExternalInput").ap()
    sd_d = nc.dram_tensor("sd", [C], F32, kind="ExternalInput").ap()
    id_d = nc.dram_tensor("ident", [128, 128], F32, kind="ExternalInput").ap()
    tri_d = nc.dram_tensor("tri", [128, 128], F32, kind="ExternalInput").ap()
    oma_d = nc.dram_tensor("oma", [128, 128], F32, kind="ExternalInput").ap()
    y_d = nc.dram_tensor("y", [2048, C], F32, kind="ExternalOutput").ap()

    WQKVT = nc.dram_tensor("WQKVT", [C, FQKV], BF16, kind="Internal").ap()
    WPROJT = nc.dram_tensor("WPROJT", [C, C], BF16, kind="Internal").ap()
    WGT = nc.dram_tensor("WGT", [C, HID], BF16, kind="Internal").ap()
    WUT = nc.dram_tensor("WUT", [C, HID], BF16, kind="Internal").ap()
    WDT = nc.dram_tensor("WDT", [HID, C], BF16, kind="Internal").ap()
    QT = nc.dram_tensor("QT", [C, 2048], F32, kind="Internal").ap()
    KT = nc.dram_tensor("KT", [C, NTOK], F32, kind="Internal").ap()
    VS = nc.dram_tensor("VS", [NTOK, C], F32, kind="Internal").ap()
    OS = nc.dram_tensor("OS", [2048, C], F32, kind="Internal").ap()
    XMID = nc.dram_tensor("XMID", [2048, C], F32, kind="Internal").ap()
    XHs = nc.dram_tensor("XHs", [C, 2048], BF16, kind="Internal").ap()
    XLs = nc.dram_tensor("XLs", [C, 2048], BF16, kind="Internal").ap()
    FFO = nc.dram_tensor("FFO", [2048, C], F32, kind="Internal").ap()
    scr = {"QT": QT, "KT": KT, "VS": VS, "OS": OS, "XMID": XMID, "FFO": FFO,
           "WQKVT": WQKVT, "WPROJT": WPROJT, "WGT": WGT, "WUT": WUT, "WDT": WDT,
           "XHs": XHs, "XLs": XLs}
    dbg = {}
    for nm in DEBUG_DUMPS:
        t = scr[nm]
        dbg[nm] = nc.dram_tensor("dbg_" + nm, list(t.shape), t.dtype,
                                 kind="ExternalOutput").ap()

    semc = {}
    with ExitStack() as ctx:
        e = ctx.enter_context
        sems = {n: e(nc.semaphore(n)) for n in SEM_NAMES}

        identf = e(nc.sbuf_tensor("identf", [128, 128], F32))
        identb = e(nc.sbuf_tensor("identb", [128, 128], BF16))
        tri = e(nc.sbuf_tensor("tri_s", [128, 128], F32))
        oma = e(nc.sbuf_tensor("oma_s", [128, 128], F32))
        ones = e(nc.sbuf_tensor("ones", [128, 128], F32))
        invs = e(nc.sbuf_tensor("invs", [128, NT], F32))
        qsc = e(nc.sbuf_tensor("qsc", [128, NT], F32))
        ksc = e(nc.sbuf_tensor("ksc", [128, NT], F32))
        bcw = e(nc.sbuf_tensor("bcw", [128, 4], F32))
        invl = e(nc.sbuf_tensor("invl", [128, OT * H], F32))
        sgv = e(nc.sbuf_tensor("sgv", [128, 44], F32))
        suv = e(nc.sbuf_tensor("suv", [128, 44], F32))
        sdv = e(nc.sbuf_tensor("sdv", [128, 16], F32))
        st = e(nc.sbuf_tensor("st", [128, 64], F32))
        sws = e(nc.sbuf_tensor("sws", [128, 8], F32))
        cst = e(nc.sbuf_tensor("cst", [128, 4], F32))
        nc.const_aps.register(F32, 0.5, cst[:, 0:1])
        nc.const_aps.register(F32, 0.0, cst[:, 1:2])
        nc.const_aps.register(F32, 1.0, cst[:, 2:3])
        nc.const_aps.register(F32, ZSH, cst[:, 3:4])

        wpart = e(nc.sbuf_tensor("wpart", [128, 64], F32))
        with ExitStack() as _ph:
            _en = _ph.enter_context
            wb0 = _en(nc.sbuf_tensor("wb0", [128, C], F32))
            wb1 = _en(nc.sbuf_tensor("wb1", [128, C], F32))
            ps_sw = _en(nc.psum_tensor("ps_sw", [128, 8], F32))
            sgn = _en(nc.sbuf_tensor("sgn", [128, 128], F32))
            sun = _en(nc.sbuf_tensor("sun", [128, 128], F32))
            sdn = _en(nc.sbuf_tensor("sdn", [128, 128], F32))
            psv = _en(nc.psum_tensor("psv", [128, 512], F32))
            wbuf = [wb0, wb1]
            p = Plan(semc)
            p.dma("sync", lambda s: s.dma_start(out=identf[:], in_=id_d), inc="ld")
            p.dma("sync", lambda s: s.dma_start(out=tri[:], in_=tri_d), inc="ld")
            p.dma("sync", lambda s: s.dma_start(out=oma[:], in_=oma_d), inc="ld")
            p.dma("sync", lambda s: s.dma_start(
                out=sgn[0:44, :], in_=sg_d.rearrange("(a p) -> a p", p=128)), inc="ld")
            p.dma("sync", lambda s: s.dma_start(
                out=sun[0:44, :], in_=su_d.rearrange("(a p) -> a p", p=128)), inc="ld")
            p.dma("sync", lambda s: s.dma_start(
                out=sdn[0:16, :], in_=sd_d.rearrange("(a p) -> a p", p=128)), inc="ld")
            p.op("tensor", lambda t: t.matmul(psv[:, 0:44], sgn[0:44, :],
                                              identf[0:44, 0:44], is_transpose=True,
                                              start=True, stop=True),
                 waits=[("ld", 16 * 6)], inc="tp")
            p.op("tensor", lambda t: t.matmul(psv[:, 64:108], sun[0:44, :],
                                              identf[0:44, 0:44], is_transpose=True,
                                              start=True, stop=True), inc="tp")
            p.op("tensor", lambda t: t.matmul(psv[:, 128:144], sdn[0:16, :],
                                              identf[0:16, 0:16], is_transpose=True,
                                              start=True, stop=True), inc="tp")
            p.op("vector", lambda v: v.tensor_copy(sgv[:], psv[:, 0:44]),
                 waits=[("tp", p.cur("tp"))], inc="vst")
            p.op("vector", lambda v: v.tensor_copy(suv[:], psv[:, 64:108]), inc="vst")
            p.op("vector", lambda v: v.tensor_copy(sdv[:], psv[:, 128:144]), inc="vst")
            p.op("gpsimd", lambda g: g.memset(ones[:], 1.0), inc="gp")
            p.op("gpsimd", lambda g: g.memset(cst[:, 0:1], 0.5), inc="gp")
            p.op("gpsimd", lambda g: g.memset(cst[:, 1:2], 0.0), inc="gp")
            p.op("gpsimd", lambda g: g.memset(cst[:, 2:3], 1.0), inc="gp")
            p.op("gpsimd", lambda g: g.memset(cst[:, 3:4], ZSH), inc="gp")
            p.op("vector", lambda v: v.tensor_copy(identb[:], identf[:]),
                 waits=[("ld", 96)], inc="vst")
            for t in range(64):
                src = (wqkv[t * 128:(t + 1) * 128, :] if t < 48 else
                       wproj[(t - 48) * 128:(t - 47) * 128, :])
                dsem = "da" if t % 2 == 0 else "db"
                waits = [("vst", p.cur("vst") - 1)] if t >= 2 else []
                p.dma("sync", (lambda s, _s=src, _b=wbuf[t % 2]:
                               s.dma_start(out=_b[:], in_=_s)), waits=waits, inc=dsem)
                p.op("vector", (lambda v, _b=wbuf[t % 2], _t=t:
                                v.tensor_reduce(wpart[:, _t:_t + 1], _b[:], AX.X,
                                                ALU.add, apply_absolute_value=True)),
                     waits=[(dsem, p.cur(dsem))], inc="vst")
            p.op("vector", lambda v: v.tensor_reduce(
                sws[:, 0:1], wpart[:, 0:48], AX.X, ALU.add), inc="vst")
            p.op("vector", lambda v: v.tensor_reduce(
                sws[:, 1:2], wpart[:, 48:64], AX.X, ALU.add), inc="vst")
            p.op("tensor", lambda t: t.matmul(ps_sw[0:1, 0:1], sws[:, 0:1],
                                              ones[:, 0:1], start=True, stop=True),
                 waits=[("vst", p.cur("vst")), ("gp", 1)], inc="mm")
            p.op("tensor", lambda t: t.matmul(ps_sw[0:1, 1:2], sws[:, 1:2],
                                              ones[:, 0:1], start=True, stop=True),
                 inc="mm")
            p.op("vector", lambda v: v.tensor_copy(sws[0:1, 2:4], ps_sw[0:1, 0:2]),
                 waits=[("mm", p.cur("mm"))], inc="vst")
            p.op("vector", lambda v: v.tensor_scalar(
                sws[0:1, 4:5], sws[0:1, 2:3], 1.0 / (FQKV * C), 1e-5,
                op0=ALU.mult, op1=ALU.max), inc="vst")
            p.op("vector", lambda v: v.tensor_scalar(
                sws[0:1, 5:6], sws[0:1, 3:4], 1.0 / (C * C), 1e-5,
                op0=ALU.mult, op1=ALU.max), inc="vst")
            p.op("vector", lambda v: v.reciprocal(sws[0:1, 6:8], sws[0:1, 4:6]),
                 inc="vst")
            vv = p.cur("vst")
            p.op("tensor", lambda t: t.matmul(ps_sw[:, 4:8], ones[0:1, 0:128],
                                              sws[0:1, 4:8], start=True, stop=True),
                 waits=[("vst", vv)], inc="mm")
            p.op("vector", lambda v: v.tensor_copy(bcw[:, 0:2], ps_sw[:, 6:8]),
                 waits=[("mm", p.cur("mm"))], inc="vst")
            p.op("vector", lambda v: v.tensor_copy(bcw[:, 2:4], ps_sw[:, 4:6]),
                 inc="vst")
            emit(nc, p, sems)

            with ExitStack() as _ph:
                _en = _ph.enter_context
                ybuf = _en(nc.sbuf_tensor("ybuf", [128, C], F32))
                mbuf = _en(nc.sbuf_tensor("mbuf", [128, C], F32))
                tbuf = _en(nc.sbuf_tensor("tbuf", [128, C], BF16))
                te0 = _en(nc.sbuf_tensor("te0", [128, C], BF16))
                te1 = _en(nc.sbuf_tensor("te1", [128, C], BF16))
                pt0 = _en(nc.psum_tensor("pt0", [128, C], BF16))
                pt1 = _en(nc.psum_tensor("pt1", [128, C], BF16))
                tev_ = [te0, te1]
                ptp_ = [pt0, pt1]
                p = Plan(semc)
                spq = {0: None, 1: None}
                for t in range(64):
                    qkv = t < 48
                    src = (wqkv[t * 128:(t + 1) * 128, :] if qkv else
                           wproj[(t - 48) * 128:(t - 47) * 128, :])
                    wb = wbuf[t % 2]
                    dsem = "da" if t % 2 == 0 else "db"
                    waits = [("y1", p.cur("y1") - 1)] if t >= 2 else []
                    p.dma("sync", (lambda s, _s=src, _b=wb: s.dma_start(out=_b[:], in_=_s)),
                          waits=waits, inc=dsem)
                    sc = bcw[:, 0:1] if qkv else bcw[:, 1:2]
                    p.op("scalar", (lambda s, _b=wb, _sc=sc: s.activation(
                        ybuf[:], _b[:], AF.Identity, bias=0.0, scale=_sc)),
                        waits=[(dsem, p.cur(dsem)), ("xq", p.cur("xq"))], inc="y1")
                    p.op("vector", (lambda v: v.tensor_scalar(
                        mbuf[:], ybuf[:], -0.5, None, op0=ALU.is_le)),
                        waits=[("y1", p.cur("y1"))], inc="vst")
                    p.op("vector", (lambda v: v.scalar_tensor_tensor(
                        tbuf[:], ybuf[:], 0.5, mbuf[:], ALU.is_ge, ALU.subtract)),
                        inc="xq")
                    ptp = ptp_[t % 2]
                    for k in range(16):
                        wt = []
                        if k == 0:
                            wt.append(("xq", p.cur("xq")))
                            if t >= 2:
                                wt.append(("tev", p.cur("tev") - 1))
                        p.op("tensor", (lambda te, _k=k, _p=ptp: te.matmul(
                            _p[:, _k * 128:(_k + 1) * 128],
                            tbuf[:, _k * 128:(_k + 1) * 128], identb[:],
                            is_transpose=True, start=(_k % 8 == 0), stop=(_k % 8 == 7))),
                            waits=wt, inc=("tp" if k == 15 else None))
                    evw = [("tp", p.cur("tp"))]
                    if spq[t % 2] is not None:
                        evw.append(spq[t % 2])
                    p.op("vector", (lambda v, _p=ptp, _e=tev_[t % 2]:
                                    v.tensor_copy(_e[:], _p[:])),
                         waits=evw, inc="tev")
                    if qkv:
                        dst = WQKVT.rearrange("(k p) f -> p k f", p=128)[:, :, t * 128:(t + 1) * 128]
                    else:
                        dst = WPROJT.rearrange("(k p) f -> p k f", p=128)[:, :, (t - 48) * 128:(t - 47) * 128]
                    ssem = "dc" if t % 2 == 0 else "dd"
                    p.dma("gpsimd", (lambda g, _d=dst, _e=tev_[t % 2]: g.dma_start(
                        out=_d, in_=_e[:].rearrange("p (k j) -> p k j", j=128))),
                        waits=[("tev", p.cur("tev"))], inc=ssem)
                    spq[t % 2] = (ssem, p.cur(ssem))
                p.op("gpsimd", lambda g: g.engine_nop(),
                     waits=[("dc", p.cur("dc")), ("dd", p.cur("dd"))], inc="gp")
                emit(nc, p, sems)

        _xq_scope = ExitStack()
        xqT = _xq_scope.enter_context(nc.sbuf_tensor("xqT", [128, 16 * NTOK], BF16))
        xqT3 = xqT[:].rearrange("p (k n) -> p k n", n=NTOK)
        with ExitStack() as _ph:
            _en = _ph.enter_context
            xb0 = _en(nc.sbuf_tensor("xb0", [128, C], F32))
            xb1 = _en(nc.sbuf_tensor("xb1", [128, C], F32))
            sqb = _en(nc.sbuf_tensor("sqb", [128, C], F32))
            yb = _en(nc.sbuf_tensor("yb", [128, C], F32))
            mb = _en(nc.sbuf_tensor("mb", [128, C], F32))
            ub = _en(nc.sbuf_tensor("ub", [128, C], F32))
            qbv = _en(nc.sbuf_tensor("qbv", [128, C], BF16))
            px0 = _en(nc.psum_tensor("px0", [128, C], BF16))
            px1 = _en(nc.psum_tensor("px1", [128, C], BF16))
            xb = [xb0, xb1]
            pxt = [px0, px1]
            p = Plan(semc)
            for t in range(NT):
                b = xb[t % 2]
                dsem = "da" if t % 2 == 0 else "db"
                waits = [("y1", p.cur("y1") - 1)] if t >= 2 else []
                p.dma("sync", (lambda s, _b=b, _t=t: s.dma_start(
                    out=_b[:], in_=xp_d[_t * 128:(_t + 1) * 128, :])), waits=waits, inc=dsem)
                ldv = p.cur(dsem)
                o = (t % 2) * 32

                def tiny(fn, waits=()):
                    p.op("vector", fn, waits=waits, inc="vst")
                p.op("vector", (lambda v, _b=b, _o=o: v.tensor_reduce(
                    st[:, _o:_o + 1], _b[:], AX.X, ALU.add, negate=True)),
                    waits=[(dsem, ldv)], inc="vst")
                p.op("vector", (lambda v, _b=b, _o=o: v.tensor_reduce(
                    st[:, _o + 1:_o + 2], _b[:], AX.X, ALU.max)), inc="vst")
                p.op("vector", (lambda v, _b=b, _o=o: v.tensor_reduce(
                    st[:, _o + 2:_o + 3], _b[:], AX.X, ALU.min)), inc="vst")
                p.op("scalar", (lambda s, _b=b, _o=o: s.activation(
                    sqb[:], _b[:], AF.Square, accum_out=st[:, _o + 3:_o + 4])),
                    waits=[(dsem, ldv), ("xq", p.cur("xq"))], inc="sq")
                tiny(lambda v, _o=o: v.tensor_scalar(
                    st[:, _o + 4:_o + 5], st[:, _o:_o + 1], 1.0 / C, None, op0=ALU.mult))
                p.op("vector", (lambda v, _o=o: v.tensor_scalar(
                    st[:, _o + 5:_o + 6], st[:, _o + 3:_o + 4], 1.0 / C, EPS,
                    op0=ALU.mult, op1=ALU.add)),
                    waits=[("sq", p.cur("sq"))], inc="vst")
                tiny(lambda v, _o=o: v.scalar_tensor_tensor(
                    st[:, _o + 6:_o + 7], st[:, _o + 4:_o + 5], -1.0,
                    st[:, _o + 4:_o + 5], ALU.mult, ALU.mult))
                tiny(lambda v, _o=o: v.tensor_tensor(
                    st[:, _o + 7:_o + 8], st[:, _o + 5:_o + 6],
                    st[:, _o + 6:_o + 7], ALU.add))
                p.op("scalar", (lambda s, _o=o: s.activation(
                    st[:, _o + 8:_o + 9], st[:, _o + 7:_o + 8], AF.Sqrt)),
                    waits=[("vst", p.cur("vst"))], inc="sq")
                p.op("vector", (lambda v, _o=o: v.reciprocal(
                    st[:, _o + 9:_o + 10], st[:, _o + 8:_o + 9])),
                    waits=[("sq", p.cur("sq"))], inc="vst")
                tiny(lambda v, _o=o: v.tensor_tensor(
                    st[:, _o + 10:_o + 11], st[:, _o + 1:_o + 2],
                    st[:, _o + 4:_o + 5], ALU.add))
                tiny(lambda v, _o=o: v.scalar_tensor_tensor(
                    st[:, _o + 11:_o + 12], st[:, _o + 2:_o + 3], -1.0,
                    st[:, _o + 4:_o + 5], ALU.mult, ALU.subtract))
                tiny(lambda v, _o=o: v.tensor_tensor(
                    st[:, _o + 12:_o + 13], st[:, _o + 10:_o + 11],
                    st[:, _o + 11:_o + 12], ALU.max))
                tiny(lambda v, _o=o: v.tensor_tensor(
                    st[:, _o + 13:_o + 14], st[:, _o + 12:_o + 13],
                    st[:, _o + 9:_o + 10], ALU.mult))
                tiny(lambda v, _o=o: v.tensor_scalar(
                    st[:, _o + 14:_o + 15], st[:, _o + 13:_o + 14], 1e-5, 1.0 / 127.0,
                    op0=ALU.max, op1=ALU.mult))
                tiny(lambda v, _o=o, _t=t: v.tensor_copy(
                    invs[:, _t:_t + 1], st[:, _o + 14:_o + 15]))
                tiny(lambda v, _o=o: v.reciprocal(
                    st[:, _o + 15:_o + 16], st[:, _o + 14:_o + 15]))
                tiny(lambda v, _o=o: v.tensor_tensor(
                    st[:, _o + 16:_o + 17], st[:, _o + 9:_o + 10],
                    st[:, _o + 15:_o + 16], ALU.mult))
                tiny(lambda v, _o=o: v.tensor_scalar(
                    st[:, _o + 17:_o + 18], st[:, _o + 4:_o + 5],
                    st[:, _o + 16:_o + 17], None, op0=ALU.mult))
                p.op("scalar", (lambda s, _b=b, _o=o: s.activation(
                    yb[:], _b[:], AF.Identity, bias=st[:, _o + 17:_o + 18],
                    scale=st[:, _o + 16:_o + 17])),
                    waits=[("vst", p.cur("vst"))], inc="y1")
                p.op("vector", (lambda v: v.tensor_scalar(
                    ub[:], yb[:], M32, -M32, op0=ALU.add, op1=ALU.add)),
                    waits=[("y1", p.cur("y1"))], inc="vst")
                p.op("vector", (lambda v: v.tensor_scalar(
                    mb[:], yb[:], ZSH, M64, op0=ALU.add, op1=ALU.add)), inc="vst")
                p.op("vector", (lambda v: v.tensor_scalar(
                    mb[:], mb[:], -M64, -ZSH, op0=ALU.add, op1=ALU.add)), inc="vst")
                p.op("vector", (lambda v: v.scalar_tensor_tensor(
                    yb[:], yb[:], -1.0, mb[:], ALU.mult, ALU.add)), inc="vst")
                p.op("vector", (lambda v: v.tensor_scalar(
                    mb[:], yb[:], -0.5, 0.5, op0=ALU.max, op1=ALU.min)), inc="vst")
                p.op("vector", (lambda v: v.tensor_scalar(
                    yb[:], yb[:], 1.0, -0.5, op0=ALU.add, op1=ALU.max)), inc="vst")
                p.op("vector", (lambda v: v.scalar_tensor_tensor(
                    yb[:], yb[:], 0.0, mb[:], ALU.min, ALU.subtract)), inc="vst")
                p.op("vector", (lambda v: v.scalar_tensor_tensor(
                    qbv[:], ub[:], 0.0, yb[:], ALU.add, ALU.subtract)), inc="xq")
                ptp = pxt[t % 2]
                for k in range(16):
                    wt = []
                    if k == 0:
                        wt.append(("xq", p.cur("xq")))
                        if t >= 2:
                            wt.append(("xev", p.cur("xev") - 1))
                    p.op("tensor", (lambda te, _k=k, _p=ptp: te.matmul(
                        _p[:, _k * 128:(_k + 1) * 128], qbv[:, _k * 128:(_k + 1) * 128],
                        identb[:], is_transpose=True,
                        start=(_k % 8 == 0), stop=(_k % 8 == 7))),
                        waits=wt, inc=("xtp" if k == 15 else None))
                p.op("vector", (lambda v, _p=ptp, _t=t: v.tensor_copy(
                    xqT3[:, :, _t * 128:(_t + 1) * 128],
                    _p[:].rearrange("p (k j) -> p k j", j=128))),
                    waits=[("xtp", p.cur("xtp"))], inc="xev")
            emit(nc, p, sems)

        with ExitStack() as _ph:
            _en = _ph.enter_context
            wc0 = _en(nc.sbuf_tensor("wc0", [128, 16 * 512], BF16))
            wc1 = _en(nc.sbuf_tensor("wc1", [128, 16 * 512], BF16))
            eb0 = _en(nc.sbuf_tensor("eb0", [128, 512], F32))
            eb1 = _en(nc.sbuf_tensor("eb1", [128, 512], F32))
            tb0 = _en(nc.sbuf_tensor("tb0", [128, 512], F32))
            tb1 = _en(nc.sbuf_tensor("tb1", [128, 512], F32))
            pq0 = _en(nc.psum_tensor("pq0", [128, 512], F32))
            pq1 = _en(nc.psum_tensor("pq1", [128, 512], F32))
            pq2 = _en(nc.psum_tensor("pq2", [128, 512], F32))
            pq3 = _en(nc.psum_tensor("pq3", [128, 512], F32))
            pr0 = _en(nc.psum_tensor("pr0", [128, 512], F32))
            pr1 = _en(nc.psum_tensor("pr1", [128, 512], F32))
            wck = [wc0, wc1]
            ebv = [eb0, eb1]
            tbv = [tb0, tb1]
            pq = [pq0, pq1, pq2, pq3]
            ptr = [pr0, pr1]
            p = Plan(semc)
            p.op("vector", lambda v: v.tensor_scalar(
                qsc[:], invs[:], bcw[:, 2:3], SCALE, op0=ALU.mult, op1=ALU.mult),
                waits=[("xev", 32)], inc="vst")
            p.op("vector", lambda v: v.tensor_scalar(
                ksc[:], invs[:], bcw[:, 2:3], None, op0=ALU.mult), inc="vst")
            vsc = p.cur("vst")
            ntile = 0
            nch = 0
            mm_after_chunk = {}
            spq = {0: None, 1: None}
            for half in range(2):
                for j in (range(12) if half == 0 else range(4, 12)):
                    wcb = wck[nch % 2]
                    wsem = "de" if nch % 2 == 0 else "df"
                    waits = []
                    if nch >= 2:
                        waits.append(("mm", mm_after_chunk[nch - 2]))
                    src = WQKVT.rearrange("(k p) f -> p k f", p=128)[:, :, j * 512:(j + 1) * 512]
                    p.dma("sync", (lambda s, _w=wcb, _s=src: s.dma_start(
                        out=_w[:].rearrange("p (k f) -> p k f", f=512), in_=_s)),
                        waits=waits, inc=wsem)
                    wldv = p.cur(wsem)
                    for i in range(16):
                        t = half * 16 + i
                        if j < 4 and t >= 16:
                            continue
                        bank = pq[ntile % 4]
                        eb = ebv[ntile % 2]
                        tbx = tbv[ntile % 2]
                        for k in range(16):
                            wt = []
                            if k == 0:
                                wt.append((wsem, wldv))
                                if ntile >= 4:
                                    wt.append(("ev", p.cur("ev") - 3))
                            p.op("tensor", (lambda te, _k=k, _b=bank, _t=t, _w=wcb:
                                            te.matmul(_b[:],
                                                      xqT3[:, _k, _t * 128:(_t + 1) * 128],
                                                      _w[:, _k * 512:(_k + 1) * 512],
                                                      start=(_k == 0), stop=(_k == 15))),
                                 waits=wt, inc=("mm" if k == 15 else None))
                        sc = qsc[:, t:t + 1] if j < 4 else ksc[:, t:t + 1]
                        wt = [("mm", p.cur("mm")), ("vst", vsc)]
                        if spq[ntile % 2] is not None:
                            wt.append(spq[ntile % 2])
                        p.op("scalar", (lambda s, _b=bank, _e=eb, _sc=sc:
                                        s.activation(_e[:], _b[:], AF.Identity,
                                                     scale=_sc)),
                             waits=wt, inc="ev")
                        evv = p.cur("ev")
                        ssem = "dc" if ntile % 2 == 0 else "dd"
                        if j >= 8:
                            dst = VS[t * 128:(t + 1) * 128, (j - 8) * 512:(j - 7) * 512]
                            p.dma("gpsimd", (lambda g, _d=dst, _e=eb: g.dma_start(
                                out=_d, in_=_e[:])), waits=[("ev", evv)], inc=ssem)
                            spq[ntile % 2] = (ssem, p.cur(ssem))
                        else:
                            ptb = ptr[ntile % 2]
                            for k in range(4):
                                wt = []
                                if k == 0:
                                    wt.append(("ev", evv))
                                    if ntile >= 2:
                                        wt.append(("tev", p.cur("tev") - 1))
                                p.op("tensor", (lambda te, _k=k, _p=ptb, _e=eb:
                                                te.matmul(_p[:, _k * 128:(_k + 1) * 128],
                                                          _e[:, _k * 128:(_k + 1) * 128],
                                                          identf[:], is_transpose=True,
                                                          start=(_k == 0), stop=(_k == 3))),
                                    waits=wt, inc=("tp" if k == 3 else None))
                            p.op("vector", (lambda v, _p=ptb, _t=tbx: v.tensor_copy(
                                _t[:], _p[:])), waits=[("tp", p.cur("tp"))], inc="tev")
                            if j < 4:
                                dd = QT.rearrange("(c k p) r -> c p k r", k=4, p=128)[j][:, :, t * 128:(t + 1) * 128]
                            else:
                                dd = KT.rearrange("(c k p) r -> c p k r", k=4, p=128)[j - 4][:, :, t * 128:(t + 1) * 128]
                            p.dma("gpsimd", (lambda g, _d=dd, _t=tbx: g.dma_start(
                                out=_d, in_=_t[:].rearrange("p (k j) -> p k j", j=128))),
                                waits=[("tev", p.cur("tev"))], inc=ssem)
                            spq[ntile % 2] = (ssem, p.cur(ssem))
                        ntile += 1
                    mm_after_chunk[nch] = p.cur("mm")
                    nch += 1
            p.op("gpsimd", lambda g: g.engine_nop(),
                 waits=[("dc", p.cur("dc")), ("dd", p.cur("dd"))], inc="gp")
            emit(nc, p, sems)

        _xq_scope.close()

        _ffnw_scope = ExitStack()
        if True:
            _en = _ffnw_scope.enter_context
            ib0 = _en(nc.sbuf_tensor("ib0", [128, C], I32))
            ib1 = _en(nc.sbuf_tensor("ib1", [128, C], I32))
            ibd = _en(nc.sbuf_tensor("ibd", [128, HID], I32))
            cb = _en(nc.sbuf_tensor("cb", [128, HID], BF16))
            fe0 = _en(nc.sbuf_tensor("fe0", [128, C], BF16))
            fe1 = fe0
            pf0 = _en(nc.psum_tensor("pf0", [128, C], BF16))
            ib = [ib0, ib1]
            fev = [fe0, fe1]
            pft = [pf0, pf0]
            p = Plan(semc)
            items = ([("g", t) for t in range(44)] + [("u", t) for t in range(44)]
                     + [("d", t) for t in range(16)])
            nr = 0
            spq = {0: None, 1: None}
            for it, (mat, t) in enumerate(items):
                if mat == "g":
                    src, nblk, dstT = wg_r[t * 128:(t + 1) * 128, :], 16, WGT
                elif mat == "u":
                    src, nblk, dstT = wu_r[t * 128:(t + 1) * 128, :], 16, WUT
                else:
                    src, nblk, dstT = wd_r[t * 128:(t + 1) * 128, :], 44, WDT
                if mat == "d":
                    b, dsem = ibd, "dh"
                    waits = [("y1", p.cur("y1") - 1)] if it >= 1 else []
                else:
                    b = ib[it % 2]
                    dsem = "dh" if it % 2 == 0 else "di"
                    waits = [("y1", p.cur("y1") - 1)] if it >= 2 else []
                ncols = nblk * 128
                p.dma("sync", (lambda s, _b=b, _s=src, _n=ncols: s.dma_start(
                    out=_b[:, 0:_n], in_=_s)), waits=waits, inc=dsem)
                p.op("scalar", (lambda s, _b=b, _n=ncols: s.activation(
                    cb[:, 0:_n], _b[:, 0:_n], AF.Copy, bias=-128.0)),
                    waits=[(dsem, p.cur(dsem)), ("tev", p.cur("tev"))], inc="y1")
                for (k0, cnt) in [(r * 16, min(16, nblk - r * 16))
                                  for r in range((nblk + 15) // 16)]:
                    ptb = pft[nr % 2]
                    for kk in range(cnt):
                        wt = []
                        if kk == 0:
                            wt.append(("y1", p.cur("y1")))
                            if nr >= 1:
                                wt.append(("tev", p.cur("tev")))
                        p.op("tensor", (lambda te, _k=k0 + kk, _kk=kk, _p=ptb, _c=cnt:
                                        te.matmul(_p[:, _kk * 128:(_kk + 1) * 128],
                                                  cb[:, _k * 128:(_k + 1) * 128], identb[:],
                                                  is_transpose=True,
                                                  start=(_kk % 8 == 0),
                                                  stop=(_kk % 8 == 7 or _kk == _c - 1))),
                            waits=wt, inc=("tp" if kk == cnt - 1 else None))
                    fv = fev[nr % 2]
                    evw = [("tp", p.cur("tp"))]
                    for pp in (0, 1):
                        if spq[pp] is not None:
                            evw.append(spq[pp])
                    p.op("vector", (lambda v, _p=ptb, _f=fv, _c=cnt: v.tensor_copy(
                        _f[:, 0:_c * 128], _p[:, 0:_c * 128])),
                        waits=evw, inc="tev")
                    ssem = "dc" if nr % 2 == 0 else "dd"
                    dd = dstT.rearrange("(k p) f -> p k f", p=128)[:, k0:k0 + cnt, t * 128:(t + 1) * 128]
                    p.dma("gpsimd", (lambda g, _d=dd, _f=fv, _c=cnt: g.dma_start(
                        out=_d, in_=_f[:, 0:_c * 128].rearrange("p (k j) -> p k j", j=128))),
                        waits=[("tev", p.cur("tev"))], inc=ssem)
                    spq[nr % 2] = (ssem, p.cur(ssem))
                    nr += 1
            p_ffnw = p
            _PH[0] += 1

        with ExitStack() as _ph:
            _en = _ph.enter_context
            sA = _en(nc.sbuf_tensor("sA", [128, 4096], F32))
            sB = _en(nc.sbuf_tensor("sB", [128, 4096], F32))
            sC = _en(nc.sbuf_tensor("sC", [128, 4096], F32))
            sD = _en(nc.sbuf_tensor("sD", [128, 4096], F32))
            eA = _en(nc.sbuf_tensor("eA", [128, 4096], F32))
            eB = _en(nc.sbuf_tensor("eB", [128, 4096], F32))
            eC = _en(nc.sbuf_tensor("eC", [128, 4096], F32))
            eD = _en(nc.sbuf_tensor("eD", [128, 4096], F32))
            qg = _en(nc.sbuf_tensor("qg", [128, 512], F32))
            kb0 = _en(nc.sbuf_tensor("kb0", [128, 512], F32))
            kb1 = _en(nc.sbuf_tensor("kb1", [128, 512], F32))
            vb0 = _en(nc.sbuf_tensor("vb0", [128, 128], F32))
            vb1 = _en(nc.sbuf_tensor("vb1", [128, 128], F32))
            pb0 = _en(nc.sbuf_tensor("pb0", [128, 512], F32))
            pb1 = _en(nc.sbuf_tensor("pb1", [128, 512], F32))
            otm = _en(nc.sbuf_tensor("otm", [128, 512], F32))
            obv = _en(nc.sbuf_tensor("obv", [128, 512], F32))
            pS0 = _en(nc.psum_tensor("pS0", [128, 512], F32))
            pS1 = _en(nc.psum_tensor("pS1", [128, 512], F32))
            pP0 = _en(nc.psum_tensor("pP0", [128, 512], F32))
            pP1 = _en(nc.psum_tensor("pP1", [128, 512], F32))
            pO = _en(nc.psum_tensor("pO", [128, 512], F32))
            pOt = _en(nc.psum_tensor("pOt", [128, 512], F32))
            strips = [sA, sB, sC, sD]
            estrips = [eA, eB, eC, eD]
            kbv = [kb0, kb1]
            vbv = [vb0, vb1]
            ptv = [pb0, pb1]
            pS = [pS0, pS1]
            pP = [pP0, pP1]
            p = Plan(semc)
            nkc = 0
            nvc = 0
            nsb = 0
            npt = 0
            se_bank = [[], []]
            for g in range(4):
                NCH = g + 1
                for h in range(H):
                    ghi = g * H + h
                    wq_ = [("mm", p.cur("mm"))] if ghi else []
                    p.dma("sync", (lambda s, _h=h, _g=g: s.dma_start(
                        out=qg[:], in_=QT[_h * 128:(_h + 1) * 128, _g * 512:(_g + 1) * 512])),
                        waits=wq_, inc="qld")
                    qldv = p.cur("qld")
                    for part in range(2):
                        for ch in range(NCH):
                            kbx = kbv[nkc % 2]
                            ksem = "da" if nkc % 2 == 0 else "db"
                            wt = []
                            if nkc >= 2:
                                wt.append(("mm", p.sem.get("_kmm%d" % (nkc - 2), 0)))
                            ksrc = KT[h * 128:(h + 1) * 128,
                                      part * 2048 + ch * 512: part * 2048 + (ch + 1) * 512]
                            p.dma("sync", (lambda s, _k=kbx, _s=ksrc: s.dma_start(
                                out=_k[:], in_=_s)), waits=[w for w in wt if w[1] > 0], inc=ksem)
                            kldv = p.cur(ksem)
                            for i in range(4):
                                ti = 4 * g + i
                                L = (ti + 1) * 128
                                w = min(512, L - ch * 512)
                                if w <= 0:
                                    continue
                                bank = pS[nsb % 2]
                                wt = [(ksem, kldv), ("qld", qldv)]
                                if nsb >= 2:
                                    wt.extend(se_bank[nsb % 2])
                                p.op("tensor", (lambda te, _b=bank, _k=kbx, _i=i, _w=w:
                                                te.matmul(_b[:, 0:_w],
                                                          qg[:, _i * 128:(_i + 1) * 128],
                                                          _k[:, 0:_w],
                                                          start=True, stop=True)),
                                     waits=wt, inc="mm")
                                mmv = p.cur("mm")
                                stripbase = 0 if part == 0 else L
                                dst0 = stripbase + ch * 512
                                diag = (w < 512) or (ch == NCH - 1 and i == 3)
                                strip = strips[i]
                                if diag:
                                    wfull = w - 128
                                    if wfull > 0:
                                        p.op("vector", (lambda v, _st=strip, _d=dst0, _w=wfull, _b=bank:
                                                        v.tensor_copy(_st[:, _d:_d + _w], _b[:, 0:_w])),
                                             waits=[("mm", mmv)], inc="sv")
                                    mk = tri if part == 0 else oma
                                    p.op("vector", (lambda v, _st=strip, _d=dst0 + wfull,
                                                    _b=bank, _w=wfull, _m=mk:
                                                    v.tensor_tensor(_st[:, _d:_d + 128],
                                                                    _b[:, _w:_w + 128],
                                                                    _m[:], ALU.add)),
                                         waits=[("mm", mmv)], inc="sv")
                                else:
                                    p.op("scalar", (lambda s, _st=strip, _d=dst0, _w=w, _b=bank:
                                                    s.copy(_st[:, _d:_d + _w], _b[:, 0:_w])),
                                         waits=[("mm", mmv)], inc="se")
                                se_bank[nsb % 2] = [("se", p.cur("se")), ("sv", p.cur("sv"))]
                                nsb += 1
                            p.sem["_kmm%d" % nkc] = p.cur("mm")
                            nkc += 1
                    if ATT_MODE >= 2:
                        sev, svv = p.cur("se"), p.cur("sv")
                        for i in range(4):
                            ti = 4 * g + i
                            L2 = 2 * (ti + 1) * 128
                            strip = strips[i]
                            p.op("vector", (lambda v, _st=strip, _L=L2, _i=i: v.tensor_reduce(
                                st[:, 48 + _i:49 + _i], _st[:, 0:_L], AX.X, ALU.max,
                                negate=True)),
                                waits=[("se", sev), ("sv", svv)], inc="mx")
                            p.op("scalar", (lambda s, _st=strip, _e=estrips[i], _L=L2, _i=i:
                                            s.activation(
                                _e[:, 0:_L], _st[:, 0:_L], AF.Exp,
                                bias=st[:, 48 + _i:49 + _i],
                                accum_out=st[:, 52 + _i:53 + _i])),
                                waits=[("mx", p.cur("mx"))], inc="xp")
                            col = (4 * g + i) * H + h
                            p.op("vector", (lambda v, _i=i, _c=col:
                                            v.reciprocal(invl[:, _c:_c + 1],
                                                         st[:, 52 + _i:53 + _i])),
                                 waits=[("xp", p.cur("xp"))], inc="vst")
                        xpv = p.cur("xp")
                    if ATT_MODE >= 3:
                        first_pv = True
                        for part in range(2):
                            for kc in range(4 * g + 4):
                                vs = max(0, kc - 4 * g)
                                ptb = pP[npt % 2]
                                for i in range(vs, 4):
                                    ti = 4 * g + i
                                    L = (ti + 1) * 128
                                    srccol = (0 if part == 0 else L) + kc * 128
                                    wt = []
                                    if i == vs:
                                        wt.append(("xp", xpv))
                                        if npt >= 2:
                                            wt.append(("pe", p.cur("pe") - 1))
                                    p.op("tensor", (lambda te, _i=i, _p=ptb, _st=estrips[i],
                                                    _sc=srccol, _vs=vs:
                                                    te.matmul(_p[:, _i * 128:(_i + 1) * 128],
                                                              _st[:, _sc:_sc + 128], identf[:],
                                                              is_transpose=True,
                                                              start=(_i == _vs), stop=(_i == 3))),
                                         waits=wt, inc=("pt" if i == 3 else None))
                                if vs > 0:
                                    p.op("vector", (lambda v, _b=ptv[npt % 2], _vs=vs:
                                                    v.memset(_b[:, 0:_vs * 128], 0.0)),
                                         waits=[("pv", p.cur("pv") - 1 if npt >= 2 else 0)])
                                p.op("vector", (lambda v, _p=ptb, _b=ptv[npt % 2], _vs=vs:
                                                v.tensor_copy(_b[:, _vs * 128:512],
                                                              _p[:, _vs * 128:512])),
                                     waits=[("pt", p.cur("pt")), ("pv", p.cur("pv") - 1 if npt >= 2 else 0)],
                                     inc="pe")
                                vbx = vbv[nvc % 2]
                                vsem = "de" if nvc % 2 == 0 else "df"
                                vsrc = VS[part * 2048 + kc * 128: part * 2048 + (kc + 1) * 128,
                                          h * 128:(h + 1) * 128]
                                wt = [("pv", p.sem.get("_vpv%d" % (nvc - 2), 0))]
                                p.dma("sync", (lambda s, _v=vbx, _s=vsrc: s.dma_start(
                                    out=_v[:], in_=_s)), waits=[w for w in wt if w[1] > 0], inc=vsem)
                                wt = [("pe", p.cur("pe")), (vsem, p.cur(vsem))]
                                if first_pv and ghi:
                                    wt.append(("oe", p.cur("oe")))
                                p.op("tensor", (lambda te, _v=vbx, _b=ptv[npt % 2],
                                                _st=first_pv, _sp=(part == 1 and kc == 4 * g + 3):
                                                te.matmul(pO[:], _v[:], _b[:],
                                                          start=_st, stop=_sp)),
                                     waits=wt, inc="pv")
                                p.sem["_vpv%d" % nvc] = p.cur("pv")
                                first_pv = False
                                npt += 1
                                nvc += 1
                    if ATT_MODE >= 4:
                        p.op("scalar", lambda s: s.copy(otm[:], pO[:]),
                             waits=[("pv", p.cur("pv")), ("ov", p.cur("ov"))], inc="oe")
                        for k in range(4):
                            wt = [("oe", p.cur("oe"))] if k == 0 else []
                            if ghi >= 1 and k == 0:
                                wt.append(("ov", p.cur("ov")))
                            p.op("tensor", (lambda te, _k=k: te.matmul(
                                pOt[:, _k * 128:(_k + 1) * 128], otm[:, _k * 128:(_k + 1) * 128],
                                identf[:], is_transpose=True, start=(_k == 0), stop=(_k == 3))),
                                waits=wt, inc=("ot" if k == 3 else None))
                        p.op("vector", (lambda v: v.tensor_copy(obv[:], pOt[:])),
                             waits=[("ot", p.cur("ot")), ("dg", p.cur("dg"))],
                             inc="ov")
                        dd = OS.rearrange("(t p) f -> p t f", p=128)[:, 4 * g:4 * g + 4,
                                                                    h * 128:(h + 1) * 128]
                        p.dma("gpsimd", (lambda g_, _d=dd: g_.dma_start(
                            out=_d, in_=obv[:].rearrange("p (i f) -> p i f", f=128))),
                            waits=[("ov", p.cur("ov"))], inc="dg")
            p.op("gpsimd", lambda g_: g_.engine_nop(),
                 waits=[("dg", p.cur("dg")), ("dc", p.cur("dc")),
                        ("dd", p.cur("dd"))], inc="gp")
            p.ops = interleave_ops(p.ops, p_ffnw.ops)
            emit(nc, p, sems)
        _ffnw_scope.close()

        with ExitStack() as _ph:
            _en = _ph.enter_context
            wpt = _en(nc.sbuf_tensor("wpt", [128, 16 * C], BF16))
            ob0 = _en(nc.sbuf_tensor("ob0", [128, C], F32))
            ob1 = _en(nc.sbuf_tensor("ob1", [128, C], F32))
            nb = _en(nc.sbuf_tensor("nb", [128, C], F32))
            yb2 = _en(nc.sbuf_tensor("yb2", [128, C], F32))
            mb2 = _en(nc.sbuf_tensor("mb2", [128, C], F32))
            ub2 = _en(nc.sbuf_tensor("ub2", [128, C], F32))
            oqb = _en(nc.sbuf_tensor("oqb", [128, C], BF16))
            oqT = _en(nc.sbuf_tensor("oqT", [128, C], BF16))
            prb = _en(nc.sbuf_tensor("prb", [128, C], F32))
            lnb = _en(nc.sbuf_tensor("lnb", [128, C], F32))
            xo0 = _en(nc.sbuf_tensor("xo0", [128, C], F32))
            xo1 = _en(nc.sbuf_tensor("xo1", [128, C], F32))
            xmb = _en(nc.sbuf_tensor("xmb", [128, C], F32))
            x2b = _en(nc.sbuf_tensor("x2b", [128, C], F32))
            hib = _en(nc.sbuf_tensor("hib", [128, C], BF16))
            lob = _en(nc.sbuf_tensor("lob", [128, C], BF16))
            hev = _en(nc.sbuf_tensor("hev", [128, C], BF16))
            lev = _en(nc.sbuf_tensor("lev", [128, C], BF16))
            pqt = _en(nc.psum_tensor("pqt", [128, C], BF16))
            ppr0 = _en(nc.psum_tensor("ppr0", [128, 512], F32))
            ppr1 = _en(nc.psum_tensor("ppr1", [128, 512], F32))
            phl0 = _en(nc.psum_tensor("phl0", [128, C], BF16))
            phl1 = _en(nc.psum_tensor("phl1", [128, C], BF16))
            obx = [ob0, ob1]
            xox = [xo0, xo1]
            ppr = [ppr0, ppr1]
            p = Plan(semc)
            p.dma("sync", lambda s: s.dma_start(
                out=wpt[:].rearrange("p (k f) -> p k f", f=C),
                in_=WPROJT.rearrange("(k p) f -> p k f", p=128)), inc="wld")
            wptv = p.cur("wld")
            for t in range(OT):
                ob = obx[t % 2]
                xo = xox[t % 2]
                osem = "da" if t % 2 == 0 else "db"
                xsem = "de" if t % 2 == 0 else "df"
                waits = [("prx", p.cur("prx") - 1)] if t >= 2 else []
                p.dma("sync", (lambda s, _b=ob, _t=t: s.dma_start(
                    out=_b[:], in_=OS[_t * 128:(_t + 1) * 128, :])), waits=waits, inc=osem)
                osv = p.cur(osem)
                p.dma("sync", (lambda s, _b=xo, _t=t: s.dma_start(
                    out=_b[:], in_=xp_d[_t * 128:(_t + 1) * 128, :])),
                    waits=list(waits), inc=xsem)
                xsv = p.cur(xsem)
                for hh in range(H):
                    wt = [(osem, osv)] if hh == 0 else []
                    if hh == 0 and t >= 2:
                        wt.append(("xq", p.cur("xq")))
                    p.op("scalar", (lambda s, _h=hh, _t=t, _b=ob: s.activation(
                        nb[:, _h * 128:(_h + 1) * 128],
                        _b[:, _h * 128:(_h + 1) * 128],
                        AF.Identity, scale=invl[:, _t * H + _h:_t * H + _h + 1])),
                        waits=wt, inc=("y1" if hh == H - 1 else None))
                y1v = p.cur("y1")
                o = (t % 2) * 32
                p.op("vector", (lambda v, _o=o: v.tensor_reduce(
                    st[:, _o:_o + 1], nb[:], AX.X, ALU.max, apply_absolute_value=True)),
                    waits=[("y1", y1v)], inc="vst")
                p.op("vector", (lambda v, _o=o: v.tensor_scalar(
                    st[:, _o + 1:_o + 2], st[:, _o:_o + 1], 1e-5, 1.0 / 127.0,
                    op0=ALU.max, op1=ALU.mult)), inc="vst")
                p.op("vector", (lambda v, _o=o: v.reciprocal(
                    st[:, _o + 2:_o + 3], st[:, _o + 1:_o + 2])), inc="vst")
                p.op("vector", (lambda v, _o=o: v.tensor_scalar(
                    st[:, _o + 3:_o + 4], st[:, _o + 1:_o + 2], bcw[:, 3:4], None,
                    op0=ALU.mult)), inc="vst")
                p.op("scalar", (lambda s, _o=o: s.activation(
                    yb2[:], nb[:], AF.Identity, bias=0.0, scale=st[:, _o + 2:_o + 3])),
                    waits=[("vst", p.cur("vst"))], inc="y1")
                p.op("vector", (lambda v: v.tensor_scalar(
                    ub2[:], yb2[:], M32, -M32, op0=ALU.add, op1=ALU.add)),
                    waits=[("y1", p.cur("y1"))], inc="vst")
                p.op("vector", (lambda v: v.tensor_scalar(
                    mb2[:], yb2[:], ZSH, M64, op0=ALU.add, op1=ALU.add)), inc="vst")
                p.op("vector", (lambda v: v.tensor_scalar(
                    mb2[:], mb2[:], -M64, -ZSH, op0=ALU.add, op1=ALU.add)), inc="vst")
                p.op("vector", (lambda v: v.scalar_tensor_tensor(
                    yb2[:], yb2[:], -1.0, mb2[:], ALU.mult, ALU.add)), inc="vst")
                p.op("vector", (lambda v: v.tensor_scalar(
                    mb2[:], yb2[:], -0.5, 0.5, op0=ALU.max, op1=ALU.min)), inc="vst")
                p.op("vector", (lambda v: v.tensor_scalar(
                    yb2[:], yb2[:], 1.0, -0.5, op0=ALU.add, op1=ALU.max)), inc="vst")
                p.op("vector", (lambda v: v.scalar_tensor_tensor(
                    yb2[:], yb2[:], 0.0, mb2[:], ALU.min, ALU.subtract)), inc="vst")
                p.op("vector", (lambda v: v.scalar_tensor_tensor(
                    oqb[:], ub2[:], 0.0, yb2[:], ALU.add, ALU.subtract)), inc="xq")
                for k in range(16):
                    wt = []
                    if k == 0:
                        wt.append(("xq", p.cur("xq")))
                        if t >= 1:
                            wt.append(("tev", p.cur("tev")))
                    p.op("tensor", (lambda te, _k=k: te.matmul(
                        pqt[:, _k * 128:(_k + 1) * 128], oqb[:, _k * 128:(_k + 1) * 128],
                        identb[:], is_transpose=True,
                        start=(_k % 8 == 0), stop=(_k % 8 == 7))),
                        waits=wt, inc=("tp" if k == 15 else None))
                p.op("vector", (lambda v: v.tensor_copy(oqT[:], pqt[:])),
                     waits=[("tp", p.cur("tp"))], inc="tev")
                tevv = p.cur("tev")
                for j in range(4):
                    bank = ppr[j % 2]
                    for k in range(16):
                        wt = []
                        if k == 0:
                            wt.append(("tev", tevv))
                            wt.append(("wld", wptv))
                            if t * 4 + j >= 2:
                                wt.append(("ev", p.cur("ev") - 1))
                        p.op("tensor", (lambda te, _k=k, _j=j, _b=bank: te.matmul(
                            _b[:], oqT[:, _k * 128:(_k + 1) * 128],
                            wpt[:, _k * C + _j * 512:_k * C + (_j + 1) * 512],
                            start=(_k == 0), stop=(_k == 15))),
                            waits=wt, inc=("mm" if k == 15 else None))
                    p.op("scalar", (lambda s, _j=j, _b=bank, _o=o: s.activation(
                        prb[:, _j * 512:(_j + 1) * 512], _b[:], AF.Identity,
                        scale=st[:, _o + 3:_o + 4])),
                        waits=[("mm", p.cur("mm"))], inc="ev")
                evv = p.cur("ev")
                p.op("vector", (lambda v, _o=o: v.tensor_reduce(
                    st[:, _o + 4:_o + 5], prb[:], AX.X, ALU.add, negate=True)),
                    waits=[("ev", evv)], inc="vst")
                p.op("scalar", (lambda s, _o=o: s.activation(
                    mb2[:], prb[:], AF.Square, accum_out=st[:, _o + 5:_o + 6])),
                    waits=[("ev", evv)], inc="sq")
                p.op("vector", (lambda v, _o=o: v.tensor_scalar(
                    st[:, _o + 6:_o + 7], st[:, _o + 4:_o + 5], 1.0 / C, None,
                    op0=ALU.mult)), inc="vst")
                p.op("vector", (lambda v, _o=o: v.tensor_scalar(
                    st[:, _o + 7:_o + 8], st[:, _o + 5:_o + 6], 1.0 / C, EPS,
                    op0=ALU.mult, op1=ALU.add)),
                    waits=[("sq", p.cur("sq"))], inc="vst")
                p.op("vector", (lambda v, _o=o: v.scalar_tensor_tensor(
                    st[:, _o + 8:_o + 9], st[:, _o + 6:_o + 7], -1.0,
                    st[:, _o + 6:_o + 7], ALU.mult, ALU.mult)), inc="vst")
                p.op("vector", (lambda v, _o=o: v.tensor_tensor(
                    st[:, _o + 9:_o + 10], st[:, _o + 7:_o + 8],
                    st[:, _o + 8:_o + 9], ALU.add)), inc="vst")
                p.op("scalar", (lambda s, _o=o: s.activation(
                    st[:, _o + 10:_o + 11], st[:, _o + 9:_o + 10], AF.Sqrt)),
                    waits=[("vst", p.cur("vst"))], inc="sq")
                p.op("vector", (lambda v, _o=o: v.reciprocal(
                    st[:, _o + 11:_o + 12], st[:, _o + 10:_o + 11])),
                    waits=[("sq", p.cur("sq"))], inc="vst")
                p.op("vector", (lambda v, _o=o: v.tensor_tensor(
                    st[:, _o + 12:_o + 13], st[:, _o + 6:_o + 7],
                    st[:, _o + 11:_o + 12], ALU.mult)), inc="vst")
                p.op("scalar", (lambda s, _o=o: s.activation(
                    lnb[:], prb[:], AF.Identity, bias=st[:, _o + 12:_o + 13],
                    scale=st[:, _o + 11:_o + 12])),
                    waits=[("vst", p.cur("vst"))], inc="y1")
                xw = [("y1", p.cur("y1")), (xsem, xsv)]
                if t >= 1:
                    xw.append(("dg", p.sem.get("_xmidsp%d" % (t - 1), 0)))
                p.op("vector", (lambda v, _x=xo: v.tensor_tensor(
                    xmb[:], _x[:], lnb[:], ALU.add)),
                    waits=xw, inc="fx")
                fxv = p.cur("fx")
                p.dma("gpsimd", (lambda g, _t=t: g.dma_start(
                    out=XMID[_t * 128:(_t + 1) * 128, :], in_=xmb[:])),
                    waits=[("fx", fxv)], inc="dg")
                p.sem["_xmidsp%d" % t] = p.cur("dg")
                p.op("vector", (lambda v, _o=o: v.tensor_reduce(
                    st[:, _o + 13:_o + 14], xmb[:], AX.X, ALU.add, negate=True)),
                    waits=[("fx", fxv)], inc="vst")
                p.op("scalar", (lambda s, _o=o: s.activation(
                    mb2[:], xmb[:], AF.Square, accum_out=st[:, _o + 14:_o + 15])),
                    waits=[("fx", fxv)], inc="sq")
                p.op("vector", (lambda v, _o=o: v.tensor_scalar(
                    st[:, _o + 15:_o + 16], st[:, _o + 13:_o + 14], 1.0 / C, None,
                    op0=ALU.mult)), inc="vst")
                p.op("vector", (lambda v, _o=o: v.tensor_scalar(
                    st[:, _o + 16:_o + 17], st[:, _o + 14:_o + 15], 1.0 / C, EPS,
                    op0=ALU.mult, op1=ALU.add)),
                    waits=[("sq", p.cur("sq"))], inc="vst")
                p.op("vector", (lambda v, _o=o: v.scalar_tensor_tensor(
                    st[:, _o + 17:_o + 18], st[:, _o + 15:_o + 16], -1.0,
                    st[:, _o + 15:_o + 16], ALU.mult, ALU.mult)), inc="vst")
                p.op("vector", (lambda v, _o=o: v.tensor_tensor(
                    st[:, _o + 18:_o + 19], st[:, _o + 16:_o + 17],
                    st[:, _o + 17:_o + 18], ALU.add)), inc="vst")
                p.op("scalar", (lambda s, _o=o: s.activation(
                    st[:, _o + 19:_o + 20], st[:, _o + 18:_o + 19], AF.Sqrt)),
                    waits=[("vst", p.cur("vst"))], inc="sq")
                p.op("vector", (lambda v, _o=o: v.reciprocal(
                    st[:, _o + 20:_o + 21], st[:, _o + 19:_o + 20])),
                    waits=[("sq", p.cur("sq"))], inc="vst")
                p.op("vector", (lambda v, _o=o: v.tensor_tensor(
                    st[:, _o + 21:_o + 22], st[:, _o + 15:_o + 16],
                    st[:, _o + 20:_o + 21], ALU.mult)), inc="vst")
                p.op("scalar", (lambda s, _o=o: s.activation(
                    x2b[:], xmb[:], AF.Identity, bias=st[:, _o + 21:_o + 22],
                    scale=st[:, _o + 20:_o + 21])),
                    waits=[("vst", p.cur("vst"))], inc="y1")
                y1v2 = p.cur("y1")
                hw = [("y1", y1v2)]
                if t >= 1:
                    hw.append(("tp", p.sem.get("_hltp%d" % (t - 1), 0)))
                p.op("vector", (lambda v: v.tensor_copy(hib[:], x2b[:])),
                     waits=hw, inc="vst")
                p.op("vector", (lambda v: v.scalar_tensor_tensor(
                    lob[:], hib[:], -1.0, x2b[:], ALU.mult, ALU.add)), inc="hst")
                hstv = p.cur("hst")
                for (src_, psx, evb, dstT, ssem) in ((hib, phl0, hev, XHs, "dc"),
                                                     (lob, phl1, lev, XLs, "dd")):
                    for k in range(16):
                        wt = []
                        if k == 0:
                            wt.append(("hst", hstv))
                            if t >= 1:
                                wt.append(("xev", p.cur("xev")))
                        p.op("tensor", (lambda te, _k=k, _s=src_, _p=psx: te.matmul(
                            _p[:, _k * 128:(_k + 1) * 128], _s[:, _k * 128:(_k + 1) * 128],
                            identb[:], is_transpose=True,
                            start=(_k % 8 == 0), stop=(_k % 8 == 7))),
                            waits=wt, inc=("tp" if k == 15 else None))
                    p.op("vector", (lambda v, _p=psx, _e=evb: v.tensor_copy(_e[:], _p[:])),
                         waits=[("tp", p.cur("tp")), (ssem, p.cur(ssem))],
                         inc="xev")
                    dd = dstT.rearrange("(k p) r -> p k r", p=128)[:, :, t * 128:(t + 1) * 128]
                    p.dma("gpsimd", (lambda g, _d=dd, _e=evb: g.dma_start(
                        out=_d, in_=_e[:].rearrange("p (k j) -> p k j", j=128))),
                        waits=[("xev", p.cur("xev"))], inc=ssem)
                p.sem["_hltp%d" % t] = p.cur("tp")
                p.op("vector", lambda v: v.engine_nop(), waits=[], inc="prx")
            p.op("gpsimd", lambda g: g.engine_nop(),
                 waits=[("dg", p.cur("dg")), ("dc", p.cur("dc")), ("dd", p.cur("dd"))],
                 inc="gp")
            emit(nc, p, sems)

        with ExitStack() as _ph:
            _en = _ph.enter_context
            xh = _en(nc.sbuf_tensor("xh", [128, 16 * 512], BF16))
            xl = _en(nc.sbuf_tensor("xl", [128, 16 * 512], BF16))
            sh = _en(nc.sbuf_tensor("sh", [128, 44 * 512], BF16))
            sl = _en(nc.sbuf_tensor("sl", [128, 44 * 512], BF16))
            wgb0 = _en(nc.sbuf_tensor("wgb0", [128, 16 * 128], BF16))
            wgb1 = _en(nc.sbuf_tensor("wgb1", [128, 16 * 128], BF16))
            wub0 = _en(nc.sbuf_tensor("wub0", [128, 16 * 128], BF16))
            wub1 = _en(nc.sbuf_tensor("wub1", [128, 16 * 128], BF16))
            wdb0 = _en(nc.sbuf_tensor("wdb0", [128, 44 * 128], BF16))
            wdb1 = _en(nc.sbuf_tensor("wdb1", [128, 44 * 128], BF16))
            sgb0 = _en(nc.sbuf_tensor("sgb0", [128, 512], F32))
            sgb1 = _en(nc.sbuf_tensor("sgb1", [128, 512], F32))
            sub0 = _en(nc.sbuf_tensor("sub0", [128, 512], F32))
            sub1 = _en(nc.sbuf_tensor("sub1", [128, 512], F32))
            hid = _en(nc.sbuf_tensor("hid", [128, 512], F32))
            dob = _en(nc.sbuf_tensor("dob", [128, 512], F32))
            tdb = _en(nc.sbuf_tensor("tdb", [128, 512], F32))
            pgg0 = _en(nc.psum_tensor("pgg0", [128, 512], F32))
            pgg1 = _en(nc.psum_tensor("pgg1", [128, 512], F32))
            puu0 = _en(nc.psum_tensor("puu0", [128, 512], F32))
            puu1 = _en(nc.psum_tensor("puu1", [128, 512], F32))
            pdd0 = _en(nc.psum_tensor("pdd0", [128, 512], F32))
            pdd1 = _en(nc.psum_tensor("pdd1", [128, 512], F32))
            pdt = _en(nc.psum_tensor("pdt", [128, 512], F32))
            wgb = [wgb0, wgb1]
            wub = [wub0, wub1]
            wdb = [wdb0, wdb1]
            pdd = [pdd0, pdd1]
            pgg = [pgg0, pgg1]
            puu = [puu0, puu1]
            sgbv = [sgb0, sgb1]
            subv = [sub0, sub1]
            p = Plan(semc)
            gev = {0: 0, 1: 0}
            hmul = {0: 0, 1: 0}
            p.sem.setdefault("vst", 0)
            for q in range(4):
                p.dma("sync", (lambda s, _q=q: s.dma_start(
                    out=xh[:].rearrange("p (k r) -> p k r", r=512),
                    in_=XHs.rearrange("(k p) r -> p k r", p=128)[:, :, _q * 512:(_q + 1) * 512])),
                    waits=[("dn", p.cur("dn"))], inc="da")
                p.dma("sync", (lambda s, _q=q: s.dma_start(
                    out=xl[:].rearrange("p (k r) -> p k r", r=512),
                    in_=XLs.rearrange("(k p) r -> p k r", p=128)[:, :, _q * 512:(_q + 1) * 512])),
                    waits=[("dn", p.cur("dn"))], inc="db")
                xldv = [("da", p.cur("da")), ("db", p.cur("db"))]
                for ht in range(44):
                    wgx = wgb[ht % 2]
                    wux = wub[ht % 2]
                    gsem = "de" if ht % 2 == 0 else "df"
                    usem = "dh" if ht % 2 == 0 else "di"
                    wt = []
                    if gev[ht % 2]:
                        wt.append(("ev", gev[ht % 2]))
                    p.dma("sync", (lambda s, _w=wgx, _h=ht: s.dma_start(
                        out=_w[:].rearrange("p (k j) -> p k j", j=128),
                        in_=WGT.rearrange("(k p) f -> p k f", p=128)[:, :, _h * 128:(_h + 1) * 128])),
                        waits=wt, inc=gsem)
                    p.dma("sync", (lambda s, _w=wux, _h=ht: s.dma_start(
                        out=_w[:].rearrange("p (k j) -> p k j", j=128),
                        in_=WUT.rearrange("(k p) f -> p k f", p=128)[:, :, _h * 128:(_h + 1) * 128])),
                        waits=list(wt), inc=usem)
                    wglv, wulv = p.cur(gsem), p.cur(usem)
                    par = ht % 2
                    for (wsem, wv, wbx, bank, name) in ((gsem, wglv, wgx, pgg[par], "g"),
                                                        (usem, wulv, wux, puu[par], "u")):
                        for z in range(32):
                            k, hl = z % 16, z // 16
                            rhs = xh if hl == 0 else xl
                            wt = []
                            if z == 0:
                                wt.append((wsem, wv))
                                wt.extend(xldv)
                                if gev[par]:
                                    wt.append(("ev", gev[par]))
                            p.op("tensor", (lambda te, _k=k, _w=wbx, _r=rhs, _b=bank:
                                            te.matmul(_b[:], _w[:, _k * 128:(_k + 1) * 128],
                                                      _r[:, _k * 512:(_k + 1) * 512],
                                                      start=(_k == 0 and _r is xh),
                                                      stop=(_k == 15 and _r is xl))),
                                 waits=wt, inc=("mm" if z == 31 else None))
                        wt = [("mm", p.cur("mm"))]
                        if hmul[par]:
                            wt.append(("vst", hmul[par]))
                        if name == "g":
                            p.op("scalar", (lambda s, _h=ht, _b=bank, _o=sgbv[par]:
                                            s.activation(
                                _o[:], _b[:], AF.Silu, scale=sgv[:, _h:_h + 1])),
                                waits=wt, inc="ev")
                        else:
                            p.op("scalar", (lambda s, _h=ht, _b=bank, _o=subv[par]:
                                            s.activation(
                                _o[:], _b[:], AF.Identity, scale=suv[:, _h:_h + 1])),
                                waits=wt, inc="ev")
                    gev[par] = p.cur("ev")
                    p.op("vector", (lambda v, _g=sgbv[par], _u=subv[par]:
                                    v.tensor_tensor(hid[:], _g[:], _u[:], ALU.mult)),
                        waits=[("ev", p.cur("ev"))], inc="vst")
                    hmul[par] = p.cur("vst")
                    p.op("vector", (lambda v, _h=ht: v.tensor_copy(
                        sh[:, _h * 512:(_h + 1) * 512], hid[:])), inc="vst")
                    p.op("vector", (lambda v, _h=ht: v.scalar_tensor_tensor(
                        sl[:, _h * 512:(_h + 1) * 512], sh[:, _h * 512:(_h + 1) * 512],
                        -1.0, hid[:], ALU.mult, ALU.add)), inc="gu")
                guv = p.cur("gu")
                for dt in range(16):
                    wdx = wdb[dt % 2]
                    dsem = "dj" if dt % 2 == 0 else "dk"
                    wt = [("dnv", p.cur("dnv") - 1)] if (q * 16 + dt) >= 2 else []
                    p.dma("sync", (lambda s, _w=wdx, _d=dt: s.dma_start(
                        out=_w[:].rearrange("p (k j) -> p k j", j=128),
                        in_=WDT.rearrange("(k p) f -> p k f", p=128)[:, :, _d * 128:(_d + 1) * 128])),
                        waits=wt, inc=dsem)
                    wdlv = p.cur(dsem)
                    bank = pdd[dt % 2]
                    for z in range(88):
                        k, hl = z % 44, z // 44
                        rhs = sh if hl == 0 else sl
                        wt = []
                        if z == 0:
                            wt.append((dsem, wdlv))
                            wt.append(("gu", guv))
                            if q * 16 + dt >= 2:
                                wt.append(("dnv", p.cur("dnv") - 1))
                        p.op("tensor", (lambda te, _k=k, _w=wdx, _r=rhs, _b=bank:
                                        te.matmul(_b[:], _w[:, _k * 128:(_k + 1) * 128],
                                                  _r[:, _k * 512:(_k + 1) * 512],
                                                  start=(_k == 0 and _r is sh),
                                                  stop=(_k == 43 and _r is sl))),
                             waits=wt, inc=("mm" if z == 87 else None))
                    p.op("scalar", (lambda s, _d=dt, _b=bank: s.activation(
                        dob[:], _b[:], AF.Identity, scale=sdv[:, _d:_d + 1])),
                        waits=[("mm", p.cur("mm")), ("ot", p.cur("ot"))], inc="dnv")
                    for k in range(4):
                        wt = []
                        if k == 0:
                            wt.append(("dnv", p.cur("dnv")))
                            wt.append(("ov", p.cur("ov")))
                        p.op("tensor", (lambda te, _k=k: te.matmul(
                            pdt[:, _k * 128:(_k + 1) * 128], dob[:, _k * 128:(_k + 1) * 128],
                            identf[:], is_transpose=True, start=(_k == 0), stop=(_k == 3))),
                            waits=wt, inc=("ot" if k == 3 else None))
                    p.op("vector", (lambda v: v.tensor_copy(tdb[:], pdt[:])),
                         waits=[("ot", p.cur("ot")), ("dg", p.cur("dg"))], inc="ov")
                    dd = FFO.rearrange("(t p) f -> p t f", p=128)[:, q * 4:q * 4 + 4,
                                                                 dt * 128:(dt + 1) * 128]
                    p.dma("gpsimd", (lambda g, _d=dd: g.dma_start(
                        out=_d, in_=tdb[:].rearrange("p (i f) -> p i f", f=128))),
                        waits=[("ov", p.cur("ov"))], inc="dg")
                p.op("vector", lambda v: v.engine_nop(),
                     waits=[("ov", p.cur("ov"))], inc="dn")
            p.op("gpsimd", lambda g: g.engine_nop(),
                 waits=[("dg", p.cur("dg"))], inc="gp")
            emit(nc, p, sems)

        with ExitStack() as _ph:
            _en = _ph.enter_context
            fb0 = _en(nc.sbuf_tensor("fb0", [128, C], F32))
            fb1 = _en(nc.sbuf_tensor("fb1", [128, C], F32))
            xm0 = _en(nc.sbuf_tensor("xm0", [128, C], F32))
            xm1 = _en(nc.sbuf_tensor("xm1", [128, C], F32))
            sq2 = _en(nc.sbuf_tensor("sq2", [128, C], F32))
            ln5 = _en(nc.sbuf_tensor("ln5", [128, C], F32))
            yo0 = _en(nc.sbuf_tensor("yo0", [128, C], F32))
            yo1 = _en(nc.sbuf_tensor("yo1", [128, C], F32))
            fbx = [fb0, fb1]
            xmx = [xm0, xm1]
            yox = [yo0, yo1]
            p = Plan(semc)
            for t in range(OT):
                fb = fbx[t % 2]
                xm = xmx[t % 2]
                yo = yox[t % 2]
                fsem = "da" if t % 2 == 0 else "db"
                msem = "de" if t % 2 == 0 else "df"
                waits = [("fx", p.cur("fx") - 1)] if t >= 2 else []
                p.dma("sync", (lambda s, _b=fb, _t=t: s.dma_start(
                    out=_b[:], in_=FFO[_t * 128:(_t + 1) * 128, :])), waits=waits, inc=fsem)
                fsv = p.cur(fsem)
                p.dma("sync", (lambda s, _b=xm, _t=t: s.dma_start(
                    out=_b[:], in_=XMID[_t * 128:(_t + 1) * 128, :])),
                    waits=list(waits), inc=msem)
                msv = p.cur(msem)
                o = (t % 2) * 32
                p.op("vector", (lambda v, _b=fb, _o=o: v.tensor_reduce(
                    st[:, _o:_o + 1], _b[:], AX.X, ALU.add, negate=True)),
                    waits=[(fsem, fsv)], inc="vst")
                p.op("scalar", (lambda s, _b=fb, _o=o: s.activation(
                    sq2[:], _b[:], AF.Square, accum_out=st[:, _o + 1:_o + 2])),
                    waits=[(fsem, fsv)], inc="sq")
                p.op("vector", (lambda v, _o=o: v.tensor_scalar(
                    st[:, _o + 2:_o + 3], st[:, _o:_o + 1], 1.0 / C, None,
                    op0=ALU.mult)), inc="vst")
                p.op("vector", (lambda v, _o=o: v.tensor_scalar(
                    st[:, _o + 3:_o + 4], st[:, _o + 1:_o + 2], 1.0 / C, EPS,
                    op0=ALU.mult, op1=ALU.add)),
                    waits=[("sq", p.cur("sq"))], inc="vst")
                p.op("vector", (lambda v, _o=o: v.scalar_tensor_tensor(
                    st[:, _o + 4:_o + 5], st[:, _o + 2:_o + 3], -1.0,
                    st[:, _o + 2:_o + 3], ALU.mult, ALU.mult)), inc="vst")
                p.op("vector", (lambda v, _o=o: v.tensor_tensor(
                    st[:, _o + 5:_o + 6], st[:, _o + 3:_o + 4],
                    st[:, _o + 4:_o + 5], ALU.add)), inc="vst")
                p.op("scalar", (lambda s, _o=o: s.activation(
                    st[:, _o + 6:_o + 7], st[:, _o + 5:_o + 6], AF.Sqrt)),
                    waits=[("vst", p.cur("vst"))], inc="sq")
                p.op("vector", (lambda v, _o=o: v.reciprocal(
                    st[:, _o + 7:_o + 8], st[:, _o + 6:_o + 7])),
                    waits=[("sq", p.cur("sq"))], inc="vst")
                p.op("vector", (lambda v, _o=o: v.tensor_tensor(
                    st[:, _o + 8:_o + 9], st[:, _o + 2:_o + 3],
                    st[:, _o + 7:_o + 8], ALU.mult)), inc="vst")
                p.op("scalar", (lambda s, _b=fb, _o=o: s.activation(
                    ln5[:], _b[:], AF.Identity, bias=st[:, _o + 8:_o + 9],
                    scale=st[:, _o + 7:_o + 8])),
                    waits=[("vst", p.cur("vst"))], inc="y1")
                ysem = "dc" if t % 2 == 0 else "dd"
                yw = [("y1", p.cur("y1")), (msem, msv)]
                if t >= 2:
                    yw.append((ysem, p.cur(ysem)))
                p.op("vector", (lambda v, _x=xm, _y=yo: v.tensor_tensor(
                    _y[:], _x[:], ln5[:], ALU.add)),
                    waits=yw, inc="fx")
                p.dma("gpsimd", (lambda g, _y=yo, _t=t: g.dma_start(
                    out=y_d[_t * 128:(_t + 1) * 128, :], in_=_y[:])),
                    waits=[("fx", p.cur("fx"))], inc=ysem)
            for nm in DEBUG_DUMPS:
                p.dma("gpsimd", (lambda g, _nm=nm: g.dma_start(
                    out=dbg[_nm], in_=scr[_nm])), inc="out")
            p.op("gpsimd", lambda g: g.engine_nop(),
                 waits=[("dc", p.cur("dc")), ("dd", p.cur("dd")),
                        ("out", p.cur("out"))], inc="gp")
            emit(nc, p, sems)

    if PHASE_LIMIT < 9:
        with nc.sbuf_tensor("zout", [128, C], F32) as zo, \
             nc.semaphore("zs") as zs, nc.Block() as blk:
            @blk.gpsimd
            def _(g):
                g.memset(zo[:], 0.0).then_inc(zs, 1)
                g.wait_ge(zs, 1)
                for t in range(OT):
                    g.dma_start(out=y_d[t * 128:(t + 1) * 128, :],
                                in_=zo[:]).then_inc(zs, 16)
                g.wait_ge(zs, 1 + 16 * OT)

    nc.finalize()
    return nc


def kernel(L0_attn_x, L0_attn_w_qkv, L0_attn_w_proj, L0_ffn_wq_gate,
           L0_ffn_s_gate, L0_ffn_wq_up, L0_ffn_s_up, L0_ffn_wq_down,
           L0_ffn_s_down):
    x = np.ascontiguousarray(np.asarray(L0_attn_x, dtype=np.float32))
    wq = np.ascontiguousarray(np.asarray(L0_attn_w_qkv, dtype=np.float32))
    wp = np.ascontiguousarray(np.asarray(L0_attn_w_proj, dtype=np.float32))
    wg = np.ascontiguousarray(np.asarray(L0_ffn_wq_gate, dtype=np.int32))
    wu = np.ascontiguousarray(np.asarray(L0_ffn_wq_up, dtype=np.int32))
    wd = np.ascontiguousarray(np.asarray(L0_ffn_wq_down, dtype=np.int32))
    sg = np.ascontiguousarray(np.asarray(L0_ffn_s_gate, dtype=np.float32))
    su = np.ascontiguousarray(np.asarray(L0_ffn_s_up, dtype=np.float32))
    sd = np.ascontiguousarray(np.asarray(L0_ffn_s_down, dtype=np.float32))
    ident = np.eye(128, dtype=np.float32)
    trim = (np.triu(np.ones((128, 128), dtype=np.float32), 1) * NEG).astype(np.float32)

    nc = build_program()
    in_maps = []
    orders = []
    for c in range(8):
        b, par = c // 2, c % 2
        order = list(range(par, NT, 2)) + list(range(1 - par, NT, 2))
        orders.append(order)
        xp = np.concatenate([x[b, t * 128:(t + 1) * 128, :] for t in order], axis=0)
        oma = (np.zeros((128, 128), np.float32) if par == 1 else
               np.full((128, 128), NEG, np.float32))
        in_maps.append(dict(xp=np.ascontiguousarray(xp), wqkv=wq, wproj=wp,
                            wg=wg, wu=wu, wd=wd, sg=sg, su=su, sd=sd,
                            ident=ident, tri=trim, oma=oma))
    res = run_bass_kernel_spmd(nc, in_maps, core_ids=list(range(8)))
    y = np.empty((4, NTOK, C), np.float32)
    for c in range(8):
        b, par = c // 2, c % 2
        yc = np.asarray(res.results[c]["y"])
        for i, t in enumerate(range(par, NT, 2)):
            y[b, t * 128:(t + 1) * 128, :] = yc[i * 128:(i + 1) * 128, :]
    return y


# version 06
# speedup vs baseline: 1.2161x; 1.0554x over previous
import numpy as np
from contextlib import ExitStack

import concourse.bass as bass
import concourse.mybir as mybir
from concourse.bass_utils import run_bass_kernel_spmd
from concourse.alu_op_type import AluOpType as ALU

F32 = mybir.dt.float32
BF16 = mybir.dt.bfloat16
I32 = mybir.dt.int32
AF = mybir.ActivationFunctionType
AX = mybir.AxisListType

C = 2048
H = 16
HID = 5632
NTOK = 4096
NT = 32
OT = 16
FQKV = 3 * C
NEG = np.float32(-9e15)
EPS = float(np.float32(1e-5))
SCALE = float(np.float32(128.0 ** -0.5))
M32 = 12582912.0
M64 = 6755399441055744.0
ZSH = 384.0

ENGINES = ("sync", "scalar", "vector", "tensor", "gpsimd")
DEBUG_DUMPS = []


class Plan:
    def __init__(self, sem):
        self.ops = []
        self.sem = sem

    def op(self, engine, fn, waits=(), inc=None, inc_n=1):
        w = [(s, int(v)) for (s, v) in waits if v > 0]
        iv = None
        if inc is not None:
            self.sem[inc] = self.sem.get(inc, 0) + inc_n
            iv = self.sem[inc]
        self.ops.append([engine, fn, w, inc, inc_n, iv])

    def dma(self, engine, fn, waits=(), inc=None):
        self.op(engine, fn, waits, inc, 16)

    def cur(self, name):
        return self.sem.get(name, 0)

    def finalize(self):
        last = {}
        for o in self.ops:
            en, fn, waits, inc, inc_n, iv = o
            if en in ("vector", "scalar", "gpsimd") and inc_n != 16:
                if en in last and last[en] not in waits:
                    waits.append(last[en])
                if inc is None:
                    eo = "eo_" + en
                    self.sem[eo] = self.sem.get(eo, 0) + 1
                    o[3], o[5] = eo, self.sem[eo]
                last[en] = (o[3], o[5])


PHASE_LIMIT = 9
ATT_MODE = 4
_PH = [0]


def interleave_ops(a_ops, b_ops):
    out = []
    na, nb = len(a_ops), len(b_ops)
    j = 0
    for i, op in enumerate(a_ops):
        out.append(op)
        while j < nb and j * na <= i * nb:
            out.append(b_ops[j])
            j += 1
    out.extend(b_ops[j:])
    return out


def emit(nc, plan, sems):
    i = _PH[0]
    _PH[0] += 1
    if i >= PHASE_LIMIT:
        return
    plan.finalize()
    with nc.Block() as block:
        def mk(engname):
            def body(eng):
                for (en, fn, waits, inc, inc_n, iv) in plan.ops:
                    if en != engname:
                        continue
                    for (s, v) in waits:
                        eng.wait_ge(sems[s], v)
                    instr = fn(eng)
                    if inc is not None:
                        instr.then_inc(sems[inc], inc_n)
            body.__name__ = "_"
            return body
        for en in ENGINES:
            getattr(block, en)(mk(en))


SEM_NAMES = ["ld", "wld", "vst", "sq", "y1", "xq", "xtp", "xev", "mm", "ev",
             "tp", "tev", "qld", "se", "sv", "mx", "xp",
             "pt", "pe", "pv", "oe", "ot", "ov", "gp", "out", "prx", "gu",
             "dn", "hst", "fx", "dnv",
             "da", "db", "dc", "dd", "de", "df", "dg", "dh", "di", "dj", "dk",
             "eo_vector", "eo_scalar", "eo_gpsimd"]


def build_program():
    _PH[0] = 0
    nc = bass.Bass("TRN2", target_bir_lowering=False)

    xp_d = nc.dram_tensor("xp", [NTOK, C], F32, kind="ExternalInput").ap()
    wqkv = nc.dram_tensor("wqkv", [FQKV, C], F32, kind="ExternalInput").ap()
    wproj = nc.dram_tensor("wproj", [C, C], F32, kind="ExternalInput").ap()
    wg_r = nc.dram_tensor("wg", [HID, C], I32, kind="ExternalInput").ap()
    wu_r = nc.dram_tensor("wu", [HID, C], I32, kind="ExternalInput").ap()
    wd_r = nc.dram_tensor("wd", [C, HID], I32, kind="ExternalInput").ap()
    sg_d = nc.dram_tensor("sg", [HID], F32, kind="ExternalInput").ap()
    su_d = nc.dram_tensor("su", [HID], F32, kind="ExternalInput").ap()
    sd_d = nc.dram_tensor("sd", [C], F32, kind="ExternalInput").ap()
    id_d = nc.dram_tensor("ident", [128, 128], F32, kind="ExternalInput").ap()
    tri_d = nc.dram_tensor("tri", [128, 128], F32, kind="ExternalInput").ap()
    oma_d = nc.dram_tensor("oma", [128, 128], F32, kind="ExternalInput").ap()
    y_d = nc.dram_tensor("y", [2048, C], F32, kind="ExternalOutput").ap()

    WQKVT = nc.dram_tensor("WQKVT", [C, FQKV], BF16, kind="Internal").ap()
    WPROJT = nc.dram_tensor("WPROJT", [C, C], BF16, kind="Internal").ap()
    WGT = nc.dram_tensor("WGT", [C, HID], BF16, kind="Internal").ap()
    WUT = nc.dram_tensor("WUT", [C, HID], BF16, kind="Internal").ap()
    WDT = nc.dram_tensor("WDT", [HID, C], BF16, kind="Internal").ap()
    QT = nc.dram_tensor("QT", [C, 2048], F32, kind="Internal").ap()
    KT = nc.dram_tensor("KT", [C, NTOK], F32, kind="Internal").ap()
    VS = nc.dram_tensor("VS", [NTOK, C], F32, kind="Internal").ap()
    OS = nc.dram_tensor("OS", [2048, C], F32, kind="Internal").ap()
    XMID = nc.dram_tensor("XMID", [2048, C], F32, kind="Internal").ap()
    XHs = nc.dram_tensor("XHs", [C, 2048], BF16, kind="Internal").ap()
    XLs = nc.dram_tensor("XLs", [C, 2048], BF16, kind="Internal").ap()
    FFO = nc.dram_tensor("FFO", [2048, C], F32, kind="Internal").ap()
    scr = {"QT": QT, "KT": KT, "VS": VS, "OS": OS, "XMID": XMID, "FFO": FFO,
           "WQKVT": WQKVT, "WPROJT": WPROJT, "WGT": WGT, "WUT": WUT, "WDT": WDT,
           "XHs": XHs, "XLs": XLs}
    dbg = {}
    for nm in DEBUG_DUMPS:
        t = scr[nm]
        dbg[nm] = nc.dram_tensor("dbg_" + nm, list(t.shape), t.dtype,
                                 kind="ExternalOutput").ap()

    semc = {}
    with ExitStack() as ctx:
        e = ctx.enter_context
        sems = {n: e(nc.semaphore(n)) for n in SEM_NAMES}

        identf = e(nc.sbuf_tensor("identf", [128, 128], F32))
        identb = e(nc.sbuf_tensor("identb", [128, 128], BF16))
        tri = e(nc.sbuf_tensor("tri_s", [128, 128], F32))
        oma = e(nc.sbuf_tensor("oma_s", [128, 128], F32))
        ones = e(nc.sbuf_tensor("ones", [128, 128], F32))
        invs = e(nc.sbuf_tensor("invs", [128, NT], F32))
        qsc = e(nc.sbuf_tensor("qsc", [128, NT], F32))
        ksc = e(nc.sbuf_tensor("ksc", [128, NT], F32))
        bcw = e(nc.sbuf_tensor("bcw", [128, 4], F32))
        invl = e(nc.sbuf_tensor("invl", [128, OT * H], F32))
        sgv = e(nc.sbuf_tensor("sgv", [128, 44], F32))
        suv = e(nc.sbuf_tensor("suv", [128, 44], F32))
        sdv = e(nc.sbuf_tensor("sdv", [128, 16], F32))
        st = e(nc.sbuf_tensor("st", [128, 64], F32))
        sws = e(nc.sbuf_tensor("sws", [128, 8], F32))
        cst = e(nc.sbuf_tensor("cst", [128, 4], F32))
        nc.const_aps.register(F32, 0.5, cst[:, 0:1])
        nc.const_aps.register(F32, 0.0, cst[:, 1:2])
        nc.const_aps.register(F32, 1.0, cst[:, 2:3])
        nc.const_aps.register(F32, ZSH, cst[:, 3:4])

        wpart = e(nc.sbuf_tensor("wpart", [128, 64], F32))
        with ExitStack() as _ph:
            _en = _ph.enter_context
            wb0 = _en(nc.sbuf_tensor("wb0", [128, C], F32))
            wb1 = _en(nc.sbuf_tensor("wb1", [128, C], F32))
            ps_sw = _en(nc.psum_tensor("ps_sw", [128, 8], F32))
            sgn = _en(nc.sbuf_tensor("sgn", [128, 128], F32))
            sun = _en(nc.sbuf_tensor("sun", [128, 128], F32))
            sdn = _en(nc.sbuf_tensor("sdn", [128, 128], F32))
            psv = _en(nc.psum_tensor("psv", [128, 512], F32))
            wbuf = [wb0, wb1]
            p = Plan(semc)
            p.dma("sync", lambda s: s.dma_start(out=identf[:], in_=id_d), inc="ld")
            p.dma("sync", lambda s: s.dma_start(out=tri[:], in_=tri_d), inc="ld")
            p.dma("sync", lambda s: s.dma_start(out=oma[:], in_=oma_d), inc="ld")
            p.dma("sync", lambda s: s.dma_start(
                out=sgn[0:44, :], in_=sg_d.rearrange("(a p) -> a p", p=128)), inc="ld")
            p.dma("sync", lambda s: s.dma_start(
                out=sun[0:44, :], in_=su_d.rearrange("(a p) -> a p", p=128)), inc="ld")
            p.dma("sync", lambda s: s.dma_start(
                out=sdn[0:16, :], in_=sd_d.rearrange("(a p) -> a p", p=128)), inc="ld")
            p.op("tensor", lambda t: t.matmul(psv[:, 0:44], sgn[0:44, :],
                                              identf[0:44, 0:44], is_transpose=True,
                                              start=True, stop=True),
                 waits=[("ld", 16 * 6)], inc="tp")
            p.op("tensor", lambda t: t.matmul(psv[:, 64:108], sun[0:44, :],
                                              identf[0:44, 0:44], is_transpose=True,
                                              start=True, stop=True), inc="tp")
            p.op("tensor", lambda t: t.matmul(psv[:, 128:144], sdn[0:16, :],
                                              identf[0:16, 0:16], is_transpose=True,
                                              start=True, stop=True), inc="tp")
            p.op("vector", lambda v: v.tensor_copy(sgv[:], psv[:, 0:44]),
                 waits=[("tp", p.cur("tp"))], inc="vst")
            p.op("vector", lambda v: v.tensor_copy(suv[:], psv[:, 64:108]), inc="vst")
            p.op("vector", lambda v: v.tensor_copy(sdv[:], psv[:, 128:144]), inc="vst")
            p.op("gpsimd", lambda g: g.memset(ones[:], 1.0), inc="gp")
            p.op("gpsimd", lambda g: g.memset(cst[:, 0:1], 0.5), inc="gp")
            p.op("gpsimd", lambda g: g.memset(cst[:, 1:2], 0.0), inc="gp")
            p.op("gpsimd", lambda g: g.memset(cst[:, 2:3], 1.0), inc="gp")
            p.op("gpsimd", lambda g: g.memset(cst[:, 3:4], ZSH), inc="gp")
            p.op("vector", lambda v: v.tensor_copy(identb[:], identf[:]),
                 waits=[("ld", 96)], inc="vst")
            for t in range(64):
                src = (wqkv[t * 128:(t + 1) * 128, :] if t < 48 else
                       wproj[(t - 48) * 128:(t - 47) * 128, :])
                dsem = "da" if t % 2 == 0 else "db"
                waits = [("vst", p.cur("vst") - 1)] if t >= 2 else []
                p.dma("sync", (lambda s, _s=src, _b=wbuf[t % 2]:
                               s.dma_start(out=_b[:], in_=_s)), waits=waits, inc=dsem)
                p.op("vector", (lambda v, _b=wbuf[t % 2], _t=t:
                                v.tensor_reduce(wpart[:, _t:_t + 1], _b[:], AX.X,
                                                ALU.add, apply_absolute_value=True)),
                     waits=[(dsem, p.cur(dsem))], inc="vst")
            p.op("vector", lambda v: v.tensor_reduce(
                sws[:, 0:1], wpart[:, 0:48], AX.X, ALU.add), inc="vst")
            p.op("vector", lambda v: v.tensor_reduce(
                sws[:, 1:2], wpart[:, 48:64], AX.X, ALU.add), inc="vst")
            p.op("tensor", lambda t: t.matmul(ps_sw[0:1, 0:1], sws[:, 0:1],
                                              ones[:, 0:1], start=True, stop=True),
                 waits=[("vst", p.cur("vst")), ("gp", 1)], inc="mm")
            p.op("tensor", lambda t: t.matmul(ps_sw[0:1, 1:2], sws[:, 1:2],
                                              ones[:, 0:1], start=True, stop=True),
                 inc="mm")
            p.op("vector", lambda v: v.tensor_copy(sws[0:1, 2:4], ps_sw[0:1, 0:2]),
                 waits=[("mm", p.cur("mm"))], inc="vst")
            p.op("vector", lambda v: v.tensor_scalar(
                sws[0:1, 4:5], sws[0:1, 2:3], 1.0 / (FQKV * C), 1e-5,
                op0=ALU.mult, op1=ALU.max), inc="vst")
            p.op("vector", lambda v: v.tensor_scalar(
                sws[0:1, 5:6], sws[0:1, 3:4], 1.0 / (C * C), 1e-5,
                op0=ALU.mult, op1=ALU.max), inc="vst")
            p.op("vector", lambda v: v.reciprocal(sws[0:1, 6:8], sws[0:1, 4:6]),
                 inc="vst")
            vv = p.cur("vst")
            p.op("tensor", lambda t: t.matmul(ps_sw[:, 4:8], ones[0:1, 0:128],
                                              sws[0:1, 4:8], start=True, stop=True),
                 waits=[("vst", vv)], inc="mm")
            p.op("vector", lambda v: v.tensor_copy(bcw[:, 0:2], ps_sw[:, 6:8]),
                 waits=[("mm", p.cur("mm"))], inc="vst")
            p.op("vector", lambda v: v.tensor_copy(bcw[:, 2:4], ps_sw[:, 4:6]),
                 inc="vst")
            emit(nc, p, sems)

            with ExitStack() as _ph:
                _en = _ph.enter_context
                ybuf = _en(nc.sbuf_tensor("ybuf", [128, C], F32))
                mbuf = _en(nc.sbuf_tensor("mbuf", [128, C], F32))
                tbuf = _en(nc.sbuf_tensor("tbuf", [128, C], BF16))
                te0 = _en(nc.sbuf_tensor("te0", [128, C], BF16))
                te1 = _en(nc.sbuf_tensor("te1", [128, C], BF16))
                pt0 = _en(nc.psum_tensor("pt0", [128, C], BF16))
                pt1 = _en(nc.psum_tensor("pt1", [128, C], BF16))
                tev_ = [te0, te1]
                ptp_ = [pt0, pt1]
                p = Plan(semc)
                spq = {0: None, 1: None}
                for t in range(64):
                    qkv = t < 48
                    src = (wqkv[t * 128:(t + 1) * 128, :] if qkv else
                           wproj[(t - 48) * 128:(t - 47) * 128, :])
                    wb = wbuf[t % 2]
                    dsem = "da" if t % 2 == 0 else "db"
                    waits = [("y1", p.cur("y1") - 1)] if t >= 2 else []
                    p.dma("sync", (lambda s, _s=src, _b=wb: s.dma_start(out=_b[:], in_=_s)),
                          waits=waits, inc=dsem)
                    sc = bcw[:, 0:1] if qkv else bcw[:, 1:2]
                    p.op("scalar", (lambda s, _b=wb, _sc=sc: s.activation(
                        ybuf[:], _b[:], AF.Identity, bias=0.0, scale=_sc)),
                        waits=[(dsem, p.cur(dsem)), ("xq", p.cur("xq"))], inc="y1")
                    p.op("vector", (lambda v: v.tensor_scalar(
                        mbuf[:], ybuf[:], -0.5, None, op0=ALU.is_le)),
                        waits=[("y1", p.cur("y1"))], inc="vst")
                    p.op("vector", (lambda v: v.scalar_tensor_tensor(
                        tbuf[:], ybuf[:], 0.5, mbuf[:], ALU.is_ge, ALU.subtract)),
                        inc="xq")
                    ptp = ptp_[t % 2]
                    for k in range(16):
                        wt = []
                        if k == 0:
                            wt.append(("xq", p.cur("xq")))
                            if t >= 2:
                                wt.append(("tev", p.cur("tev") - 1))
                        p.op("tensor", (lambda te, _k=k, _p=ptp: te.matmul(
                            _p[:, _k * 128:(_k + 1) * 128],
                            tbuf[:, _k * 128:(_k + 1) * 128], identb[:],
                            is_transpose=True, start=(_k % 8 == 0), stop=(_k % 8 == 7))),
                            waits=wt, inc=("tp" if k == 15 else None))
                    evw = [("tp", p.cur("tp"))]
                    if spq[t % 2] is not None:
                        evw.append(spq[t % 2])
                    p.op("vector", (lambda v, _p=ptp, _e=tev_[t % 2]:
                                    v.tensor_copy(_e[:], _p[:])),
                         waits=evw, inc="tev")
                    if qkv:
                        dst = WQKVT.rearrange("(k p) f -> p k f", p=128)[:, :, t * 128:(t + 1) * 128]
                    else:
                        dst = WPROJT.rearrange("(k p) f -> p k f", p=128)[:, :, (t - 48) * 128:(t - 47) * 128]
                    ssem = "dc" if t % 2 == 0 else "dd"
                    p.dma("gpsimd", (lambda g, _d=dst, _e=tev_[t % 2]: g.dma_start(
                        out=_d, in_=_e[:].rearrange("p (k j) -> p k j", j=128))),
                        waits=[("tev", p.cur("tev"))], inc=ssem)
                    spq[t % 2] = (ssem, p.cur(ssem))
                p.op("gpsimd", lambda g: g.engine_nop(),
                     waits=[("dc", p.cur("dc")), ("dd", p.cur("dd"))], inc="gp")
                emit(nc, p, sems)

        _xq_scope = ExitStack()
        xqT = _xq_scope.enter_context(nc.sbuf_tensor("xqT", [128, 16 * NTOK], BF16))
        xqT3 = xqT[:].rearrange("p (k n) -> p k n", n=NTOK)
        with ExitStack() as _ph:
            _en = _ph.enter_context
            xb0 = _en(nc.sbuf_tensor("xb0", [128, C], F32))
            xb1 = _en(nc.sbuf_tensor("xb1", [128, C], F32))
            sqb = _en(nc.sbuf_tensor("sqb", [128, C], F32))
            yb = _en(nc.sbuf_tensor("yb", [128, C], F32))
            mb = _en(nc.sbuf_tensor("mb", [128, C], F32))
            ub = _en(nc.sbuf_tensor("ub", [128, C], F32))
            qbv = _en(nc.sbuf_tensor("qbv", [128, C], BF16))
            px0 = _en(nc.psum_tensor("px0", [128, C], BF16))
            px1 = _en(nc.psum_tensor("px1", [128, C], BF16))
            xb = [xb0, xb1]
            pxt = [px0, px1]
            p = Plan(semc)
            for t in range(NT):
                b = xb[t % 2]
                dsem = "da" if t % 2 == 0 else "db"
                waits = [("y1", p.cur("y1") - 1)] if t >= 2 else []
                p.dma("sync", (lambda s, _b=b, _t=t: s.dma_start(
                    out=_b[:], in_=xp_d[_t * 128:(_t + 1) * 128, :])), waits=waits, inc=dsem)
                ldv = p.cur(dsem)
                o = (t % 2) * 32

                def tiny(fn, waits=()):
                    p.op("vector", fn, waits=waits, inc="vst")
                p.op("vector", (lambda v, _b=b, _o=o: v.tensor_reduce(
                    st[:, _o:_o + 1], _b[:], AX.X, ALU.add, negate=True)),
                    waits=[(dsem, ldv)], inc="vst")
                p.op("vector", (lambda v, _b=b, _o=o: v.tensor_reduce(
                    st[:, _o + 1:_o + 2], _b[:], AX.X, ALU.max)), inc="vst")
                p.op("vector", (lambda v, _b=b, _o=o: v.tensor_reduce(
                    st[:, _o + 2:_o + 3], _b[:], AX.X, ALU.min)), inc="vst")
                p.op("scalar", (lambda s, _b=b, _o=o: s.activation(
                    sqb[:], _b[:], AF.Square, accum_out=st[:, _o + 3:_o + 4])),
                    waits=[(dsem, ldv), ("xq", p.cur("xq"))], inc="sq")
                tiny(lambda v, _o=o: v.tensor_scalar(
                    st[:, _o + 4:_o + 5], st[:, _o:_o + 1], 1.0 / C, None, op0=ALU.mult))
                p.op("vector", (lambda v, _o=o: v.tensor_scalar(
                    st[:, _o + 5:_o + 6], st[:, _o + 3:_o + 4], 1.0 / C, EPS,
                    op0=ALU.mult, op1=ALU.add)),
                    waits=[("sq", p.cur("sq"))], inc="vst")
                tiny(lambda v, _o=o: v.scalar_tensor_tensor(
                    st[:, _o + 6:_o + 7], st[:, _o + 4:_o + 5], -1.0,
                    st[:, _o + 4:_o + 5], ALU.mult, ALU.mult))
                tiny(lambda v, _o=o: v.tensor_tensor(
                    st[:, _o + 7:_o + 8], st[:, _o + 5:_o + 6],
                    st[:, _o + 6:_o + 7], ALU.add))
                p.op("scalar", (lambda s, _o=o: s.activation(
                    st[:, _o + 8:_o + 9], st[:, _o + 7:_o + 8], AF.Sqrt)),
                    waits=[("vst", p.cur("vst"))], inc="sq")
                p.op("vector", (lambda v, _o=o: v.reciprocal(
                    st[:, _o + 9:_o + 10], st[:, _o + 8:_o + 9])),
                    waits=[("sq", p.cur("sq"))], inc="vst")
                tiny(lambda v, _o=o: v.tensor_tensor(
                    st[:, _o + 10:_o + 11], st[:, _o + 1:_o + 2],
                    st[:, _o + 4:_o + 5], ALU.add))
                tiny(lambda v, _o=o: v.scalar_tensor_tensor(
                    st[:, _o + 11:_o + 12], st[:, _o + 2:_o + 3], -1.0,
                    st[:, _o + 4:_o + 5], ALU.mult, ALU.subtract))
                tiny(lambda v, _o=o: v.tensor_tensor(
                    st[:, _o + 12:_o + 13], st[:, _o + 10:_o + 11],
                    st[:, _o + 11:_o + 12], ALU.max))
                tiny(lambda v, _o=o: v.tensor_tensor(
                    st[:, _o + 13:_o + 14], st[:, _o + 12:_o + 13],
                    st[:, _o + 9:_o + 10], ALU.mult))
                tiny(lambda v, _o=o: v.tensor_scalar(
                    st[:, _o + 14:_o + 15], st[:, _o + 13:_o + 14], 1e-5, 1.0 / 127.0,
                    op0=ALU.max, op1=ALU.mult))
                tiny(lambda v, _o=o, _t=t: v.tensor_copy(
                    invs[:, _t:_t + 1], st[:, _o + 14:_o + 15]))
                tiny(lambda v, _o=o: v.reciprocal(
                    st[:, _o + 15:_o + 16], st[:, _o + 14:_o + 15]))
                tiny(lambda v, _o=o: v.tensor_tensor(
                    st[:, _o + 16:_o + 17], st[:, _o + 9:_o + 10],
                    st[:, _o + 15:_o + 16], ALU.mult))
                tiny(lambda v, _o=o: v.tensor_scalar(
                    st[:, _o + 17:_o + 18], st[:, _o + 4:_o + 5],
                    st[:, _o + 16:_o + 17], None, op0=ALU.mult))
                p.op("scalar", (lambda s, _b=b, _o=o: s.activation(
                    yb[:], _b[:], AF.Identity, bias=st[:, _o + 17:_o + 18],
                    scale=st[:, _o + 16:_o + 17])),
                    waits=[("vst", p.cur("vst"))], inc="y1")
                p.op("vector", (lambda v: v.tensor_scalar(
                    ub[:], yb[:], M32, -M32, op0=ALU.add, op1=ALU.add)),
                    waits=[("y1", p.cur("y1"))], inc="vst")
                p.op("vector", (lambda v: v.tensor_scalar(
                    mb[:], yb[:], ZSH, M64, op0=ALU.add, op1=ALU.add)), inc="vst")
                p.op("vector", (lambda v: v.tensor_scalar(
                    mb[:], mb[:], -M64, -ZSH, op0=ALU.add, op1=ALU.add)), inc="vst")
                p.op("vector", (lambda v: v.scalar_tensor_tensor(
                    yb[:], yb[:], -1.0, mb[:], ALU.mult, ALU.add)), inc="vst")
                p.op("vector", (lambda v: v.tensor_scalar(
                    mb[:], yb[:], -0.5, 0.5, op0=ALU.max, op1=ALU.min)), inc="vst")
                p.op("vector", (lambda v: v.tensor_scalar(
                    yb[:], yb[:], 1.0, -0.5, op0=ALU.add, op1=ALU.max)), inc="vst")
                p.op("vector", (lambda v: v.scalar_tensor_tensor(
                    yb[:], yb[:], 0.0, mb[:], ALU.min, ALU.subtract)), inc="vst")
                p.op("vector", (lambda v: v.scalar_tensor_tensor(
                    qbv[:], ub[:], 0.0, yb[:], ALU.add, ALU.subtract)), inc="xq")
                ptp = pxt[t % 2]
                for k in range(16):
                    wt = []
                    if k == 0:
                        wt.append(("xq", p.cur("xq")))
                        if t >= 2:
                            wt.append(("xev", p.cur("xev") - 1))
                    p.op("tensor", (lambda te, _k=k, _p=ptp: te.matmul(
                        _p[:, _k * 128:(_k + 1) * 128], qbv[:, _k * 128:(_k + 1) * 128],
                        identb[:], is_transpose=True,
                        start=(_k % 8 == 0), stop=(_k % 8 == 7))),
                        waits=wt, inc=("xtp" if k == 15 else None))
                p.op("vector", (lambda v, _p=ptp, _t=t: v.tensor_copy(
                    xqT3[:, :, _t * 128:(_t + 1) * 128],
                    _p[:].rearrange("p (k j) -> p k j", j=128))),
                    waits=[("xtp", p.cur("xtp"))], inc="xev")
            emit(nc, p, sems)

        with ExitStack() as _ph:
            _en = _ph.enter_context
            wc0 = _en(nc.sbuf_tensor("wc0", [128, 16 * 512], BF16))
            wc1 = _en(nc.sbuf_tensor("wc1", [128, 16 * 512], BF16))
            eb0 = _en(nc.sbuf_tensor("eb0", [128, 512], F32))
            eb1 = _en(nc.sbuf_tensor("eb1", [128, 512], F32))
            tb0 = _en(nc.sbuf_tensor("tb0", [128, 512], F32))
            tb1 = _en(nc.sbuf_tensor("tb1", [128, 512], F32))
            pq0 = _en(nc.psum_tensor("pq0", [128, 512], F32))
            pq1 = _en(nc.psum_tensor("pq1", [128, 512], F32))
            pq2 = _en(nc.psum_tensor("pq2", [128, 512], F32))
            pq3 = _en(nc.psum_tensor("pq3", [128, 512], F32))
            pr0 = _en(nc.psum_tensor("pr0", [128, 512], F32))
            pr1 = _en(nc.psum_tensor("pr1", [128, 512], F32))
            wck = [wc0, wc1]
            ebv = [eb0, eb1]
            tbv = [tb0, tb1]
            pq = [pq0, pq1, pq2, pq3]
            ptr = [pr0, pr1]
            p = Plan(semc)
            p.op("vector", lambda v: v.tensor_scalar(
                qsc[:], invs[:], bcw[:, 2:3], SCALE, op0=ALU.mult, op1=ALU.mult),
                waits=[("xev", 32)], inc="vst")
            p.op("vector", lambda v: v.tensor_scalar(
                ksc[:], invs[:], bcw[:, 2:3], None, op0=ALU.mult), inc="vst")
            vsc = p.cur("vst")
            ntile = 0
            nch = 0
            mm_after_chunk = {}
            spq = {0: None, 1: None}
            for half in range(2):
                for j in (range(12) if half == 0 else range(4, 12)):
                    wcb = wck[nch % 2]
                    wsem = "de" if nch % 2 == 0 else "df"
                    waits = []
                    if nch >= 2:
                        waits.append(("mm", mm_after_chunk[nch - 2]))
                    src = WQKVT.rearrange("(k p) f -> p k f", p=128)[:, :, j * 512:(j + 1) * 512]
                    p.dma("sync", (lambda s, _w=wcb, _s=src: s.dma_start(
                        out=_w[:].rearrange("p (k f) -> p k f", f=512), in_=_s)),
                        waits=waits, inc=wsem)
                    wldv = p.cur(wsem)
                    for i in range(16):
                        t = half * 16 + i
                        if j < 4 and t >= 16:
                            continue
                        bank = pq[ntile % 4]
                        eb = ebv[ntile % 2]
                        tbx = tbv[ntile % 2]
                        for k in range(16):
                            wt = []
                            if k == 0:
                                wt.append((wsem, wldv))
                                if ntile >= 4:
                                    wt.append(("ev", p.cur("ev") - 3))
                            p.op("tensor", (lambda te, _k=k, _b=bank, _t=t, _w=wcb:
                                            te.matmul(_b[:],
                                                      xqT3[:, _k, _t * 128:(_t + 1) * 128],
                                                      _w[:, _k * 512:(_k + 1) * 512],
                                                      start=(_k == 0), stop=(_k == 15))),
                                 waits=wt, inc=("mm" if k == 15 else None))
                        sc = qsc[:, t:t + 1] if j < 4 else ksc[:, t:t + 1]
                        wt = [("mm", p.cur("mm")), ("vst", vsc)]
                        if spq[ntile % 2] is not None:
                            wt.append(spq[ntile % 2])
                        p.op("scalar", (lambda s, _b=bank, _e=eb, _sc=sc:
                                        s.activation(_e[:], _b[:], AF.Identity,
                                                     scale=_sc)),
                             waits=wt, inc="ev")
                        evv = p.cur("ev")
                        ssem = "dc" if ntile % 2 == 0 else "dd"
                        if j >= 8:
                            dst = VS[t * 128:(t + 1) * 128, (j - 8) * 512:(j - 7) * 512]
                            p.dma("gpsimd", (lambda g, _d=dst, _e=eb: g.dma_start(
                                out=_d, in_=_e[:])), waits=[("ev", evv)], inc=ssem)
                            spq[ntile % 2] = (ssem, p.cur(ssem))
                        else:
                            ptb = ptr[ntile % 2]
                            for k in range(4):
                                wt = []
                                if k == 0:
                                    wt.append(("ev", evv))
                                    if ntile >= 2:
                                        wt.append(("tev", p.cur("tev") - 1))
                                p.op("tensor", (lambda te, _k=k, _p=ptb, _e=eb:
                                                te.matmul(_p[:, _k * 128:(_k + 1) * 128],
                                                          _e[:, _k * 128:(_k + 1) * 128],
                                                          identf[:], is_transpose=True,
                                                          start=(_k == 0), stop=(_k == 3))),
                                    waits=wt, inc=("tp" if k == 3 else None))
                            p.op("vector", (lambda v, _p=ptb, _t=tbx: v.tensor_copy(
                                _t[:], _p[:])), waits=[("tp", p.cur("tp"))], inc="tev")
                            if j < 4:
                                dd = QT.rearrange("(c k p) r -> c p k r", k=4, p=128)[j][:, :, t * 128:(t + 1) * 128]
                            else:
                                dd = KT.rearrange("(c k p) r -> c p k r", k=4, p=128)[j - 4][:, :, t * 128:(t + 1) * 128]
                            p.dma("gpsimd", (lambda g, _d=dd, _t=tbx: g.dma_start(
                                out=_d, in_=_t[:].rearrange("p (k j) -> p k j", j=128))),
                                waits=[("tev", p.cur("tev"))], inc=ssem)
                            spq[ntile % 2] = (ssem, p.cur(ssem))
                        ntile += 1
                    mm_after_chunk[nch] = p.cur("mm")
                    nch += 1
            p.op("gpsimd", lambda g: g.engine_nop(),
                 waits=[("dc", p.cur("dc")), ("dd", p.cur("dd"))], inc="gp")
            emit(nc, p, sems)

        _xq_scope.close()

        _ffnw_scope = ExitStack()
        if True:
            _en = _ffnw_scope.enter_context
            ib0 = _en(nc.sbuf_tensor("ib0", [128, C], I32))
            ib1 = _en(nc.sbuf_tensor("ib1", [128, C], I32))
            ibd = _en(nc.sbuf_tensor("ibd", [128, HID], I32))
            cb = _en(nc.sbuf_tensor("cb", [128, HID], BF16))
            fe0 = _en(nc.sbuf_tensor("fe0", [128, C], BF16))
            fe1 = fe0
            pf0 = _en(nc.psum_tensor("pf0", [128, C], BF16))
            ib = [ib0, ib1]
            fev = [fe0, fe1]
            pft = [pf0, pf0]
            p = Plan(semc)
            items = ([("g", t) for t in range(44)] + [("u", t) for t in range(44)]
                     + [("d", t) for t in range(16)])
            nr = 0
            spq = {0: None, 1: None}
            for it, (mat, t) in enumerate(items):
                if mat == "g":
                    src, nblk, dstT = wg_r[t * 128:(t + 1) * 128, :], 16, WGT
                elif mat == "u":
                    src, nblk, dstT = wu_r[t * 128:(t + 1) * 128, :], 16, WUT
                else:
                    src, nblk, dstT = wd_r[t * 128:(t + 1) * 128, :], 44, WDT
                if mat == "d":
                    b, dsem = ibd, "dh"
                    waits = [("y1", p.cur("y1") - 1)] if it >= 1 else []
                else:
                    b = ib[it % 2]
                    dsem = "dh" if it % 2 == 0 else "di"
                    waits = [("y1", p.cur("y1") - 1)] if it >= 2 else []
                ncols = nblk * 128
                p.dma("sync", (lambda s, _b=b, _s=src, _n=ncols: s.dma_start(
                    out=_b[:, 0:_n], in_=_s)), waits=waits, inc=dsem)
                p.op("scalar", (lambda s, _b=b, _n=ncols: s.activation(
                    cb[:, 0:_n], _b[:, 0:_n], AF.Copy, bias=-128.0)),
                    waits=[(dsem, p.cur(dsem)), ("tev", p.cur("tev"))], inc="y1")
                for (k0, cnt) in [(r * 16, min(16, nblk - r * 16))
                                  for r in range((nblk + 15) // 16)]:
                    ptb = pft[nr % 2]
                    for kk in range(cnt):
                        wt = []
                        if kk == 0:
                            wt.append(("y1", p.cur("y1")))
                            if nr >= 1:
                                wt.append(("tev", p.cur("tev")))
                        p.op("tensor", (lambda te, _k=k0 + kk, _kk=kk, _p=ptb, _c=cnt:
                                        te.matmul(_p[:, _kk * 128:(_kk + 1) * 128],
                                                  cb[:, _k * 128:(_k + 1) * 128], identb[:],
                                                  is_transpose=True,
                                                  start=(_kk % 8 == 0),
                                                  stop=(_kk % 8 == 7 or _kk == _c - 1))),
                            waits=wt, inc=("tp" if kk == cnt - 1 else None))
                    fv = fev[nr % 2]
                    evw = [("tp", p.cur("tp"))]
                    for pp in (0, 1):
                        if spq[pp] is not None:
                            evw.append(spq[pp])
                    p.op("vector", (lambda v, _p=ptb, _f=fv, _c=cnt: v.tensor_copy(
                        _f[:, 0:_c * 128], _p[:, 0:_c * 128])),
                        waits=evw, inc="tev")
                    ssem = "dc" if nr % 2 == 0 else "dd"
                    dd = dstT.rearrange("(k p) f -> p k f", p=128)[:, k0:k0 + cnt, t * 128:(t + 1) * 128]
                    p.dma("gpsimd", (lambda g, _d=dd, _f=fv, _c=cnt: g.dma_start(
                        out=_d, in_=_f[:, 0:_c * 128].rearrange("p (k j) -> p k j", j=128))),
                        waits=[("tev", p.cur("tev"))], inc=ssem)
                    spq[nr % 2] = (ssem, p.cur(ssem))
                    nr += 1
            p_ffnw = p
            _PH[0] += 1

        with ExitStack() as _ph:
            _en = _ph.enter_context
            sA = _en(nc.sbuf_tensor("sA", [128, 4096], F32))
            sB = _en(nc.sbuf_tensor("sB", [128, 4096], F32))
            sC = _en(nc.sbuf_tensor("sC", [128, 4096], F32))
            sD = _en(nc.sbuf_tensor("sD", [128, 4096], F32))
            eA = _en(nc.sbuf_tensor("eA", [128, 4096], F32))
            eB = _en(nc.sbuf_tensor("eB", [128, 4096], F32))
            eC = _en(nc.sbuf_tensor("eC", [128, 4096], F32))
            eD = _en(nc.sbuf_tensor("eD", [128, 4096], F32))
            qg = _en(nc.sbuf_tensor("qg", [128, 512], F32))
            kb0 = _en(nc.sbuf_tensor("kb0", [128, 512], F32))
            kb1 = _en(nc.sbuf_tensor("kb1", [128, 512], F32))
            vb0 = _en(nc.sbuf_tensor("vb0", [128, 128], F32))
            vb1 = _en(nc.sbuf_tensor("vb1", [128, 128], F32))
            pb0 = _en(nc.sbuf_tensor("pb0", [128, 512], F32))
            pb1 = _en(nc.sbuf_tensor("pb1", [128, 512], F32))
            otm = _en(nc.sbuf_tensor("otm", [128, 512], F32))
            obv = _en(nc.sbuf_tensor("obv", [128, 512], F32))
            pS0 = _en(nc.psum_tensor("pS0", [128, 512], F32))
            pS1 = _en(nc.psum_tensor("pS1", [128, 512], F32))
            pP0 = _en(nc.psum_tensor("pP0", [128, 512], F32))
            pP1 = _en(nc.psum_tensor("pP1", [128, 512], F32))
            pO = _en(nc.psum_tensor("pO", [128, 512], F32))
            pOt = _en(nc.psum_tensor("pOt", [128, 512], F32))
            strips = [sA, sB, sC, sD]
            estrips = [eA, eB, eC, eD]
            kbv = [kb0, kb1]
            vbv = [vb0, vb1]
            ptv = [pb0, pb1]
            pS = [pS0, pS1]
            pP = [pP0, pP1]
            p = Plan(semc)
            nkc = 0
            nvc = 0
            nsb = 0
            npt = 0
            se_bank = [[], []]
            for g in range(4):
                NCH = g + 1
                for h in range(H):
                    ghi = g * H + h
                    wq_ = [("mm", p.cur("mm"))] if ghi else []
                    p.dma("sync", (lambda s, _h=h, _g=g: s.dma_start(
                        out=qg[:], in_=QT[_h * 128:(_h + 1) * 128, _g * 512:(_g + 1) * 512])),
                        waits=wq_, inc="qld")
                    qldv = p.cur("qld")
                    for part in range(2):
                        for ch in range(NCH):
                            kbx = kbv[nkc % 2]
                            ksem = "da" if nkc % 2 == 0 else "db"
                            wt = []
                            if nkc >= 2:
                                wt.append(("mm", p.sem.get("_kmm%d" % (nkc - 2), 0)))
                            ksrc = KT[h * 128:(h + 1) * 128,
                                      part * 2048 + ch * 512: part * 2048 + (ch + 1) * 512]
                            p.dma("sync", (lambda s, _k=kbx, _s=ksrc: s.dma_start(
                                out=_k[:], in_=_s)), waits=[w for w in wt if w[1] > 0], inc=ksem)
                            kldv = p.cur(ksem)
                            for i in range(4):
                                ti = 4 * g + i
                                L = (ti + 1) * 128
                                w = min(512, L - ch * 512)
                                if w <= 0:
                                    continue
                                bank = pS[nsb % 2]
                                wt = [(ksem, kldv), ("qld", qldv)]
                                if nsb >= 2:
                                    wt.extend(se_bank[nsb % 2])
                                p.op("tensor", (lambda te, _b=bank, _k=kbx, _i=i, _w=w:
                                                te.matmul(_b[:, 0:_w],
                                                          qg[:, _i * 128:(_i + 1) * 128],
                                                          _k[:, 0:_w],
                                                          start=True, stop=True)),
                                     waits=wt, inc="mm")
                                mmv = p.cur("mm")
                                stripbase = 0 if part == 0 else L
                                dst0 = stripbase + ch * 512
                                diag = (w < 512) or (ch == NCH - 1 and i == 3)
                                strip = strips[i]
                                if diag:
                                    wfull = w - 128
                                    if wfull > 0:
                                        p.op("vector", (lambda v, _st=strip, _d=dst0, _w=wfull, _b=bank:
                                                        v.tensor_copy(_st[:, _d:_d + _w], _b[:, 0:_w])),
                                             waits=[("mm", mmv)], inc="sv")
                                    mk = tri if part == 0 else oma
                                    p.op("vector", (lambda v, _st=strip, _d=dst0 + wfull,
                                                    _b=bank, _w=wfull, _m=mk:
                                                    v.tensor_tensor(_st[:, _d:_d + 128],
                                                                    _b[:, _w:_w + 128],
                                                                    _m[:], ALU.add)),
                                         waits=[("mm", mmv)], inc="sv")
                                else:
                                    p.op("scalar", (lambda s, _st=strip, _d=dst0, _w=w, _b=bank:
                                                    s.copy(_st[:, _d:_d + _w], _b[:, 0:_w])),
                                         waits=[("mm", mmv)], inc="se")
                                se_bank[nsb % 2] = [("se", p.cur("se")), ("sv", p.cur("sv"))]
                                nsb += 1
                            p.sem["_kmm%d" % nkc] = p.cur("mm")
                            nkc += 1
                    if ATT_MODE >= 2:
                        sev, svv = p.cur("se"), p.cur("sv")
                        for i in range(4):
                            ti = 4 * g + i
                            L2 = 2 * (ti + 1) * 128
                            strip = strips[i]
                            p.op("vector", (lambda v, _st=strip, _L=L2, _i=i: v.tensor_reduce(
                                st[:, 48 + _i:49 + _i], _st[:, 0:_L], AX.X, ALU.max,
                                negate=True)),
                                waits=[("se", sev), ("sv", svv)], inc="mx")
                            p.op("scalar", (lambda s, _st=strip, _e=estrips[i], _L=L2, _i=i:
                                            s.activation(
                                _e[:, 0:_L], _st[:, 0:_L], AF.Exp,
                                bias=st[:, 48 + _i:49 + _i],
                                accum_out=st[:, 52 + _i:53 + _i])),
                                waits=[("mx", p.cur("mx"))], inc="xp")
                            col = (4 * g + i) * H + h
                            p.op("vector", (lambda v, _i=i, _c=col:
                                            v.reciprocal(invl[:, _c:_c + 1],
                                                         st[:, 52 + _i:53 + _i])),
                                 waits=[("xp", p.cur("xp"))], inc="vst")
                        xpv = p.cur("xp")
                    if ATT_MODE >= 3:
                        chunks = [(part, kc) for part in range(2)
                                  for kc in range(4 * g + 4)]
                        ptv_done = {}
                        pe_done = {}
                        pv_done = {}

                        def emit_PT(j):
                            part, kc = chunks[j]
                            vs = max(0, kc - 4 * g)
                            ptb = pP[j % 2]
                            for i in range(vs, 4):
                                ti = 4 * g + i
                                L = (ti + 1) * 128
                                srccol = (0 if part == 0 else L) + kc * 128
                                wt = []
                                if i == vs:
                                    wt.append(("xp", xpv))
                                    if j >= 2:
                                        wt.append(("pe", pe_done[j - 2]))
                                p.op("tensor", (lambda te, _i=i, _p=ptb,
                                                _st=estrips[i], _sc=srccol, _vs=vs:
                                                te.matmul(_p[:, _i * 128:(_i + 1) * 128],
                                                          _st[:, _sc:_sc + 128], identf[:],
                                                          is_transpose=True,
                                                          start=(_i == _vs), stop=(_i == 3))),
                                     waits=wt, inc=("pt" if i == 3 else None))
                            ptv_done[j] = p.cur("pt")

                        def emit_B(j, first_pv, ghi):
                            part, kc = chunks[j]
                            vs = max(0, kc - 4 * g)
                            ptb = pP[j % 2]
                            pvw = pv_done.get(j - 2, 0)
                            if vs > 0:
                                p.op("vector", (lambda v, _b=ptv[j % 2], _vs=vs:
                                                v.memset(_b[:, 0:_vs * 128], 0.0)),
                                     waits=[("pv", pvw)] if pvw else [])
                            p.op("vector", (lambda v, _p=ptb, _b=ptv[j % 2], _vs=vs:
                                            v.tensor_copy(_b[:, _vs * 128:512],
                                                          _p[:, _vs * 128:512])),
                                 waits=([("pt", ptv_done[j])]
                                        + ([("pv", pvw)] if pvw else [])),
                                 inc="pe")
                            pe_done[j] = p.cur("pe")
                            vbx = vbv[j % 2]
                            vsem = "de" if j % 2 == 0 else "df"
                            vsrc = VS[part * 2048 + kc * 128: part * 2048 + (kc + 1) * 128,
                                      h * 128:(h + 1) * 128]
                            wt = [("pv", pv_done.get(j - 2, 0))]
                            p.dma("sync", (lambda s_, _v=vbx, _s=vsrc: s_.dma_start(
                                out=_v[:], in_=_s)), waits=[w for w in wt if w[1] > 0],
                                inc=vsem)
                            wt = [("pe", pe_done[j]), (vsem, p.cur(vsem))]
                            if first_pv and ghi:
                                wt.append(("oe", p.cur("oe")))
                            p.op("tensor", (lambda te, _v=vbx, _b=ptv[j % 2],
                                            _st=first_pv,
                                            _sp=(part == 1 and kc == 4 * g + 3):
                                            te.matmul(pO[:], _v[:], _b[:],
                                                      start=_st, stop=_sp)),
                                 waits=wt, inc="pv")
                            pv_done[j] = p.cur("pv")
                            

                        emit_PT(0)
                        for j in range(len(chunks)):
                            if j + 1 < len(chunks):
                                emit_PT(j + 1)
                            emit_B(j, j == 0, ghi)
                    if ATT_MODE >= 4:
                        p.op("scalar", lambda s: s.copy(otm[:], pO[:]),
                             waits=[("pv", p.cur("pv")), ("ov", p.cur("ov"))], inc="oe")
                        for k in range(4):
                            wt = [("oe", p.cur("oe"))] if k == 0 else []
                            if ghi >= 1 and k == 0:
                                wt.append(("ov", p.cur("ov")))
                            p.op("tensor", (lambda te, _k=k: te.matmul(
                                pOt[:, _k * 128:(_k + 1) * 128], otm[:, _k * 128:(_k + 1) * 128],
                                identf[:], is_transpose=True, start=(_k == 0), stop=(_k == 3))),
                                waits=wt, inc=("ot" if k == 3 else None))
                        p.op("vector", (lambda v: v.tensor_copy(obv[:], pOt[:])),
                             waits=[("ot", p.cur("ot")), ("dg", p.cur("dg"))],
                             inc="ov")
                        dd = OS.rearrange("(t p) f -> p t f", p=128)[:, 4 * g:4 * g + 4,
                                                                    h * 128:(h + 1) * 128]
                        p.dma("gpsimd", (lambda g_, _d=dd: g_.dma_start(
                            out=_d, in_=obv[:].rearrange("p (i f) -> p i f", f=128))),
                            waits=[("ov", p.cur("ov"))], inc="dg")
            p.op("gpsimd", lambda g_: g_.engine_nop(),
                 waits=[("dg", p.cur("dg")), ("dc", p.cur("dc")),
                        ("dd", p.cur("dd"))], inc="gp")
            p.ops = interleave_ops(p.ops, p_ffnw.ops)
            emit(nc, p, sems)
        _ffnw_scope.close()

        with ExitStack() as _ph:
            _en = _ph.enter_context
            wpt = _en(nc.sbuf_tensor("wpt", [128, 16 * C], BF16))
            ob0 = _en(nc.sbuf_tensor("ob0", [128, C], F32))
            ob1 = _en(nc.sbuf_tensor("ob1", [128, C], F32))
            nb = _en(nc.sbuf_tensor("nb", [128, C], F32))
            yb2 = _en(nc.sbuf_tensor("yb2", [128, C], F32))
            mb2 = _en(nc.sbuf_tensor("mb2", [128, C], F32))
            ub2 = _en(nc.sbuf_tensor("ub2", [128, C], F32))
            oqb = _en(nc.sbuf_tensor("oqb", [128, C], BF16))
            oqT = _en(nc.sbuf_tensor("oqT", [128, C], BF16))
            prb = _en(nc.sbuf_tensor("prb", [128, C], F32))
            lnb = _en(nc.sbuf_tensor("lnb", [128, C], F32))
            xo0 = _en(nc.sbuf_tensor("xo0", [128, C], F32))
            xo1 = _en(nc.sbuf_tensor("xo1", [128, C], F32))
            xmb = _en(nc.sbuf_tensor("xmb", [128, C], F32))
            x2b = _en(nc.sbuf_tensor("x2b", [128, C], F32))
            hib = _en(nc.sbuf_tensor("hib", [128, C], BF16))
            lob = _en(nc.sbuf_tensor("lob", [128, C], BF16))
            hev = _en(nc.sbuf_tensor("hev", [128, C], BF16))
            lev = _en(nc.sbuf_tensor("lev", [128, C], BF16))
            pqt = _en(nc.psum_tensor("pqt", [128, C], BF16))
            ppr0 = _en(nc.psum_tensor("ppr0", [128, 512], F32))
            ppr1 = _en(nc.psum_tensor("ppr1", [128, 512], F32))
            phl0 = _en(nc.psum_tensor("phl0", [128, C], BF16))
            phl1 = _en(nc.psum_tensor("phl1", [128, C], BF16))
            obx = [ob0, ob1]
            xox = [xo0, xo1]
            ppr = [ppr0, ppr1]
            p = Plan(semc)
            p.dma("sync", lambda s: s.dma_start(
                out=wpt[:].rearrange("p (k f) -> p k f", f=C),
                in_=WPROJT.rearrange("(k p) f -> p k f", p=128)), inc="wld")
            wptv = p.cur("wld")
            for t in range(OT):
                ob = obx[t % 2]
                xo = xox[t % 2]
                osem = "da" if t % 2 == 0 else "db"
                xsem = "de" if t % 2 == 0 else "df"
                waits = [("prx", p.cur("prx") - 1)] if t >= 2 else []
                p.dma("sync", (lambda s, _b=ob, _t=t: s.dma_start(
                    out=_b[:], in_=OS[_t * 128:(_t + 1) * 128, :])), waits=waits, inc=osem)
                osv = p.cur(osem)
                p.dma("sync", (lambda s, _b=xo, _t=t: s.dma_start(
                    out=_b[:], in_=xp_d[_t * 128:(_t + 1) * 128, :])),
                    waits=list(waits), inc=xsem)
                xsv = p.cur(xsem)
                for hh in range(H):
                    wt = [(osem, osv)] if hh == 0 else []
                    if hh == 0 and t >= 2:
                        wt.append(("xq", p.cur("xq")))
                    p.op("scalar", (lambda s, _h=hh, _t=t, _b=ob: s.activation(
                        nb[:, _h * 128:(_h + 1) * 128],
                        _b[:, _h * 128:(_h + 1) * 128],
                        AF.Identity, scale=invl[:, _t * H + _h:_t * H + _h + 1])),
                        waits=wt, inc=("y1" if hh == H - 1 else None))
                y1v = p.cur("y1")
                o = (t % 2) * 32
                p.op("vector", (lambda v, _o=o: v.tensor_reduce(
                    st[:, _o:_o + 1], nb[:], AX.X, ALU.max, apply_absolute_value=True)),
                    waits=[("y1", y1v)], inc="vst")
                p.op("vector", (lambda v, _o=o: v.tensor_scalar(
                    st[:, _o + 1:_o + 2], st[:, _o:_o + 1], 1e-5, 1.0 / 127.0,
                    op0=ALU.max, op1=ALU.mult)), inc="vst")
                p.op("vector", (lambda v, _o=o: v.reciprocal(
                    st[:, _o + 2:_o + 3], st[:, _o + 1:_o + 2])), inc="vst")
                p.op("vector", (lambda v, _o=o: v.tensor_scalar(
                    st[:, _o + 3:_o + 4], st[:, _o + 1:_o + 2], bcw[:, 3:4], None,
                    op0=ALU.mult)), inc="vst")
                p.op("scalar", (lambda s, _o=o: s.activation(
                    yb2[:], nb[:], AF.Identity, bias=0.0, scale=st[:, _o + 2:_o + 3])),
                    waits=[("vst", p.cur("vst"))], inc="y1")
                p.op("vector", (lambda v: v.tensor_scalar(
                    ub2[:], yb2[:], M32, -M32, op0=ALU.add, op1=ALU.add)),
                    waits=[("y1", p.cur("y1"))], inc="vst")
                p.op("vector", (lambda v: v.tensor_scalar(
                    mb2[:], yb2[:], ZSH, M64, op0=ALU.add, op1=ALU.add)), inc="vst")
                p.op("vector", (lambda v: v.tensor_scalar(
                    mb2[:], mb2[:], -M64, -ZSH, op0=ALU.add, op1=ALU.add)), inc="vst")
                p.op("vector", (lambda v: v.scalar_tensor_tensor(
                    yb2[:], yb2[:], -1.0, mb2[:], ALU.mult, ALU.add)), inc="vst")
                p.op("vector", (lambda v: v.tensor_scalar(
                    mb2[:], yb2[:], -0.5, 0.5, op0=ALU.max, op1=ALU.min)), inc="vst")
                p.op("vector", (lambda v: v.tensor_scalar(
                    yb2[:], yb2[:], 1.0, -0.5, op0=ALU.add, op1=ALU.max)), inc="vst")
                p.op("vector", (lambda v: v.scalar_tensor_tensor(
                    yb2[:], yb2[:], 0.0, mb2[:], ALU.min, ALU.subtract)), inc="vst")
                p.op("vector", (lambda v: v.scalar_tensor_tensor(
                    oqb[:], ub2[:], 0.0, yb2[:], ALU.add, ALU.subtract)), inc="xq")
                for k in range(16):
                    wt = []
                    if k == 0:
                        wt.append(("xq", p.cur("xq")))
                        if t >= 1:
                            wt.append(("tev", p.cur("tev")))
                    p.op("tensor", (lambda te, _k=k: te.matmul(
                        pqt[:, _k * 128:(_k + 1) * 128], oqb[:, _k * 128:(_k + 1) * 128],
                        identb[:], is_transpose=True,
                        start=(_k % 8 == 0), stop=(_k % 8 == 7))),
                        waits=wt, inc=("tp" if k == 15 else None))
                p.op("vector", (lambda v: v.tensor_copy(oqT[:], pqt[:])),
                     waits=[("tp", p.cur("tp"))], inc="tev")
                tevv = p.cur("tev")
                for j in range(4):
                    bank = ppr[j % 2]
                    for k in range(16):
                        wt = []
                        if k == 0:
                            wt.append(("tev", tevv))
                            wt.append(("wld", wptv))
                            if t * 4 + j >= 2:
                                wt.append(("ev", p.cur("ev") - 1))
                        p.op("tensor", (lambda te, _k=k, _j=j, _b=bank: te.matmul(
                            _b[:], oqT[:, _k * 128:(_k + 1) * 128],
                            wpt[:, _k * C + _j * 512:_k * C + (_j + 1) * 512],
                            start=(_k == 0), stop=(_k == 15))),
                            waits=wt, inc=("mm" if k == 15 else None))
                    p.op("scalar", (lambda s, _j=j, _b=bank, _o=o: s.activation(
                        prb[:, _j * 512:(_j + 1) * 512], _b[:], AF.Identity,
                        scale=st[:, _o + 3:_o + 4])),
                        waits=[("mm", p.cur("mm"))], inc="ev")
                evv = p.cur("ev")
                p.op("vector", (lambda v, _o=o: v.tensor_reduce(
                    st[:, _o + 4:_o + 5], prb[:], AX.X, ALU.add, negate=True)),
                    waits=[("ev", evv)], inc="vst")
                p.op("scalar", (lambda s, _o=o: s.activation(
                    mb2[:], prb[:], AF.Square, accum_out=st[:, _o + 5:_o + 6])),
                    waits=[("ev", evv)], inc="sq")
                p.op("vector", (lambda v, _o=o: v.tensor_scalar(
                    st[:, _o + 6:_o + 7], st[:, _o + 4:_o + 5], 1.0 / C, None,
                    op0=ALU.mult)), inc="vst")
                p.op("vector", (lambda v, _o=o: v.tensor_scalar(
                    st[:, _o + 7:_o + 8], st[:, _o + 5:_o + 6], 1.0 / C, EPS,
                    op0=ALU.mult, op1=ALU.add)),
                    waits=[("sq", p.cur("sq"))], inc="vst")
                p.op("vector", (lambda v, _o=o: v.scalar_tensor_tensor(
                    st[:, _o + 8:_o + 9], st[:, _o + 6:_o + 7], -1.0,
                    st[:, _o + 6:_o + 7], ALU.mult, ALU.mult)), inc="vst")
                p.op("vector", (lambda v, _o=o: v.tensor_tensor(
                    st[:, _o + 9:_o + 10], st[:, _o + 7:_o + 8],
                    st[:, _o + 8:_o + 9], ALU.add)), inc="vst")
                p.op("scalar", (lambda s, _o=o: s.activation(
                    st[:, _o + 10:_o + 11], st[:, _o + 9:_o + 10], AF.Sqrt)),
                    waits=[("vst", p.cur("vst"))], inc="sq")
                p.op("vector", (lambda v, _o=o: v.reciprocal(
                    st[:, _o + 11:_o + 12], st[:, _o + 10:_o + 11])),
                    waits=[("sq", p.cur("sq"))], inc="vst")
                p.op("vector", (lambda v, _o=o: v.tensor_tensor(
                    st[:, _o + 12:_o + 13], st[:, _o + 6:_o + 7],
                    st[:, _o + 11:_o + 12], ALU.mult)), inc="vst")
                p.op("scalar", (lambda s, _o=o: s.activation(
                    lnb[:], prb[:], AF.Identity, bias=st[:, _o + 12:_o + 13],
                    scale=st[:, _o + 11:_o + 12])),
                    waits=[("vst", p.cur("vst"))], inc="y1")
                xw = [("y1", p.cur("y1")), (xsem, xsv)]
                if t >= 1:
                    xw.append(("dg", p.sem.get("_xmidsp%d" % (t - 1), 0)))
                p.op("vector", (lambda v, _x=xo: v.tensor_tensor(
                    xmb[:], _x[:], lnb[:], ALU.add)),
                    waits=xw, inc="fx")
                fxv = p.cur("fx")
                p.dma("gpsimd", (lambda g, _t=t: g.dma_start(
                    out=XMID[_t * 128:(_t + 1) * 128, :], in_=xmb[:])),
                    waits=[("fx", fxv)], inc="dg")
                p.sem["_xmidsp%d" % t] = p.cur("dg")
                p.op("vector", (lambda v, _o=o: v.tensor_reduce(
                    st[:, _o + 13:_o + 14], xmb[:], AX.X, ALU.add, negate=True)),
                    waits=[("fx", fxv)], inc="vst")
                p.op("scalar", (lambda s, _o=o: s.activation(
                    mb2[:], xmb[:], AF.Square, accum_out=st[:, _o + 14:_o + 15])),
                    waits=[("fx", fxv)], inc="sq")
                p.op("vector", (lambda v, _o=o: v.tensor_scalar(
                    st[:, _o + 15:_o + 16], st[:, _o + 13:_o + 14], 1.0 / C, None,
                    op0=ALU.mult)), inc="vst")
                p.op("vector", (lambda v, _o=o: v.tensor_scalar(
                    st[:, _o + 16:_o + 17], st[:, _o + 14:_o + 15], 1.0 / C, EPS,
                    op0=ALU.mult, op1=ALU.add)),
                    waits=[("sq", p.cur("sq"))], inc="vst")
                p.op("vector", (lambda v, _o=o: v.scalar_tensor_tensor(
                    st[:, _o + 17:_o + 18], st[:, _o + 15:_o + 16], -1.0,
                    st[:, _o + 15:_o + 16], ALU.mult, ALU.mult)), inc="vst")
                p.op("vector", (lambda v, _o=o: v.tensor_tensor(
                    st[:, _o + 18:_o + 19], st[:, _o + 16:_o + 17],
                    st[:, _o + 17:_o + 18], ALU.add)), inc="vst")
                p.op("scalar", (lambda s, _o=o: s.activation(
                    st[:, _o + 19:_o + 20], st[:, _o + 18:_o + 19], AF.Sqrt)),
                    waits=[("vst", p.cur("vst"))], inc="sq")
                p.op("vector", (lambda v, _o=o: v.reciprocal(
                    st[:, _o + 20:_o + 21], st[:, _o + 19:_o + 20])),
                    waits=[("sq", p.cur("sq"))], inc="vst")
                p.op("vector", (lambda v, _o=o: v.tensor_tensor(
                    st[:, _o + 21:_o + 22], st[:, _o + 15:_o + 16],
                    st[:, _o + 20:_o + 21], ALU.mult)), inc="vst")
                p.op("scalar", (lambda s, _o=o: s.activation(
                    x2b[:], xmb[:], AF.Identity, bias=st[:, _o + 21:_o + 22],
                    scale=st[:, _o + 20:_o + 21])),
                    waits=[("vst", p.cur("vst"))], inc="y1")
                y1v2 = p.cur("y1")
                hw = [("y1", y1v2)]
                if t >= 1:
                    hw.append(("tp", p.sem.get("_hltp%d" % (t - 1), 0)))
                p.op("vector", (lambda v: v.tensor_copy(hib[:], x2b[:])),
                     waits=hw, inc="vst")
                p.op("vector", (lambda v: v.scalar_tensor_tensor(
                    lob[:], hib[:], -1.0, x2b[:], ALU.mult, ALU.add)), inc="hst")
                hstv = p.cur("hst")
                for (src_, psx, evb, dstT, ssem) in ((hib, phl0, hev, XHs, "dc"),
                                                     (lob, phl1, lev, XLs, "dd")):
                    for k in range(16):
                        wt = []
                        if k == 0:
                            wt.append(("hst", hstv))
                            if t >= 1:
                                wt.append(("xev", p.cur("xev")))
                        p.op("tensor", (lambda te, _k=k, _s=src_, _p=psx: te.matmul(
                            _p[:, _k * 128:(_k + 1) * 128], _s[:, _k * 128:(_k + 1) * 128],
                            identb[:], is_transpose=True,
                            start=(_k % 8 == 0), stop=(_k % 8 == 7))),
                            waits=wt, inc=("tp" if k == 15 else None))
                    p.op("vector", (lambda v, _p=psx, _e=evb: v.tensor_copy(_e[:], _p[:])),
                         waits=[("tp", p.cur("tp")), (ssem, p.cur(ssem))],
                         inc="xev")
                    dd = dstT.rearrange("(k p) r -> p k r", p=128)[:, :, t * 128:(t + 1) * 128]
                    p.dma("gpsimd", (lambda g, _d=dd, _e=evb: g.dma_start(
                        out=_d, in_=_e[:].rearrange("p (k j) -> p k j", j=128))),
                        waits=[("xev", p.cur("xev"))], inc=ssem)
                p.sem["_hltp%d" % t] = p.cur("tp")
                p.op("vector", lambda v: v.engine_nop(), waits=[], inc="prx")
            p.op("gpsimd", lambda g: g.engine_nop(),
                 waits=[("dg", p.cur("dg")), ("dc", p.cur("dc")), ("dd", p.cur("dd"))],
                 inc="gp")
            emit(nc, p, sems)

        with ExitStack() as _ph:
            _en = _ph.enter_context
            xh = _en(nc.sbuf_tensor("xh", [128, 16 * 512], BF16))
            xl = _en(nc.sbuf_tensor("xl", [128, 16 * 512], BF16))
            sh = _en(nc.sbuf_tensor("sh", [128, 44 * 512], BF16))
            sl = _en(nc.sbuf_tensor("sl", [128, 44 * 512], BF16))
            wgb0 = _en(nc.sbuf_tensor("wgb0", [128, 16 * 128], BF16))
            wgb1 = _en(nc.sbuf_tensor("wgb1", [128, 16 * 128], BF16))
            wub0 = _en(nc.sbuf_tensor("wub0", [128, 16 * 128], BF16))
            wub1 = _en(nc.sbuf_tensor("wub1", [128, 16 * 128], BF16))
            wdb0 = _en(nc.sbuf_tensor("wdb0", [128, 44 * 128], BF16))
            wdb1 = _en(nc.sbuf_tensor("wdb1", [128, 44 * 128], BF16))
            sgb0 = _en(nc.sbuf_tensor("sgb0", [128, 512], F32))
            sgb1 = _en(nc.sbuf_tensor("sgb1", [128, 512], F32))
            sub0 = _en(nc.sbuf_tensor("sub0", [128, 512], F32))
            sub1 = _en(nc.sbuf_tensor("sub1", [128, 512], F32))
            hid = _en(nc.sbuf_tensor("hid", [128, 512], F32))
            dob = _en(nc.sbuf_tensor("dob", [128, 512], F32))
            tdb = _en(nc.sbuf_tensor("tdb", [128, 512], F32))
            pgg0 = _en(nc.psum_tensor("pgg0", [128, 512], F32))
            pgg1 = _en(nc.psum_tensor("pgg1", [128, 512], F32))
            puu0 = _en(nc.psum_tensor("puu0", [128, 512], F32))
            puu1 = _en(nc.psum_tensor("puu1", [128, 512], F32))
            pdd0 = _en(nc.psum_tensor("pdd0", [128, 512], F32))
            pdd1 = _en(nc.psum_tensor("pdd1", [128, 512], F32))
            pdt = _en(nc.psum_tensor("pdt", [128, 512], F32))
            wgb = [wgb0, wgb1]
            wub = [wub0, wub1]
            wdb = [wdb0, wdb1]
            pdd = [pdd0, pdd1]
            pgg = [pgg0, pgg1]
            puu = [puu0, puu1]
            sgbv = [sgb0, sgb1]
            subv = [sub0, sub1]
            p = Plan(semc)
            gev = {0: 0, 1: 0}
            hmul = {0: 0, 1: 0}
            p.sem.setdefault("vst", 0)
            for q in range(4):
                p.dma("sync", (lambda s, _q=q: s.dma_start(
                    out=xh[:].rearrange("p (k r) -> p k r", r=512),
                    in_=XHs.rearrange("(k p) r -> p k r", p=128)[:, :, _q * 512:(_q + 1) * 512])),
                    waits=[("dn", p.cur("dn"))], inc="da")
                p.dma("sync", (lambda s, _q=q: s.dma_start(
                    out=xl[:].rearrange("p (k r) -> p k r", r=512),
                    in_=XLs.rearrange("(k p) r -> p k r", p=128)[:, :, _q * 512:(_q + 1) * 512])),
                    waits=[("dn", p.cur("dn"))], inc="db")
                xldv = [("da", p.cur("da")), ("db", p.cur("db"))]
                for ht in range(44):
                    wgx = wgb[ht % 2]
                    wux = wub[ht % 2]
                    gsem = "de" if ht % 2 == 0 else "df"
                    usem = "dh" if ht % 2 == 0 else "di"
                    wt = []
                    if gev[ht % 2]:
                        wt.append(("ev", gev[ht % 2]))
                    p.dma("sync", (lambda s, _w=wgx, _h=ht: s.dma_start(
                        out=_w[:].rearrange("p (k j) -> p k j", j=128),
                        in_=WGT.rearrange("(k p) f -> p k f", p=128)[:, :, _h * 128:(_h + 1) * 128])),
                        waits=wt, inc=gsem)
                    p.dma("sync", (lambda s, _w=wux, _h=ht: s.dma_start(
                        out=_w[:].rearrange("p (k j) -> p k j", j=128),
                        in_=WUT.rearrange("(k p) f -> p k f", p=128)[:, :, _h * 128:(_h + 1) * 128])),
                        waits=list(wt), inc=usem)
                    wglv, wulv = p.cur(gsem), p.cur(usem)
                    par = ht % 2
                    for (wsem, wv, wbx, bank, name) in ((gsem, wglv, wgx, pgg[par], "g"),
                                                        (usem, wulv, wux, puu[par], "u")):
                        for z in range(32):
                            k, hl = z % 16, z // 16
                            rhs = xh if hl == 0 else xl
                            wt = []
                            if z == 0:
                                wt.append((wsem, wv))
                                wt.extend(xldv)
                                if gev[par]:
                                    wt.append(("ev", gev[par]))
                            p.op("tensor", (lambda te, _k=k, _w=wbx, _r=rhs, _b=bank:
                                            te.matmul(_b[:], _w[:, _k * 128:(_k + 1) * 128],
                                                      _r[:, _k * 512:(_k + 1) * 512],
                                                      start=(_k == 0 and _r is xh),
                                                      stop=(_k == 15 and _r is xl))),
                                 waits=wt, inc=("mm" if z == 31 else None))
                        wt = [("mm", p.cur("mm"))]
                        if hmul[par]:
                            wt.append(("vst", hmul[par]))
                        if name == "g":
                            p.op("scalar", (lambda s, _h=ht, _b=bank, _o=sgbv[par]:
                                            s.activation(
                                _o[:], _b[:], AF.Silu, scale=sgv[:, _h:_h + 1])),
                                waits=wt, inc="ev")
                        else:
                            p.op("scalar", (lambda s, _h=ht, _b=bank, _o=subv[par]:
                                            s.activation(
                                _o[:], _b[:], AF.Identity, scale=suv[:, _h:_h + 1])),
                                waits=wt, inc="ev")
                    gev[par] = p.cur("ev")
                    p.op("vector", (lambda v, _g=sgbv[par], _u=subv[par]:
                                    v.tensor_tensor(hid[:], _g[:], _u[:], ALU.mult)),
                        waits=[("ev", p.cur("ev"))], inc="vst")
                    hmul[par] = p.cur("vst")
                    p.op("vector", (lambda v, _h=ht: v.tensor_copy(
                        sh[:, _h * 512:(_h + 1) * 512], hid[:])), inc="vst")
                    p.op("vector", (lambda v, _h=ht: v.scalar_tensor_tensor(
                        sl[:, _h * 512:(_h + 1) * 512], sh[:, _h * 512:(_h + 1) * 512],
                        -1.0, hid[:], ALU.mult, ALU.add)), inc="gu")
                guv = p.cur("gu")
                for dt in range(16):
                    wdx = wdb[dt % 2]
                    dsem = "dj" if dt % 2 == 0 else "dk"
                    wt = [("dnv", p.cur("dnv") - 1)] if (q * 16 + dt) >= 2 else []
                    p.dma("sync", (lambda s, _w=wdx, _d=dt: s.dma_start(
                        out=_w[:].rearrange("p (k j) -> p k j", j=128),
                        in_=WDT.rearrange("(k p) f -> p k f", p=128)[:, :, _d * 128:(_d + 1) * 128])),
                        waits=wt, inc=dsem)
                    wdlv = p.cur(dsem)
                    bank = pdd[dt % 2]
                    for z in range(88):
                        k, hl = z % 44, z // 44
                        rhs = sh if hl == 0 else sl
                        wt = []
                        if z == 0:
                            wt.append((dsem, wdlv))
                            wt.append(("gu", guv))
                            if q * 16 + dt >= 2:
                                wt.append(("dnv", p.cur("dnv") - 1))
                        p.op("tensor", (lambda te, _k=k, _w=wdx, _r=rhs, _b=bank:
                                        te.matmul(_b[:], _w[:, _k * 128:(_k + 1) * 128],
                                                  _r[:, _k * 512:(_k + 1) * 512],
                                                  start=(_k == 0 and _r is sh),
                                                  stop=(_k == 43 and _r is sl))),
                             waits=wt, inc=("mm" if z == 87 else None))
                    p.op("scalar", (lambda s, _d=dt, _b=bank: s.activation(
                        dob[:], _b[:], AF.Identity, scale=sdv[:, _d:_d + 1])),
                        waits=[("mm", p.cur("mm")), ("ot", p.cur("ot"))], inc="dnv")
                    for k in range(4):
                        wt = []
                        if k == 0:
                            wt.append(("dnv", p.cur("dnv")))
                            wt.append(("ov", p.cur("ov")))
                        p.op("tensor", (lambda te, _k=k: te.matmul(
                            pdt[:, _k * 128:(_k + 1) * 128], dob[:, _k * 128:(_k + 1) * 128],
                            identf[:], is_transpose=True, start=(_k == 0), stop=(_k == 3))),
                            waits=wt, inc=("ot" if k == 3 else None))
                    p.op("vector", (lambda v: v.tensor_copy(tdb[:], pdt[:])),
                         waits=[("ot", p.cur("ot")), ("dg", p.cur("dg"))], inc="ov")
                    dd = FFO.rearrange("(t p) f -> p t f", p=128)[:, q * 4:q * 4 + 4,
                                                                 dt * 128:(dt + 1) * 128]
                    p.dma("gpsimd", (lambda g, _d=dd: g.dma_start(
                        out=_d, in_=tdb[:].rearrange("p (i f) -> p i f", f=128))),
                        waits=[("ov", p.cur("ov"))], inc="dg")
                p.op("vector", lambda v: v.engine_nop(),
                     waits=[("ov", p.cur("ov"))], inc="dn")
            p.op("gpsimd", lambda g: g.engine_nop(),
                 waits=[("dg", p.cur("dg"))], inc="gp")
            emit(nc, p, sems)

        with ExitStack() as _ph:
            _en = _ph.enter_context
            fb0 = _en(nc.sbuf_tensor("fb0", [128, C], F32))
            fb1 = _en(nc.sbuf_tensor("fb1", [128, C], F32))
            xm0 = _en(nc.sbuf_tensor("xm0", [128, C], F32))
            xm1 = _en(nc.sbuf_tensor("xm1", [128, C], F32))
            sq2 = _en(nc.sbuf_tensor("sq2", [128, C], F32))
            ln5 = _en(nc.sbuf_tensor("ln5", [128, C], F32))
            yo0 = _en(nc.sbuf_tensor("yo0", [128, C], F32))
            yo1 = _en(nc.sbuf_tensor("yo1", [128, C], F32))
            fbx = [fb0, fb1]
            xmx = [xm0, xm1]
            yox = [yo0, yo1]
            p = Plan(semc)
            for t in range(OT):
                fb = fbx[t % 2]
                xm = xmx[t % 2]
                yo = yox[t % 2]
                fsem = "da" if t % 2 == 0 else "db"
                msem = "de" if t % 2 == 0 else "df"
                waits = [("fx", p.cur("fx") - 1)] if t >= 2 else []
                p.dma("sync", (lambda s, _b=fb, _t=t: s.dma_start(
                    out=_b[:], in_=FFO[_t * 128:(_t + 1) * 128, :])), waits=waits, inc=fsem)
                fsv = p.cur(fsem)
                p.dma("sync", (lambda s, _b=xm, _t=t: s.dma_start(
                    out=_b[:], in_=XMID[_t * 128:(_t + 1) * 128, :])),
                    waits=list(waits), inc=msem)
                msv = p.cur(msem)
                o = (t % 2) * 32
                p.op("vector", (lambda v, _b=fb, _o=o: v.tensor_reduce(
                    st[:, _o:_o + 1], _b[:], AX.X, ALU.add, negate=True)),
                    waits=[(fsem, fsv)], inc="vst")
                p.op("scalar", (lambda s, _b=fb, _o=o: s.activation(
                    sq2[:], _b[:], AF.Square, accum_out=st[:, _o + 1:_o + 2])),
                    waits=[(fsem, fsv)], inc="sq")
                p.op("vector", (lambda v, _o=o: v.tensor_scalar(
                    st[:, _o + 2:_o + 3], st[:, _o:_o + 1], 1.0 / C, None,
                    op0=ALU.mult)), inc="vst")
                p.op("vector", (lambda v, _o=o: v.tensor_scalar(
                    st[:, _o + 3:_o + 4], st[:, _o + 1:_o + 2], 1.0 / C, EPS,
                    op0=ALU.mult, op1=ALU.add)),
                    waits=[("sq", p.cur("sq"))], inc="vst")
                p.op("vector", (lambda v, _o=o: v.scalar_tensor_tensor(
                    st[:, _o + 4:_o + 5], st[:, _o + 2:_o + 3], -1.0,
                    st[:, _o + 2:_o + 3], ALU.mult, ALU.mult)), inc="vst")
                p.op("vector", (lambda v, _o=o: v.tensor_tensor(
                    st[:, _o + 5:_o + 6], st[:, _o + 3:_o + 4],
                    st[:, _o + 4:_o + 5], ALU.add)), inc="vst")
                p.op("scalar", (lambda s, _o=o: s.activation(
                    st[:, _o + 6:_o + 7], st[:, _o + 5:_o + 6], AF.Sqrt)),
                    waits=[("vst", p.cur("vst"))], inc="sq")
                p.op("vector", (lambda v, _o=o: v.reciprocal(
                    st[:, _o + 7:_o + 8], st[:, _o + 6:_o + 7])),
                    waits=[("sq", p.cur("sq"))], inc="vst")
                p.op("vector", (lambda v, _o=o: v.tensor_tensor(
                    st[:, _o + 8:_o + 9], st[:, _o + 2:_o + 3],
                    st[:, _o + 7:_o + 8], ALU.mult)), inc="vst")
                p.op("scalar", (lambda s, _b=fb, _o=o: s.activation(
                    ln5[:], _b[:], AF.Identity, bias=st[:, _o + 8:_o + 9],
                    scale=st[:, _o + 7:_o + 8])),
                    waits=[("vst", p.cur("vst"))], inc="y1")
                ysem = "dc" if t % 2 == 0 else "dd"
                yw = [("y1", p.cur("y1")), (msem, msv)]
                if t >= 2:
                    yw.append((ysem, p.cur(ysem)))
                p.op("vector", (lambda v, _x=xm, _y=yo: v.tensor_tensor(
                    _y[:], _x[:], ln5[:], ALU.add)),
                    waits=yw, inc="fx")
                p.dma("gpsimd", (lambda g, _y=yo, _t=t: g.dma_start(
                    out=y_d[_t * 128:(_t + 1) * 128, :], in_=_y[:])),
                    waits=[("fx", p.cur("fx"))], inc=ysem)
            for nm in DEBUG_DUMPS:
                p.dma("gpsimd", (lambda g, _nm=nm: g.dma_start(
                    out=dbg[_nm], in_=scr[_nm])), inc="out")
            p.op("gpsimd", lambda g: g.engine_nop(),
                 waits=[("dc", p.cur("dc")), ("dd", p.cur("dd")),
                        ("out", p.cur("out"))], inc="gp")
            emit(nc, p, sems)

    if PHASE_LIMIT < 9:
        with nc.sbuf_tensor("zout", [128, C], F32) as zo, \
             nc.semaphore("zs") as zs, nc.Block() as blk:
            @blk.gpsimd
            def _(g):
                g.memset(zo[:], 0.0).then_inc(zs, 1)
                g.wait_ge(zs, 1)
                for t in range(OT):
                    g.dma_start(out=y_d[t * 128:(t + 1) * 128, :],
                                in_=zo[:]).then_inc(zs, 16)
                g.wait_ge(zs, 1 + 16 * OT)

    nc.finalize()
    return nc


def kernel(L0_attn_x, L0_attn_w_qkv, L0_attn_w_proj, L0_ffn_wq_gate,
           L0_ffn_s_gate, L0_ffn_wq_up, L0_ffn_s_up, L0_ffn_wq_down,
           L0_ffn_s_down):
    x = np.ascontiguousarray(np.asarray(L0_attn_x, dtype=np.float32))
    wq = np.ascontiguousarray(np.asarray(L0_attn_w_qkv, dtype=np.float32))
    wp = np.ascontiguousarray(np.asarray(L0_attn_w_proj, dtype=np.float32))
    wg = np.ascontiguousarray(np.asarray(L0_ffn_wq_gate, dtype=np.int32))
    wu = np.ascontiguousarray(np.asarray(L0_ffn_wq_up, dtype=np.int32))
    wd = np.ascontiguousarray(np.asarray(L0_ffn_wq_down, dtype=np.int32))
    sg = np.ascontiguousarray(np.asarray(L0_ffn_s_gate, dtype=np.float32))
    su = np.ascontiguousarray(np.asarray(L0_ffn_s_up, dtype=np.float32))
    sd = np.ascontiguousarray(np.asarray(L0_ffn_s_down, dtype=np.float32))
    ident = np.eye(128, dtype=np.float32)
    trim = (np.triu(np.ones((128, 128), dtype=np.float32), 1) * NEG).astype(np.float32)

    nc = build_program()
    in_maps = []
    orders = []
    for c in range(8):
        b, par = c // 2, c % 2
        order = list(range(par, NT, 2)) + list(range(1 - par, NT, 2))
        orders.append(order)
        xp = np.concatenate([x[b, t * 128:(t + 1) * 128, :] for t in order], axis=0)
        oma = (np.zeros((128, 128), np.float32) if par == 1 else
               np.full((128, 128), NEG, np.float32))
        in_maps.append(dict(xp=np.ascontiguousarray(xp), wqkv=wq, wproj=wp,
                            wg=wg, wu=wu, wd=wd, sg=sg, su=su, sd=sd,
                            ident=ident, tri=trim, oma=oma))
    res = run_bass_kernel_spmd(nc, in_maps, core_ids=list(range(8)))
    y = np.empty((4, NTOK, C), np.float32)
    for c in range(8):
        b, par = c // 2, c % 2
        yc = np.asarray(res.results[c]["y"])
        for i, t in enumerate(range(par, NT, 2)):
            y[b, t * 128:(t + 1) * 128, :] = yc[i * 128:(i + 1) * 128, :]
    return y


# version 07
# speedup vs baseline: 1.2578x; 1.0342x over previous
import numpy as np
from contextlib import ExitStack

import concourse.bass as bass
import concourse.mybir as mybir
from concourse.bass_utils import run_bass_kernel_spmd
from concourse.alu_op_type import AluOpType as ALU

F32 = mybir.dt.float32
BF16 = mybir.dt.bfloat16
I32 = mybir.dt.int32
AF = mybir.ActivationFunctionType
AX = mybir.AxisListType

C = 2048
H = 16
HID = 5632
NTOK = 4096
NT = 32
OT = 16
FQKV = 3 * C
NEG = np.float32(-9e15)
EPS = float(np.float32(1e-5))
SCALE = float(np.float32(128.0 ** -0.5))
M32 = 12582912.0
M64 = 6755399441055744.0
ZSH = 384.0

ENGINES = ("sync", "scalar", "vector", "tensor", "gpsimd")
DEBUG_DUMPS = []


class Plan:
    def __init__(self, sem):
        self.ops = []
        self.sem = sem

    def op(self, engine, fn, waits=(), inc=None, inc_n=1):
        w = [(s, int(v)) for (s, v) in waits if v > 0]
        iv = None
        if inc is not None:
            self.sem[inc] = self.sem.get(inc, 0) + inc_n
            iv = self.sem[inc]
        self.ops.append([engine, fn, w, inc, inc_n, iv])

    def dma(self, engine, fn, waits=(), inc=None):
        self.op(engine, fn, waits, inc, 16)

    def cur(self, name):
        return self.sem.get(name, 0)

    def finalize(self):
        last = {}
        for o in self.ops:
            en, fn, waits, inc, inc_n, iv = o
            if en in ("vector", "scalar", "gpsimd") and inc_n != 16:
                if en in last and last[en] not in waits:
                    waits.append(last[en])
                if inc is None:
                    eo = "eo_" + en
                    self.sem[eo] = self.sem.get(eo, 0) + 1
                    o[3], o[5] = eo, self.sem[eo]
                last[en] = (o[3], o[5])


PHASE_LIMIT = 9
ATT_MODE = 4
_PH = [0]


def interleave_ops(a_ops, b_ops):
    out = []
    na, nb = len(a_ops), len(b_ops)
    j = 0
    for i, op in enumerate(a_ops):
        out.append(op)
        while j < nb and j * na <= i * nb:
            out.append(b_ops[j])
            j += 1
    out.extend(b_ops[j:])
    return out


def emit(nc, plan, sems):
    i = _PH[0]
    _PH[0] += 1
    if i >= PHASE_LIMIT:
        return
    plan.finalize()
    with nc.Block() as block:
        def mk(engname):
            def body(eng):
                for (en, fn, waits, inc, inc_n, iv) in plan.ops:
                    if en != engname:
                        continue
                    for (s, v) in waits:
                        eng.wait_ge(sems[s], v)
                    instr = fn(eng)
                    if inc is not None:
                        instr.then_inc(sems[inc], inc_n)
            body.__name__ = "_"
            return body
        for en in ENGINES:
            getattr(block, en)(mk(en))


SEM_NAMES = ["ld", "wld", "vst", "sq", "y1", "xq", "xtp", "xev", "mm", "ev",
             "tp", "tev", "qld", "se", "sv", "mx", "xp",
             "pt", "pe", "pv", "oe", "ot", "ov", "gp", "out", "prx", "gu",
             "dn", "hst", "fx", "dnv",
             "da", "db", "dc", "dd", "de", "df", "dg", "dh", "di", "dj", "dk",
             "eo_vector", "eo_scalar", "eo_gpsimd"]


def build_program():
    _PH[0] = 0
    nc = bass.Bass("TRN2", target_bir_lowering=False)

    xp_d = nc.dram_tensor("xp", [NTOK, C], F32, kind="ExternalInput").ap()
    wqkv = nc.dram_tensor("wqkv", [FQKV, C], F32, kind="ExternalInput").ap()
    wproj = nc.dram_tensor("wproj", [C, C], F32, kind="ExternalInput").ap()
    wg_r = nc.dram_tensor("wg", [HID, C], I32, kind="ExternalInput").ap()
    wu_r = nc.dram_tensor("wu", [HID, C], I32, kind="ExternalInput").ap()
    wd_r = nc.dram_tensor("wd", [C, HID], I32, kind="ExternalInput").ap()
    sg_d = nc.dram_tensor("sg", [HID], F32, kind="ExternalInput").ap()
    su_d = nc.dram_tensor("su", [HID], F32, kind="ExternalInput").ap()
    sd_d = nc.dram_tensor("sd", [C], F32, kind="ExternalInput").ap()
    id_d = nc.dram_tensor("ident", [128, 128], F32, kind="ExternalInput").ap()
    tri_d = nc.dram_tensor("tri", [128, 128], F32, kind="ExternalInput").ap()
    oma_d = nc.dram_tensor("oma", [128, 128], F32, kind="ExternalInput").ap()
    y_d = nc.dram_tensor("y", [2048, C], F32, kind="ExternalOutput").ap()

    WQKVT = nc.dram_tensor("WQKVT", [C, FQKV], BF16, kind="Internal").ap()
    WPROJT = nc.dram_tensor("WPROJT", [C, C], BF16, kind="Internal").ap()
    WGT = nc.dram_tensor("WGT", [C, HID], BF16, kind="Internal").ap()
    WUT = nc.dram_tensor("WUT", [C, HID], BF16, kind="Internal").ap()
    WDT = nc.dram_tensor("WDT", [HID, C], BF16, kind="Internal").ap()
    QT = nc.dram_tensor("QT", [C, 2048], F32, kind="Internal").ap()
    KT = nc.dram_tensor("KT", [C, NTOK], F32, kind="Internal").ap()
    VS = nc.dram_tensor("VS", [NTOK, C], F32, kind="Internal").ap()
    OS = nc.dram_tensor("OS", [2048, C], F32, kind="Internal").ap()
    XMID = nc.dram_tensor("XMID", [2048, C], F32, kind="Internal").ap()
    XHs = nc.dram_tensor("XHs", [C, 2048], BF16, kind="Internal").ap()
    XLs = nc.dram_tensor("XLs", [C, 2048], BF16, kind="Internal").ap()
    FFO = nc.dram_tensor("FFO", [2048, C], F32, kind="Internal").ap()
    scr = {"QT": QT, "KT": KT, "VS": VS, "OS": OS, "XMID": XMID, "FFO": FFO,
           "WQKVT": WQKVT, "WPROJT": WPROJT, "WGT": WGT, "WUT": WUT, "WDT": WDT,
           "XHs": XHs, "XLs": XLs}
    dbg = {}
    for nm in DEBUG_DUMPS:
        t = scr[nm]
        dbg[nm] = nc.dram_tensor("dbg_" + nm, list(t.shape), t.dtype,
                                 kind="ExternalOutput").ap()

    semc = {}
    with ExitStack() as ctx:
        e = ctx.enter_context
        sems = {n: e(nc.semaphore(n)) for n in SEM_NAMES}

        identf = e(nc.sbuf_tensor("identf", [128, 128], F32))
        identb = e(nc.sbuf_tensor("identb", [128, 128], BF16))
        tri = e(nc.sbuf_tensor("tri_s", [128, 128], F32))
        oma = e(nc.sbuf_tensor("oma_s", [128, 128], F32))
        ones = e(nc.sbuf_tensor("ones", [128, 128], F32))
        invs = e(nc.sbuf_tensor("invs", [128, NT], F32))
        qsc = e(nc.sbuf_tensor("qsc", [128, NT], F32))
        ksc = e(nc.sbuf_tensor("ksc", [128, NT], F32))
        bcw = e(nc.sbuf_tensor("bcw", [128, 4], F32))
        invl = e(nc.sbuf_tensor("invl", [128, OT * H], F32))
        sgv = e(nc.sbuf_tensor("sgv", [128, 44], F32))
        suv = e(nc.sbuf_tensor("suv", [128, 44], F32))
        sdv = e(nc.sbuf_tensor("sdv", [128, 16], F32))
        st = e(nc.sbuf_tensor("st", [128, 64], F32))
        sws = e(nc.sbuf_tensor("sws", [128, 8], F32))
        cst = e(nc.sbuf_tensor("cst", [128, 4], F32))
        nc.const_aps.register(F32, 0.5, cst[:, 0:1])
        nc.const_aps.register(F32, 0.0, cst[:, 1:2])
        nc.const_aps.register(F32, 1.0, cst[:, 2:3])
        nc.const_aps.register(F32, ZSH, cst[:, 3:4])

        wpart = e(nc.sbuf_tensor("wpart", [128, 64], F32))
        with ExitStack() as _ph:
            _en = _ph.enter_context
            wb0 = _en(nc.sbuf_tensor("wb0", [128, C], F32))
            wb1 = _en(nc.sbuf_tensor("wb1", [128, C], F32))
            ps_sw = _en(nc.psum_tensor("ps_sw", [128, 8], F32))
            sgn = _en(nc.sbuf_tensor("sgn", [128, 128], F32))
            sun = _en(nc.sbuf_tensor("sun", [128, 128], F32))
            sdn = _en(nc.sbuf_tensor("sdn", [128, 128], F32))
            psv = _en(nc.psum_tensor("psv", [128, 512], F32))
            wbuf = [wb0, wb1]
            p = Plan(semc)
            p.dma("sync", lambda s: s.dma_start(out=identf[:], in_=id_d), inc="ld")
            p.dma("sync", lambda s: s.dma_start(out=tri[:], in_=tri_d), inc="ld")
            p.dma("sync", lambda s: s.dma_start(out=oma[:], in_=oma_d), inc="ld")
            p.dma("sync", lambda s: s.dma_start(
                out=sgn[0:44, :], in_=sg_d.rearrange("(a p) -> a p", p=128)), inc="ld")
            p.dma("sync", lambda s: s.dma_start(
                out=sun[0:44, :], in_=su_d.rearrange("(a p) -> a p", p=128)), inc="ld")
            p.dma("sync", lambda s: s.dma_start(
                out=sdn[0:16, :], in_=sd_d.rearrange("(a p) -> a p", p=128)), inc="ld")
            p.op("tensor", lambda t: t.matmul(psv[:, 0:44], sgn[0:44, :],
                                              identf[0:44, 0:44], is_transpose=True,
                                              start=True, stop=True),
                 waits=[("ld", 16 * 6)], inc="tp")
            p.op("tensor", lambda t: t.matmul(psv[:, 64:108], sun[0:44, :],
                                              identf[0:44, 0:44], is_transpose=True,
                                              start=True, stop=True), inc="tp")
            p.op("tensor", lambda t: t.matmul(psv[:, 128:144], sdn[0:16, :],
                                              identf[0:16, 0:16], is_transpose=True,
                                              start=True, stop=True), inc="tp")
            p.op("vector", lambda v: v.tensor_copy(sgv[:], psv[:, 0:44]),
                 waits=[("tp", p.cur("tp"))], inc="vst")
            p.op("vector", lambda v: v.tensor_copy(suv[:], psv[:, 64:108]), inc="vst")
            p.op("vector", lambda v: v.tensor_copy(sdv[:], psv[:, 128:144]), inc="vst")
            p.op("gpsimd", lambda g: g.memset(ones[:], 1.0), inc="gp")
            p.op("gpsimd", lambda g: g.memset(cst[:, 0:1], 0.5), inc="gp")
            p.op("gpsimd", lambda g: g.memset(cst[:, 1:2], 0.0), inc="gp")
            p.op("gpsimd", lambda g: g.memset(cst[:, 2:3], 1.0), inc="gp")
            p.op("gpsimd", lambda g: g.memset(cst[:, 3:4], ZSH), inc="gp")
            p.op("vector", lambda v: v.tensor_copy(identb[:], identf[:]),
                 waits=[("ld", 96)], inc="vst")
            for t in range(64):
                src = (wqkv[t * 128:(t + 1) * 128, :] if t < 48 else
                       wproj[(t - 48) * 128:(t - 47) * 128, :])
                dsem = "da" if t % 2 == 0 else "db"
                waits = [("vst", p.cur("vst") - 1)] if t >= 2 else []
                p.dma("sync", (lambda s, _s=src, _b=wbuf[t % 2]:
                               s.dma_start(out=_b[:], in_=_s)), waits=waits, inc=dsem)
                p.op("vector", (lambda v, _b=wbuf[t % 2], _t=t:
                                v.tensor_reduce(wpart[:, _t:_t + 1], _b[:], AX.X,
                                                ALU.add, apply_absolute_value=True)),
                     waits=[(dsem, p.cur(dsem))], inc="vst")
            p.op("vector", lambda v: v.tensor_reduce(
                sws[:, 0:1], wpart[:, 0:48], AX.X, ALU.add), inc="vst")
            p.op("vector", lambda v: v.tensor_reduce(
                sws[:, 1:2], wpart[:, 48:64], AX.X, ALU.add), inc="vst")
            p.op("tensor", lambda t: t.matmul(ps_sw[0:1, 0:1], sws[:, 0:1],
                                              ones[:, 0:1], start=True, stop=True),
                 waits=[("vst", p.cur("vst")), ("gp", 1)], inc="mm")
            p.op("tensor", lambda t: t.matmul(ps_sw[0:1, 1:2], sws[:, 1:2],
                                              ones[:, 0:1], start=True, stop=True),
                 inc="mm")
            p.op("vector", lambda v: v.tensor_copy(sws[0:1, 2:4], ps_sw[0:1, 0:2]),
                 waits=[("mm", p.cur("mm"))], inc="vst")
            p.op("vector", lambda v: v.tensor_scalar(
                sws[0:1, 4:5], sws[0:1, 2:3], 1.0 / (FQKV * C), 1e-5,
                op0=ALU.mult, op1=ALU.max), inc="vst")
            p.op("vector", lambda v: v.tensor_scalar(
                sws[0:1, 5:6], sws[0:1, 3:4], 1.0 / (C * C), 1e-5,
                op0=ALU.mult, op1=ALU.max), inc="vst")
            p.op("vector", lambda v: v.reciprocal(sws[0:1, 6:8], sws[0:1, 4:6]),
                 inc="vst")
            vv = p.cur("vst")
            p.op("tensor", lambda t: t.matmul(ps_sw[:, 4:8], ones[0:1, 0:128],
                                              sws[0:1, 4:8], start=True, stop=True),
                 waits=[("vst", vv)], inc="mm")
            p.op("vector", lambda v: v.tensor_copy(bcw[:, 0:2], ps_sw[:, 6:8]),
                 waits=[("mm", p.cur("mm"))], inc="vst")
            p.op("vector", lambda v: v.tensor_copy(bcw[:, 2:4], ps_sw[:, 4:6]),
                 inc="vst")
            emit(nc, p, sems)

            with ExitStack() as _ph:
                _en = _ph.enter_context
                ybuf = _en(nc.sbuf_tensor("ybuf", [128, C], F32))
                mbuf = _en(nc.sbuf_tensor("mbuf", [128, C], F32))
                tbuf = _en(nc.sbuf_tensor("tbuf", [128, C], BF16))
                te0 = _en(nc.sbuf_tensor("te0", [128, C], BF16))
                te1 = _en(nc.sbuf_tensor("te1", [128, C], BF16))
                pt0 = _en(nc.psum_tensor("pt0", [128, C], BF16))
                pt1 = _en(nc.psum_tensor("pt1", [128, C], BF16))
                tev_ = [te0, te1]
                ptp_ = [pt0, pt1]
                p = Plan(semc)
                spq = {0: None, 1: None}
                for t in range(64):
                    qkv = t < 48
                    src = (wqkv[t * 128:(t + 1) * 128, :] if qkv else
                           wproj[(t - 48) * 128:(t - 47) * 128, :])
                    wb = wbuf[t % 2]
                    dsem = "da" if t % 2 == 0 else "db"
                    waits = [("y1", p.cur("y1") - 1)] if t >= 2 else []
                    p.dma("sync", (lambda s, _s=src, _b=wb: s.dma_start(out=_b[:], in_=_s)),
                          waits=waits, inc=dsem)
                    sc = bcw[:, 0:1] if qkv else bcw[:, 1:2]
                    p.op("scalar", (lambda s, _b=wb, _sc=sc: s.activation(
                        ybuf[:], _b[:], AF.Identity, bias=0.0, scale=_sc)),
                        waits=[(dsem, p.cur(dsem)), ("xq", p.cur("xq"))], inc="y1")
                    p.op("vector", (lambda v: v.tensor_scalar(
                        mbuf[:], ybuf[:], -0.5, None, op0=ALU.is_le)),
                        waits=[("y1", p.cur("y1"))], inc="vst")
                    p.op("vector", (lambda v: v.scalar_tensor_tensor(
                        tbuf[:], ybuf[:], 0.5, mbuf[:], ALU.is_ge, ALU.subtract)),
                        inc="xq")
                    ptp = ptp_[t % 2]
                    for k in range(16):
                        wt = []
                        if k == 0:
                            wt.append(("xq", p.cur("xq")))
                            if t >= 2:
                                wt.append(("tev", p.cur("tev") - 1))
                        p.op("tensor", (lambda te, _k=k, _p=ptp: te.matmul(
                            _p[:, _k * 128:(_k + 1) * 128],
                            tbuf[:, _k * 128:(_k + 1) * 128], identb[:],
                            is_transpose=True, start=(_k % 8 == 0), stop=(_k % 8 == 7))),
                            waits=wt, inc=("tp" if k == 15 else None))
                    evw = [("tp", p.cur("tp"))]
                    if spq[t % 2] is not None:
                        evw.append(spq[t % 2])
                    p.op("vector", (lambda v, _p=ptp, _e=tev_[t % 2]:
                                    v.tensor_copy(_e[:], _p[:])),
                         waits=evw, inc="tev")
                    if qkv:
                        dst = WQKVT.rearrange("(k p) f -> p k f", p=128)[:, :, t * 128:(t + 1) * 128]
                    else:
                        dst = WPROJT.rearrange("(k p) f -> p k f", p=128)[:, :, (t - 48) * 128:(t - 47) * 128]
                    ssem = "dc" if t % 2 == 0 else "dd"
                    p.dma("gpsimd", (lambda g, _d=dst, _e=tev_[t % 2]: g.dma_start(
                        out=_d, in_=_e[:].rearrange("p (k j) -> p k j", j=128))),
                        waits=[("tev", p.cur("tev"))], inc=ssem)
                    spq[t % 2] = (ssem, p.cur(ssem))
                p.op("gpsimd", lambda g: g.engine_nop(),
                     waits=[("dc", p.cur("dc")), ("dd", p.cur("dd"))], inc="gp")
                emit(nc, p, sems)

        _xq_scope = ExitStack()
        xqT = _xq_scope.enter_context(nc.sbuf_tensor("xqT", [128, 16 * NTOK], BF16))
        xqT3 = xqT[:].rearrange("p (k n) -> p k n", n=NTOK)
        with ExitStack() as _ph:
            _en = _ph.enter_context
            xb0 = _en(nc.sbuf_tensor("xb0", [128, C], F32))
            xb1 = _en(nc.sbuf_tensor("xb1", [128, C], F32))
            sqb = _en(nc.sbuf_tensor("sqb", [128, C], F32))
            yb = _en(nc.sbuf_tensor("yb", [128, C], F32))
            mb = _en(nc.sbuf_tensor("mb", [128, C], F32))
            ub = _en(nc.sbuf_tensor("ub", [128, C], F32))
            qbv = _en(nc.sbuf_tensor("qbv", [128, C], BF16))
            px0 = _en(nc.psum_tensor("px0", [128, C], BF16))
            px1 = _en(nc.psum_tensor("px1", [128, C], BF16))
            xb = [xb0, xb1]
            pxt = [px0, px1]
            p = Plan(semc)
            for t in range(NT):
                b = xb[t % 2]
                dsem = "da" if t % 2 == 0 else "db"
                waits = [("y1", p.cur("y1") - 1)] if t >= 2 else []
                p.dma("sync", (lambda s, _b=b, _t=t: s.dma_start(
                    out=_b[:], in_=xp_d[_t * 128:(_t + 1) * 128, :])), waits=waits, inc=dsem)
                ldv = p.cur(dsem)
                o = (t % 2) * 32

                def tiny(fn, waits=()):
                    p.op("vector", fn, waits=waits, inc="vst")
                p.op("vector", (lambda v, _b=b, _o=o: v.tensor_reduce(
                    st[:, _o:_o + 1], _b[:], AX.X, ALU.add, negate=True)),
                    waits=[(dsem, ldv)], inc="vst")
                p.op("vector", (lambda v, _b=b, _o=o: v.tensor_reduce(
                    st[:, _o + 1:_o + 2], _b[:], AX.X, ALU.max)), inc="vst")
                p.op("vector", (lambda v, _b=b, _o=o: v.tensor_reduce(
                    st[:, _o + 2:_o + 3], _b[:], AX.X, ALU.min)), inc="vst")
                p.op("scalar", (lambda s, _b=b, _o=o: s.activation(
                    sqb[:], _b[:], AF.Square, accum_out=st[:, _o + 3:_o + 4])),
                    waits=[(dsem, ldv), ("xq", p.cur("xq"))], inc="sq")
                tiny(lambda v, _o=o: v.tensor_scalar(
                    st[:, _o + 4:_o + 5], st[:, _o:_o + 1], 1.0 / C, None, op0=ALU.mult))
                p.op("vector", (lambda v, _o=o: v.tensor_scalar(
                    st[:, _o + 5:_o + 6], st[:, _o + 3:_o + 4], 1.0 / C, EPS,
                    op0=ALU.mult, op1=ALU.add)),
                    waits=[("sq", p.cur("sq"))], inc="vst")
                tiny(lambda v, _o=o: v.scalar_tensor_tensor(
                    st[:, _o + 6:_o + 7], st[:, _o + 4:_o + 5], -1.0,
                    st[:, _o + 4:_o + 5], ALU.mult, ALU.mult))
                tiny(lambda v, _o=o: v.tensor_tensor(
                    st[:, _o + 7:_o + 8], st[:, _o + 5:_o + 6],
                    st[:, _o + 6:_o + 7], ALU.add))
                p.op("scalar", (lambda s, _o=o: s.activation(
                    st[:, _o + 8:_o + 9], st[:, _o + 7:_o + 8], AF.Sqrt)),
                    waits=[("vst", p.cur("vst"))], inc="sq")
                p.op("vector", (lambda v, _o=o: v.reciprocal(
                    st[:, _o + 9:_o + 10], st[:, _o + 8:_o + 9])),
                    waits=[("sq", p.cur("sq"))], inc="vst")
                tiny(lambda v, _o=o: v.tensor_tensor(
                    st[:, _o + 10:_o + 11], st[:, _o + 1:_o + 2],
                    st[:, _o + 4:_o + 5], ALU.add))
                tiny(lambda v, _o=o: v.scalar_tensor_tensor(
                    st[:, _o + 11:_o + 12], st[:, _o + 2:_o + 3], -1.0,
                    st[:, _o + 4:_o + 5], ALU.mult, ALU.subtract))
                tiny(lambda v, _o=o: v.tensor_tensor(
                    st[:, _o + 12:_o + 13], st[:, _o + 10:_o + 11],
                    st[:, _o + 11:_o + 12], ALU.max))
                tiny(lambda v, _o=o: v.tensor_tensor(
                    st[:, _o + 13:_o + 14], st[:, _o + 12:_o + 13],
                    st[:, _o + 9:_o + 10], ALU.mult))
                tiny(lambda v, _o=o: v.tensor_scalar(
                    st[:, _o + 14:_o + 15], st[:, _o + 13:_o + 14], 1e-5, 1.0 / 127.0,
                    op0=ALU.max, op1=ALU.mult))
                tiny(lambda v, _o=o, _t=t: v.tensor_copy(
                    invs[:, _t:_t + 1], st[:, _o + 14:_o + 15]))
                tiny(lambda v, _o=o: v.reciprocal(
                    st[:, _o + 15:_o + 16], st[:, _o + 14:_o + 15]))
                tiny(lambda v, _o=o: v.tensor_tensor(
                    st[:, _o + 16:_o + 17], st[:, _o + 9:_o + 10],
                    st[:, _o + 15:_o + 16], ALU.mult))
                tiny(lambda v, _o=o: v.tensor_scalar(
                    st[:, _o + 17:_o + 18], st[:, _o + 4:_o + 5],
                    st[:, _o + 16:_o + 17], None, op0=ALU.mult))
                p.op("scalar", (lambda s, _b=b, _o=o: s.activation(
                    yb[:], _b[:], AF.Identity, bias=st[:, _o + 17:_o + 18],
                    scale=st[:, _o + 16:_o + 17])),
                    waits=[("vst", p.cur("vst"))], inc="y1")
                p.op("vector", (lambda v: v.tensor_scalar(
                    ub[:], yb[:], M32, -M32, op0=ALU.add, op1=ALU.add)),
                    waits=[("y1", p.cur("y1"))], inc="vst")
                p.op("vector", (lambda v: v.tensor_scalar(
                    mb[:], yb[:], ZSH, M64, op0=ALU.add, op1=ALU.add)), inc="vst")
                p.op("vector", (lambda v: v.tensor_scalar(
                    mb[:], mb[:], -M64, -ZSH, op0=ALU.add, op1=ALU.add)), inc="vst")
                p.op("vector", (lambda v: v.scalar_tensor_tensor(
                    yb[:], yb[:], -1.0, mb[:], ALU.mult, ALU.add)), inc="vst")
                p.op("vector", (lambda v: v.tensor_scalar(
                    mb[:], yb[:], -0.5, 0.5, op0=ALU.max, op1=ALU.min)), inc="vst")
                p.op("vector", (lambda v: v.tensor_scalar(
                    yb[:], yb[:], 1.0, -0.5, op0=ALU.add, op1=ALU.max)), inc="vst")
                p.op("vector", (lambda v: v.scalar_tensor_tensor(
                    yb[:], yb[:], 0.0, mb[:], ALU.min, ALU.subtract)), inc="vst")
                p.op("vector", (lambda v: v.scalar_tensor_tensor(
                    qbv[:], ub[:], 0.0, yb[:], ALU.add, ALU.subtract)), inc="xq")
                ptp = pxt[t % 2]
                for k in range(16):
                    wt = []
                    if k == 0:
                        wt.append(("xq", p.cur("xq")))
                        if t >= 2:
                            wt.append(("xev", p.cur("xev") - 1))
                    p.op("tensor", (lambda te, _k=k, _p=ptp: te.matmul(
                        _p[:, _k * 128:(_k + 1) * 128], qbv[:, _k * 128:(_k + 1) * 128],
                        identb[:], is_transpose=True,
                        start=(_k % 8 == 0), stop=(_k % 8 == 7))),
                        waits=wt, inc=("xtp" if k == 15 else None))
                p.op("vector", (lambda v, _p=ptp, _t=t: v.tensor_copy(
                    xqT3[:, :, _t * 128:(_t + 1) * 128],
                    _p[:].rearrange("p (k j) -> p k j", j=128))),
                    waits=[("xtp", p.cur("xtp"))], inc="xev")
            emit(nc, p, sems)

        with ExitStack() as _ph:
            _en = _ph.enter_context
            wc0 = _en(nc.sbuf_tensor("wc0", [128, 16 * 512], BF16))
            wc1 = _en(nc.sbuf_tensor("wc1", [128, 16 * 512], BF16))
            eb0 = _en(nc.sbuf_tensor("eb0", [128, 512], F32))
            eb1 = _en(nc.sbuf_tensor("eb1", [128, 512], F32))
            tb0 = _en(nc.sbuf_tensor("tb0", [128, 512], F32))
            tb1 = _en(nc.sbuf_tensor("tb1", [128, 512], F32))
            pq0 = _en(nc.psum_tensor("pq0", [128, 512], F32))
            pq1 = _en(nc.psum_tensor("pq1", [128, 512], F32))
            pq2 = _en(nc.psum_tensor("pq2", [128, 512], F32))
            pq3 = _en(nc.psum_tensor("pq3", [128, 512], F32))
            pr0 = _en(nc.psum_tensor("pr0", [128, 512], F32))
            pr1 = _en(nc.psum_tensor("pr1", [128, 512], F32))
            wck = [wc0, wc1]
            ebv = [eb0, eb1]
            tbv = [tb0, tb1]
            pq = [pq0, pq1, pq2, pq3]
            ptr = [pr0, pr1]
            p = Plan(semc)
            p.op("vector", lambda v: v.tensor_scalar(
                qsc[:], invs[:], bcw[:, 2:3], SCALE, op0=ALU.mult, op1=ALU.mult),
                waits=[("xev", 32)], inc="vst")
            p.op("vector", lambda v: v.tensor_scalar(
                ksc[:], invs[:], bcw[:, 2:3], None, op0=ALU.mult), inc="vst")
            vsc = p.cur("vst")
            ntile = 0
            nch = 0
            mm_after_chunk = {}
            spq = {0: None, 1: None}
            for half in range(2):
                for j in (range(12) if half == 0 else range(4, 12)):
                    wcb = wck[nch % 2]
                    wsem = "de" if nch % 2 == 0 else "df"
                    waits = []
                    if nch >= 2:
                        waits.append(("mm", mm_after_chunk[nch - 2]))
                    src = WQKVT.rearrange("(k p) f -> p k f", p=128)[:, :, j * 512:(j + 1) * 512]
                    p.dma("sync", (lambda s, _w=wcb, _s=src: s.dma_start(
                        out=_w[:].rearrange("p (k f) -> p k f", f=512), in_=_s)),
                        waits=waits, inc=wsem)
                    wldv = p.cur(wsem)
                    for i in range(16):
                        t = half * 16 + i
                        if j < 4 and t >= 16:
                            continue
                        bank = pq[ntile % 4]
                        eb = ebv[ntile % 2]
                        tbx = tbv[ntile % 2]
                        for k in range(16):
                            wt = []
                            if k == 0:
                                wt.append((wsem, wldv))
                                if ntile >= 4:
                                    wt.append(("ev", p.cur("ev") - 3))
                            p.op("tensor", (lambda te, _k=k, _b=bank, _t=t, _w=wcb:
                                            te.matmul(_b[:],
                                                      xqT3[:, _k, _t * 128:(_t + 1) * 128],
                                                      _w[:, _k * 512:(_k + 1) * 512],
                                                      start=(_k == 0), stop=(_k == 15))),
                                 waits=wt, inc=("mm" if k == 15 else None))
                        sc = qsc[:, t:t + 1] if j < 4 else ksc[:, t:t + 1]
                        wt = [("mm", p.cur("mm")), ("vst", vsc)]
                        if spq[ntile % 2] is not None:
                            wt.append(spq[ntile % 2])
                        p.op("scalar", (lambda s, _b=bank, _e=eb, _sc=sc:
                                        s.activation(_e[:], _b[:], AF.Identity,
                                                     scale=_sc)),
                             waits=wt, inc="ev")
                        evv = p.cur("ev")
                        ssem = "dc" if ntile % 2 == 0 else "dd"
                        if j >= 8:
                            dst = VS[t * 128:(t + 1) * 128, (j - 8) * 512:(j - 7) * 512]
                            p.dma("gpsimd", (lambda g, _d=dst, _e=eb: g.dma_start(
                                out=_d, in_=_e[:])), waits=[("ev", evv)], inc=ssem)
                            spq[ntile % 2] = (ssem, p.cur(ssem))
                        else:
                            ptb = ptr[ntile % 2]
                            for k in range(4):
                                wt = []
                                if k == 0:
                                    wt.append(("ev", evv))
                                    if ntile >= 2:
                                        wt.append(("tev", p.cur("tev") - 1))
                                p.op("tensor", (lambda te, _k=k, _p=ptb, _e=eb:
                                                te.matmul(_p[:, _k * 128:(_k + 1) * 128],
                                                          _e[:, _k * 128:(_k + 1) * 128],
                                                          identf[:], is_transpose=True,
                                                          start=(_k == 0), stop=(_k == 3))),
                                    waits=wt, inc=("tp" if k == 3 else None))
                            p.op("vector", (lambda v, _p=ptb, _t=tbx: v.tensor_copy(
                                _t[:], _p[:])), waits=[("tp", p.cur("tp"))], inc="tev")
                            if j < 4:
                                dd = QT.rearrange("(c k p) r -> c p k r", k=4, p=128)[j][:, :, t * 128:(t + 1) * 128]
                            else:
                                dd = KT.rearrange("(c k p) r -> c p k r", k=4, p=128)[j - 4][:, :, t * 128:(t + 1) * 128]
                            p.dma("gpsimd", (lambda g, _d=dd, _t=tbx: g.dma_start(
                                out=_d, in_=_t[:].rearrange("p (k j) -> p k j", j=128))),
                                waits=[("tev", p.cur("tev"))], inc=ssem)
                            spq[ntile % 2] = (ssem, p.cur(ssem))
                        ntile += 1
                    mm_after_chunk[nch] = p.cur("mm")
                    nch += 1
            p.op("gpsimd", lambda g: g.engine_nop(),
                 waits=[("dc", p.cur("dc")), ("dd", p.cur("dd"))], inc="gp")
            emit(nc, p, sems)

        _xq_scope.close()

        _ffnw_scope = ExitStack()
        if True:
            _en = _ffnw_scope.enter_context
            ib0 = _en(nc.sbuf_tensor("ib0", [128, C], I32))
            ib1 = _en(nc.sbuf_tensor("ib1", [128, C], I32))
            ibd = _en(nc.sbuf_tensor("ibd", [128, HID], I32))
            cb = _en(nc.sbuf_tensor("cb", [128, HID], BF16))
            fe0 = _en(nc.sbuf_tensor("fe0", [128, C], BF16))
            fe1 = fe0
            pf0 = _en(nc.psum_tensor("pf0", [128, C], BF16))
            ib = [ib0, ib1]
            fev = [fe0, fe1]
            pft = [pf0, pf0]
            p = Plan(semc)
            items = ([("g", t) for t in range(44)] + [("u", t) for t in range(44)]
                     + [("d", t) for t in range(16)])
            nr = 0
            spq = {0: None, 1: None}
            for it, (mat, t) in enumerate(items):
                if mat == "g":
                    src, nblk, dstT = wg_r[t * 128:(t + 1) * 128, :], 16, WGT
                elif mat == "u":
                    src, nblk, dstT = wu_r[t * 128:(t + 1) * 128, :], 16, WUT
                else:
                    src, nblk, dstT = wd_r[t * 128:(t + 1) * 128, :], 44, WDT
                if mat == "d":
                    b, dsem = ibd, "dh"
                    waits = [("y1", p.cur("y1") - 1)] if it >= 1 else []
                else:
                    b = ib[it % 2]
                    dsem = "dh" if it % 2 == 0 else "di"
                    waits = [("y1", p.cur("y1") - 1)] if it >= 2 else []
                ncols = nblk * 128
                p.dma("sync", (lambda s, _b=b, _s=src, _n=ncols: s.dma_start(
                    out=_b[:, 0:_n], in_=_s)), waits=waits, inc=dsem)
                p.op("scalar", (lambda s, _b=b, _n=ncols: s.activation(
                    cb[:, 0:_n], _b[:, 0:_n], AF.Copy, bias=-128.0)),
                    waits=[(dsem, p.cur(dsem)), ("tev", p.cur("tev"))], inc="y1")
                for (k0, cnt) in [(r * 16, min(16, nblk - r * 16))
                                  for r in range((nblk + 15) // 16)]:
                    ptb = pft[nr % 2]
                    for kk in range(cnt):
                        wt = []
                        if kk == 0:
                            wt.append(("y1", p.cur("y1")))
                            if nr >= 1:
                                wt.append(("tev", p.cur("tev")))
                        p.op("tensor", (lambda te, _k=k0 + kk, _kk=kk, _p=ptb, _c=cnt:
                                        te.matmul(_p[:, _kk * 128:(_kk + 1) * 128],
                                                  cb[:, _k * 128:(_k + 1) * 128], identb[:],
                                                  is_transpose=True,
                                                  start=(_kk % 8 == 0),
                                                  stop=(_kk % 8 == 7 or _kk == _c - 1))),
                            waits=wt, inc=("tp" if kk == cnt - 1 else None))
                    fv = fev[nr % 2]
                    evw = [("tp", p.cur("tp"))]
                    for pp in (0, 1):
                        if spq[pp] is not None:
                            evw.append(spq[pp])
                    p.op("vector", (lambda v, _p=ptb, _f=fv, _c=cnt: v.tensor_copy(
                        _f[:, 0:_c * 128], _p[:, 0:_c * 128])),
                        waits=evw, inc="tev")
                    ssem = "dc" if nr % 2 == 0 else "dd"
                    dd = dstT.rearrange("(k p) f -> p k f", p=128)[:, k0:k0 + cnt, t * 128:(t + 1) * 128]
                    p.dma("gpsimd", (lambda g, _d=dd, _f=fv, _c=cnt: g.dma_start(
                        out=_d, in_=_f[:, 0:_c * 128].rearrange("p (k j) -> p k j", j=128))),
                        waits=[("tev", p.cur("tev"))], inc=ssem)
                    spq[nr % 2] = (ssem, p.cur(ssem))
                    nr += 1
            p_ffnw = p
            _PH[0] += 1

        with ExitStack() as _ph:
            _en = _ph.enter_context
            sA = _en(nc.sbuf_tensor("sA", [128, 4096], F32))
            sB = _en(nc.sbuf_tensor("sB", [128, 4096], F32))
            sC = _en(nc.sbuf_tensor("sC", [128, 4096], F32))
            sD = _en(nc.sbuf_tensor("sD", [128, 4096], F32))
            eA = _en(nc.sbuf_tensor("eA", [128, 4096], F32))
            eB = _en(nc.sbuf_tensor("eB", [128, 4096], F32))
            eC = _en(nc.sbuf_tensor("eC", [128, 4096], F32))
            eD = _en(nc.sbuf_tensor("eD", [128, 4096], F32))
            qg = _en(nc.sbuf_tensor("qg", [128, 512], F32))
            kb0 = _en(nc.sbuf_tensor("kb0", [128, 512], F32))
            kb1 = _en(nc.sbuf_tensor("kb1", [128, 512], F32))
            vb0 = _en(nc.sbuf_tensor("vb0", [128, 128], F32))
            vb1 = _en(nc.sbuf_tensor("vb1", [128, 128], F32))
            pb0 = _en(nc.sbuf_tensor("pb0", [128, 512], F32))
            pb1 = _en(nc.sbuf_tensor("pb1", [128, 512], F32))
            otm = _en(nc.sbuf_tensor("otm", [128, 512], F32))
            obv = _en(nc.sbuf_tensor("obv", [128, 512], F32))
            pS0 = _en(nc.psum_tensor("pS0", [128, 512], F32))
            pS1 = _en(nc.psum_tensor("pS1", [128, 512], F32))
            pP0 = _en(nc.psum_tensor("pP0", [128, 512], F32))
            pP1 = _en(nc.psum_tensor("pP1", [128, 512], F32))
            pO = _en(nc.psum_tensor("pO", [128, 512], F32))
            pOt = _en(nc.psum_tensor("pOt", [128, 512], F32))
            strips = [sA, sB, sC, sD]
            estrips = [eA, eB, eC, eD]
            kbv = [kb0, kb1]
            vbv = [vb0, vb1]
            ptv = [pb0, pb1]
            pS = [pS0, pS1]
            pP = [pP0, pP1]
            p = Plan(semc)
            nkc = 0
            nvc = 0
            nsb = 0
            npt = 0
            se_bank = [[], []]
            for g in range(4):
                NCH = g + 1
                for h in range(H):
                    ghi = g * H + h
                    wq_ = [("mm", p.cur("mm"))] if ghi else []
                    p.dma("sync", (lambda s, _h=h, _g=g: s.dma_start(
                        out=qg[:], in_=QT[_h * 128:(_h + 1) * 128, _g * 512:(_g + 1) * 512])),
                        waits=wq_, inc="qld")
                    qldv = p.cur("qld")
                    for part in range(2):
                        for ch in range(NCH):
                            kbx = kbv[nkc % 2]
                            ksem = "da" if nkc % 2 == 0 else "db"
                            wt = []
                            if nkc >= 2:
                                wt.append(("mm", p.sem.get("_kmm%d" % (nkc - 2), 0)))
                            ksrc = KT[h * 128:(h + 1) * 128,
                                      part * 2048 + ch * 512: part * 2048 + (ch + 1) * 512]
                            p.dma("sync", (lambda s, _k=kbx, _s=ksrc: s.dma_start(
                                out=_k[:], in_=_s)), waits=[w for w in wt if w[1] > 0], inc=ksem)
                            kldv = p.cur(ksem)
                            for i in range(4):
                                ti = 4 * g + i
                                L = (ti + 1) * 128
                                w = min(512, L - ch * 512)
                                if w <= 0:
                                    continue
                                bank = pS[nsb % 2]
                                wt = [(ksem, kldv), ("qld", qldv)]
                                if nsb >= 2:
                                    wt.extend(se_bank[nsb % 2])
                                p.op("tensor", (lambda te, _b=bank, _k=kbx, _i=i, _w=w:
                                                te.matmul(_b[:, 0:_w],
                                                          qg[:, _i * 128:(_i + 1) * 128],
                                                          _k[:, 0:_w],
                                                          start=True, stop=True)),
                                     waits=wt, inc="mm")
                                mmv = p.cur("mm")
                                stripbase = 0 if part == 0 else L
                                dst0 = stripbase + ch * 512
                                diag = (w < 512) or (ch == NCH - 1 and i == 3)
                                strip = strips[i]
                                if diag:
                                    wfull = w - 128
                                    if wfull > 0:
                                        p.op("vector", (lambda v, _st=strip, _d=dst0, _w=wfull, _b=bank:
                                                        v.tensor_copy(_st[:, _d:_d + _w], _b[:, 0:_w])),
                                             waits=[("mm", mmv)], inc="sv")
                                    mk = tri if part == 0 else oma
                                    p.op("vector", (lambda v, _st=strip, _d=dst0 + wfull,
                                                    _b=bank, _w=wfull, _m=mk:
                                                    v.tensor_tensor(_st[:, _d:_d + 128],
                                                                    _b[:, _w:_w + 128],
                                                                    _m[:], ALU.add)),
                                         waits=[("mm", mmv)], inc="sv")
                                else:
                                    p.op("scalar", (lambda s, _st=strip, _d=dst0, _w=w, _b=bank:
                                                    s.copy(_st[:, _d:_d + _w], _b[:, 0:_w])),
                                         waits=[("mm", mmv)], inc="se")
                                se_bank[nsb % 2] = [("se", p.cur("se")), ("sv", p.cur("sv"))]
                                nsb += 1
                            p.sem["_kmm%d" % nkc] = p.cur("mm")
                            nkc += 1
                    xp_tile = {}
                    if ATT_MODE >= 2:
                        sev, svv = p.cur("se"), p.cur("sv")
                        for i in range(4):
                            ti = 4 * g + i
                            L2 = 2 * (ti + 1) * 128
                            strip = strips[i]
                            p.op("vector", (lambda v, _st=strip, _L=L2, _i=i: v.tensor_reduce(
                                st[:, 48 + _i:49 + _i], _st[:, 0:_L], AX.X, ALU.max,
                                negate=True)),
                                waits=[("se", sev), ("sv", svv)], inc="mx")
                            p.op("scalar", (lambda s, _st=strip, _e=estrips[i], _L=L2, _i=i:
                                            s.activation(
                                _e[:, 0:_L], _st[:, 0:_L], AF.Exp,
                                bias=st[:, 48 + _i:49 + _i],
                                accum_out=st[:, 52 + _i:53 + _i])),
                                waits=[("mx", p.cur("mx"))], inc="xp")
                            xp_tile[i] = p.cur("xp")
                            col = (4 * g + i) * H + h
                            p.op("vector", (lambda v, _i=i, _c=col:
                                            v.reciprocal(invl[:, _c:_c + 1],
                                                         st[:, 52 + _i:53 + _i])),
                                 waits=[("xp", p.cur("xp"))], inc="vst")
                        xpv = p.cur("xp")
                    if ATT_MODE >= 3:
                        chunks = [(part, kc) for part in range(2)
                                  for kc in range(4 * g + 4)]
                        ptv_done = {}
                        pe_done = {}
                        pv_done = {}

                        def emit_PT(j):
                            part, kc = chunks[j]
                            vs = max(0, kc - 4 * g)
                            ptb = pP[j % 2]
                            for i in range(vs, 4):
                                ti = 4 * g + i
                                L = (ti + 1) * 128
                                srccol = (0 if part == 0 else L) + kc * 128
                                wt = []
                                if j < 2:
                                    wt.append(("xp", xp_tile[i]))
                                if i == vs and j >= 2:
                                    wt.append(("pe", pe_done[j - 2]))
                                p.op("tensor", (lambda te, _i=i, _p=ptb,
                                                _st=estrips[i], _sc=srccol, _vs=vs:
                                                te.matmul(_p[:, _i * 128:(_i + 1) * 128],
                                                          _st[:, _sc:_sc + 128], identf[:],
                                                          is_transpose=True,
                                                          start=(_i == _vs), stop=(_i == 3))),
                                     waits=wt, inc=("pt" if i == 3 else None))
                            ptv_done[j] = p.cur("pt")

                        def emit_B(j, first_pv, ghi):
                            part, kc = chunks[j]
                            vs = max(0, kc - 4 * g)
                            ptb = pP[j % 2]
                            pvw = pv_done.get(j - 2, 0)
                            if vs > 0:
                                p.op("vector", (lambda v, _b=ptv[j % 2], _vs=vs:
                                                v.memset(_b[:, 0:_vs * 128], 0.0)),
                                     waits=[("pv", pvw)] if pvw else [])
                            p.op("vector", (lambda v, _p=ptb, _b=ptv[j % 2], _vs=vs:
                                            v.tensor_copy(_b[:, _vs * 128:512],
                                                          _p[:, _vs * 128:512])),
                                 waits=([("pt", ptv_done[j])]
                                        + ([("pv", pvw)] if pvw else [])),
                                 inc="pe")
                            pe_done[j] = p.cur("pe")
                            vbx = vbv[j % 2]
                            vsem = "de" if j % 2 == 0 else "df"
                            vsrc = VS[part * 2048 + kc * 128: part * 2048 + (kc + 1) * 128,
                                      h * 128:(h + 1) * 128]
                            wt = [("pv", pv_done.get(j - 2, 0))]
                            p.dma("sync", (lambda s_, _v=vbx, _s=vsrc: s_.dma_start(
                                out=_v[:], in_=_s)), waits=[w for w in wt if w[1] > 0],
                                inc=vsem)
                            wt = [("pe", pe_done[j]), (vsem, p.cur(vsem))]
                            if first_pv and ghi:
                                wt.append(("oe", p.cur("oe")))
                            p.op("tensor", (lambda te, _v=vbx, _b=ptv[j % 2],
                                            _st=first_pv,
                                            _sp=(part == 1 and kc == 4 * g + 3):
                                            te.matmul(pO[:], _v[:], _b[:],
                                                      start=_st, stop=_sp)),
                                 waits=wt, inc="pv")
                            pv_done[j] = p.cur("pv")
                            

                        emit_PT(0)
                        for j in range(len(chunks)):
                            if j + 1 < len(chunks):
                                emit_PT(j + 1)
                            emit_B(j, j == 0, ghi)
                    if ATT_MODE >= 4:
                        p.op("scalar", lambda s: s.copy(otm[:], pO[:]),
                             waits=[("pv", p.cur("pv")), ("ov", p.cur("ov"))], inc="oe")
                        for k in range(4):
                            wt = [("oe", p.cur("oe"))] if k == 0 else []
                            if ghi >= 1 and k == 0:
                                wt.append(("ov", p.cur("ov")))
                            p.op("tensor", (lambda te, _k=k: te.matmul(
                                pOt[:, _k * 128:(_k + 1) * 128], otm[:, _k * 128:(_k + 1) * 128],
                                identf[:], is_transpose=True, start=(_k == 0), stop=(_k == 3))),
                                waits=wt, inc=("ot" if k == 3 else None))
                        p.op("vector", (lambda v: v.tensor_copy(obv[:], pOt[:])),
                             waits=[("ot", p.cur("ot")), ("dg", p.cur("dg"))],
                             inc="ov")
                        dd = OS.rearrange("(t p) f -> p t f", p=128)[:, 4 * g:4 * g + 4,
                                                                    h * 128:(h + 1) * 128]
                        p.dma("gpsimd", (lambda g_, _d=dd: g_.dma_start(
                            out=_d, in_=obv[:].rearrange("p (i f) -> p i f", f=128))),
                            waits=[("ov", p.cur("ov"))], inc="dg")
            p.op("gpsimd", lambda g_: g_.engine_nop(),
                 waits=[("dg", p.cur("dg")), ("dc", p.cur("dc")),
                        ("dd", p.cur("dd"))], inc="gp")
            p.ops = interleave_ops(p.ops, p_ffnw.ops)
            emit(nc, p, sems)
        _ffnw_scope.close()

        with ExitStack() as _ph:
            _en = _ph.enter_context
            wpt = _en(nc.sbuf_tensor("wpt", [128, 16 * C], BF16))
            ob0 = _en(nc.sbuf_tensor("ob0", [128, C], F32))
            ob1 = _en(nc.sbuf_tensor("ob1", [128, C], F32))
            nb = _en(nc.sbuf_tensor("nb", [128, C], F32))
            yb2 = _en(nc.sbuf_tensor("yb2", [128, C], F32))
            mb2 = _en(nc.sbuf_tensor("mb2", [128, C], F32))
            ub2 = _en(nc.sbuf_tensor("ub2", [128, C], F32))
            oqb = _en(nc.sbuf_tensor("oqb", [128, C], BF16))
            oqT = _en(nc.sbuf_tensor("oqT", [128, C], BF16))
            prb = _en(nc.sbuf_tensor("prb", [128, C], F32))
            lnb = _en(nc.sbuf_tensor("lnb", [128, C], F32))
            xo0 = _en(nc.sbuf_tensor("xo0", [128, C], F32))
            xo1 = _en(nc.sbuf_tensor("xo1", [128, C], F32))
            xmb = _en(nc.sbuf_tensor("xmb", [128, C], F32))
            x2b = _en(nc.sbuf_tensor("x2b", [128, C], F32))
            hib = _en(nc.sbuf_tensor("hib", [128, C], BF16))
            lob = _en(nc.sbuf_tensor("lob", [128, C], BF16))
            hev = _en(nc.sbuf_tensor("hev", [128, C], BF16))
            lev = _en(nc.sbuf_tensor("lev", [128, C], BF16))
            pqt = _en(nc.psum_tensor("pqt", [128, C], BF16))
            ppr0 = _en(nc.psum_tensor("ppr0", [128, 512], F32))
            ppr1 = _en(nc.psum_tensor("ppr1", [128, 512], F32))
            phl0 = _en(nc.psum_tensor("phl0", [128, C], BF16))
            phl1 = _en(nc.psum_tensor("phl1", [128, C], BF16))
            obx = [ob0, ob1]
            xox = [xo0, xo1]
            ppr = [ppr0, ppr1]
            p = Plan(semc)
            p.dma("sync", lambda s: s.dma_start(
                out=wpt[:].rearrange("p (k f) -> p k f", f=C),
                in_=WPROJT.rearrange("(k p) f -> p k f", p=128)), inc="wld")
            wptv = p.cur("wld")
            for t in range(OT):
                ob = obx[t % 2]
                xo = xox[t % 2]
                osem = "da" if t % 2 == 0 else "db"
                xsem = "de" if t % 2 == 0 else "df"
                waits = [("prx", p.cur("prx") - 1)] if t >= 2 else []
                p.dma("sync", (lambda s, _b=ob, _t=t: s.dma_start(
                    out=_b[:], in_=OS[_t * 128:(_t + 1) * 128, :])), waits=waits, inc=osem)
                osv = p.cur(osem)
                p.dma("sync", (lambda s, _b=xo, _t=t: s.dma_start(
                    out=_b[:], in_=xp_d[_t * 128:(_t + 1) * 128, :])),
                    waits=list(waits), inc=xsem)
                xsv = p.cur(xsem)
                for hh in range(H):
                    wt = [(osem, osv)] if hh == 0 else []
                    if hh == 0 and t >= 2:
                        wt.append(("xq", p.cur("xq")))
                    p.op("scalar", (lambda s, _h=hh, _t=t, _b=ob: s.activation(
                        nb[:, _h * 128:(_h + 1) * 128],
                        _b[:, _h * 128:(_h + 1) * 128],
                        AF.Identity, scale=invl[:, _t * H + _h:_t * H + _h + 1])),
                        waits=wt, inc=("y1" if hh == H - 1 else None))
                y1v = p.cur("y1")
                o = (t % 2) * 32
                p.op("vector", (lambda v, _o=o: v.tensor_reduce(
                    st[:, _o:_o + 1], nb[:], AX.X, ALU.max, apply_absolute_value=True)),
                    waits=[("y1", y1v)], inc="vst")
                p.op("vector", (lambda v, _o=o: v.tensor_scalar(
                    st[:, _o + 1:_o + 2], st[:, _o:_o + 1], 1e-5, 1.0 / 127.0,
                    op0=ALU.max, op1=ALU.mult)), inc="vst")
                p.op("vector", (lambda v, _o=o: v.reciprocal(
                    st[:, _o + 2:_o + 3], st[:, _o + 1:_o + 2])), inc="vst")
                p.op("vector", (lambda v, _o=o: v.tensor_scalar(
                    st[:, _o + 3:_o + 4], st[:, _o + 1:_o + 2], bcw[:, 3:4], None,
                    op0=ALU.mult)), inc="vst")
                p.op("scalar", (lambda s, _o=o: s.activation(
                    yb2[:], nb[:], AF.Identity, bias=0.0, scale=st[:, _o + 2:_o + 3])),
                    waits=[("vst", p.cur("vst"))], inc="y1")
                p.op("vector", (lambda v: v.tensor_scalar(
                    ub2[:], yb2[:], M32, -M32, op0=ALU.add, op1=ALU.add)),
                    waits=[("y1", p.cur("y1"))], inc="vst")
                p.op("vector", (lambda v: v.tensor_scalar(
                    mb2[:], yb2[:], ZSH, M64, op0=ALU.add, op1=ALU.add)), inc="vst")
                p.op("vector", (lambda v: v.tensor_scalar(
                    mb2[:], mb2[:], -M64, -ZSH, op0=ALU.add, op1=ALU.add)), inc="vst")
                p.op("vector", (lambda v: v.scalar_tensor_tensor(
                    yb2[:], yb2[:], -1.0, mb2[:], ALU.mult, ALU.add)), inc="vst")
                p.op("vector", (lambda v: v.tensor_scalar(
                    mb2[:], yb2[:], -0.5, 0.5, op0=ALU.max, op1=ALU.min)), inc="vst")
                p.op("vector", (lambda v: v.tensor_scalar(
                    yb2[:], yb2[:], 1.0, -0.5, op0=ALU.add, op1=ALU.max)), inc="vst")
                p.op("vector", (lambda v: v.scalar_tensor_tensor(
                    yb2[:], yb2[:], 0.0, mb2[:], ALU.min, ALU.subtract)), inc="vst")
                p.op("vector", (lambda v: v.scalar_tensor_tensor(
                    oqb[:], ub2[:], 0.0, yb2[:], ALU.add, ALU.subtract)), inc="xq")
                for k in range(16):
                    wt = []
                    if k == 0:
                        wt.append(("xq", p.cur("xq")))
                        if t >= 1:
                            wt.append(("tev", p.cur("tev")))
                    p.op("tensor", (lambda te, _k=k: te.matmul(
                        pqt[:, _k * 128:(_k + 1) * 128], oqb[:, _k * 128:(_k + 1) * 128],
                        identb[:], is_transpose=True,
                        start=(_k % 8 == 0), stop=(_k % 8 == 7))),
                        waits=wt, inc=("tp" if k == 15 else None))
                p.op("vector", (lambda v: v.tensor_copy(oqT[:], pqt[:])),
                     waits=[("tp", p.cur("tp"))], inc="tev")
                tevv = p.cur("tev")
                for j in range(4):
                    bank = ppr[j % 2]
                    for k in range(16):
                        wt = []
                        if k == 0:
                            wt.append(("tev", tevv))
                            wt.append(("wld", wptv))
                            if t * 4 + j >= 2:
                                wt.append(("ev", p.cur("ev") - 1))
                        p.op("tensor", (lambda te, _k=k, _j=j, _b=bank: te.matmul(
                            _b[:], oqT[:, _k * 128:(_k + 1) * 128],
                            wpt[:, _k * C + _j * 512:_k * C + (_j + 1) * 512],
                            start=(_k == 0), stop=(_k == 15))),
                            waits=wt, inc=("mm" if k == 15 else None))
                    p.op("scalar", (lambda s, _j=j, _b=bank, _o=o: s.activation(
                        prb[:, _j * 512:(_j + 1) * 512], _b[:], AF.Identity,
                        scale=st[:, _o + 3:_o + 4])),
                        waits=[("mm", p.cur("mm"))], inc="ev")
                evv = p.cur("ev")
                p.op("vector", (lambda v, _o=o: v.tensor_reduce(
                    st[:, _o + 4:_o + 5], prb[:], AX.X, ALU.add, negate=True)),
                    waits=[("ev", evv)], inc="vst")
                p.op("scalar", (lambda s, _o=o: s.activation(
                    mb2[:], prb[:], AF.Square, accum_out=st[:, _o + 5:_o + 6])),
                    waits=[("ev", evv)], inc="sq")
                p.op("vector", (lambda v, _o=o: v.tensor_scalar(
                    st[:, _o + 6:_o + 7], st[:, _o + 4:_o + 5], 1.0 / C, None,
                    op0=ALU.mult)), inc="vst")
                p.op("vector", (lambda v, _o=o: v.tensor_scalar(
                    st[:, _o + 7:_o + 8], st[:, _o + 5:_o + 6], 1.0 / C, EPS,
                    op0=ALU.mult, op1=ALU.add)),
                    waits=[("sq", p.cur("sq"))], inc="vst")
                p.op("vector", (lambda v, _o=o: v.scalar_tensor_tensor(
                    st[:, _o + 8:_o + 9], st[:, _o + 6:_o + 7], -1.0,
                    st[:, _o + 6:_o + 7], ALU.mult, ALU.mult)), inc="vst")
                p.op("vector", (lambda v, _o=o: v.tensor_tensor(
                    st[:, _o + 9:_o + 10], st[:, _o + 7:_o + 8],
                    st[:, _o + 8:_o + 9], ALU.add)), inc="vst")
                p.op("scalar", (lambda s, _o=o: s.activation(
                    st[:, _o + 10:_o + 11], st[:, _o + 9:_o + 10], AF.Sqrt)),
                    waits=[("vst", p.cur("vst"))], inc="sq")
                p.op("vector", (lambda v, _o=o: v.reciprocal(
                    st[:, _o + 11:_o + 12], st[:, _o + 10:_o + 11])),
                    waits=[("sq", p.cur("sq"))], inc="vst")
                p.op("vector", (lambda v, _o=o: v.tensor_tensor(
                    st[:, _o + 12:_o + 13], st[:, _o + 6:_o + 7],
                    st[:, _o + 11:_o + 12], ALU.mult)), inc="vst")
                p.op("scalar", (lambda s, _o=o: s.activation(
                    lnb[:], prb[:], AF.Identity, bias=st[:, _o + 12:_o + 13],
                    scale=st[:, _o + 11:_o + 12])),
                    waits=[("vst", p.cur("vst"))], inc="y1")
                xw = [("y1", p.cur("y1")), (xsem, xsv)]
                if t >= 1:
                    xw.append(("dg", p.sem.get("_xmidsp%d" % (t - 1), 0)))
                p.op("vector", (lambda v, _x=xo: v.tensor_tensor(
                    xmb[:], _x[:], lnb[:], ALU.add)),
                    waits=xw, inc="fx")
                fxv = p.cur("fx")
                p.dma("gpsimd", (lambda g, _t=t: g.dma_start(
                    out=XMID[_t * 128:(_t + 1) * 128, :], in_=xmb[:])),
                    waits=[("fx", fxv)], inc="dg")
                p.sem["_xmidsp%d" % t] = p.cur("dg")
                p.op("vector", (lambda v, _o=o: v.tensor_reduce(
                    st[:, _o + 13:_o + 14], xmb[:], AX.X, ALU.add, negate=True)),
                    waits=[("fx", fxv)], inc="vst")
                p.op("scalar", (lambda s, _o=o: s.activation(
                    mb2[:], xmb[:], AF.Square, accum_out=st[:, _o + 14:_o + 15])),
                    waits=[("fx", fxv)], inc="sq")
                p.op("vector", (lambda v, _o=o: v.tensor_scalar(
                    st[:, _o + 15:_o + 16], st[:, _o + 13:_o + 14], 1.0 / C, None,
                    op0=ALU.mult)), inc="vst")
                p.op("vector", (lambda v, _o=o: v.tensor_scalar(
                    st[:, _o + 16:_o + 17], st[:, _o + 14:_o + 15], 1.0 / C, EPS,
                    op0=ALU.mult, op1=ALU.add)),
                    waits=[("sq", p.cur("sq"))], inc="vst")
                p.op("vector", (lambda v, _o=o: v.scalar_tensor_tensor(
                    st[:, _o + 17:_o + 18], st[:, _o + 15:_o + 16], -1.0,
                    st[:, _o + 15:_o + 16], ALU.mult, ALU.mult)), inc="vst")
                p.op("vector", (lambda v, _o=o: v.tensor_tensor(
                    st[:, _o + 18:_o + 19], st[:, _o + 16:_o + 17],
                    st[:, _o + 17:_o + 18], ALU.add)), inc="vst")
                p.op("scalar", (lambda s, _o=o: s.activation(
                    st[:, _o + 19:_o + 20], st[:, _o + 18:_o + 19], AF.Sqrt)),
                    waits=[("vst", p.cur("vst"))], inc="sq")
                p.op("vector", (lambda v, _o=o: v.reciprocal(
                    st[:, _o + 20:_o + 21], st[:, _o + 19:_o + 20])),
                    waits=[("sq", p.cur("sq"))], inc="vst")
                p.op("vector", (lambda v, _o=o: v.tensor_tensor(
                    st[:, _o + 21:_o + 22], st[:, _o + 15:_o + 16],
                    st[:, _o + 20:_o + 21], ALU.mult)), inc="vst")
                p.op("scalar", (lambda s, _o=o: s.activation(
                    x2b[:], xmb[:], AF.Identity, bias=st[:, _o + 21:_o + 22],
                    scale=st[:, _o + 20:_o + 21])),
                    waits=[("vst", p.cur("vst"))], inc="y1")
                y1v2 = p.cur("y1")
                hw = [("y1", y1v2)]
                if t >= 1:
                    hw.append(("tp", p.sem.get("_hltp%d" % (t - 1), 0)))
                p.op("vector", (lambda v: v.tensor_copy(hib[:], x2b[:])),
                     waits=hw, inc="vst")
                p.op("vector", (lambda v: v.scalar_tensor_tensor(
                    lob[:], hib[:], -1.0, x2b[:], ALU.mult, ALU.add)), inc="hst")
                hstv = p.cur("hst")
                for (src_, psx, evb, dstT, ssem) in ((hib, phl0, hev, XHs, "dc"),
                                                     (lob, phl1, lev, XLs, "dd")):
                    for k in range(16):
                        wt = []
                        if k == 0:
                            wt.append(("hst", hstv))
                            if t >= 1:
                                wt.append(("xev", p.cur("xev")))
                        p.op("tensor", (lambda te, _k=k, _s=src_, _p=psx: te.matmul(
                            _p[:, _k * 128:(_k + 1) * 128], _s[:, _k * 128:(_k + 1) * 128],
                            identb[:], is_transpose=True,
                            start=(_k % 8 == 0), stop=(_k % 8 == 7))),
                            waits=wt, inc=("tp" if k == 15 else None))
                    p.op("vector", (lambda v, _p=psx, _e=evb: v.tensor_copy(_e[:], _p[:])),
                         waits=[("tp", p.cur("tp")), (ssem, p.cur(ssem))],
                         inc="xev")
                    dd = dstT.rearrange("(k p) r -> p k r", p=128)[:, :, t * 128:(t + 1) * 128]
                    p.dma("gpsimd", (lambda g, _d=dd, _e=evb: g.dma_start(
                        out=_d, in_=_e[:].rearrange("p (k j) -> p k j", j=128))),
                        waits=[("xev", p.cur("xev"))], inc=ssem)
                p.sem["_hltp%d" % t] = p.cur("tp")
                p.op("vector", lambda v: v.engine_nop(), waits=[], inc="prx")
            p.op("gpsimd", lambda g: g.engine_nop(),
                 waits=[("dg", p.cur("dg")), ("dc", p.cur("dc")), ("dd", p.cur("dd"))],
                 inc="gp")
            emit(nc, p, sems)

        with ExitStack() as _ph:
            _en = _ph.enter_context
            xh = _en(nc.sbuf_tensor("xh", [128, 16 * 512], BF16))
            xl = _en(nc.sbuf_tensor("xl", [128, 16 * 512], BF16))
            sh = _en(nc.sbuf_tensor("sh", [128, 44 * 512], BF16))
            sl = _en(nc.sbuf_tensor("sl", [128, 44 * 512], BF16))
            wgb0 = _en(nc.sbuf_tensor("wgb0", [128, 16 * 128], BF16))
            wgb1 = _en(nc.sbuf_tensor("wgb1", [128, 16 * 128], BF16))
            wub0 = _en(nc.sbuf_tensor("wub0", [128, 16 * 128], BF16))
            wub1 = _en(nc.sbuf_tensor("wub1", [128, 16 * 128], BF16))
            wdb0 = _en(nc.sbuf_tensor("wdb0", [128, 44 * 128], BF16))
            wdb1 = _en(nc.sbuf_tensor("wdb1", [128, 44 * 128], BF16))
            sgb0 = _en(nc.sbuf_tensor("sgb0", [128, 512], F32))
            sgb1 = _en(nc.sbuf_tensor("sgb1", [128, 512], F32))
            sub0 = _en(nc.sbuf_tensor("sub0", [128, 512], F32))
            sub1 = _en(nc.sbuf_tensor("sub1", [128, 512], F32))
            hid = _en(nc.sbuf_tensor("hid", [128, 512], F32))
            dob = _en(nc.sbuf_tensor("dob", [128, 512], F32))
            tdb = _en(nc.sbuf_tensor("tdb", [128, 512], F32))
            pgg0 = _en(nc.psum_tensor("pgg0", [128, 512], F32))
            pgg1 = _en(nc.psum_tensor("pgg1", [128, 512], F32))
            puu0 = _en(nc.psum_tensor("puu0", [128, 512], F32))
            puu1 = _en(nc.psum_tensor("puu1", [128, 512], F32))
            pdd0 = _en(nc.psum_tensor("pdd0", [128, 512], F32))
            pdd1 = _en(nc.psum_tensor("pdd1", [128, 512], F32))
            pdt = _en(nc.psum_tensor("pdt", [128, 512], F32))
            wgb = [wgb0, wgb1]
            wub = [wub0, wub1]
            wdb = [wdb0, wdb1]
            pdd = [pdd0, pdd1]
            pgg = [pgg0, pgg1]
            puu = [puu0, puu1]
            sgbv = [sgb0, sgb1]
            subv = [sub0, sub1]
            p = Plan(semc)
            gev = {0: 0, 1: 0}
            hmul = {0: 0, 1: 0}
            p.sem.setdefault("vst", 0)
            for q in range(4):
                p.dma("sync", (lambda s, _q=q: s.dma_start(
                    out=xh[:].rearrange("p (k r) -> p k r", r=512),
                    in_=XHs.rearrange("(k p) r -> p k r", p=128)[:, :, _q * 512:(_q + 1) * 512])),
                    waits=[("dn", p.cur("dn"))], inc="da")
                p.dma("sync", (lambda s, _q=q: s.dma_start(
                    out=xl[:].rearrange("p (k r) -> p k r", r=512),
                    in_=XLs.rearrange("(k p) r -> p k r", p=128)[:, :, _q * 512:(_q + 1) * 512])),
                    waits=[("dn", p.cur("dn"))], inc="db")
                xldv = [("da", p.cur("da")), ("db", p.cur("db"))]
                for ht in range(44):
                    wgx = wgb[ht % 2]
                    wux = wub[ht % 2]
                    gsem = "de" if ht % 2 == 0 else "df"
                    usem = "dh" if ht % 2 == 0 else "di"
                    wt = []
                    if gev[ht % 2]:
                        wt.append(("ev", gev[ht % 2]))
                    p.dma("sync", (lambda s, _w=wgx, _h=ht: s.dma_start(
                        out=_w[:].rearrange("p (k j) -> p k j", j=128),
                        in_=WGT.rearrange("(k p) f -> p k f", p=128)[:, :, _h * 128:(_h + 1) * 128])),
                        waits=wt, inc=gsem)
                    p.dma("sync", (lambda s, _w=wux, _h=ht: s.dma_start(
                        out=_w[:].rearrange("p (k j) -> p k j", j=128),
                        in_=WUT.rearrange("(k p) f -> p k f", p=128)[:, :, _h * 128:(_h + 1) * 128])),
                        waits=list(wt), inc=usem)
                    wglv, wulv = p.cur(gsem), p.cur(usem)
                    par = ht % 2
                    for (wsem, wv, wbx, bank, name) in ((gsem, wglv, wgx, pgg[par], "g"),
                                                        (usem, wulv, wux, puu[par], "u")):
                        for z in range(32):
                            k, hl = z % 16, z // 16
                            rhs = xh if hl == 0 else xl
                            wt = []
                            if z == 0:
                                wt.append((wsem, wv))
                                wt.extend(xldv)
                                if gev[par]:
                                    wt.append(("ev", gev[par]))
                            p.op("tensor", (lambda te, _k=k, _w=wbx, _r=rhs, _b=bank:
                                            te.matmul(_b[:], _w[:, _k * 128:(_k + 1) * 128],
                                                      _r[:, _k * 512:(_k + 1) * 512],
                                                      start=(_k == 0 and _r is xh),
                                                      stop=(_k == 15 and _r is xl))),
                                 waits=wt, inc=("mm" if z == 31 else None))
                        wt = [("mm", p.cur("mm"))]
                        if hmul[par]:
                            wt.append(("vst", hmul[par]))
                        if name == "g":
                            p.op("scalar", (lambda s, _h=ht, _b=bank, _o=sgbv[par]:
                                            s.activation(
                                _o[:], _b[:], AF.Silu, scale=sgv[:, _h:_h + 1])),
                                waits=wt, inc="ev")
                        else:
                            p.op("scalar", (lambda s, _h=ht, _b=bank, _o=subv[par]:
                                            s.activation(
                                _o[:], _b[:], AF.Identity, scale=suv[:, _h:_h + 1])),
                                waits=wt, inc="ev")
                    gev[par] = p.cur("ev")
                    p.op("vector", (lambda v, _g=sgbv[par], _u=subv[par]:
                                    v.tensor_tensor(hid[:], _g[:], _u[:], ALU.mult)),
                        waits=[("ev", p.cur("ev"))], inc="vst")
                    hmul[par] = p.cur("vst")
                    p.op("vector", (lambda v, _h=ht: v.tensor_copy(
                        sh[:, _h * 512:(_h + 1) * 512], hid[:])), inc="vst")
                    p.op("vector", (lambda v, _h=ht: v.scalar_tensor_tensor(
                        sl[:, _h * 512:(_h + 1) * 512], sh[:, _h * 512:(_h + 1) * 512],
                        -1.0, hid[:], ALU.mult, ALU.add)), inc="gu")
                guv = p.cur("gu")
                for dt in range(16):
                    wdx = wdb[dt % 2]
                    dsem = "dj" if dt % 2 == 0 else "dk"
                    wt = [("dnv", p.cur("dnv") - 1)] if (q * 16 + dt) >= 2 else []
                    p.dma("sync", (lambda s, _w=wdx, _d=dt: s.dma_start(
                        out=_w[:].rearrange("p (k j) -> p k j", j=128),
                        in_=WDT.rearrange("(k p) f -> p k f", p=128)[:, :, _d * 128:(_d + 1) * 128])),
                        waits=wt, inc=dsem)
                    wdlv = p.cur(dsem)
                    bank = pdd[dt % 2]
                    for z in range(88):
                        k, hl = z % 44, z // 44
                        rhs = sh if hl == 0 else sl
                        wt = []
                        if z == 0:
                            wt.append((dsem, wdlv))
                            wt.append(("gu", guv))
                            if q * 16 + dt >= 2:
                                wt.append(("dnv", p.cur("dnv") - 1))
                        p.op("tensor", (lambda te, _k=k, _w=wdx, _r=rhs, _b=bank:
                                        te.matmul(_b[:], _w[:, _k * 128:(_k + 1) * 128],
                                                  _r[:, _k * 512:(_k + 1) * 512],
                                                  start=(_k == 0 and _r is sh),
                                                  stop=(_k == 43 and _r is sl))),
                             waits=wt, inc=("mm" if z == 87 else None))
                    p.op("scalar", (lambda s, _d=dt, _b=bank: s.activation(
                        dob[:], _b[:], AF.Identity, scale=sdv[:, _d:_d + 1])),
                        waits=[("mm", p.cur("mm")), ("ot", p.cur("ot"))], inc="dnv")
                    for k in range(4):
                        wt = []
                        if k == 0:
                            wt.append(("dnv", p.cur("dnv")))
                            wt.append(("ov", p.cur("ov")))
                        p.op("tensor", (lambda te, _k=k: te.matmul(
                            pdt[:, _k * 128:(_k + 1) * 128], dob[:, _k * 128:(_k + 1) * 128],
                            identf[:], is_transpose=True, start=(_k == 0), stop=(_k == 3))),
                            waits=wt, inc=("ot" if k == 3 else None))
                    p.op("vector", (lambda v: v.tensor_copy(tdb[:], pdt[:])),
                         waits=[("ot", p.cur("ot")), ("dg", p.cur("dg"))], inc="ov")
                    dd = FFO.rearrange("(t p) f -> p t f", p=128)[:, q * 4:q * 4 + 4,
                                                                 dt * 128:(dt + 1) * 128]
                    p.dma("gpsimd", (lambda g, _d=dd: g.dma_start(
                        out=_d, in_=tdb[:].rearrange("p (i f) -> p i f", f=128))),
                        waits=[("ov", p.cur("ov"))], inc="dg")
                p.op("vector", lambda v: v.engine_nop(),
                     waits=[("ov", p.cur("ov"))], inc="dn")
            p.op("gpsimd", lambda g: g.engine_nop(),
                 waits=[("dg", p.cur("dg"))], inc="gp")
            emit(nc, p, sems)

        with ExitStack() as _ph:
            _en = _ph.enter_context
            fb0 = _en(nc.sbuf_tensor("fb0", [128, C], F32))
            fb1 = _en(nc.sbuf_tensor("fb1", [128, C], F32))
            xm0 = _en(nc.sbuf_tensor("xm0", [128, C], F32))
            xm1 = _en(nc.sbuf_tensor("xm1", [128, C], F32))
            sq2 = _en(nc.sbuf_tensor("sq2", [128, C], F32))
            ln5 = _en(nc.sbuf_tensor("ln5", [128, C], F32))
            yo0 = _en(nc.sbuf_tensor("yo0", [128, C], F32))
            yo1 = _en(nc.sbuf_tensor("yo1", [128, C], F32))
            fbx = [fb0, fb1]
            xmx = [xm0, xm1]
            yox = [yo0, yo1]
            p = Plan(semc)
            for t in range(OT):
                fb = fbx[t % 2]
                xm = xmx[t % 2]
                yo = yox[t % 2]
                fsem = "da" if t % 2 == 0 else "db"
                msem = "de" if t % 2 == 0 else "df"
                waits = [("fx", p.cur("fx") - 1)] if t >= 2 else []
                p.dma("sync", (lambda s, _b=fb, _t=t: s.dma_start(
                    out=_b[:], in_=FFO[_t * 128:(_t + 1) * 128, :])), waits=waits, inc=fsem)
                fsv = p.cur(fsem)
                p.dma("sync", (lambda s, _b=xm, _t=t: s.dma_start(
                    out=_b[:], in_=XMID[_t * 128:(_t + 1) * 128, :])),
                    waits=list(waits), inc=msem)
                msv = p.cur(msem)
                o = (t % 2) * 32
                p.op("vector", (lambda v, _b=fb, _o=o: v.tensor_reduce(
                    st[:, _o:_o + 1], _b[:], AX.X, ALU.add, negate=True)),
                    waits=[(fsem, fsv)], inc="vst")
                p.op("scalar", (lambda s, _b=fb, _o=o: s.activation(
                    sq2[:], _b[:], AF.Square, accum_out=st[:, _o + 1:_o + 2])),
                    waits=[(fsem, fsv)], inc="sq")
                p.op("vector", (lambda v, _o=o: v.tensor_scalar(
                    st[:, _o + 2:_o + 3], st[:, _o:_o + 1], 1.0 / C, None,
                    op0=ALU.mult)), inc="vst")
                p.op("vector", (lambda v, _o=o: v.tensor_scalar(
                    st[:, _o + 3:_o + 4], st[:, _o + 1:_o + 2], 1.0 / C, EPS,
                    op0=ALU.mult, op1=ALU.add)),
                    waits=[("sq", p.cur("sq"))], inc="vst")
                p.op("vector", (lambda v, _o=o: v.scalar_tensor_tensor(
                    st[:, _o + 4:_o + 5], st[:, _o + 2:_o + 3], -1.0,
                    st[:, _o + 2:_o + 3], ALU.mult, ALU.mult)), inc="vst")
                p.op("vector", (lambda v, _o=o: v.tensor_tensor(
                    st[:, _o + 5:_o + 6], st[:, _o + 3:_o + 4],
                    st[:, _o + 4:_o + 5], ALU.add)), inc="vst")
                p.op("scalar", (lambda s, _o=o: s.activation(
                    st[:, _o + 6:_o + 7], st[:, _o + 5:_o + 6], AF.Sqrt)),
                    waits=[("vst", p.cur("vst"))], inc="sq")
                p.op("vector", (lambda v, _o=o: v.reciprocal(
                    st[:, _o + 7:_o + 8], st[:, _o + 6:_o + 7])),
                    waits=[("sq", p.cur("sq"))], inc="vst")
                p.op("vector", (lambda v, _o=o: v.tensor_tensor(
                    st[:, _o + 8:_o + 9], st[:, _o + 2:_o + 3],
                    st[:, _o + 7:_o + 8], ALU.mult)), inc="vst")
                p.op("scalar", (lambda s, _b=fb, _o=o: s.activation(
                    ln5[:], _b[:], AF.Identity, bias=st[:, _o + 8:_o + 9],
                    scale=st[:, _o + 7:_o + 8])),
                    waits=[("vst", p.cur("vst"))], inc="y1")
                ysem = "dc" if t % 2 == 0 else "dd"
                yw = [("y1", p.cur("y1")), (msem, msv)]
                if t >= 2:
                    yw.append((ysem, p.cur(ysem)))
                p.op("vector", (lambda v, _x=xm, _y=yo: v.tensor_tensor(
                    _y[:], _x[:], ln5[:], ALU.add)),
                    waits=yw, inc="fx")
                p.dma("gpsimd", (lambda g, _y=yo, _t=t: g.dma_start(
                    out=y_d[_t * 128:(_t + 1) * 128, :], in_=_y[:])),
                    waits=[("fx", p.cur("fx"))], inc=ysem)
            for nm in DEBUG_DUMPS:
                p.dma("gpsimd", (lambda g, _nm=nm: g.dma_start(
                    out=dbg[_nm], in_=scr[_nm])), inc="out")
            p.op("gpsimd", lambda g: g.engine_nop(),
                 waits=[("dc", p.cur("dc")), ("dd", p.cur("dd")),
                        ("out", p.cur("out"))], inc="gp")
            emit(nc, p, sems)

    if PHASE_LIMIT < 9:
        with nc.sbuf_tensor("zout", [128, C], F32) as zo, \
             nc.semaphore("zs") as zs, nc.Block() as blk:
            @blk.gpsimd
            def _(g):
                g.memset(zo[:], 0.0).then_inc(zs, 1)
                g.wait_ge(zs, 1)
                for t in range(OT):
                    g.dma_start(out=y_d[t * 128:(t + 1) * 128, :],
                                in_=zo[:]).then_inc(zs, 16)
                g.wait_ge(zs, 1 + 16 * OT)

    nc.finalize()
    return nc


def kernel(L0_attn_x, L0_attn_w_qkv, L0_attn_w_proj, L0_ffn_wq_gate,
           L0_ffn_s_gate, L0_ffn_wq_up, L0_ffn_s_up, L0_ffn_wq_down,
           L0_ffn_s_down):
    x = np.ascontiguousarray(np.asarray(L0_attn_x, dtype=np.float32))
    wq = np.ascontiguousarray(np.asarray(L0_attn_w_qkv, dtype=np.float32))
    wp = np.ascontiguousarray(np.asarray(L0_attn_w_proj, dtype=np.float32))
    wg = np.ascontiguousarray(np.asarray(L0_ffn_wq_gate, dtype=np.int32))
    wu = np.ascontiguousarray(np.asarray(L0_ffn_wq_up, dtype=np.int32))
    wd = np.ascontiguousarray(np.asarray(L0_ffn_wq_down, dtype=np.int32))
    sg = np.ascontiguousarray(np.asarray(L0_ffn_s_gate, dtype=np.float32))
    su = np.ascontiguousarray(np.asarray(L0_ffn_s_up, dtype=np.float32))
    sd = np.ascontiguousarray(np.asarray(L0_ffn_s_down, dtype=np.float32))
    ident = np.eye(128, dtype=np.float32)
    trim = (np.triu(np.ones((128, 128), dtype=np.float32), 1) * NEG).astype(np.float32)

    nc = build_program()
    in_maps = []
    orders = []
    for c in range(8):
        b, par = c // 2, c % 2
        order = list(range(par, NT, 2)) + list(range(1 - par, NT, 2))
        orders.append(order)
        xp = np.concatenate([x[b, t * 128:(t + 1) * 128, :] for t in order], axis=0)
        oma = (np.zeros((128, 128), np.float32) if par == 1 else
               np.full((128, 128), NEG, np.float32))
        in_maps.append(dict(xp=np.ascontiguousarray(xp), wqkv=wq, wproj=wp,
                            wg=wg, wu=wu, wd=wd, sg=sg, su=su, sd=sd,
                            ident=ident, tri=trim, oma=oma))
    res = run_bass_kernel_spmd(nc, in_maps, core_ids=list(range(8)))
    y = np.empty((4, NTOK, C), np.float32)
    for c in range(8):
        b, par = c // 2, c % 2
        yc = np.asarray(res.results[c]["y"])
        for i, t in enumerate(range(par, NT, 2)):
            y[b, t * 128:(t + 1) * 128, :] = yc[i * 128:(i + 1) * 128, :]
    return y
